# version 03
# baseline (speedup 1.0000x reference)
import numpy as np
import concourse.bass as bass
import concourse.mybir as mybir
from concourse.bass_utils import run_bass_kernel_spmd
from contextlib import ExitStack

F32 = mybir.dt.float32
BF16 = mybir.dt.bfloat16
I32 = mybir.dt.int32
AF = mybir.ActivationFunctionType
ALU = mybir.AluOpType
AX = mybir.AxisListType

TLOC = 2048
D = 1024
DC = D // 128
TT = 512
NTT = TLOC // TT
N_CORES = 8
_RD = False



class Sem:
    __slots__ = ("h", "n", "name")

    def __init__(self, h, name):
        self.h = h
        self.n = 0
        self.name = name


class KB:
    ENGINES = ("tensor", "vector", "scalar", "gpsimd", "sync")

    def __init__(self, nc, stack: ExitStack):
        self.nc = nc
        self.stack = stack
        self.q = {e: [] for e in self.ENGINES}
        self.sems = {}
        self.sbuf = {}
        self.psum = {}

    def sem(self, name) -> Sem:
        if name not in self.sems:
            h = self.stack.enter_context(self.nc.semaphore(name))
            self.sems[name] = Sem(h, name)
        return self.sems[name]

    def sb(self, name, shape=None, dtype=None):
        if name not in self.sbuf:
            self.sbuf[name] = self.stack.enter_context(
                self.nc.sbuf_tensor(name, shape, dtype))
        return self.sbuf[name]

    def ps(self, name, shape=None, dtype=F32):
        if name not in self.psum:
            self.psum[name] = self.stack.enter_context(
                self.nc.psum_tensor(name, shape, dtype))
        return self.psum[name]

    def on(self, eng, fn):
        self.q[eng].append(fn)

    def wait(self, eng, sem: Sem, val=None):
        v = sem.n if val is None else val
        if v > 0:
            self.on(eng, lambda e, h=sem.h, v=v: e.wait_ge(h, v))

    def op(self, eng, fn, sig: Sem = None, inc=1):
        if sig is not None:
            sig.n += inc
            self.on(eng, lambda e, fn=fn, h=sig.h, inc=inc: fn(e).then_inc(h, inc))
        else:
            self.on(eng, fn)

    def chain(self, eng, fns, sig: Sem = None, inc=1):
        s = self.sem(eng + "_self")
        for i, fn in enumerate(fns):
            last = i == len(fns) - 1
            self.op(eng, fn, sig=(sig if last and sig is not None else s),
                    inc=(inc if last and sig is not None else 1))
            if not last:
                self.wait(eng, s)

    def dma(self, eng, out, in_, sig: Sem):
        self.op(eng, lambda e, out=out, in_=in_: e.dma_start(out=out, in_=in_),
                sig, 16)

    def play(self):
        block = self.stack.enter_context(self.nc.Block())
        for eng in self.ENGINES:
            thunks = self.q[eng]
            if not thunks:
                continue
            getattr(block, eng)(lambda e, thunks=thunks: [f(e) for f in thunks])


def emit_consts(k: KB, consts=()):
    oc = k.sb("ones_col", [128, 1], F32)
    orow = k.sb("ones_row", [1, 128], F32)
    s = k.sem("consts")
    orbf = k.sb("ones_row_bf", [1, 128], BF16)
    ocbf = k.sb("ones_col_bf", [128, 1], BF16)
    k.op("gpsimd", lambda e: e.memset(oc[:], 1.0))
    k.op("gpsimd", lambda e: e.memset(orbf[:], 1.0))
    k.op("gpsimd", lambda e: e.memset(ocbf[:], 1.0))
    for name, val in dict(consts).items():
        t = k.sb("c_" + name, [1, 1], F32)
        k.op("gpsimd", lambda e, t=t, val=val: e.memset(t[:], val))
    k.op("gpsimd", lambda e: e.memset(orow[:], 1.0), s)
    for eng in ("tensor", "scalar", "vector"):
        k.wait(eng, s)
    return s



def emit_ln_tile(k: KB, x_ap, xn_ap, tt=TT, eps_name="eps_ln", nchunks=DC,
                 pre_apply_waits=(), affine=None):
    sq = k.sb("ln_sq", [128, TT], F32)
    apt = k.sb("vscratch", [128, TT], F32)
    rows = k.sb("ln_rows", [1, 4 * TT], F32)
    ps_s1 = k.ps("ln_ps_s1", [1, TT])
    ps_s2 = k.ps("ln_ps_s2", [1, TT])
    ps_b0 = k.ps("ln_ps_b0", [128, TT])
    ps_b1 = k.ps("ln_ps_b1", [128, TT])
    oc = k.sbuf["ones_col"]
    orow = k.sbuf["ones_row"]

    s_sq = k.sem("ln_sq_s")
    s_sum = k.sem("ln_sum_s")
    s_row = k.sem("ln_row_s")
    s_rstd = k.sem("ln_rstd_s")
    s_row2 = k.sem("ln_row2_s")
    s_bc = k.sem("ln_bc_s")
    s_ap = k.sem("ln_apply_s")

    prev_ap = s_ap.n
    prev_row = s_row.n

    inv_d = 1.0 / (128 * nchunks)

    k.wait("tensor", s_row, prev_row)
    for c in range(nchunks):
        k.wait("scalar", s_sum)
        k.op("scalar", lambda e, c=c: e.activation(sq[:, :tt], x_ap(c), AF.Square),
             s_sq)
        k.wait("tensor", s_sq)
        k.op("tensor", lambda e, c=c: e.matmul(
            ps_s1[:1, :tt], oc[:], x_ap(c), start=(c == 0), stop=(c == nchunks - 1)))
        k.op("tensor", lambda e, c=c: e.matmul(
            ps_s2[:1, :tt], oc[:], sq[:, :tt], start=(c == 0), stop=(c == nchunks - 1)),
            s_sum)
    k.wait("vector", s_sum)
    k.chain("vector", [
        lambda e: e.tensor_scalar_mul(rows[:1, 0 * TT:0 * TT + tt], ps_s1[:1, :tt], inv_d),
        lambda e: e.tensor_scalar_mul(rows[:1, 1 * TT:1 * TT + tt], ps_s2[:1, :tt], inv_d),
        lambda e: e.tensor_mul(rows[:1, 3 * TT:3 * TT + tt], rows[:1, 0 * TT:0 * TT + tt], rows[:1, 0 * TT:0 * TT + tt]),
        lambda e: e.tensor_sub(rows[:1, 1 * TT:1 * TT + tt], rows[:1, 1 * TT:1 * TT + tt], rows[:1, 3 * TT:3 * TT + tt]),
    ], s_row)
    k.wait("scalar", s_row)
    epsap = k.sbuf["c_" + eps_name]
    k.chain("scalar", [
        lambda e: e.activation(rows[:1, 1 * TT:1 * TT + tt],
                               rows[:1, 1 * TT:1 * TT + tt], AF.Ln,
                               bias=epsap[:1, :1]),
        lambda e: e.activation(rows[:1, 1 * TT:1 * TT + tt],
                               rows[:1, 1 * TT:1 * TT + tt], AF.Exp,
                               scale=-0.5),
    ], s_rstd)
    k.wait("vector", s_rstd)
    k.chain("vector", [
        lambda e: e.tensor_mul(rows[:1, 2 * TT:2 * TT + tt], rows[:1, 0 * TT:0 * TT + tt], rows[:1, 1 * TT:1 * TT + tt]),
        lambda e: e.tensor_scalar_mul(rows[:1, 2 * TT:2 * TT + tt], rows[:1, 2 * TT:2 * TT + tt], -1.0),
    ], s_row2)
    k.wait("tensor", s_row2)
    k.wait("tensor", s_ap, prev_ap)
    k.op("tensor", lambda e: e.matmul(ps_b0[:, :tt], orow[:], rows[:1, 1 * TT:1 * TT + tt],
                                      start=True, stop=True))
    k.op("tensor", lambda e: e.matmul(ps_b1[:, :tt], orow[:], rows[:1, 2 * TT:2 * TT + tt],
                                      start=True, stop=True), s_bc)
    vself = k.sem("vector_self")
    k.wait("vector", s_bc)
    for (sem, val) in pre_apply_waits:
        k.wait("vector", sem, val)
    for c in range(nchunks):
        if c > 0:
            k.wait("vector", s_ap)
        if affine is None:
            k.op("vector", lambda e, c=c: e.tensor_mul(apt[:, :tt], x_ap(c),
                                                       ps_b0[:, :tt]), vself)
            k.wait("vector", vself)
            k.op("vector", lambda e, c=c: e.tensor_add(xn_ap(c), apt[:, :tt],
                                                       ps_b1[:, :tt]), s_ap)
        else:
            g_ap, b_ap = affine
            k.chain("vector", [
                lambda e, c=c: e.tensor_mul(apt[:, :tt], x_ap(c), ps_b0[:, :tt]),
                lambda e, c=c: e.tensor_add(apt[:, :tt], apt[:, :tt],
                                            ps_b1[:, :tt]),
                lambda e, c=c: e.tensor_scalar(xn_ap(c), apt[:, :tt], g_ap(c),
                                               b_ap(c), ALU.mult, ALU.add),
            ], s_ap)
    return s_ap



def emit_proj(k: KB, lhsT, rhs, epi, n_ic, n_oc, banks, s_mm: Sem, s_free: Sem):
    freed = {}
    base_free = s_free.n
    for oc in range(n_oc):
        b = banks[oc % len(banks)]
        if oc >= len(banks):
            k.wait("tensor", s_free, freed[oc - len(banks)])
        for ic in range(n_ic):
            k.op("tensor", lambda e, b=b, ic=ic, oc=oc: e.matmul(
                b[:], lhsT(ic, oc), rhs(ic), start=(ic == 0), stop=(ic == n_ic - 1)),
                sig=s_mm if ic == n_ic - 1 else None)
        epi(oc, b[:], s_mm.n)
        assert s_free.n > base_free, "epi must signal s_free"
        freed[oc] = s_free.n
        base_free = s_free.n



def emit_ternarize(k: KB, tag, src, rows, cols, dst_sb, s_done: Sem):
    n_rt = rows // 128
    CB = 512
    n_cb = (cols + CB - 1) // CB
    wload = k.sb("tz_load", [128, CB], F32)
    wabs = k.sb("tz_abs_t", [128, CB], F32)
    sgn = k.sb("tz_sgn", [128, CB], F32)
    stage = k.sb("tz_stage", [128, 512], F32)
    colsum = k.sb("tz_colsum", [128, 1], F32)
    tot = k.sb("tz_tot", [1, 1], F32)
    thr = k.sb("tz_thr", [128, 1], F32)
    ps_tot = k.ps("tz_ps_tot", [1, 1])
    ps_thr = k.ps("tz_ps_thr", [128, 1])
    oc = k.sbuf["ones_col"]
    orow = k.sbuf["ones_row"]

    s_ld = k.sem(f"tz_ld_{tag}")
    s_abs = k.sem(f"tz_abs_{tag}")
    s_red = k.sem(f"tz_red_{tag}")
    s_thr = k.sem(f"tz_thr_{tag}")
    s_trn = k.sem(f"tz_trn_{tag}")

    tiles = [(r, cb) for r in range(n_rt) for cb in range(n_cb)]
    assert len(tiles) <= 512
    assert cols % CB == 0

    for name, sem in list(k.sems.items()):
        if (name.startswith("tz_abs_") or name.startswith("tz_trn_")) \
                and not name.endswith(f"_{tag}"):
            k.wait("sync", sem)

    for i, (r, cb) in enumerate(tiles):
        k.wait("sync", s_abs)
        k.dma("sync", wload[:], src[r * 128:(r + 1) * 128, cb * CB:(cb + 1) * CB],
              s_ld)
        k.wait("vector", s_ld)
        k.op("vector", lambda e, i=i: e.tensor_reduce(
            stage[:, i:i + 1], wload[:], AX.X, ALU.add,
            apply_absolute_value=True), s_abs)
    k.op("vector", lambda e, n=len(tiles): e.tensor_reduce(
        colsum[:], stage[:, :n], AX.X, ALU.add), s_red)
    k.wait("tensor", s_red)
    k.op("tensor", lambda e: e.matmul(ps_tot[:1, :1], oc[:], colsum[:],
                                      start=True, stop=True), s_red)
    k.wait("vector", s_red)
    k.op("vector", lambda e: e.tensor_scalar(
        tot[:1, :1], ps_tot[:1, :1], 0.5 / (rows * cols), 5e-9, ALU.mult, ALU.max),
        s_red)
    k.wait("tensor", s_red)
    k.op("tensor", lambda e: e.matmul(ps_thr[:, :1], orow[:], tot[:1, :1],
                                      start=True, stop=True), s_thr)
    k.wait("vector", s_thr)
    k.op("vector", lambda e: e.tensor_copy(thr[:], ps_thr[:, :1]), s_thr)
    k.wait("sync", s_abs)
    for i, (r, cb) in enumerate(tiles):
        k.wait("sync", s_trn)
        k.dma("sync", wload[:], src[r * 128:(r + 1) * 128, cb * CB:(cb + 1) * CB],
              s_ld)
        k.wait("scalar", s_ld)
        k.wait("scalar", s_trn)
        k.op("scalar", lambda e: e.activation(sgn[:], wload[:], AF.Sign))
        k.op("scalar", lambda e: e.activation(wabs[:], wload[:], AF.Abs), s_abs)
        k.wait("vector", s_abs)
        k.op("vector", lambda e: e.tensor_scalar(
            wabs[:], wabs[:], thr[:, :1], None, ALU.is_ge))
        dst = dst_sb[:, r * cols + cb * CB: r * cols + (cb + 1) * CB]
        k.op("vector", lambda e, dst=dst: e.tensor_mul(dst, wabs[:], sgn[:]),
             s_trn)
    s_done.n += 1
    k.on("vector", lambda e, h=s_done.h: e.sem_inc(h, 1))



def build_l0ffn(gate_only):
    nc = bass.Bass("TRN2", target_bir_lowering=False,
                   detect_race_conditions=_RD)
    x_in = nc.dram_tensor("x_in", [D, TLOC], F32, kind="ExternalInput").ap()
    gwt = nc.dram_tensor("gwt", [D, 8], F32, kind="ExternalInput").ap()
    gbr = nc.dram_tensor("gbr", [1, 8], F32, kind="ExternalInput").ap()
    cnt_out = nc.dram_tensor("cnt_out", [1, 8], F32, kind="ExternalOutput").ap()
    if not gate_only:
        ewt = nc.dram_tensor("ewt", [8, D, D], F32, kind="ExternalInput").ap()
        ebc = nc.dram_tensor("ebc", [128, 64], F32, kind="ExternalInput").ap()
        counts_all = nc.dram_tensor("counts_all", [1, 64], F32,
                                    kind="ExternalInput").ap()
        x_out = nc.dram_tensor("x_out", [D, TLOC], F32, kind="ExternalOutput").ap()

    with ExitStack() as stack:
        k = KB(nc, stack)
        emit_consts(k, {"eps_ln": 1e-5})

        x_t = k.sb("x_t", [128, DC * TT], F32)
        xn = k.sb("xn", [128, DC * TT], BF16)
        gw_sb = k.sb("gw_sb", [128, DC * 8], BF16)
        gb_sb = k.sb("gb_sb", [1, 8], BF16)
        m1 = k.sb("m1", [128, 1], F32)
        m2 = k.sb("m2", [128, 1], F32)
        eqb = k.sb("eqb", [128, 8], F32)
        mkd = k.sb("mkd", [128, 8], F32)
        indb = k.sb("indb", [128, 8], F32)
        cnt_sb = k.sb("cnt_sb", [1, 8], F32)
        ps_g = k.ps("ps_g", [128, 8])
        ps_cnt = k.ps("ps_cnt", [1, 8])

        s_xt = k.sem("x_ld")
        s_w = k.sem("w_ld")
        s_g = k.sem("g_mm")
        s_ind = k.sem("ind")
        s_cm = k.sem("cnt_mm")
        s_cnt = k.sem("cnt_done")
        s_st = k.sem("st")

        if not gate_only:
            ew_sb = k.sb("ew_sb", [128, 8 * DC * D], BF16)
            eb_sb = k.sb("eb_sb", [128, 64], F32)
            cb_sb = k.sb("cb_sb", [128, 8], F32)
            acc = k.sb("acc", [128, DC * TT], F32)
            ept = k.sb("ept", [128, TT], F32)
            ps_mm = [k.ps(f"mm{i}", [128, TT]) for i in range(2)]
            s_mm = k.sem("mm")
            s_ac = k.sem("acc_s")
            s_cb = k.sem("cb_s")
            s_ws = k.sem("w_ld_sync")
            k.dma("sync", eb_sb[:], ebc, s_ws)
            ca_sb = k.sb("ca_sb", [1, 64], F32)
            k.dma("sync", ca_sb[:], counts_all, s_ws)
            for e in range(8):
                k.dma("gpsimd", ew_sb[:].rearrange(
                    "p (e c h) -> p e c h", e=8, c=DC)[:, e:e + 1],
                    ewt.rearrange("e (c p) h -> p e c h", p=128)[:, e:e + 1],
                    s_w)
            k.wait("vector", s_w)
            k.wait("vector", k.sems["w_ld_sync"])
            k.op("vector", lambda e: e.tensor_reduce(
                cnt_sb[:1, :8], bass.AP(ca_sb, 0, [[64, 1], [1, 8], [8, 8]]),
                AX.X, ALU.add), s_cb)
            k.wait("tensor", s_cb)
            k.op("tensor", lambda e: e.matmul(ps_g[:, :8],
                                              k.sbuf["ones_row"][:1, :],
                                              cnt_sb[:1, :8],
                                              start=True, stop=True), s_cb)
            k.wait("vector", s_cb)
            k.op("vector", lambda e: e.tensor_copy(cb_sb[:], ps_g[:, :8]), s_cb)
            k.wait("tensor", s_cb)

        k.dma("gpsimd", gw_sb[:].rearrange("p (c e) -> p c e", c=DC),
              gwt.rearrange("(c p) e -> p c e", p=128), s_w)
        k.dma("gpsimd", gb_sb[:], gbr, s_w)

        k.wait("tensor", s_w)
        if not gate_only:
            k.wait("tensor", k.sems["w_ld_sync"])
            k.wait("scalar", k.sems["w_ld_sync"])

        for it in range(NTT):
            t0 = it * TT
            if gate_only:
                k.wait("gpsimd", k.sem("go_rel"), it)
            else:
                k.wait("gpsimd", s_st, 16 * it)
            k.dma("gpsimd", x_t[:].rearrange("p (c t) -> p c t", c=DC),
                  x_in.rearrange("(c p) t -> p c t", p=128)[:, :, t0:t0 + TT],
                  s_xt)
            for eng in ("scalar", "tensor", "vector"):
                k.wait(eng, s_xt)

            def x_ap(c):
                return x_t[:, c * TT:(c + 1) * TT]

            def xn_ap(c):
                return xn[:, c * TT:(c + 1) * TT]

            paw = ([(s_g, 4 * it)] if gate_only else [(k.sems["mm"], 64 * it)])
            s_ap = emit_ln_tile(k, x_ap, xn_ap, pre_apply_waits=paw)
            k.wait("tensor", s_ap)

            for st in range(4):
                ts = t0 + st * 128
                sub = it * 4 + st
                k.wait("tensor", s_ind)
                for ic in range(DC):
                    k.op("tensor", lambda e, ic=ic, st=st: e.matmul(
                        ps_g[:, :8], xn[:, ic * TT + st * 128: ic * TT + st * 128 + 128],
                        gw_sb[:, ic * 8:(ic + 1) * 8], start=(ic == 0), stop=False))
                k.op("tensor", lambda e: e.matmul(
                    ps_g[:, :8], k.sbuf["ones_row_bf"][:1, :128], gb_sb[:1, :8],
                    start=False, stop=True), s_g)
                k.wait("vector", s_g)
                k.chain("vector", [
                    lambda e: e.reduce_max(m1[:], ps_g[:, :8], AX.X),
                    lambda e: e.tensor_scalar(
                        eqb[:], ps_g[:, :8], m1[:, :1], None, ALU.is_ge),
                    lambda e: e.scalar_tensor_tensor(
                        mkd[:], eqb[:], -1e9, ps_g[:, :8], ALU.mult, ALU.add),
                    lambda e: e.reduce_max(m2[:], mkd[:], AX.X),
                    lambda e: e.tensor_scalar(
                        indb[:], ps_g[:, :8], m2[:, :1], None, ALU.is_ge),
                ], s_ind)
                k.wait("tensor", s_ind)
                k.op("tensor", lambda e, sub=sub: e.matmul(
                    ps_cnt[:1, :8], k.sbuf["ones_col"][:], indb[:],
                    start=(sub == 0), stop=(sub == 4 * NTT - 1)), s_cm)

            if not gate_only:
                s_mm = k.sems["mm"]
                s_ac = k.sems["acc_s"]
                for e_i in range(8):
                    for oc in range(DC):
                        b = ps_mm[(e_i * DC + oc) % 2][:]
                        if e_i * DC + oc >= 2:
                            k.wait("tensor", s_ac, s_ac.n - 1)
                        for ic in range(DC):
                            k.op("tensor", lambda e, b=b, e_i=e_i, ic=ic, oc=oc:
                                 e.matmul(b, ew_sb[:, (e_i * DC + ic) * D + oc * 128:
                                                   (e_i * DC + ic) * D + oc * 128 + 128],
                                          xn[:, ic * TT:(ic + 1) * TT],
                                          start=(ic == 0), stop=(ic == DC - 1)),
                                 sig=s_mm if ic == DC - 1 else None)
                        k.wait("vector", s_mm)
                        k.wait("vector", s_ac)
                        if e_i == 0:
                            k.chain("vector", [
                                lambda e, b=b, e_i=e_i, oc=oc: e.tensor_scalar(
                                    ept[:], b,
                                    eb_sb[:, e_i * 8 + oc:e_i * 8 + oc + 1], 0.0,
                                    ALU.add, ALU.max),
                                lambda e, oc=oc: e.tensor_scalar(
                                    acc[:, oc * TT:(oc + 1) * TT], ept[:],
                                    cb_sb[:, 0:1], None, ALU.mult),
                            ], s_ac)
                        else:
                            k.chain("vector", [
                                lambda e, b=b, e_i=e_i, oc=oc: e.tensor_scalar(
                                    ept[:], b,
                                    eb_sb[:, e_i * 8 + oc:e_i * 8 + oc + 1], 0.0,
                                    ALU.add, ALU.max),
                                lambda e, e_i=e_i, oc=oc: e.scalar_tensor_tensor(
                                    acc[:, oc * TT:(oc + 1) * TT], ept[:],
                                    cb_sb[:, e_i:e_i + 1],
                                    acc[:, oc * TT:(oc + 1) * TT],
                                    ALU.mult, ALU.add),
                            ], s_ac)
                s_of = k.sem("o_free")
                for oc in range(DC):
                    xa = x_t[:, oc * TT:(oc + 1) * TT]
                    aa = acc[:, oc * TT:(oc + 1) * TT]
                    k.op("vector", lambda e, xa=xa, aa=aa: e.scalar_tensor_tensor(
                        xa, aa, 0.5 / 32768.0, xa, ALU.mult, ALU.add), s_of)
                k.wait("gpsimd", s_of)
                k.dma("gpsimd",
                      x_out.rearrange("(c p) t -> p c t", p=128)[:, :, t0:t0 + TT],
                      x_t[:].rearrange("p (c t) -> p c t", c=DC), s_st)
            else:
                s_go = k.sem("go_rel")
                k.wait("gpsimd", s_ap)
                s_go.n += 1
                k.on("gpsimd", lambda e, h=s_go.h: e.sem_inc(h, 1))

        k.wait("vector", k.sems["cnt_mm"])
        k.op("vector", lambda e: e.tensor_copy(cnt_sb[:1, :8], ps_cnt[:1, :8]),
             s_cnt)
        k.wait("gpsimd", s_cnt)
        k.dma("gpsimd", cnt_out, cnt_sb[:1, :8], s_st)
        k.wait("gpsimd", s_st)
        k.play()
    return nc



def build_l1ffn():
    nc = bass.Bass("TRN2", target_bir_lowering=False,
                   detect_race_conditions=_RD)
    x_in = nc.dram_tensor("x_in", [D, TLOC], F32, kind="ExternalInput").ap()
    rwt = nc.dram_tensor("rwt", [9, D, 2048], F32, kind="ExternalInput").ap()
    rw2t = nc.dram_tensor("rw2t", [9, 2048, D], F32, kind="ExternalInput").ap()
    rb1c = nc.dram_tensor("rb1c", [128, 9 * 16], F32, kind="ExternalInput").ap()
    rb2r = nc.dram_tensor("rb2r", [1, 9 * D], F32, kind="ExternalInput").ap()
    rbr = nc.dram_tensor("rbr", [1, 8], F32, kind="ExternalInput").ap()
    rwtr = nc.dram_tensor("rwtr", [D, 8], F32, kind="ExternalInput").ap()
    x_out = nc.dram_tensor("x_out", [D, TLOC], F32, kind="ExternalOutput").ap()

    H = 2048
    HC = H // 128

    with ExitStack() as stack:
        k = KB(nc, stack)
        emit_consts(k, {"eps_ln": 1e-5})

        x_t = k.sb("x_t", [128, DC * TT], F32)
        xn = k.sb("xnf", [128, DC * TLOC], BF16)
        h1 = k.sb("h1", [128, HC * TT], BF16)
        acc = k.sb("acc", [128, 16 * D], BF16)
        rw1_sb = k.sb("rw1_sb", [128, DC * H], BF16)
        rw2_sb = k.sb("rw2_sb", [128, HC * D], BF16)
        ident = k.sb("ident", [128, 128], BF16)
        rb1_sb = k.sb("rb1_sb", [128, 9 * 16], F32)
        rb2_sb = k.sb("rb2_sb", [1, 9 * D], BF16)
        rbr_sb = k.sb("rbr_sb", [1, 8], BF16)
        rw_sb = k.sb("rw_sb", [128, DC * 8], BF16)
        wgt = k.sb("wgt", [128, 16 * 8], F32)
        m1 = k.sb("m1", [128, 1], F32)
        nm = k.sb("nm", [128, 1], F32)
        exs = k.sb("exs", [128, 8], F32)
        ssum = k.sb("ssum", [128, 1], F32)
        rs = k.sb("rs", [128, 1], F32)
        pp = k.sb("pp", [128, 8], F32)
        eqb = k.sb("eqb", [128, 8], F32)
        mkd = k.sb("mkd", [128, 8], F32)
        indb = k.sb("indb", [128, 8], F32)
        wsc = k.sb("wsc", [128, TT], BF16)

        ps_mm = [k.ps(f"mm{i}", [128, TT]) for i in range(2)]
        ps_g = k.ps("ps_g", [128, 8])
        ps_T = k.ps("ps_T", [128, 128], BF16)

        s_xt = k.sem("x_ld")
        s_b = k.sem("b_ld")
        s_we = k.sem("w_e")
        s_rt = k.sem("rt")
        s_sm1 = k.sem("sm1")
        s_sm2 = k.sem("sm2")
        s_wg = k.sem("wg")
        s_mm1 = k.sem("mm1")
        s_hf = k.sem("h_free")
        s_mm2 = k.sem("mm2")
        s_acd = k.sem("acd")
        s_tp = k.sem("tp")
        s_res = k.sem("res")
        s_st = k.sem("st")

        s_bi = k.sem("b_eng")
        s_bg = k.sem("b_gp")
        k.chain("gpsimd", [
            lambda e: e.memset(ident[:], 0.0),
            lambda e: e.affine_select(
                ident[:], ident[:], [[-1, 128]], ALU.not_equal, 1.0, base=0,
                channel_multiplier=1),
        ], s_bi)

        k.dma("sync", rb1_sb[:], rb1c, s_b)
        k.dma("gpsimd", rb2_sb[:], rb2r, s_bg)
        k.dma("gpsimd", rbr_sb[:], rbr, s_bg)
        k.dma("gpsimd", rw_sb[:].rearrange("p (c e) -> p c e", c=DC),
              rwtr.rearrange("(c p) e -> p c e", p=128), s_bg)
        k.op("vector", lambda e: e.memset(acc[:], 0.0), s_bi)

        for eng in ("tensor", "scalar", "vector"):
            k.wait(eng, s_b)
            k.wait(eng, s_bg)
            k.wait(eng, s_bi)

        s_lnap = k.sem("ln_apply_s")
        for it in range(NTT):
            t0 = it * TT
            k.wait("gpsimd", s_lnap, DC * it)
            k.dma("gpsimd", x_t[:].rearrange("p (c t) -> p c t", c=DC),
                  x_in.rearrange("(c p) t -> p c t", p=128)[:, :, t0:t0 + TT],
                  s_xt)
            for eng in ("scalar", "tensor", "vector"):
                k.wait(eng, s_xt)

            def x_ap(c):
                return x_t[:, c * TT:(c + 1) * TT]

            def xn_ap(c, t0=t0):
                return xn[:, c * TLOC + t0: c * TLOC + t0 + TT]

            s_ap = emit_ln_tile(k, x_ap, xn_ap)
            k.wait("tensor", s_ap)

            for st in range(4):
                sub = it * 4 + st
                k.wait("tensor", s_wg)
                for ic in range(DC):
                    k.op("tensor", lambda e, ic=ic, t0=t0, st=st: e.matmul(
                        ps_g[:, :8],
                        xn[:, ic * TLOC + t0 + st * 128: ic * TLOC + t0 + st * 128 + 128],
                        rw_sb[:, ic * 8:(ic + 1) * 8], start=(ic == 0), stop=False))
                k.op("tensor", lambda e: e.matmul(
                    ps_g[:, :8], k.sbuf["ones_row_bf"][:1, :128], rbr_sb[:1, :8],
                    start=False, stop=True), s_rt)
                k.wait("vector", s_rt)
                k.chain("vector", [
                    lambda e: e.reduce_max(m1[:], ps_g[:, :8], AX.X),
                    lambda e: e.tensor_scalar_mul(nm[:], m1[:], -1.0),
                ], s_sm1)
                k.wait("scalar", s_sm1)
                k.op("scalar", lambda e: e.activation(
                    exs[:], ps_g[:, :8], AF.Exp, bias=nm[:, :1]), s_sm2)
                k.wait("vector", s_sm2)
                k.chain("vector", [
                    lambda e: e.reduce_sum(ssum[:], exs[:], AX.X),
                    lambda e: e.reciprocal(rs[:], ssum[:]),
                    lambda e: e.tensor_scalar(pp[:], exs[:], rs[:, :1], None,
                                              ALU.mult),
                    lambda e: e.reduce_max(m1[:], pp[:], AX.X),
                    lambda e: e.tensor_scalar(eqb[:], pp[:], m1[:, :1], None,
                                              ALU.is_ge),
                    lambda e: e.scalar_tensor_tensor(mkd[:], eqb[:], -1e9, pp[:],
                                                     ALU.mult, ALU.add),
                    lambda e: e.reduce_max(m1[:], mkd[:], AX.X),
                    lambda e: e.tensor_scalar(indb[:], pp[:], m1[:, :1], None,
                                              ALU.is_ge),
                    lambda e, sub=sub: e.tensor_mul(wgt[:, sub * 8:(sub + 1) * 8],
                                                    pp[:], indb[:]),
                ], s_wg)

        for ei in range(9):
            k.wait("gpsimd", s_mm1, ei * NTT * HC)
            k.wait("gpsimd", s_mm2, ei * NTT * 8)
            k.dma("gpsimd", rw1_sb[:].rearrange("p (c h) -> p c h", c=DC),
                  rwt.rearrange("e (c p) h -> p e c h", p=128)[:, ei], s_we)
            k.dma("gpsimd", rw2_sb[:].rearrange("p (c h) -> p c h", c=HC),
                  rw2t.rearrange("e (c p) h -> p e c h", p=128)[:, ei], s_we)
            k.wait("tensor", s_we)
            for it in range(NTT):
                t0 = it * TT
                k.wait("tensor", s_acd)
                for oc in range(HC):
                    b = ps_mm[oc % 2][:]
                    k.wait("tensor", s_hf, s_hf.n - 1)
                    for ic in range(DC):
                        k.op("tensor", lambda e, b=b, ic=ic, oc=oc, t0=t0:
                             e.matmul(b, rw1_sb[:, ic * H + oc * 128:
                                              ic * H + oc * 128 + 128],
                                      xn[:, ic * TLOC + t0: ic * TLOC + t0 + TT],
                                      start=(ic == 0), stop=(ic == DC - 1)),
                             sig=s_mm1 if ic == DC - 1 else None)
                    k.wait("scalar", s_mm1)
                    if oc == 0:
                        k.wait("scalar", s_mm2)
                    k.op("scalar", lambda e, b=b, oc=oc, ei=ei: e.activation(
                        h1[:, oc * TT:(oc + 1) * TT], b, AF.Gelu,
                        bias=rb1_sb[:, ei * 16 + oc:ei * 16 + oc + 1]), s_hf)
                for st in range(4):
                    sub = it * 4 + st
                    for dh in range(2):
                        b = ps_mm[dh][:]
                        k.wait("tensor", s_acd, s_acd.n - 1)
                        k.wait("tensor", s_hf)
                        for fc in range(HC):
                            k.op("tensor", lambda e, b=b, fc=fc, st=st, dh=dh:
                                 e.matmul(b, h1[:, fc * TT + st * 128:
                                                fc * TT + st * 128 + 128],
                                          rw2_sb[:, fc * D + dh * TT:
                                                 fc * D + dh * TT + TT],
                                          start=(fc == 0), stop=False))
                        k.op("tensor", lambda e, b=b, ei=ei, dh=dh: e.matmul(
                            b, k.sbuf["ones_row_bf"][:1, :128],
                            rb2_sb[:1, ei * D + dh * TT: ei * D + dh * TT + TT],
                            start=False, stop=True), s_mm2)
                        k.wait("vector", s_mm2)
                        k.wait("vector", s_acd)
                        aslc = acc[:, sub * D + dh * TT: sub * D + dh * TT + TT]
                        if ei < 8:
                            fn1 = (lambda e, b=b, sub=sub, ei=ei: e.tensor_scalar(
                                wsc[:], b, wgt[:, sub * 8 + ei:sub * 8 + ei + 1],
                                None, ALU.mult))
                        else:
                            fn1 = (lambda e, b=b: e.tensor_copy(wsc[:], b))
                        k.chain("vector", [
                            fn1,
                            lambda e, aslc=aslc: e.tensor_add(aslc, aslc, wsc[:]),
                        ], s_acd)

        k.wait("gpsimd", s_lnap)
        for it in range(NTT):
            t0 = it * TT
            k.wait("gpsimd", s_acd)
            k.wait("gpsimd", s_st, 16 * it)
            k.dma("gpsimd", x_t[:].rearrange("p (c t) -> p c t", c=DC),
                  x_in.rearrange("(c p) t -> p c t", p=128)[:, :, t0:t0 + TT],
                  s_xt)
            k.wait("tensor", s_xt)
            k.wait("vector", s_xt)
            for st in range(4):
                sub = it * 4 + st
                for c in range(DC):
                    k.wait("tensor", s_acd)
                    k.wait("tensor", s_res)
                    k.op("tensor", lambda e, sub=sub, c=c: e.transpose(
                        ps_T[:], acc[:, sub * D + c * 128: sub * D + c * 128 + 128],
                        ident[:]), s_tp)
                    k.wait("vector", s_tp)
                    xa = x_t[:, c * TT + st * 128: c * TT + st * 128 + 128]
                    k.op("vector", lambda e, xa=xa: e.scalar_tensor_tensor(
                        xa, ps_T[:], 0.5, xa, ALU.mult, ALU.add), s_res)
            k.wait("gpsimd", s_res)
            k.dma("gpsimd",
                  x_out.rearrange("(c p) t -> p c t", p=128)[:, :, t0:t0 + TT],
                  x_t[:].rearrange("p (c t) -> p c t", c=DC), s_st)
        k.wait("gpsimd", s_st)
        k.play()
    return nc



def build_l3attn(phase=99):
    nc = bass.Bass("TRN2", target_bir_lowering=False,
                   detect_race_conditions=_RD)
    x_in = nc.dram_tensor("x_in", [D, TLOC], F32, kind="ExternalInput").ap()
    xk1 = nc.dram_tensor("xk1_in", [D, TLOC], F32, kind="ExternalInput").ap()
    xk2 = nc.dram_tensor("xk2_in", [D, TLOC], F32, kind="ExternalInput").ap()
    wdq_t = nc.dram_tensor("wdq_t", [D, 128], F32, kind="ExternalInput").ap()
    wdq = nc.dram_tensor("wdq", [128, D], F32, kind="ExternalInput").ap()
    wuq = nc.dram_tensor("wuq", [D, 128], F32, kind="ExternalInput").ap()
    wuq_tt = nc.dram_tensor("wuq_tt", [128, D], F32, kind="ExternalInput").ap()
    wuk = nc.dram_tensor("wuk", [D, 128], F32, kind="ExternalInput").ap()
    wuv = nc.dram_tensor("wuv", [D, 128], F32, kind="ExternalInput").ap()
    wo_tt = nc.dram_tensor("wo_tt", [D, D], F32, kind="ExternalInput").ap()
    wdkv_t = nc.dram_tensor("wdkv_t", [D, 128], F32, kind="ExternalInput").ap()
    qposr = nc.dram_tensor("qposr", [1, TLOC], F32, kind="ExternalInput").ap()
    scolc = nc.dram_tensor("scolc", [128, 32], F32, kind="ExternalInput").ap()
    x_out = nc.dram_tensor("x_out", [D, TLOC], F32, kind="ExternalOutput").ap()

    S = 4096
    SB = S // 128
    NH = 16

    with ExitStack() as stack:
        k = KB(nc, stack)
        emit_consts(k, {"eps_ln": 1e-5})

        x_t = k.sb("x_t", [128, DC * TT], F32)
        xn = k.sb("xn", [128, DC * TT], BF16)
        q_sb = k.sb("q_sb", [128, DC * TLOC], BF16)
        q1_sb = k.sb("q1_sb", [128, TT], BF16)
        ckv = k.sb("ckv", [128, S], BF16)
        ckvT = k.sb("ckvT", [128, SB * 128], BF16)
        keepq = k.sb("keepq", [128, SB * TT], BF16)
        pb = k.sb("pb", [128, SB * TT], BF16)
        y_sb = k.sb("y_sb", [128, DC * TT], BF16)
        ql_sb = k.sb("ql_sb", [128, TT], BF16)
        ctx_sb = k.sb("ctx_sb", [128, TT], BF16)
        rden = k.sb("rden", [1, TT], F32)
        rden_b = k.sb("rden_b", [128, TT], F32)
        qrow_b = k.sb("qrow_b", [128, TT], F32)
        onec_bf = k.sb("onec_bf", [128, 1], BF16)
        m32c = k.sb("m32c", [128, 1], F32)
        wsm = k.sb("wsm", [128, 5 * DC * 128], BF16)
        wdq_sb = k.sb("wdq_sb", [128, D], BF16)
        wuqtt_sb = k.sb("wuqtt_sb", [128, D], BF16)
        Bm = k.sb("Bm", [128, 128], BF16)
        keff = k.sb("keff", [128, DC * 128], BF16)
        veff = k.sb("veff", [128, D], BF16)
        scol_sb = k.sb("scol_sb", [128, 32], F32)

        ps_mm = [k.ps(f"mm{i}", [128, TT]) for i in range(2)]
        ps_den = k.ps("ps_den", [1, TT])
        ps_tp = k.ps("ps_tp", [128, 128], BF16)

        s_ld = k.sem("ld")
        s_pre = k.sem("pre")
        s_evac = k.sem("evac")
        s_ck = k.sem("ck")
        s_tp = k.sem("tp")
        s_qd = k.sem("qd")
        s_kp = k.sem("kp")
        s_sc = k.sem("sc")
        s_ex = k.sem("ex")
        s_pk = k.sem("pk")
        s_dn = k.sem("dn")
        s_rd = k.sem("rd")
        s_cx = k.sem("cx")
        s_cn = k.sem("cn")
        s_y = k.sem("y")
        s_ya = k.sem("ya")
        s_res = k.sem("res")
        s_st = k.sem("st")
        s_xt = k.sem("x_ld")

        s_cst = k.sem("l3_consts")
        k.op("gpsimd", lambda e: e.memset(onec_bf[:], 1.0))
        k.op("gpsimd", lambda e: e.memset(m32c[:], -32.0), s_cst)
        k.wait("tensor", s_cst)
        k.wait("scalar", s_cst)

        for name, dst, srcap, nch in [
            ("wdqt", "A", wdq_t, 1), ("wuq", "B", wuq, 1), ("wuk", "C", wuk, 1),
            ("wuv", "D2", wuv, 1), ("wdkvt", "E", wdkv_t, 1),
        ]:
            pass
        k.dma("gpsimd", wsm[:].rearrange("p (n c x) -> p n c x", n=5, c=DC),
              None, s_ld) if False else None
        for n, srcap in enumerate([wdq_t, wuq, wuk, wuv, wdkv_t]):
            k.dma("gpsimd",
                  wsm[:].rearrange("p (n c x) -> p n c x", n=5, c=DC)[:, n],
                  srcap.rearrange("(c p) x -> p c x", p=128), s_ld)

        def wslot(n, ic):
            return wsm[:, (n * DC + ic) * 128:(n * DC + ic) * 128 + 128]

        k.dma("gpsimd", wdq_sb[:], wdq, s_ld)
        k.dma("gpsimd", wuqtt_sb[:], wuq_tt, s_ld)
        k.dma("gpsimd", pb[:, :DC * D].rearrange("p (c x) -> p c x", c=DC),
              wo_tt.rearrange("(c p) x -> p c x", p=128), s_ld)
        s_ld2 = k.sem("ld2")
        k.dma("sync", scol_sb[:], scolc, s_ld2)
        qrow_sb = k.sb("qrow_sb", [1, TLOC], F32)
        k.dma("sync", qrow_sb[:], qposr, s_ld2)

        k.wait("tensor", s_ld)
        k.wait("vector", s_ld)
        k.wait("tensor", s_ld2)
        k.wait("vector", s_ld2)

        k.op("tensor", lambda e: [e.matmul(ps_mm[0][:, :128], wslot(1, ic),
                                           wslot(2, ic), start=(ic == 0),
                                           stop=(ic == DC - 1))
                                  for ic in range(DC)][-1], s_pre)
        k.wait("vector", s_pre)
        k.op("vector", lambda e: e.tensor_copy(Bm[:], ps_mm[0][:, :128]), s_evac)
        k.wait("tensor", s_evac)
        for c in (range(DC) if phase != 10 else []):
            k.wait("tensor", s_evac)
            k.op("tensor", lambda e, c=c: e.matmul(
                ps_mm[1][:, :128], wdq_sb[:, c * 128:(c + 1) * 128], Bm[:],
                start=True, stop=True), s_pre)
            k.wait("vector", s_pre)
            k.op("vector", lambda e, c=c: e.tensor_copy(
                keff[:, c * 128:(c + 1) * 128], ps_mm[1][:, :128]), s_evac)
        for j in (range(2) if phase not in (10, 11) else []):
            k.wait("tensor", s_evac)
            k.op("tensor", lambda e, j=j: [e.matmul(
                ps_mm[0][:], wslot(3, ic), pb[:, ic * D + j * TT: ic * D + j * TT + TT],
                start=(ic == 0), stop=(ic == DC - 1)) for ic in range(DC)][-1],
                s_pre)
            k.wait("vector", s_pre)
            k.op("vector", lambda e, j=j: e.tensor_copy(
                veff[:, j * TT:(j + 1) * TT], ps_mm[0][:]), s_evac)

        for half, src in (enumerate([xk1, xk2]) if phase >= 2 else []):
            for it in range(NTT):
                t0 = it * TT
                k.wait("gpsimd", k.sem("ln_apply_s"), DC * (half * NTT + it))
                k.dma("gpsimd", x_t[:].rearrange("p (c t) -> p c t", c=DC),
                      src.rearrange("(c p) t -> p c t", p=128)[:, :, t0:t0 + TT],
                      s_xt)
                for eng in ("scalar", "tensor", "vector"):
                    k.wait(eng, s_xt)
                s_ap = emit_ln_tile(k, lambda c: x_t[:, c * TT:(c + 1) * TT],
                                    lambda c: xn[:, c * TT:(c + 1) * TT])
                k.wait("tensor", s_ap)
                k.wait("tensor", s_ck)
                k.wait("tensor", s_evac)
                k.op("tensor", lambda e: [e.matmul(
                    ps_mm[0][:], wslot(4, ic), xn[:, ic * TT:(ic + 1) * TT],
                    start=(ic == 0), stop=(ic == DC - 1)) for ic in range(DC)][-1],
                    s_sc)
                k.wait("vector", s_sc)
                off = half * TLOC + t0
                k.op("vector", lambda e, off=off: e.tensor_copy(
                    ckv[:, off:off + TT], ps_mm[0][:]), s_ck)
        DO_T = phase >= 2
        ident = k.sb("identb", [128, 128], BF16)
        k.chain("gpsimd", [
            lambda e: e.memset(ident[:], 0.0),
            lambda e: e.affine_select(
                ident[:], ident[:], [[-1, 128]], ALU.not_equal, 1.0, base=0,
                channel_multiplier=1),
        ], s_pre)
        k.wait("tensor", s_pre)
        k.wait("tensor", s_ck)
        for sb in (range(SB) if DO_T else []):
            k.wait("tensor", s_tp)
            k.op("tensor", lambda e, sb=sb: e.transpose(
                ps_tp[:], ckv[:, sb * 128:(sb + 1) * 128], ident[:]), s_tp)
            k.wait("vector", s_tp)
            k.op("vector", lambda e, sb=sb: e.tensor_copy(
                ckvT[:, sb * 128:(sb + 1) * 128], ps_tp[:]), s_tp)

        for it in (range(NTT) if phase >= 3 else []):
            t0 = it * TT
            k.wait("gpsimd", k.sem("ln_apply_s"), DC * (2 * NTT + it))
            k.dma("gpsimd", x_t[:].rearrange("p (c t) -> p c t", c=DC),
                  x_in.rearrange("(c p) t -> p c t", p=128)[:, :, t0:t0 + TT],
                  s_xt)
            for eng in ("scalar", "tensor", "vector"):
                k.wait(eng, s_xt)
            s_ap = emit_ln_tile(k, lambda c: x_t[:, c * TT:(c + 1) * TT],
                                lambda c: xn[:, c * TT:(c + 1) * TT])
            k.wait("tensor", s_ap)
            k.wait("tensor", s_qd)
            k.op("tensor", lambda e: [e.matmul(
                ps_mm[0][:], wslot(0, ic), xn[:, ic * TT:(ic + 1) * TT],
                start=(ic == 0), stop=(ic == DC - 1)) for ic in range(DC)][-1],
                s_sc)
            k.wait("vector", s_sc)
            k.op("vector", lambda e: e.tensor_copy(q1_sb[:], ps_mm[0][:]), s_qd)
            k.wait("tensor", s_qd)
            for oc in range(DC):
                k.wait("tensor", s_qd)
                k.op("tensor", lambda e, oc=oc: e.matmul(
                    ps_mm[1][:], wuqtt_sb[:, oc * 128:(oc + 1) * 128], q1_sb[:],
                    start=True, stop=True), s_sc)
                k.wait("vector", s_sc)
                k.op("vector", lambda e, oc=oc, t0=t0: e.tensor_copy(
                    q_sb[:, oc * TLOC + t0: oc * TLOC + t0 + TT], ps_mm[1][:]),
                    s_qd)

        NHL = 0 if phase < 4 else (1 if phase == 4 else NH)
        NQTL = NTT if phase >= 5 else 1
        for qt in range(NTT if phase >= 4 else 0):
            t0 = qt * TT
            k.wait("tensor", s_kp)
            k.wait("tensor", s_ya)
            k.op("tensor", lambda e, t0=t0: e.matmul(
                ps_mm[0][:], k.sbuf["ones_row"][:1, :],
                qrow_sb[:1, t0:t0 + TT], start=True, stop=True), s_pre)
            k.wait("vector", s_pre)
            k.op("vector", lambda e: e.tensor_copy(qrow_b[:], ps_mm[0][:]),
                 s_evac)
            k.wait("vector", s_evac)
            for sb in range(SB):
                k.op("vector", lambda e, sb=sb: e.tensor_scalar(
                    keepq[:, sb * TT:(sb + 1) * TT],
                    qrow_b[:], scol_sb[:, sb:sb + 1], None,
                    ALU.is_ge), s_kp)
            for h in range(NHL):
                hc, hr = h // 2, (h % 2) * 64
                k.wait("tensor", s_ya)
                k.wait("tensor", s_evac)
                k.op("tensor", lambda e, hc=hc, hr=hr, t0=t0: e.matmul(
                    ps_mm[0][:], keff[hr:hr + 64, hc * 128:hc * 128 + 128],
                    q_sb[hr:hr + 64, hc * TLOC + t0: hc * TLOC + t0 + TT],
                    start=True, stop=True), s_sc)
                k.wait("vector", s_sc)
                k.op("vector", lambda e: e.tensor_scalar(
                    ql_sb[:], ps_mm[0][:], 0.125, None, ALU.mult), s_qd)
                k.wait("tensor", s_qd)
                k.wait("scalar", s_cx)
                for sb in range(SB):
                    bk = ps_mm[sb % 2][:]
                    k.wait("tensor", s_ex, s_ex.n - 1)
                    k.op("tensor", lambda e, bk=bk, sb=sb: e.matmul(
                        bk, ckv[:, sb * 128:(sb + 1) * 128], ql_sb[:],
                        start=True, stop=True), s_sc)
                    k.wait("scalar", s_sc)
                    k.op("scalar", lambda e, bk=bk, sb=sb: e.activation(
                        pb[:, sb * TT:(sb + 1) * TT], bk, AF.Exp,
                        bias=m32c[:, :1]), s_ex)
                k.wait("vector", s_ex)
                k.wait("vector", s_kp)
                for sb in range(SB):
                    k.op("vector", lambda e, sb=sb: e.tensor_mul(
                        pb[:, sb * TT:(sb + 1) * TT], pb[:, sb * TT:(sb + 1) * TT],
                        keepq[:, sb * TT:(sb + 1) * TT]), s_pk)
                k.wait("tensor", s_pk)
                k.wait("tensor", s_rd)
                k.op("tensor", lambda e: [e.matmul(
                    ps_den[:1, :], onec_bf[:], pb[:, sb * TT:(sb + 1) * TT],
                    start=(sb == 0), stop=(sb == SB - 1)) for sb in range(SB)][-1],
                    s_dn)
                k.wait("vector", s_dn)
                k.chain("vector", [
                    lambda e: e.reciprocal(rden[:1, :], ps_den[:1, :]),
                ], s_rd)
                k.wait("tensor", s_rd)
                k.op("tensor", lambda e: e.matmul(
                    ps_mm[0][:], k.sbuf["ones_row"][:1, :], rden[:1, :],
                    start=True, stop=True), s_dn)
                k.wait("vector", s_dn)
                k.op("vector", lambda e: e.tensor_copy(rden_b[:], ps_mm[0][:]),
                     s_rd)
                k.wait("tensor", s_rd)
                k.op("tensor", lambda e: [e.matmul(
                    ps_mm[1][:], ckvT[:, sb * 128:(sb + 1) * 128],
                    pb[:, sb * TT:(sb + 1) * TT],
                    start=(sb == 0), stop=(sb == SB - 1)) for sb in range(SB)][-1],
                    s_cx)
                k.wait("vector", s_cx)
                k.wait("vector", s_rd)
                k.op("vector", lambda e: e.tensor_mul(
                    ctx_sb[:], ps_mm[1][:], rden_b[:]), s_cn)
                k.wait("tensor", s_cn)
                k.op("tensor", lambda e, h=h: e.matmul(
                    ps_mm[0][:64, :], veff[:, h * 64:(h + 1) * 64], ctx_sb[:],
                    start=True, stop=True), s_y)
                k.wait("vector", s_y)
                k.op("vector", lambda e, hc=hc, hr=hr: e.tensor_copy(
                    y_sb[hr:hr + 64, hc * TT:hc * TT + TT], ps_mm[0][:64, :]),
                    s_ya)
            if NHL == 0:
                k.op("vector", lambda e: e.memset(y_sb[:], 0.0), s_ya)
            k.wait("gpsimd", s_ya)
            k.wait("gpsimd", k.sem("ln_apply_s"), DC * 3 * NTT)
            k.wait("gpsimd", s_st, 16 * qt)
            k.dma("gpsimd", x_t[:].rearrange("p (c t) -> p c t", c=DC),
                  x_in.rearrange("(c p) t -> p c t", p=128)[:, :, t0:t0 + TT],
                  s_xt)
            k.wait("vector", s_xt)
            for c in range(DC):
                xa = x_t[:, c * TT:(c + 1) * TT]
                k.op("vector", lambda e, xa=xa, c=c: e.tensor_add(
                    xa, xa, y_sb[:, c * TT:(c + 1) * TT]), s_res)
            k.wait("gpsimd", s_res)
            k.dma("gpsimd",
                  x_out.rearrange("(c p) t -> p c t", p=128)[:, :, t0:t0 + TT],
                  x_t[:].rearrange("p (c t) -> p c t", c=DC), s_st)
        k.wait("gpsimd", s_st)
        k.play()
    return nc



L2_EPS_K = 1e-06
L2_RATIO = 0.125
L2_DN2H = 0.0625


def _l2_common(nc):
    x_in = nc.dram_tensor("x_in", [D, TLOC], F32, kind="ExternalInput").ap()
    return x_in


def _emit_k_dd(k, wk_sb, projdn_sb, xn, kf_cb, t0):
    pass


def build_l2attn_p1():
    nc = bass.Bass("TRN2", target_bir_lowering=False,
                   detect_race_conditions=_RD)
    x_in = _l2_common(nc)
    wk = nc.dram_tensor("wk", [D, D], F32, kind="ExternalInput").ap()
    projdn = nc.dram_tensor("projdn", [128, 64], F32, kind="ExternalInput").ap()
    kmax_out = nc.dram_tensor("kmax_out", [1, 1], F32, kind="ExternalOutput").ap()

    with ExitStack() as stack:
        k = KB(nc, stack)
        emit_consts(k, {"eps_ln": 1e-5})
        x_t = k.sb("x_t", [128, DC * TT], F32)
        xn = k.sb("xn", [128, DC * TT], BF16)
        wk_sb = k.sb("wk_sb", [128, DC * D], BF16)
        pj_sb = k.sb("pj_sb", [128, 64], BF16)
        kfm = k.sb("kfm", [128, TT], BF16)
        mcol = k.sb("mcol", [64, 1], F32)
        mrow = k.sb("mrow", [1, 64], F32)
        mloc = k.sb("mloc", [64, 1], F32)
        mloc_bf = k.sb("mloc_bf", [64, 1], BF16)
        ident = k.sb("identb", [128, 128], BF16)
        ps_mm = [k.ps(f"mm{i}", [128, TT]) for i in range(2)]
        ps_dd = k.ps("ps_dd", [64, TT])
        ps_tp = k.ps("ps_tp", [128, 128], BF16)

        s_ld = k.sem("ld")
        s_xt = k.sem("x_ld")
        s_mm = k.sem("mm")
        s_ev = k.sem("evac")
        s_dd = k.sem("dd")
        s_mx = k.sem("mx")
        s_st = k.sem("st")
        vself = k.sem("vector_self")

        k.chain("gpsimd", [
            lambda e: e.memset(ident[:], 0.0),
            lambda e: e.affine_select(ident[:], ident[:], [[-1, 128]],
                                      ALU.not_equal, 1.0, base=0,
                                      channel_multiplier=1),
        ], k.sem("idn"))
        k.dma("gpsimd", wk_sb[:].rearrange("p (c x) -> p c x", c=DC),
              wk.rearrange("(c p) x -> p c x", p=128), s_ld)
        k.dma("gpsimd", pj_sb[:], projdn, s_ld)
        k.wait("vector", s_ld)
        k.op("vector", lambda e: e.tensor_scalar_mul(
            pj_sb[:], pj_sb[:], 2.0 ** -1.5), k.sem("pjs"))
        k.wait("tensor", k.sems["pjs"])
        k.wait("tensor", s_ld)
        k.wait("tensor", k.sems["idn"])
        k.op("vector", lambda e: e.memset(mloc[:], -1e30), k.sem("ml0"))
        k.wait("vector", k.sems["ml0"])

        for it in range(NTT):
            t0 = it * TT
            k.wait("gpsimd", k.sem("ln_apply_s"), DC * it)
            k.dma("gpsimd", x_t[:].rearrange("p (c t) -> p c t", c=DC),
                  x_in.rearrange("(c p) t -> p c t", p=128)[:, :, t0:t0 + TT],
                  s_xt)
            for eng in ("scalar", "tensor", "vector"):
                k.wait(eng, s_xt)
            s_ap = emit_ln_tile(k, lambda c: x_t[:, c * TT:(c + 1) * TT],
                                lambda c: xn[:, c * TT:(c + 1) * TT])
            k.wait("tensor", s_ap)
            for hc in range(DC):
                k.wait("tensor", s_dd)
                k.op("tensor", lambda e, hc=hc: [e.matmul(
                    ps_mm[0][:], wk_sb[:, ic * D + hc * 128: ic * D + hc * 128 + 128],
                    xn[:, ic * TT:(ic + 1) * TT], start=(ic == 0),
                    stop=(ic == DC - 1)) for ic in range(DC)][-1], s_mm)
                k.wait("vector", s_mm)
                k.op("vector", lambda e: e.tensor_copy(kfm[:], ps_mm[0][:]), s_ev)
                k.wait("tensor", s_ev)
                for hh in range(2):
                    k.wait("tensor", s_dd)
                    k.op("tensor", lambda e, hh=hh: e.matmul(
                        ps_dd[:, :TT], pj_sb[hh * 64:(hh + 1) * 64, :],
                        kfm[hh * 64:(hh + 1) * 64, :],
                        start=True, stop=True), s_mm)
                    k.wait("vector", s_mm)
                    k.chain("vector", [
                        lambda e: e.reduce_max(mcol[:], ps_dd[:, :TT], AX.X),
                        lambda e: e.tensor_max(mloc[:], mloc[:], mcol[:]),
                    ], s_dd)
        k.wait("vector", s_dd)
        k.op("vector", lambda e: e.tensor_copy(mloc_bf[:], mloc[:]), s_mx)
        k.wait("tensor", s_mx)
        k.op("tensor", lambda e: e.transpose(ps_tp[:1, :64], mloc_bf[:, :1],
                                             ident[:64, :64]), s_mx)
        k.wait("vector", s_mx)
        k.chain("vector", [
            lambda e: e.tensor_copy(mrow[:1, :64], ps_tp[:1, :64]),
            lambda e: e.reduce_max(mcol[:1, :1], mrow[:1, :64], AX.X),
        ], s_mx)
        k.wait("gpsimd", s_mx)
        k.dma("gpsimd", kmax_out, mcol[:1, :1], s_st)
        k.wait("gpsimd", s_st)
        k.play()
    return nc


def build_l2attn_p2():
    nc = bass.Bass("TRN2", target_bir_lowering=False,
                   detect_race_conditions=_RD)
    x_in = _l2_common(nc)
    wk = nc.dram_tensor("wk", [D, D], F32, kind="ExternalInput").ap()
    wv = nc.dram_tensor("wv", [D, D], F32, kind="ExternalInput").ap()
    projdn = nc.dram_tensor("projdn", [128, 64], F32, kind="ExternalInput").ap()
    kmax_all = nc.dram_tensor("kmax_all", [1, 8], F32, kind="ExternalInput").ap()
    kvp_out = nc.dram_tensor("kvp_out", [64, 16 * 65], F32,
                             kind="ExternalOutput").ap()
    sv_out = nc.dram_tensor("sv_out", [1, D], F32, kind="ExternalOutput").ap()

    with ExitStack() as stack:
        k = KB(nc, stack)
        emit_consts(k, {"eps_ln": 1e-5})
        x_t = k.sb("x_t", [128, DC * TT], F32)
        xn = k.sb("xn", [128, DC * TT], BF16)
        wk_sb = k.sb("wk_sb", [128, DC * D], BF16)
        wv_sb = k.sb("wv_sb", [128, DC * D], BF16)
        pj_sb = k.sb("pj_sb", [128, 64], BF16)
        ident = k.sb("identb", [128, 128], BF16)
        kfm = k.sb("kfm", [128, TT], BF16)
        k2f = k.sb("k2f", [128, TT], F32)
        ek = k.sb("ek", [64, TT], BF16)
        wrow = k.sb("wrow", [1, TT], BF16)
        wcolb = k.sb("wcolb", [128, 4], F32)
        v_tm = k.sb("v_tm", [128, 4 * 128], BF16)
        kfT = k.sb("kfT", [128, 64], BF16)
        mxc = k.sb("mxc", [1, 2], F32)
        mx8 = k.sb("mx8", [1, 8], F32)
        kvacc = k.sb("kvacc", [64, 16 * 65], F32)
        sxc = k.sb("sxc", [128, DC], F32)
        sxb = k.sb("sxb", [128, DC], BF16)
        sv_sb = k.sb("sv_sb", [1, D], F32)
        ps_mm = [k.ps(f"mm{i}", [128, TT]) for i in range(2)]
        ps_dd = k.ps("ps_dd", [64, TT])
        ps_tp = k.ps("ps_tp", [128, 128], BF16)
        ps_kv = ps_dd
        ps_r = k.ps("ln_ps_s1", [1, TT])

        s_ld = k.sem("ld")
        s_ld2 = k.sem("ld2")
        s_xt = k.sem("x_ld")
        s_mm = k.sem("mm")
        s_ev = k.sem("evac")
        s_w = k.sem("wr")
        s_tp = k.sem("tp")
        s_kv = k.sem("kv")
        s_sx = k.sem("sx")
        s_st = k.sem("st")

        k.chain("gpsimd", [
            lambda e: e.memset(ident[:], 0.0),
            lambda e: e.affine_select(ident[:], ident[:], [[-1, 128]],
                                      ALU.not_equal, 1.0, base=0,
                                      channel_multiplier=1),
        ], k.sem("idn"))
        k.dma("gpsimd", wk_sb[:].rearrange("p (c x) -> p c x", c=DC),
              wk.rearrange("(c p) x -> p c x", p=128), s_ld)
        k.dma("gpsimd", wv_sb[:].rearrange("p (c x) -> p c x", c=DC),
              wv.rearrange("(c p) x -> p c x", p=128), s_ld)
        k.dma("gpsimd", pj_sb[:], projdn, s_ld)
        k.dma("sync", mx8[:], kmax_all, s_ld2)
        k.wait("vector", s_ld)
        k.op("vector", lambda e: e.tensor_scalar_mul(
            pj_sb[:], pj_sb[:], 2.0 ** -1.5), k.sem("pjs"))
        k.wait("tensor", k.sems["pjs"])
        k.wait("vector", s_ld2)
        k.chain("vector", [
            lambda e: e.reduce_max(mxc[:1, :1], mx8[:1, :8], AX.X),
            lambda e: e.tensor_scalar_mul(mxc[:1, 1:2], mxc[:1, :1], -1.0),
            lambda e: e.memset(kvacc[:], 0.0),
            lambda e: e.memset(sxc[:], 0.0),
        ], k.sem("init"))
        k.wait("scalar", k.sems["init"])
        k.wait("vector", k.sems["init"])
        k.wait("tensor", s_ld)
        k.wait("tensor", k.sems["idn"])

        for it in range(NTT):
            t0 = it * TT
            k.wait("gpsimd", k.sem("ln_apply_s"), DC * it)
            k.dma("gpsimd", x_t[:].rearrange("p (c t) -> p c t", c=DC),
                  x_in.rearrange("(c p) t -> p c t", p=128)[:, :, t0:t0 + TT],
                  s_xt)
            for eng in ("scalar", "tensor", "vector"):
                k.wait(eng, s_xt)
            k.wait("tensor", s_w)
            s_ap = emit_ln_tile(k, lambda c: x_t[:, c * TT:(c + 1) * TT],
                                lambda c: xn[:, c * TT:(c + 1) * TT])
            k.wait("tensor", s_ap)
            sxt = k.sb("sxt", [128, 1], F32)
            for c in range(DC):
                k.wait("vector", s_sx)
                k.chain("vector", [
                    lambda e, c=c: e.tensor_reduce(sxt[:, :1],
                                                   xn[:, c * TT:(c + 1) * TT],
                                                   AX.X, ALU.add),
                    lambda e, c=c: e.tensor_add(sxc[:, c:c + 1], sxc[:, c:c + 1],
                                                sxt[:, :1]),
                ], s_sx)
            for hc in range(DC):
                k.wait("tensor", s_ev)
                k.op("tensor", lambda e, hc=hc: [e.matmul(
                    ps_mm[0][:], wk_sb[:, ic * D + hc * 128: ic * D + hc * 128 + 128],
                    xn[:, ic * TT:(ic + 1) * TT], start=(ic == 0),
                    stop=(ic == DC - 1)) for ic in range(DC)][-1], s_mm)
                k.wait("vector", s_mm)
                k.wait("vector", s_tp)
                k.op("vector", lambda e: e.tensor_copy(kfm[:], ps_mm[0][:]), s_ev)
                k.wait("scalar", s_ev)
                k.op("scalar", lambda e: e.activation(k2f[:], kfm[:], AF.Square),
                     s_ev)
                for st in range(4):
                    k.wait("tensor", s_tp)
                    k.op("tensor", lambda e, st=st, hc=hc: [e.matmul(
                        ps_mm[1][:, :128],
                        xn[:, ic * TT + st * 128: ic * TT + st * 128 + 128],
                        wv_sb[:, ic * D + hc * 128: ic * D + hc * 128 + 128],
                        start=(ic == 0), stop=(ic == DC - 1))
                        for ic in range(DC)][-1], s_mm)
                    k.wait("vector", s_mm)
                    k.op("vector", lambda e, st=st: e.tensor_copy(
                        v_tm[:, st * 128:(st + 1) * 128], ps_mm[1][:, :128]), s_tp)
                for hh in range(2):
                    h = hc * 2 + hh
                    k.wait("tensor", s_ev)
                    k.wait("tensor", s_kv)
                    k.op("tensor", lambda e, hh=hh: e.matmul(
                        ps_dd[:, :TT], pj_sb[hh * 64:(hh + 1) * 64, :],
                        kfm[hh * 64:(hh + 1) * 64, :],
                        start=True, stop=True), s_mm)
                    k.wait("tensor", k.sem("ln_row_s"))
                    k.wait("tensor", s_w)
                    k.op("tensor", lambda e, hh=hh: e.matmul(
                        ps_r[:1, :], k.sbuf["ones_col"][hh * 64:hh * 64 + 64, :],
                        k2f[hh * 64:(hh + 1) * 64, :], start=True, stop=True),
                        s_mm)
                    k.wait("scalar", s_mm)
                    k.wait("scalar", s_tp)
                    k.op("scalar", lambda e: e.activation(ek[:], ps_dd[:, :TT],
                                                          AF.Exp), s_ev)
                    k.op("scalar", lambda e: e.activation(
                        wrow[:1, :], ps_r[:1, :], AF.Exp, scale=-L2_DN2H,
                        bias=mxc[:1, 1:2]), s_w)
                    for st in range(4):
                        k.wait("tensor", s_w)
                        k.wait("tensor", s_tp)
                        k.op("tensor", lambda e, st=st: e.transpose(
                            ps_tp[:128, :1], wrow[:1, st * 128:(st + 1) * 128],
                            ident[:1, :1]), s_tp)
                        k.wait("vector", s_tp)
                        k.op("vector", lambda e, st=st: e.tensor_copy(
                            wcolb[:, st:st + 1], ps_tp[:128, :1]), s_tp)
                    for st in range(4):
                        k.wait("tensor", s_tp)
                        k.wait("tensor", s_kv)
                        k.op("tensor", lambda e, st=st: e.transpose(
                            ps_tp[:, :64], ek[:, st * 128:(st + 1) * 128],
                            ident[:64, :64]), s_tp)
                        k.wait("vector", s_tp)
                        k.op("vector", lambda e, st=st: e.tensor_scalar(
                            kfT[:], ps_tp[:, :64], wcolb[:, st:st + 1], None,
                            ALU.mult), s_tp)
                        k.wait("tensor", s_tp)
                        k.op("tensor", lambda e, hh=hh, st=st: e.matmul(
                            ps_kv[:, :64], kfT[:, :64],
                            v_tm[:, st * 128 + hh * 64: st * 128 + hh * 64 + 64],
                            start=True, stop=False))
                        k.op("tensor", lambda e: e.matmul(
                            ps_kv[:, 64:65], kfT[:, :64],
                            k.sbuf["ones_col_bf"][:, :1],
                            start=False, stop=True), s_mm)
                        k.wait("vector", s_mm)
                        k.op("vector", lambda e, h=h: e.tensor_add(
                            kvacc[:, h * 65:(h + 1) * 65],
                            kvacc[:, h * 65:(h + 1) * 65], ps_kv[:, :65]), s_kv)
        k.wait("vector", s_sx)
        k.op("vector", lambda e: e.tensor_copy(sxb[:], sxc[:]), s_sx)
        k.wait("tensor", s_sx)
        k.wait("tensor", s_kv)
        k.wait("tensor", k.sem("ln_row_s"))
        k.wait("tensor", s_w)
        for j in range(2):
            k.wait("tensor", s_ev)
            k.op("tensor", lambda e, j=j: [e.matmul(
                ps_r[:1, :], sxb[:, ic:ic + 1],
                wv_sb[:, ic * D + j * TT: ic * D + j * TT + TT],
                start=(ic == 0), stop=(ic == DC - 1)) for ic in range(DC)][-1],
                s_mm)
            k.wait("vector", s_mm)
            k.op("vector", lambda e, j=j: e.tensor_copy(
                sv_sb[:1, j * TT:(j + 1) * TT], ps_r[:1, :]), s_ev)
        k.wait("gpsimd", s_kv)
        k.wait("gpsimd", s_ev)
        k.dma("gpsimd", kvp_out, kvacc[:], s_st)
        k.dma("gpsimd", sv_out, sv_sb[:], s_st)
        k.wait("gpsimd", s_st)
        k.play()
    return nc


L2P3A_PHASE = 99


def build_l2attn_p3a():
    nc = bass.Bass("TRN2", target_bir_lowering=False,
                   detect_race_conditions=_RD)
    x_in = _l2_common(nc)
    wq = nc.dram_tensor("wq", [D, D], F32, kind="ExternalInput").ap()
    projdn = nc.dram_tensor("projdn", [128, 64], F32, kind="ExternalInput").ap()
    wo_r = nc.dram_tensor("wo_r", [D, D], F32, kind="ExternalInput").ap()
    kvp2 = nc.dram_tensor("kvp2", [64, 2 * 16 * 65], F32, kind="ExternalInput").ap()
    sv2 = nc.dram_tensor("sv2", [1, 2 * D], F32, kind="ExternalInput").ap()
    ln1gc = nc.dram_tensor("ln1gc", [128, 8], F32, kind="ExternalInput").ap()
    ln1bc = nc.dram_tensor("ln1bc", [128, 8], F32, kind="ExternalInput").ap()
    out1 = nc.dram_tensor("out1", [D, TLOC], F32, kind="ExternalOutput").ap()

    with ExitStack() as stack:
        k = KB(nc, stack)
        emit_consts(k, {"eps_ln": 1e-5, "eps_ln1": 1e-6})
        x_t = k.sb("x_t", [128, DC * TT], F32)
        xn = k.sb("xn", [128, DC * TT], BF16)
        wq_sb = k.sb("wq_sb", [128, DC * D], BF16)
        wo_sb = k.sb("wo_sb", [128, DC * D], BF16)
        pj_sb = k.sb("pj_sb", [128, 64], BF16)
        ident = k.sb("identb", [128, 128], BF16)
        qfm = k.sb("qfm", [128, TT], BF16)
        q2f = k.sb("q2f", [128, TT], F32)
        eq = k.sb("eq", [64, TT], BF16)
        kv_sb = k.sb("kv_sb", [64, 16 * 65], F32)
        kvb = k.sb("kvb", [64, 16 * 65], BF16)
        n2b = k.sb("n2b", [128, 16 * 64], BF16)
        d2c = k.sb("d2c", [128, 1], F32)
        c625 = k.sb("c625", [128, 1], F32)
        fcol = k.sb("fcol", [128, 1], F32)
        mxq = k.sb("mxq", [128, 1], F32)
        dgq = k.sb("dgq", [128, 1], F32)
        rden = k.sb("rden", [128, 1], F32)
        frc = k.sb("frc", [128, 1], F32)
        erc = k.sb("erc", [128, 1], F32)
        at_sb = k.sb("at_sb", [128, 64], BF16)
        atT = k.sb("atT", [128, 8 * 4 * 128], BF16)
        o1p = k.sb("o1p", [128, DC * TT], F32)
        n2r = k.sb("n2r", [1, 16 * 64], F32)
        g1 = k.sb("g1", [128, DC], F32)
        b1 = k.sb("b1", [128, DC], F32)
        ps_mm = [k.ps(f"mm{i}", [128, TT]) for i in range(2)]
        ps_dd = k.ps("ps_dd", [64, TT])
        ps_tp = k.ps("ps_tp", [128, 128], BF16)

        s_ld = k.sem("ld")
        s_ld2 = k.sem("ld2")
        s_xt = k.sem("x_ld")
        s_mm = k.sem("mm")
        s_ev = k.sem("evac")
        s_tp = k.sem("tp")
        s_at = k.sem("at")
        s_wo = k.sem("wo")
        s_st = k.sem("st")
        vself = k.sem("vector_self")

        k.chain("gpsimd", [
            lambda e: e.memset(ident[:], 0.0),
            lambda e: e.affine_select(ident[:], ident[:], [[-1, 128]],
                                      ALU.not_equal, 1.0, base=0,
                                      channel_multiplier=1),
            lambda e: e.memset(c625[:], L2_DN2H),
        ], k.sem("idn"))
        k.dma("gpsimd", wq_sb[:].rearrange("p (c x) -> p c x", c=DC),
              wq.rearrange("(c p) x -> p c x", p=128), s_ld)
        k.dma("gpsimd", wo_sb[:].rearrange("p (c x) -> p c x", c=DC),
              wo_r.rearrange("(c p) x -> p c x", p=128), s_ld)
        k.dma("gpsimd", pj_sb[:], projdn, s_ld)
        k.wait("vector", s_ld)
        k.op("vector", lambda e: e.tensor_scalar_mul(
            pj_sb[:], pj_sb[:], 2.0 ** -1.5), k.sem("pjs"))
        k.wait("tensor", k.sems["pjs"])
        kvp_sb = k.sb("kvp_sb", [64, 2 * 16 * 65], F32)
        sv_sb = k.sb("sv_sb", [1, 2 * D], F32)
        k.dma("sync", kvp_sb[:], kvp2, s_ld2)
        k.dma("sync", sv_sb[:], sv2, s_ld2)
        k.dma("sync", g1[:], ln1gc, s_ld2)
        k.dma("sync", b1[:], ln1bc, s_ld2)
        k.wait("vector", s_ld2)
        k.wait("tensor", s_ld)
        k.wait("tensor", s_ld2)
        k.wait("tensor", k.sems["idn"])

        s_ini = k.sem("ini")
        k.op("vector", lambda e: e.tensor_add(kv_sb[:], kvp_sb[:, :16 * 65],
                                              kvp_sb[:, 16 * 65:]), s_ini)
        k.wait("tensor", s_ini)
        for h in range(16):
            k.wait("tensor", s_ini)
            k.op("tensor", lambda e, h=h: e.matmul(
                ps_dd[:, :64], k.sbuf["ones_row"][:1, :64],
                sv_sb[:1, h * 64:(h + 1) * 64], start=True, stop=False))
            k.op("tensor", lambda e, h=h: e.matmul(
                ps_dd[:, :64], k.sbuf["ones_row"][:1, :64],
                sv_sb[:1, D + h * 64: D + h * 64 + 64], start=False, stop=True),
                s_mm)
            k.wait("vector", s_mm)
            k.chain("vector", [
                lambda e, h=h: e.tensor_scalar_mul(
                    kv_sb[:, h * 65:h * 65 + 64], kv_sb[:, h * 65:h * 65 + 64],
                    L2_RATIO),
                lambda e, h=h: e.scalar_tensor_tensor(
                    kv_sb[:, h * 65:h * 65 + 64], ps_dd[:, :64],
                    L2_RATIO * L2_EPS_K, kv_sb[:, h * 65:h * 65 + 64],
                    ALU.mult, ALU.add),
                lambda e, h=h: e.tensor_scalar(
                    kv_sb[:, h * 65 + 64:h * 65 + 65],
                    kv_sb[:, h * 65 + 64:h * 65 + 65],
                    L2_RATIO, L2_RATIO * L2_EPS_K * 4096.0, ALU.mult, ALU.add),
            ], s_ini)
        k.wait("vector", s_ini)
        k.op("vector", lambda e: e.tensor_copy(kvb[:], kv_sb[:]), s_ini)
        dsc = k.sb("dsc", [1, 17], F32)
        k.wait("tensor", s_ini)
        for h in range(16):
            k.wait("tensor", s_tp)
            k.op("tensor", lambda e, h=h: e.matmul(
                ps_mm[0][:1, :64], k.sbuf["ones_col"][:64, :],
                kv_sb[:, h * 65:h * 65 + 64], start=True, stop=True), s_mm)
            k.wait("vector", s_mm)
            k.op("vector", lambda e, h=h: e.tensor_copy(
                n2r[:1, h * 64:(h + 1) * 64], ps_mm[0][:1, :64]), s_tp)
        ksg = k.sb("ksg", [64, 16], F32)
        k.op("vector", lambda e: e.tensor_copy(
            ksg[:], bass.AP(kv_sb, 64, [[65 * 16, 64], [65, 16]])), s_ini)
        k.wait("tensor", s_ini)
        k.wait("tensor", s_tp)
        k.op("tensor", lambda e: e.matmul(
            ps_mm[0][:1, :16], k.sbuf["ones_col"][:64, :], ksg[:],
            start=True, stop=True), s_mm)
        k.wait("vector", s_mm)
        k.chain("vector", [
            lambda e: e.tensor_copy(dsc[:1, :16], ps_mm[0][:1, :16]),
            lambda e: e.tensor_reduce(dsc[:1, 16:17], dsc[:1, :16], AX.X, ALU.add),
        ], s_tp)
        k.wait("tensor", s_tp)
        for j in range(2):
            k.op("tensor", lambda e, j=j: e.matmul(
                ps_mm[0][:], k.sbuf["ones_row"][:1, :],
                n2r[:1, j * TT:(j + 1) * TT], start=True, stop=True), s_mm)
            k.wait("vector", s_mm)
            k.op("vector", lambda e, j=j: e.tensor_copy(
                n2b[:, j * TT:(j + 1) * TT], ps_mm[0][:]), s_tp)
            k.wait("tensor", s_tp)
        k.op("tensor", lambda e: e.matmul(
            ps_mm[0][:, :1], k.sbuf["ones_row"][:1, :], dsc[:1, 16:17],
            start=True, stop=True), s_mm)
        k.wait("vector", s_mm)
        k.op("vector", lambda e: e.tensor_copy(d2c[:], ps_mm[0][:, :1]), s_tp)
        k.wait("tensor", s_tp)

        for it in range(NTT if L2P3A_PHASE >= 2 else 0):
            t0 = it * TT
            k.wait("gpsimd", k.sem("ln_apply_s"), DC * 2 * it)
            k.dma("gpsimd", x_t[:].rearrange("p (c t) -> p c t", c=DC),
                  x_in.rearrange("(c p) t -> p c t", p=128)[:, :, t0:t0 + TT],
                  s_xt)
            for eng in ("scalar", "tensor", "vector"):
                k.wait(eng, s_xt)
            k.wait("vector", s_st, 16 * it)
            s_ap = emit_ln_tile(k, lambda c: x_t[:, c * TT:(c + 1) * TT],
                                lambda c: xn[:, c * TT:(c + 1) * TT])
            k.wait("tensor", s_ap)
            for hc in range(DC if L2P3A_PHASE >= 3 else 0):
                k.wait("tensor", s_ev)
                k.op("tensor", lambda e, hc=hc: [e.matmul(
                    ps_mm[0][:], wq_sb[:, ic * D + hc * 128: ic * D + hc * 128 + 128],
                    xn[:, ic * TT:(ic + 1) * TT], start=(ic == 0),
                    stop=(ic == DC - 1)) for ic in range(DC)][-1], s_mm)
                k.wait("vector", s_mm)
                k.wait("vector", s_at)
                k.wait("vector", s_ev)
                k.op("vector", lambda e: e.tensor_copy(qfm[:], ps_mm[0][:]), s_ev)
                k.wait("scalar", s_ev)
                k.op("scalar", lambda e: e.activation(q2f[:], qfm[:], AF.Square),
                     s_ev)
                for hh in range(2):
                    h = hc * 2 + hh
                    k.wait("tensor", s_ev)
                    k.op("tensor", lambda e, hh=hh: e.matmul(
                        ps_dd[:, :TT], pj_sb[hh * 64:(hh + 1) * 64, :],
                        qfm[hh * 64:(hh + 1) * 64, :],
                        start=True, stop=True), s_mm)
                    k.wait("scalar", s_mm)
                    k.wait("scalar", s_at)
                    k.op("scalar", lambda e: e.activation(eq[:], ps_dd[:, :TT],
                                                          AF.Exp), s_ev)
                    for st in range(4):
                        k.wait("tensor", s_ev)
                        k.wait("tensor", s_at)
                        k.op("tensor", lambda e, hh=hh, st=st: e.matmul(
                            ps_mm[1][:, :64],
                            qfm[hh * 64:(hh + 1) * 64, st * 128:(st + 1) * 128],
                            pj_sb[hh * 64:(hh + 1) * 64, :],
                            start=True, stop=True), s_mm)
                        k.wait("vector", s_mm)
                        k.op("vector", lambda e: e.reduce_max(
                            mxq[:], ps_mm[1][:, :64], AX.X), s_at)
                        k.wait("tensor", s_at)
                        k.op("tensor", lambda e, hh=hh, st=st: e.matmul(
                            ps_mm[1][:, 64:65],
                            q2f[hh * 64:(hh + 1) * 64, st * 128:(st + 1) * 128],
                            c625[hh * 64:(hh + 1) * 64, :],
                            start=True, stop=True), s_mm)
                        k.wait("vector", s_mm)
                        k.chain("vector", [
                            lambda e: e.tensor_add(dgq[:], ps_mm[1][:, 64:65],
                                                   mxq[:]),
                        ], s_at)
                        k.wait("scalar", s_at)
                        k.op("scalar", lambda e: e.activation(
                            fcol[:], dgq[:], AF.Exp, scale=-1.0), s_at)
                        k.wait("tensor", s_at)
                        k.op("tensor", lambda e, hh=hh, st=st, h=h: e.matmul(
                            ps_mm[1][:, :65], eq[:, st * 128:(st + 1) * 128],
                            kvb[:, h * 65:(h + 1) * 65], start=True, stop=True),
                            s_mm)
                        k.wait("vector", s_mm)
                        k.chain("vector", [
                            lambda e: e.tensor_scalar(rden[:], ps_mm[1][:, 64:65],
                                                      fcol[:, :1], None, ALU.mult),
                            lambda e: e.scalar_tensor_tensor(
                                rden[:], d2c[:], L2_EPS_K, rden[:],
                                ALU.mult, ALU.add),
                            lambda e: e.reciprocal(rden[:], rden[:]),
                            lambda e: e.tensor_mul(frc[:], fcol[:], rden[:]),
                            lambda e: e.tensor_scalar_mul(erc[:], rden[:],
                                                          L2_EPS_K),
                            lambda e: e.tensor_scalar(at_sb[:], ps_mm[1][:, :64],
                                                      frc[:, :1], None, ALU.mult),
                            lambda e, h=h: e.scalar_tensor_tensor(
                                at_sb[:], n2b[:, h * 64:(h + 1) * 64], erc[:, :1],
                                at_sb[:], ALU.mult, ALU.add),
                        ], s_at)
                        k.wait("tensor", s_at)
                        k.wait("tensor", s_tp)
                        k.op("tensor", lambda e: e.transpose(
                            ps_tp[:64, :128], at_sb[:, :64], ident[:]), s_tp)
                        k.wait("vector", s_tp)
                        slot = ((h // 2) * 4 + st) * 128
                        hb = (h % 2) * 64
                        k.op("vector", lambda e, hb=hb, slot=slot: e.tensor_copy(
                            atT[hb:hb + 64, slot:slot + 128],
                            ps_tp[:64, :128]), s_tp)
            k.wait("tensor", s_tp)
            for oc in range(DC if L2P3A_PHASE >= 4 else 0):
                for st in range(4):
                    k.wait("tensor", s_wo)
                    k.op("tensor", lambda e, oc=oc, st=st: [e.matmul(
                        ps_mm[0][:, :128],
                        wo_sb[:, hc2 * D + oc * 128: hc2 * D + oc * 128 + 128],
                        atT[:, (hc2 * 4 + st) * 128:(hc2 * 4 + st) * 128 + 128],
                        start=(hc2 == 0), stop=(hc2 == DC - 1))
                        for hc2 in range(DC)][-1], s_mm)
                    k.wait("vector", s_mm)
                    k.op("vector", lambda e, oc=oc, st=st: e.tensor_add(
                        o1p[:, oc * TT + st * 128: oc * TT + st * 128 + 128],
                        x_t[:, oc * TT + st * 128: oc * TT + st * 128 + 128],
                        ps_mm[0][:, :128]), s_wo)
            for eng in ("scalar", "tensor", "vector"):
                k.wait(eng, s_wo)
            s_ap1 = emit_ln_tile(k, lambda c: o1p[:, c * TT:(c + 1) * TT],
                                 lambda c: o1p[:, c * TT:(c + 1) * TT],
                                 eps_name="eps_ln1",
                                 affine=(lambda c: g1[:, c:c + 1],
                                         lambda c: b1[:, c:c + 1]))
            k.wait("gpsimd", s_ap1)
            k.dma("gpsimd",
                  out1.rearrange("(c p) t -> p c t", p=128)[:, :, t0:t0 + TT],
                  o1p[:].rearrange("p (c t) -> p c t", c=DC), s_st)
        k.wait("gpsimd", s_st)
        k.play()
    return nc


def build_l2attn_p3b():
    nc = bass.Bass("TRN2", target_bir_lowering=False,
                   detect_race_conditions=_RD)
    x_in = _l2_common(nc)
    o1_in = nc.dram_tensor("o1_in", [D, TLOC], F32, kind="ExternalInput").ap()
    w1t = nc.dram_tensor("w1t", [D, 4096], F32, kind="ExternalInput").ap()
    b1c = nc.dram_tensor("b1c", [128, 32], F32, kind="ExternalInput").ap()
    w2t = nc.dram_tensor("w2t", [4096, D], F32, kind="ExternalInput").ap()
    b2c = nc.dram_tensor("b2c", [128, 8], F32, kind="ExternalInput").ap()
    ln2gc = nc.dram_tensor("ln2gc", [128, 8], F32, kind="ExternalInput").ap()
    ln2bc = nc.dram_tensor("ln2bc", [128, 8], F32, kind="ExternalInput").ap()
    x_out = nc.dram_tensor("x_out", [D, TLOC], F32, kind="ExternalOutput").ap()

    FF = 4096
    FC = FF // 128

    with ExitStack() as stack:
        k = KB(nc, stack)
        emit_consts(k, {"eps_ln2": 1e-6})
        x_t = k.sb("x_t", [128, DC * TT], F32)
        o1f = k.sb("o1f", [128, DC * TT], F32)
        o1b = k.sb("o1b", [128, DC * TT], BF16)
        h_sb = k.sb("h_sb", [128, FC * (TT // 2)], BF16)
        w1_sb = k.sb("w1_sb", [128, DC * FF], BF16)
        w2_sb = k.sb("w2_sb", [128, FC * D], BF16)
        b1_sb = k.sb("b1_sb", [128, 32], F32)
        b2_sb = k.sb("b2_sb", [128, 8], F32)
        g2 = k.sb("g2", [128, DC], F32)
        bb2 = k.sb("bb2", [128, DC], F32)
        zt = k.sb("vscratch", [128, TT], F32)
        am = k.sb("am", [128, TT], BF16)
        eb = k.sb("eb", [128, TT], BF16)
        rm = am
        ps_mm = [k.ps(f"mm{i}", [128, TT]) for i in range(2)]

        s_w = k.sem("w_ld")
        s_b = k.sem("b_ld")
        s_xt = k.sem("x_ld")
        s_o1 = k.sem("o1_ld")
        s_o1s = k.sem("o1_lds")
        s_mm1 = k.sem("mm1")
        s_e1 = k.sem("e1")
        s_e2 = k.sem("e2")
        s_hf = k.sem("h_free")
        s_mm2 = k.sem("mm2")
        s_of = k.sem("o_free")
        s_res = k.sem("res")
        s_st = k.sem("st")

        k.dma("gpsimd", w1_sb[:].rearrange("p (c x) -> p c x", c=DC),
              w1t.rearrange("(c p) x -> p c x", p=128), s_w)
        k.dma("gpsimd", w2_sb[:].rearrange("p (c x) -> p c x", c=FC),
              w2t.rearrange("(c p) x -> p c x", p=128), s_w)
        k.dma("sync", b1_sb[:], b1c, s_b)
        k.dma("sync", b2_sb[:], b2c, s_b)
        k.dma("sync", g2[:], ln2gc, s_b)
        k.dma("sync", bb2[:], ln2bc, s_b)
        k.wait("tensor", s_w)
        for eng in ("scalar", "vector"):
            k.wait(eng, s_b)

        for it in range(NTT):
            t0 = it * TT
            k.wait("gpsimd", s_st, 16 * it)
            k.dma("gpsimd", x_t[:].rearrange("p (c t) -> p c t", c=DC),
                  x_in.rearrange("(c p) t -> p c t", p=128)[:, :, t0:t0 + TT],
                  s_xt)
            k.wait("gpsimd", s_res, DC * it)
            k.dma("gpsimd", o1b[:].rearrange("p (c t) -> p c t", c=DC),
                  o1_in.rearrange("(c p) t -> p c t", p=128)[:, :, t0:t0 + TT],
                  s_o1)
            k.wait("sync", s_res, DC * it)
            k.dma("sync", o1f[:].rearrange("p (c t) -> p c t", c=DC),
                  o1_in.rearrange("(c p) t -> p c t", p=128)[:, :, t0:t0 + TT],
                  s_o1s)
            k.wait("tensor", s_o1)
            k.wait("tensor", s_o1s)
            for eng in ("scalar", "vector"):
                k.wait(eng, s_o1)
                k.wait(eng, s_o1s)
                k.wait(eng, s_xt)

            HT = TT // 2
            for tb in range(2):
                tof = tb * HT
                k.wait("tensor", s_of)
                for oc in range(FC):
                    b = ps_mm[oc % 2][:, :HT]
                    k.wait("tensor", s_hf, k.sems["h_free"].n - 1)
                    for ic in range(DC):
                        k.op("tensor", lambda e, b=b, ic=ic, oc=oc, tof=tof:
                             e.matmul(b, w1_sb[:, ic * FF + oc * 128:
                                               ic * FF + oc * 128 + 128],
                                      o1b[:, ic * TT + tof: ic * TT + tof + HT],
                                      start=(ic == 0), stop=(ic == DC - 1)),
                             sig=s_mm1 if ic == DC - 1 else None)
                    k.wait("vector", s_mm1)
                    k.wait("vector", s_hf)
                    if oc == 0:
                        k.wait("vector", s_mm2, 16 * it + 8 * tb)
                    k.chain("vector", [
                        lambda e, b=b, oc=oc: e.tensor_scalar(
                            zt[:, :HT], b, b1_sb[:, oc:oc + 1], None, ALU.add),
                        lambda e: e.tensor_scalar(am[:, :HT], zt[:, :HT], 0.0,
                                                  None, ALU.min),
                    ], s_e1)
                    k.wait("scalar", s_e1)
                    k.op("scalar", lambda e: e.activation(eb[:, :HT], am[:, :HT],
                                                          AF.Exp), s_e2)
                    k.wait("vector", s_e2)
                    k.chain("vector", [
                        lambda e: e.tensor_scalar(am[:, :HT], zt[:, :HT], 0.0,
                                                  None, ALU.max),
                        lambda e, oc=oc: e.scalar_tensor_tensor(
                            h_sb[:, oc * HT:(oc + 1) * HT], am[:, :HT], -1.0,
                            eb[:, :HT], ALU.add, ALU.add),
                    ], s_hf)
                k.wait("tensor", s_hf)
                for oc in range(DC):
                    b = ps_mm[oc % 2][:, :HT]
                    k.wait("tensor", s_of, k.sems["o_free"].n - 1)
                    for ic in range(FC):
                        k.op("tensor", lambda e, b=b, ic=ic, oc=oc: e.matmul(
                            b, w2_sb[:, ic * D + oc * 128: ic * D + oc * 128 + 128],
                            h_sb[:, ic * HT:(ic + 1) * HT],
                            start=(ic == 0), stop=(ic == FC - 1)),
                            sig=s_mm2 if ic == FC - 1 else None)
                    k.wait("vector", s_mm2)
                    k.wait("vector", s_of)
                    k.chain("vector", [
                        lambda e, b=b, oc=oc: e.tensor_scalar(
                            zt[:, :HT], b, b2_sb[:, oc:oc + 1], None, ALU.add),
                        lambda e, oc=oc, tof=tof: e.tensor_add(
                            o1f[:, oc * TT + tof: oc * TT + tof + HT],
                            o1f[:, oc * TT + tof: oc * TT + tof + HT],
                            zt[:, :HT]),
                    ], s_of)
            for eng in ("scalar", "tensor", "vector"):
                k.wait(eng, s_of)
            s_ap2 = emit_ln_tile(k, lambda c: o1f[:, c * TT:(c + 1) * TT],
                                 lambda c: o1f[:, c * TT:(c + 1) * TT],
                                 eps_name="eps_ln2",
                                 affine=(lambda c: g2[:, c:c + 1],
                                         lambda c: bb2[:, c:c + 1]))
            k.wait("vector", s_ap2)
            for c in range(DC):
                k.op("vector", lambda e, c=c: e.tensor_add(
                    x_t[:, c * TT:(c + 1) * TT], x_t[:, c * TT:(c + 1) * TT],
                    o1f[:, c * TT:(c + 1) * TT]), s_res)
            k.wait("gpsimd", s_res)
            k.dma("gpsimd",
                  x_out.rearrange("(c p) t -> p c t", p=128)[:, :, t0:t0 + TT],
                  x_t[:].rearrange("p (c t) -> p c t", c=DC), s_st)
        k.wait("gpsimd", s_st)
        k.play()
    return nc


def host_l2attn_inputs(z):
    Wq = z["L2_attn_Wq"].reshape(D, D)
    Wk = z["L2_attn_Wk"].reshape(D, D)
    Wv = z["L2_attn_Wv"].reshape(D, D)
    Wo = z["L2_attn_Wo"].reshape(D, D)
    proj = z["L2_attn_proj"]
    pj1 = np.ascontiguousarray(proj.T)
    projdn = np.concatenate([pj1, pj1], axis=0)
    return dict(
        wq=np.ascontiguousarray(Wq), wk=np.ascontiguousarray(Wk),
        wv=np.ascontiguousarray(Wv), wo_r=np.ascontiguousarray(Wo),
        projdn=projdn.astype(np.float32),
        w1t=np.ascontiguousarray(z["L2_attn_W1"]),
        b1c=col128(z["L2_attn_b1"]),
        w2t=np.ascontiguousarray(z["L2_attn_W2"]),
        b2c=col128(z["L2_attn_b2"]),
        ln1gc=col128(z["L2_attn_ln1_g"]), ln1bc=col128(z["L2_attn_ln1_b"]),
        ln2gc=col128(z["L2_attn_ln2_g"]), ln2bc=col128(z["L2_attn_ln2_b"]),
    )



L3H = 384
L3T = TLOC + L3H
L3ST = L3T // 128
L3QT = TLOC // 256
INNER = 1024
DIMI2 = 5632
HCI = DIMI2 // 128


def build_l3pre():
    nc = bass.Bass("TRN2", target_bir_lowering=False,
                   detect_race_conditions=_RD)
    x_in = nc.dram_tensor("x_in", [D, TLOC], F32, kind="ExternalInput").ap()
    z_out = nc.dram_tensor("z_out", [D, TLOC], F32, kind="ExternalOutput").ap()
    with ExitStack() as stack:
        k = KB(nc, stack)
        emit_consts(k, {"eps_ln": 1e-5})
        x_t = k.sb("x_t", [128, DC * TT], F32)
        zn = k.sb("zn", [128, DC * TT], F32)
        s_xt = k.sem("x_ld")
        s_st = k.sem("st")
        for it in range(NTT):
            t0 = it * TT
            k.wait("gpsimd", s_st, 16 * it)
            k.dma("gpsimd", x_t[:].rearrange("p (c t) -> p c t", c=DC),
                  x_in.rearrange("(c p) t -> p c t", p=128)[:, :, t0:t0 + TT],
                  s_xt)
            for eng in ("scalar", "tensor", "vector"):
                k.wait(eng, s_xt)
            s_ap = emit_ln_tile(k, lambda c: x_t[:, c * TT:(c + 1) * TT],
                                lambda c: zn[:, c * TT:(c + 1) * TT])
            k.wait("gpsimd", s_ap)
            k.dma("gpsimd",
                  z_out.rearrange("(c p) t -> p c t", p=128)[:, :, t0:t0 + TT],
                  zn[:].rearrange("p (c t) -> p c t", c=DC), s_st)
        k.wait("gpsimd", s_st)
        k.play()
    return nc


def build_l3post():
    nc = bass.Bass("TRN2", target_bir_lowering=False,
                   detect_race_conditions=_RD)
    x_in = nc.dram_tensor("x_in", [D, TLOC], F32, kind="ExternalInput").ap()
    z_in = nc.dram_tensor("z_in", [D, TLOC], F32, kind="ExternalInput").ap()
    x_out = nc.dram_tensor("x_out", [D, TLOC], F32, kind="ExternalOutput").ap()
    with ExitStack() as stack:
        k = KB(nc, stack)
        x_t = k.sb("x_t", [128, DC * TT], F32)
        z_t = k.sb("z_t", [128, DC * TT], F32)
        s_xt = k.sem("x_ld")
        s_zt = k.sem("z_ld")
        s_ad = k.sem("add")
        s_st = k.sem("st")
        for it in range(NTT):
            t0 = it * TT
            k.wait("gpsimd", s_st, 16 * it)
            k.dma("gpsimd", x_t[:].rearrange("p (c t) -> p c t", c=DC),
                  x_in.rearrange("(c p) t -> p c t", p=128)[:, :, t0:t0 + TT],
                  s_xt)
            k.wait("sync", s_ad, DC * it)
            k.dma("sync", z_t[:].rearrange("p (c t) -> p c t", c=DC),
                  z_in.rearrange("(c p) t -> p c t", p=128)[:, :, t0:t0 + TT],
                  s_zt)
            k.wait("vector", s_xt)
            k.wait("vector", s_zt)
            for c in range(DC):
                k.op("vector", lambda e, c=c: e.scalar_tensor_tensor(
                    x_t[:, c * TT:(c + 1) * TT], z_t[:, c * TT:(c + 1) * TT],
                    0.5, x_t[:, c * TT:(c + 1) * TT], ALU.mult, ALU.add), s_ad)
            k.wait("gpsimd", s_ad)
            k.dma("gpsimd",
                  x_out.rearrange("(c p) t -> p c t", p=128)[:, :, t0:t0 + TT],
                  x_t[:].rearrange("p (c t) -> p c t", c=DC), s_st)
        k.wait("gpsimd", s_st)
        k.play()
    return nc


def build_l3ffn_attn():
    nc = bass.Bass("TRN2", target_bir_lowering=False,
                   detect_race_conditions=_RD)
    zh_in = nc.dram_tensor("zh_in", [D, L3T], F32, kind="ExternalInput").ap()
    wqkv = nc.dram_tensor("wqkv", [D, 3 * INNER], F32, kind="ExternalInput").ap()
    wo = nc.dram_tensor("wo", [INNER, D], F32, kind="ExternalInput").ap()
    tq = nc.dram_tensor("tq", [64, 2 * TLOC], F32, kind="ExternalInput").ap()
    tk = nc.dram_tensor("tk", [64, 2 * L3T], F32, kind="ExternalInput").ap()
    qks2 = nc.dram_tensor("qks2", [64, 4], F32, kind="ExternalInput").ap()
    qposr = nc.dram_tensor("qposr", [1, TLOC], F32, kind="ExternalInput").ap()
    sposc = nc.dram_tensor("sposc", [128, L3ST], F32, kind="ExternalInput").ap()
    za_out = nc.dram_tensor("za_out", [D, TLOC], F32, kind="ExternalOutput").ap()

    QTILES = [(0, 512), (512, 512), (1024, 512), (1536, 512)]
    KTILES = QTILES + [(2048, 384)]
    I3 = 3 * INNER

    with ExitStack() as stack:
        k = KB(nc, stack)
        emit_consts(k, {})
        zb = k.sb("zb", [128, DC * L3T], BF16)
        qpre = k.sb("qpre", [64, 512], BF16)
        kpre = k.sb("kpre", [64, 512], BF16)
        qh = k.sb("qh", [64, TLOC], BF16)
        kh = k.sb("kh", [64, L3T], BF16)
        rotb = k.sb("rotb", [64, 512], BF16)
        rt1 = k.sb("rt1", [64, 512], BF16)
        nrow = k.sb("nrow", [1, 512], F32)
        rdb = k.sb("rdb", [64, 256], F32)
        vp = k.sb("vp", [128, L3ST * 128], BF16)
        pb = k.sb("pb", [128, 4 * 256], BF16)
        pbf = pb.bitcast(F32)
        q2f = pbf
        kt1 = pbf
        kt2 = pbf
        keepall = k.sb("keepall", [128, L3QT * 4 * 256], BF16)
        x_t = keepall.bitcast(F32)
        ctxs = k.sb("ctxs", [128, 4 * TLOC], BF16)
        aacc = k.sb("aacc", [128, DC * TLOC], BF16)
        wqkv_sb = k.sb("wqkv_sb", [128, DC * I3], BF16)
        wo_sb = k.sb("wo_sb", [128, DC * D], BF16)
        tq_sb = k.sb("tq_sb", [64, 2 * TLOC], BF16)
        tk_sb = k.sb("tk_sb", [64, 2 * L3T], BF16)
        qpr = ctxs.bitcast(F32)
        spc = k.sb("spc", [128, L3ST], F32)
        spc2 = k.sb("spc2", [128, L3ST], F32)
        m8c = k.sb("m8c", [128, 1], F32)
        o64bf = k.sb("o64bf", [128, 64], BF16)

        ps_mm = [k.ps(f"mm{i}", [128, TT]) for i in range(2)]
        ps_r = k.ps("ps_r", [1, TT])
        ps_bc = k.ps("ps_bc", [64, TT])
        ps_v2 = k.ps("ps_v2", [128, TT])
        ps_q2 = k.ps("ps_q2", [128, TT])

        s_ld = k.sem("ld")
        s_ld2 = k.sem("ld2")
        s_xt = k.sem("x_ld")
        s_mm = k.sem("mm")
        s_ev = k.sem("evac")
        s_rw = k.sem("rw")
        s_rot = k.sem("rot")
        s_kp = k.sem("kp")
        s_ex = k.sem("ex")
        s_pk = k.sem("pk")
        s_cx = k.sem("cx")
        s_wo = k.sem("wo_s")
        s_res = k.sem("res")
        s_st = k.sem("st")
        s_c2 = k.sem("cst2")

        k.op("gpsimd", lambda e: e.memset(m8c[:], -8.0))
        k.op("gpsimd", lambda e: e.memset(o64bf[:], 1.0), s_c2)
        k.dma("gpsimd", zb[:].rearrange("p (c t) -> p c t", c=DC),
              zh_in.rearrange("(c p) t -> p c t", p=128), s_ld)
        k.dma("gpsimd", wqkv_sb[:].rearrange("p (c x) -> p c x", c=DC),
              wqkv.rearrange("(c p) x -> p c x", p=128), s_ld)
        k.dma("gpsimd", wo_sb[:].rearrange("p (c x) -> p c x", c=DC),
              wo.rearrange("(c p) x -> p c x", p=128), s_ld)
        k.dma("gpsimd", tq_sb[:], tq, s_ld)
        k.dma("gpsimd", tk_sb[:], tk, s_ld)
        qk2 = k.sb("qk2", [64, 4], F32)
        k.dma("sync", qpr[:1, :TLOC], qposr, s_ld2)
        k.dma("sync", spc[:], sposc, s_ld2)
        k.dma("sync", qk2[:], qks2, s_ld2)
        for eng in ("tensor", "scalar", "vector"):
            k.wait(eng, s_ld)
            k.wait(eng, s_ld2)
            k.wait(eng, s_c2)
        for i, (tb, w) in enumerate(((tq_sb, TLOC), (tk_sb, L3T))):
            for half in range(2):
                k.op("vector", lambda e, tb=tb, w=w, half=half, i=i:
                     e.tensor_scalar(tb[:, half * w:(half + 1) * w],
                                     tb[:, half * w:(half + 1) * w],
                                     qk2[:, 2 * i + half:2 * i + half + 1],
                                     None, ALU.mult), k.sem("tbl"))
        k.wait("tensor", k.sems["tbl"])
        k.wait("scalar", k.sems["tbl"])
        k.op("vector", lambda e: e.tensor_scalar_add(spc2[:], spc[:], 256.0),
             s_kp)
        k.wait("vector", s_kp)

        def zs(c, col, w):
            off = c * L3T + col - (1 if c >= 4 else 0)
            return zb[:, off:off + w]

        for qt in range(L3QT):
            q0 = qt * 256
            k.wait("tensor", s_kp)
            k.op("tensor", lambda e, q0=q0: e.matmul(
                ps_mm[0][:, :256], k.sbuf["ones_row"][:1, :],
                qpr[:1, q0:q0 + 256], start=True, stop=True), s_mm)
            k.wait("vector", s_mm)
            for j in range(4):
                ss = qt * 2 + 1 + j
                kslc = keepall[:, (qt * 4 + j) * 256:(qt * 4 + j) * 256 + 256]
                k.wait("vector", s_kp)
                k.chain("vector", [
                    lambda e, ss=ss: e.tensor_scalar(
                        kt1[:, :256], ps_mm[0][:, :256], spc[:, ss:ss + 1],
                        None, ALU.is_ge),
                    lambda e, ss=ss: e.tensor_scalar(
                        kt2[:, 256:512], ps_mm[0][:, :256], spc2[:, ss:ss + 1],
                        None, ALU.is_le),
                    lambda e, kslc=kslc: e.tensor_mul(
                        kslc, kt1[:, :256], kt2[:, 256:512]),
                ], s_kp)

        for hp in range(DC):
            half, c2 = hp // 4, hp % 4
            for st in range(L3ST):
                vb = ps_mm[1] if (st % 2 == 0 or not (L3A_PIPE & 1)) else ps_v2
                k.wait("tensor", s_cx)
                if st < 2 or not (L3A_PIPE & 1):
                    k.wait("tensor", s_ev)
                    k.wait("tensor", s_wo)
                else:
                    k.wait("tensor", s_ev, s_ev.n - 1)
                k.op("tensor", lambda e, st=st, hp=hp, vb=vb: [e.matmul(
                    vb[:, :128], zs(ic, st * 128, 128),
                    wqkv_sb[:, ic * I3 + 2 * INNER + hp * 128:
                            ic * I3 + 2 * INNER + hp * 128 + 128],
                    start=(ic == 0), stop=(ic == DC - 1))
                    for ic in range(DC)][-1], s_mm)
                k.wait("vector", s_mm)
                k.op("vector", lambda e, st=st, vb=vb: e.tensor_copy(
                    vp[:, st * 128:(st + 1) * 128], vb[:, :128]), s_ev)
            for hh in range(2):
                h = hp * 2 + hh
                hcol = h * 64
                k.wait("vector", s_cx)
                k.wait("scalar", s_cx)
                k.wait("scalar", s_kp)
                ti = 0
                for which in range(2):
                    tiles = QTILES if which == 0 else KTILES
                    base = L3H if which == 0 else 0
                    proj_off = hcol if which == 0 else INNER + hcol
                    tbl = tq_sb if which == 0 else tk_sb
                    tspan = TLOC if which == 0 else L3T
                    dst = qh if which == 0 else kh
                    for (t0, w) in tiles:
                        if L3A_PIPE & 4:
                            pre = qpre if ti % 2 == 0 else kpre
                            slot = ps_mm[0] if ti % 2 == 0 else ps_q2
                        else:
                            pre = qpre if which == 0 else kpre
                            slot = ps_mm[0]
                        ti += 1
                        k.wait("tensor", s_rot,
                               max(0, s_rot.n - 1) if L3A_PIPE & 4
                               else s_rot.n)
                        k.wait("tensor", s_cx)
                        k.op("tensor", lambda e, t0=t0, w=w, base=base,
                             proj_off=proj_off, slot=slot: [e.matmul(
                                 slot[:64, :w],
                                 wqkv_sb[:, ic * I3 + proj_off:
                                         ic * I3 + proj_off + 64],
                                 zs(ic, base + t0, w),
                                 start=(ic == 0), stop=(ic == DC - 1))
                                 for ic in range(DC)][-1], s_mm)
                        k.wait("vector", s_mm)
                        k.op("vector", lambda e, pre=pre, w=w, slot=slot:
                             e.tensor_copy(pre[:, :w], slot[:64, :w]), s_ev)
                        k.wait("scalar", s_ev)
                        k.wait("scalar", s_mm, s_mm.n)
                        k.op("scalar", lambda e, pre=pre, w=w: e.activation(
                            q2f[:64, :w], pre[:, :w], AF.Square), s_ev)
                        k.wait("tensor", s_ev)
                        k.wait("tensor", s_rw)
                        k.op("tensor", lambda e, w=w: e.matmul(
                            ps_r[:1, :w], k.sbuf["ones_col"][:64, :],
                            q2f[:64, :w], start=True, stop=True), s_mm)
                        k.wait("vector", s_mm)
                        k.op("vector", lambda e, w=w: e.tensor_scalar(
                            nrow[:1, :w], ps_r[:1, :w], 1e-24, None, ALU.max),
                            s_rw)
                        k.wait("scalar", s_rw)
                        k.chain("scalar", [
                            lambda e, w=w: e.activation(nrow[:1, :w],
                                                        nrow[:1, :w], AF.Ln),
                            lambda e, w=w: e.activation(nrow[:1, :w],
                                                        nrow[:1, :w], AF.Exp,
                                                        scale=-0.5),
                        ], s_rw)
                        k.wait("tensor", s_rw)
                        k.op("tensor", lambda e, w=w: e.matmul(
                            ps_bc[:, :w], k.sbuf["ones_row"][:1, :64],
                            nrow[:1, :w], start=True, stop=True), s_mm)
                        k.wait("vector", s_mm)
                        k.chain("vector", [
                            lambda e, pre=pre, w=w: e.tensor_copy(
                                rotb[0:32, :w], pre[32:64, :w]),
                            lambda e, pre=pre, w=w: e.tensor_copy(
                                rotb[32:64, :w], pre[0:32, :w]),
                            lambda e, pre=pre, tbl=tbl, t0=t0, w=w:
                                e.tensor_mul(rt1[:, :w], pre[:, :w],
                                             tbl[:, t0:t0 + w]),
                            lambda e, tbl=tbl, t0=t0, w=w, tspan=tspan:
                                e.tensor_mul(rotb[:, :w], rotb[:, :w],
                                             tbl[:, tspan + t0:tspan + t0 + w]),
                            lambda e, w=w: e.tensor_add(rt1[:, :w], rt1[:, :w],
                                                        rotb[:, :w]),
                            lambda e, dst=dst, t0=t0, w=w: e.tensor_mul(
                                dst[:, t0:t0 + w], rt1[:, :w], ps_bc[:, :w]),
                        ], s_rot)
                k.wait("tensor", s_ev)
                for qt in range(L3QT):
                    q0 = qt * 256
                    for j in range(4):
                        ss = qt * 2 + 1 + j
                        sb_ = (ps_mm[1] if (j % 2 == 0 or not (L3A_PIPE & 2))
                               else ps_v2)
                        k.wait("tensor", s_ex,
                               max(0, s_ex.n - 1) if L3A_PIPE & 2
                               else s_ex.n)
                        k.wait("tensor", s_rot)
                        k.op("tensor", lambda e, ss=ss, q0=q0, sb_=sb_:
                             e.matmul(sb_[:, :256],
                                      kh[:, ss * 128:(ss + 1) * 128],
                                      qh[:, q0:q0 + 256],
                                      start=True, stop=True), s_mm)
                        k.wait("scalar", s_mm)
                        k.wait("scalar", s_cx)
                        k.op("scalar", lambda e, j=j, sb_=sb_: e.activation(
                            pb[:, j * 256:(j + 1) * 256], sb_[:, :256],
                            AF.Exp, bias=m8c[:, :1], scale=8.0), s_ex)
                    k.wait("vector", s_ex)
                    for j in range(4):
                        kslc = keepall[:, (qt * 4 + j) * 256:(qt * 4 + j) * 256 + 256]
                        k.op("vector", lambda e, j=j, kslc=kslc: e.tensor_mul(
                            pb[:, j * 256:(j + 1) * 256],
                            pb[:, j * 256:(j + 1) * 256], kslc), s_pk)
                    k.wait("tensor", s_pk)
                    k.wait("tensor", s_rw)
                    k.op("tensor", lambda e: [e.matmul(
                        ps_bc[:, :256], o64bf[:, :],
                        pb[:, j * 256:(j + 1) * 256],
                        start=(j == 0), stop=(j == 3)) for j in range(4)][-1],
                        s_mm)
                    k.wait("vector", s_mm)
                    k.op("vector", lambda e: e.reciprocal(rdb[:, :256],
                                                          ps_bc[:, :256]), s_rw)
                    k.wait("tensor", s_rw)
                    k.wait("tensor", s_cx)
                    k.op("tensor", lambda e, qt=qt, hh=hh: [e.matmul(
                        ps_mm[0][hh * 64:hh * 64 + 64, :256],
                        vp[:, (qt * 2 + 1 + j) * 128 + hh * 64:
                           (qt * 2 + 1 + j) * 128 + hh * 64 + 64],
                        pb[:, j * 256:(j + 1) * 256],
                        start=(j == 0), stop=(j == 3)) for j in range(4)][-1],
                        s_mm)
                    k.wait("vector", s_mm)
                    slot = (c2 * L3QT + qt) * 256
                    k.op("vector", lambda e, hh=hh, slot=slot: e.tensor_mul(
                        ctxs[hh * 64:hh * 64 + 64, slot:slot + 256],
                        ps_mm[0][hh * 64:hh * 64 + 64, :256], rdb[:, :256]),
                        s_cx)
            if c2 == 3:
                k.wait("tensor", s_cx)
                for oc in range(DC):
                    for tl in range(NTT):
                        k.wait("tensor", s_wo)
                        k.op("tensor", lambda e, oc=oc, tl=tl, half=half:
                             [e.matmul(
                                 ps_mm[1][:],
                                 wo_sb[:, (half * 4 + cc) * D + oc * 128:
                                       (half * 4 + cc) * D + oc * 128 + 128],
                                 ctxs[:, (cc * L3QT + 2 * tl) * 256:
                                      (cc * L3QT + 2 * tl) * 256 + 512],
                                 start=(cc == 0), stop=(cc == 3))
                                 for cc in range(4)][-1], s_mm)
                        k.wait("vector", s_mm)
                        aslc = aacc[:, oc * TLOC + tl * TT: oc * TLOC + tl * TT + TT]
                        if half == 0:
                            k.op("vector", lambda e, aslc=aslc: e.tensor_copy(
                                aslc, ps_mm[1][:]), s_wo)
                        else:
                            k.op("vector", lambda e, aslc=aslc: e.tensor_add(
                                aslc, aslc, ps_mm[1][:]), s_wo)

        for it in range(NTT):
            t0 = it * TT
            k.wait("gpsimd", s_wo)
            k.wait("gpsimd", s_st, 16 * it)
            k.dma("gpsimd", x_t[:].rearrange("p (c t) -> p c t", c=DC),
                  zh_in.rearrange("(c p) t -> p c t", p=128)[:, :, L3H + t0:
                                                            L3H + t0 + TT],
                  s_xt)
            k.wait("vector", s_xt)
            for c in range(DC):
                k.op("vector", lambda e, c=c, t0=t0: e.tensor_add(
                    x_t[:, c * TT:(c + 1) * TT], x_t[:, c * TT:(c + 1) * TT],
                    aacc[:, c * TLOC + t0: c * TLOC + t0 + TT]), s_res)
            k.wait("gpsimd", s_res)
            k.dma("gpsimd",
                  za_out.rearrange("(c p) t -> p c t", p=128)[:, :, t0:t0 + TT],
                  x_t[:].rearrange("p (c t) -> p c t", c=DC), s_st)
        k.wait("gpsimd", s_st)
        k.play()
    return nc


def build_l3ffn_ffn():
    nc = bass.Bass("TRN2", target_bir_lowering=False,
                   detect_race_conditions=_RD)
    zh_in = nc.dram_tensor("zh_in", [D, L3T], F32, kind="ExternalInput").ap()
    lngc = nc.dram_tensor("lngc", [128, 8], F32, kind="ExternalInput").ap()
    lnbc = nc.dram_tensor("lnbc", [128, 8], F32, kind="ExternalInput").ap()
    w1p = nc.dram_tensor("w1p", [D, DIMI2], F32, kind="ExternalInput").ap()
    b1ac = nc.dram_tensor("b1ac", [128, 22], F32, kind="ExternalInput").ap()
    b1gc = nc.dram_tensor("b1gc", [128, 22], F32, kind="ExternalInput").ap()
    w2p = nc.dram_tensor("w2p", [DIMI2 // 2, D], F32, kind="ExternalInput").ap()
    b2c = nc.dram_tensor("b2c", [128, 8], F32, kind="ExternalInput").ap()
    z_out = nc.dram_tensor("z_out", [D, TLOC], F32, kind="ExternalOutput").ap()

    HC2 = 22

    with ExitStack() as stack:
        k = KB(nc, stack)
        emit_consts(k, {"eps_l3": 1e-5})
        x_t = k.sb("x_t", [128, DC * TT], F32)
        xn = k.sb("xnf", [128, DC * TLOC], BF16)
        h_sb = k.sb("h_sb", [128, HC2 * 256], BF16)
        w1_sb = k.sb("w1_sb", [128, DC * DIMI2], BF16)
        w2_sb = k.sb("w2_sb", [128, HC2 * D], BF16)
        g1 = k.sb("g1c", [128, DC], F32)
        b1 = k.sb("b1c_", [128, DC], F32)
        b1a = k.sb("b1a", [128, HC2], F32)
        b1g = k.sb("b1g", [128, HC2], F32)
        b2 = k.sb("b2_", [128, DC], F32)
        gsc = k.sb("gsc", [128, 512], F32)
        av = k.sb("av", [128, 256], F32)

        ps_mm = [k.ps(f"mm{i}", [128, TT]) for i in range(4)]

        zo = x_t[:, :DC * 256]
        xr = x_t[:, DC * 256:DC * 512]

        s_ld = k.sem("ld")
        s_ld2 = k.sem("ld2")
        s_xt = k.sem("x_ld")
        s_mm1 = k.sem("mm1")
        s_gel = k.sem("gel")
        s_hf = k.sem("h_free")
        s_mm2 = k.sem("mm2")
        s_of = k.sem("o_free")
        s_st = k.sem("st")

        k.dma("gpsimd", w1_sb[:].rearrange("p (c x) -> p c x", c=DC),
              w1p.rearrange("(c p) x -> p c x", p=128), s_ld)
        k.dma("gpsimd", w2_sb[:].rearrange("p (c x) -> p c x", c=HC2),
              w2p.rearrange("(c p) x -> p c x", p=128), s_ld)
        for name, dst in [("lngc", g1), ("lnbc", b1), ("b1ac", b1a),
                          ("b1gc", b1g), ("b2c", b2)]:
            k.dma("sync", dst[:], {"lngc": lngc, "lnbc": lnbc, "b1ac": b1ac,
                                   "b1gc": b1gc, "b2c": b2c}[name], s_ld2)
        k.wait("tensor", s_ld)
        for eng in ("scalar", "vector"):
            k.wait(eng, s_ld2)

        for it in range(NTT):
            t0 = it * TT
            k.wait("gpsimd", k.sem("ln_apply_s"), DC * it)
            for cg, sh in ((0, 0), (4, 1)):
                k.dma("gpsimd",
                      x_t[:].rearrange("p (c t) -> p c t", c=DC)[:, cg:cg + 4],
                      zh_in.rearrange("(c p) t -> p c t", p=128)[
                          :, cg:cg + 4, L3H + t0 - sh: L3H + t0 - sh + TT],
                      s_xt)
            for eng in ("scalar", "tensor", "vector"):
                k.wait(eng, s_xt)
            s_ap = emit_ln_tile(k, lambda c: x_t[:, c * TT:(c + 1) * TT],
                                lambda c, t0=t0: xn[:, c * TLOC + t0:
                                                    c * TLOC + t0 + TT],
                                eps_name="eps_l3",
                                affine=(lambda c: g1[:, c:c + 1],
                                        lambda c: b1[:, c:c + 1]))
        for tt2 in range(TLOC // 256):
            t0 = tt2 * 256
            k.wait("tensor", k.sems["ln_apply_s"])
            k.wait("tensor", s_of)
            for oc in range(HC2):
                p = oc % 2
                ba = ps_mm[2 * p][:, :256]
                bg = ps_mm[2 * p + 1][:, :256]
                k.wait("tensor", s_hf, max(0, s_hf.n - 1))
                k.wait("tensor", s_gel, max(0, s_gel.n - 1))
                for ic in range(DC):
                    k.op("tensor", lambda e, ba=ba, ic=ic, oc=oc, t0=t0:
                         e.matmul(ba, w1_sb[:, ic * DIMI2 + oc * 128:
                                            ic * DIMI2 + oc * 128 + 128],
                                  xn[:, ic * TLOC + t0: ic * TLOC + t0 + 256],
                                  start=(ic == 0), stop=(ic == DC - 1)))
                for ic in range(DC):
                    k.op("tensor", lambda e, bg=bg, ic=ic, oc=oc, t0=t0:
                         e.matmul(bg, w1_sb[:, ic * DIMI2 + (HC2 + oc) * 128:
                                            ic * DIMI2 + (HC2 + oc) * 128 + 128],
                                  xn[:, ic * TLOC + t0: ic * TLOC + t0 + 256],
                                  start=(ic == 0), stop=(ic == DC - 1)),
                         sig=s_mm1 if ic == DC - 1 else None)
                k.wait("scalar", s_mm1)
                k.wait("scalar", s_hf, max(0, s_hf.n - 1))
                k.op("scalar", lambda e, bg=bg, oc=oc, p=p: e.activation(
                    gsc[:, p * 256:p * 256 + 256], bg, AF.Gelu,
                    bias=b1g[:, oc:oc + 1]), s_gel)
                k.wait("vector", s_gel)
                k.wait("vector", s_hf, s_hf.n)
                k.chain("vector", [
                    lambda e, ba=ba, oc=oc: e.tensor_scalar(
                        av[:], ba, b1a[:, oc:oc + 1], None, ALU.add),
                    lambda e, oc=oc, p=p: e.tensor_mul(
                        h_sb[:, oc * 256:(oc + 1) * 256], av[:],
                        gsc[:, p * 256:p * 256 + 256]),
                ], s_hf)
            k.wait("tensor", s_hf)
            for oc in range(DC):
                b = ps_mm[oc % 2][:, :256]
                k.wait("tensor", s_of, k.sems["o_free"].n - 1)
                for ic in range(HC2):
                    k.op("tensor", lambda e, b=b, ic=ic, oc=oc: e.matmul(
                        b, w2_sb[:, ic * D + oc * 128: ic * D + oc * 128 + 128],
                        h_sb[:, ic * 256:(ic + 1) * 256],
                        start=(ic == 0), stop=(ic == HC2 - 1)),
                        sig=s_mm2 if ic == HC2 - 1 else None)
                k.wait("vector", s_mm2)
                k.op("vector", lambda e, b=b, oc=oc: e.tensor_scalar(
                    zo[:, oc * 256:(oc + 1) * 256], b, b2[:, oc:oc + 1],
                    None, ALU.add), s_of)
            k.wait("gpsimd", s_of)
            k.wait("gpsimd", s_st, 16 * tt2)
            k.dma("gpsimd", xr.rearrange("p (c t) -> p c t", c=DC),
                  zh_in.rearrange("(c p) t -> p c t", p=128)[:, :, L3H + t0:
                                                            L3H + t0 + 256],
                  s_xt)
            k.wait("vector", s_xt)
            for c in range(DC):
                k.op("vector", lambda e, c=c: e.tensor_add(
                    xr[:, c * 256:(c + 1) * 256], xr[:, c * 256:(c + 1) * 256],
                    zo[:, c * 256:(c + 1) * 256]), s_of)
            k.wait("gpsimd", s_of)
            k.dma("gpsimd",
                  z_out.rearrange("(c p) t -> p c t", p=128)[:, :, t0:t0 + 256],
                  xr.rearrange("p (c t) -> p c t", c=DC), s_st)
        k.wait("gpsimd", s_st)
        k.play()
    return nc



M_TT = 256
M_NT = TLOC // M_TT
MB_DBG = None
MB_PHASE = 9
MB_SKIP = ()
L3A_PIPE = 7
M_CD = 2176
M_CC = 17


def build_l0attn_a():
    nc = bass.Bass("TRN2", target_bir_lowering=False,
                   detect_race_conditions=_RD)
    x_in = nc.dram_tensor("x_in", [D, 2051], F32, kind="ExternalInput").ap()
    w_in = nc.dram_tensor("w_in", [D, 4256], F32, kind="ExternalInput").ap()
    convw = nc.dram_tensor("convw", [128, 68], F32, kind="ExternalInput").ap()
    convb = nc.dram_tensor("convb", [128, 17], F32, kind="ExternalInput").ap()
    dtb = nc.dram_tensor("dtb", [32, 1], F32, kind="ExternalInput").ap()
    alog = nc.dram_tensor("alog", [32, 1], F32, kind="ExternalInput").ap()
    xbc_out = nc.dram_tensor("xbc_out", [M_CD, TLOC], F32,
                             kind="ExternalOutput").ap()
    z_out = nc.dram_tensor("z_out", [2048, TLOC], F32,
                           kind="ExternalOutput").ap()
    dtad_out = nc.dram_tensor("dtad_out", [64, TLOC], F32,
                              kind="ExternalOutput").ap()

    TW = 259

    with ExitStack() as stack:
        k = KB(nc, stack)
        emit_consts(k, {"eps_ln": 1e-5})
        w_sb = k.sb("w_sb", [128, 8 * 4256], BF16)
        xt = k.sb("xt", [128, 8 * 260], F32)
        u_bf = k.sb("u_bf", [128, 8 * 260], BF16)
        xbcr = k.sb("xbcr", [128, 17 * 260], F32)
        xbcs = k.sb("xbcs", [128, 17 * M_TT], F32)
        zst = k.sb("zst", [128, 16 * M_TT], F32)
        dtr = k.sb("dtr", [32, M_TT], F32)
        dte = k.sb("dte", [32, M_TT], F32)
        dtad = k.sb("dtad", [64, M_TT], F32)
        cw = k.sb("cw", [128, 68], F32)
        cb = k.sb("cb", [128, 17], F32)
        dtbc_ = k.sb("dtbc", [32, 1], F32)
        alg = k.sb("alg", [32, 1], F32)
        aneg = k.sb("aneg", [32, 1], F32)
        cacc = k.sb("cacc", [128, M_TT], F32)
        cm0 = k.sb("cm0", [128, M_TT], F32)

        banks = [k.ps(f"pp{i}", [128, 512]) for i in range(4)]

        s_ld = k.sem("ld")
        s_ld2 = k.sem("ld2")
        s_xt = k.sem("x_ld")
        s_mm = k.sem("mm")
        s_free = k.sem("free")
        s_cv = k.sem("cv")
        s_sl = k.sem("sl")
        s_dt = k.sem("dt")
        s_st = k.sem("st")
        vself = k.sem("vector_self")

        k.dma("gpsimd", w_sb[:].rearrange("p (c x) -> p c x", c=8),
              w_in.rearrange("(c p) x -> p c x", p=128), s_ld)
        for dst, src in ((cw, convw), (cb, convb), (dtbc_, dtb), (alg, alog)):
            k.dma("sync", dst[:], src, s_ld2)
        for eng in ("scalar", "vector"):
            k.wait(eng, s_ld2)
        k.wait("scalar", k.sems["consts"])
        k.op("scalar", lambda e: e.activation(aneg[:], alg[:], AF.Exp), s_dt)
        k.wait("vector", s_dt)
        k.op("vector", lambda e: e.tensor_scalar_mul(aneg[:], aneg[:], -1.0),
             s_dt)
        k.wait("tensor", s_ld)

        for it in range(M_NT):
            t0 = it * M_TT
            k.wait("gpsimd", k.sem("ln_apply_s"), 8 * it)
            k.wait("vector", s_st, 48 * it)
            k.dma("gpsimd", xt[:].rearrange("p (c t) -> p c t", c=8)[:, :, :TW],
                  x_in.rearrange("(c p) t -> p c t", p=128)[:, :, t0:t0 + TW],
                  s_xt)
            for eng in ("scalar", "tensor", "vector"):
                k.wait(eng, s_xt)
            emit_ln_tile(k, lambda c: xt[:, c * 260:c * 260 + TW],
                         lambda c: u_bf[:, c * 260:c * 260 + TW],
                         tt=TW, eps_name="eps_ln", nchunks=8,
                         pre_apply_waits=[(s_mm, 34 * it)])
            k.wait("tensor", k.sems["ln_apply_s"])

            def epi(oc, ps_ap, mmv, it=it):
                k.wait("vector", s_mm, mmv)
                if oc < 16:
                    k.op("vector", lambda e, oc=oc: e.tensor_copy(
                        zst[:, oc * M_TT:(oc + 1) * M_TT], ps_ap[:, 3:TW]),
                        s_free)
                elif oc < 33:
                    c = oc - 16
                    k.op("vector", lambda e, c=c: e.tensor_copy(
                        xbcr[:, c * 260:c * 260 + TW], ps_ap[:, :TW]), s_free)
                else:
                    k.wait("vector", s_dt)
                    k.op("vector", lambda e: e.tensor_copy(
                        dtr[:], ps_ap[0:32, 3:TW]), s_free)

            freed = {}
            base_free = s_free.n
            for oc in range(34):
                b = banks[oc % 4]
                wid = 128 if oc < 33 else 32
                if oc >= 4:
                    k.wait("tensor", s_free, freed[oc - 4])
                for ic in range(8):
                    k.op("tensor", lambda e, b=b, ic=ic, oc=oc, wid=wid:
                         e.matmul(b[:wid, :TW],
                                  w_sb[:, ic * 4256 + oc * 128:
                                       ic * 4256 + oc * 128 + wid],
                                  u_bf[:, ic * 260:ic * 260 + TW],
                                  start=(ic == 0), stop=(ic == 7)),
                         sig=s_mm if ic == 7 else None)
                epi(oc, b[:], s_mm.n)
                freed[oc] = s_free.n

            k.wait("vector", s_free)
            k.wait("scalar", s_free)
            for c in range(M_CC):
                r = lambda j, c=c: xbcr[:, c * 260 + j:c * 260 + j + M_TT]
                k.wait("vector", s_sl)
                k.chain("vector", [
                    lambda e, c=c, r=r: e.tensor_scalar(
                        cacc[:], r(0), cw[:, c * 4:c * 4 + 1], None, ALU.mult),
                    lambda e, c=c, r=r: e.tensor_scalar(
                        cm0[:], r(1), cw[:, c * 4 + 1:c * 4 + 2], None,
                        ALU.mult),
                    lambda e: e.tensor_add(cacc[:], cacc[:], cm0[:]),
                    lambda e, c=c, r=r: e.tensor_scalar(
                        cm0[:], r(2), cw[:, c * 4 + 2:c * 4 + 3], None,
                        ALU.mult),
                    lambda e: e.tensor_add(cacc[:], cacc[:], cm0[:]),
                    lambda e, c=c, r=r: e.tensor_scalar(
                        cm0[:], r(3), cw[:, c * 4 + 3:c * 4 + 4], None,
                        ALU.mult),
                    lambda e: e.tensor_add(cacc[:], cacc[:], cm0[:]),
                ], s_cv)
                k.wait("scalar", s_cv)
                k.wait("scalar", s_st, 48 * it)
                k.op("scalar", lambda e, c=c: e.activation(
                    xbcs[:, c * M_TT:(c + 1) * M_TT], cacc[:], AF.Silu,
                    bias=cb[:, c:c + 1]), s_sl)
            k.wait("scalar", s_dt)
            k.chain("scalar", [
                lambda e: e.activation(dte[:], dtr[:], AF.Exp,
                                       bias=dtbc_[:, :1]),
                lambda e: e.activation(dtad[0:32, :], dte[:], AF.Ln,
                                       bias=k.sbuf["ones_col"][0:32, :1]),
            ], s_dt)
            k.wait("vector", s_dt)
            k.op("vector", lambda e: e.tensor_scalar(
                dtad[32:64, :], dtad[0:32, :], aneg[:, :1], None, ALU.mult),
                s_dt)
            k.wait("sync", s_sl, 17 * (it + 1))
            k.wait("sync", s_dt, 2 + 2 * (it + 1))
            k.wait("sync", s_free)
            k.dma("sync", xbc_out.rearrange("(c p) t -> p c t", p=128)
                  [:, :, t0:t0 + M_TT],
                  xbcs[:].rearrange("p (c t) -> p c t", c=M_CC), s_st)
            k.dma("sync", z_out.rearrange("(c p) t -> p c t", p=128)
                  [:, :, t0:t0 + M_TT],
                  zst[:].rearrange("p (c t) -> p c t", c=16), s_st)
            k.dma("sync", dtad_out[:, t0:t0 + M_TT], dtad[:], s_st)
        k.wait("sync", s_st)
        k.play()
    return nc


def build_l0attn_b():
    nc = bass.Bass("TRN2", target_bir_lowering=False,
                   detect_race_conditions=_RD)
    x_in = nc.dram_tensor("x_in", [D, TLOC], F32, kind="ExternalInput").ap()
    xbc = nc.dram_tensor("xbc", [M_CD, TLOC], F32, kind="ExternalInput").ap()
    z_in = nc.dram_tensor("z_in", [2048, TLOC], F32, kind="ExternalInput").ap()
    dtad_in = nc.dram_tensor("dtad", [64, TLOC], F32,
                             kind="ExternalInput").ap()
    w_out = nc.dram_tensor("w_out", [2048, D], F32, kind="ExternalInput").ap()
    dcol_in = nc.dram_tensor("dcol", [128, 16], F32, kind="ExternalInput").ap()
    nw_in = nc.dram_tensor("nwcol", [128, 16], F32, kind="ExternalInput").ap()
    iden_in = nc.dram_tensor("iden", [128, 128], F32, kind="ExternalInput").ap()
    tri_in = nc.dram_tensor("tri64", [64, 64], F32, kind="ExternalInput").ap()
    self_in = nc.dram_tensor("self_f", [32, 2048], F32,
                             kind="ExternalInput").ap()
    rin = nc.dram_tensor("r_in", [64, 2048], F32, kind="ExternalInput").ap()
    x_out = nc.dram_tensor("x_out", [D, TLOC], F32, kind="ExternalOutput").ap()
    r_out = nc.dram_tensor("r_out", [64, 2048], F32,
                           kind="ExternalOutput").ap()

    with ExitStack() as stack:
        k = KB(nc, stack)
        emit_consts(k, {"eps_rms": 1e-5})
        wo_sb = k.sb("wo_sb", [128, 16 * 1024], BF16)
        xbc_t = k.sb("xbc_t", [128, 17 * M_TT], F32)
        z_t = k.sb("z_t", [128, 16 * M_TT], F32)
        x_t = k.sb("x_t", [128, 8 * M_TT], F32)
        dtad_t = k.sb("dtad_t", [64, M_TT], F32)
        cc0 = k.sb("cc0", [64, M_TT], F32)
        cc0b = k.sb("cc0b", [64, M_TT], BF16)
        dtT = k.sb("dtT", [64, 64], F32)
        acT = k.sb("acT", [64, 32], F32)
        acfm = k.sb("acfm", [32, 64], F32)
        nacfm = k.sb("nacfm", [32, 64], F32)
        dscfm = k.sb("dscfm", [32, 64], F32)
        combfm = k.sb("combfm", [32, 64], F32)
        rowf = k.sb("rowf", [1, 2048], F32)
        asb = k.sb("asb", [32, 2048], F32)
        dtb_b = k.sb("dtb_b", [64, 2048], F32)
        cmb_b = k.sb("cmb_b", [64, 2048], F32)
        eas_b = k.sb("eas_b", [64, 2048], F32)
        GtM8 = k.sb("GtM8", [64, 512], BF16)
        BTbf = k.sb("BTbf", [64, 64], BF16)
        Lt8 = k.sb("Lt8", [64, 512], BF16)
        sc8 = k.sb("sc8", [64, 512], F32)
        Mt08 = k.sb("Mt08", [64, 512], BF16)
        ep8 = k.sb("ep8", [64, 512], F32)
        vsc = k.sb("vsc", [64, 512], F32)
        xsT = k.sb("xsT", [64, 2048], BF16)
        xsd = k.sb("xsd", [64, 2048], BF16)
        y_tm = k.sb("y_tm", [64, 2048], F32)
        y_fm = k.sb("y_fm", [128, 16 * M_TT], F32)
        Rf = k.sb("Rf", [64, 2048], F32)
        Rbf = k.sb("Rbf", [64, 2048], BF16)
        iden = k.sb("iden_s", [128, 128], F32)
        tri64 = k.sb("tri64_s", [64, 64], F32)
        self_f = k.sb("self_s", [32, 2048], F32)
        dcol = k.sb("dcol_s", [128, 16], F32)
        nwcol = k.sb("nw_s", [128, 16], F32)
        o3264 = k.sb("o3264", [32, 64], F32)
        gsc = k.sb("gsc", [128, M_TT], F32)
        zsl = k.sb("zsl", [128, M_TT], F32)
        sq = k.sb("sq", [128, M_TT], F32)
        rrow = k.sb("rrow", [1, M_TT], F32)
        ynbf = k.sb("ynbf", [128, 16 * M_TT], BF16)
        outs = k.sb("outs", [128, 8 * M_TT], F32)

        pm = [k.ps(f"pm{i}", [128, 128]) for i in range(2)]
        pms = k.ps("pms", [128, 512])
        pbc = k.ps("pbc", [128, 512])
        prow = k.ps("prow", [128, 512])
        pcol = k.ps("pcol", [128, 512])
        pY = k.ps("pY", [128, 512])
        pV = k.ps("pV", [128, 512])

        s_ld = k.sem("ld")
        s_xt = k.sem("x_ld")
        s_mv = k.sem("mv")
        s_st = k.sem("st")
        s_t1 = k.sem("t1")
        s_tf = k.sem("tf")
        s_tb = k.sem("tb")
        s_tg = k.sem("tg")
        s_bc = k.sem("bc")
        s_rc = k.sem("rc")
        s_ms = k.sem("ms")
        s_v1 = k.sem("v1")
        s_vbc = k.sem("vbc")
        s_vf = k.sem("vf")
        s_vb = k.sem("vb")
        s_v3 = k.sem("v3")
        s_epi = k.sem("epi")
        s_vr = k.sem("vr")
        s_rbf = k.sem("rbf")
        s_gt = k.sem("gt")
        s_cc = k.sem("cc")
        s_ds = k.sem("ds")
        s_lt = k.sem("lt")
        s_ep = k.sem("ep")
        s_ea = k.sem("ea")
        s_asb = k.sem("asb_s")
        s_zs = k.sem("zs")
        s_sq = k.sem("sq_s")
        s_gm = k.sem("gm")
        s_ap2 = k.sem("ap2")
        s_mm = k.sem("mm")
        s_free = k.sem("free")

        k.dma("gpsimd", wo_sb[:].rearrange("p (c x) -> p c x", c=16),
              w_out.rearrange("(c p) x -> p c x", p=128), s_ld)
        for dst, src in ((iden, iden_in), (tri64, tri_in), (self_f, self_in),
                         (dcol, dcol_in), (nwcol, nw_in), (Rf, rin)):
            k.dma("gpsimd", dst[:], src, s_ld)
        s_o = k.sem("o_s")
        k.op("gpsimd", lambda e: e.memset(o3264[:], 1.0), s_o)
        k.wait("tensor", s_o)
        k.wait("vector", s_ld)
        k.op("vector", lambda e: e.tensor_copy(Rbf[:], Rf[:]), s_rbf)
        for eng in ("tensor", "scalar", "sync"):
            k.wait(eng, s_ld)
        iden64 = iden[0:64, 0:64]

        for it in range(M_NT):
            t0 = it * M_TT
            for eng_s in (s_gm, s_vb, s_zs, s_free, s_bc, s_t1, s_cc, s_v1):
                k.wait("gpsimd", eng_s)
            k.dma("gpsimd", xbc_t[:].rearrange("p (c t) -> p c t", c=17),
                  xbc.rearrange("(c p) t -> p c t", p=128)[:, :, t0:t0 + M_TT],
                  s_xt)
            k.dma("gpsimd", z_t[:].rearrange("p (c t) -> p c t", c=16),
                  z_in.rearrange("(c p) t -> p c t", p=128)[:, :, t0:t0 + M_TT],
                  s_xt)
            k.dma("gpsimd", x_t[:].rearrange("p (c t) -> p c t", c=8),
                  x_in.rearrange("(c p) t -> p c t", p=128)[:, :, t0:t0 + M_TT],
                  s_xt)
            k.dma("gpsimd", dtad_t[:], dtad_in[:, t0:t0 + M_TT], s_xt)
            for eng in ("tensor", "scalar", "vector", "sync"):
                k.wait(eng, s_xt)
            k.chain("vector", [
                lambda e: e.tensor_copy(cc0[:], xbc_t[64:128,
                                                      16 * M_TT:17 * M_TT]),
                lambda e: e.tensor_copy(cc0b[:], cc0[:]),
            ], s_cc, inc=2)

            for c in range(4):
                c64 = c * 64
                cs = slice(c64, c64 + 64)

                k.wait("tensor", s_v1)
                k.wait("tensor", s_gt, s_gt.n)
                if "dtadT" not in MB_SKIP:
                    k.op("tensor", lambda e, cs=cs: e.transpose(
                        pms[0:64, 0:64], dtad_t[:, cs], iden64), s_t1)
                    k.wait("vector", s_t1)
                    k.op("vector", lambda e: e.tensor_copy(dtT[:],
                                                           pms[0:64, 0:64]),
                         s_v1)
                if "acum" not in MB_SKIP:
                    k.wait("tensor", s_v1)
                    k.op("tensor", lambda e: e.matmul(
                        pms[0:64, 64:96], tri64[:], dtT[:, 32:64],
                        start=True, stop=True), s_t1)
                    k.wait("vector", s_t1)
                    k.op("vector", lambda e: e.tensor_copy(
                        acT[:], pms[0:64, 64:96]), s_v1)
                if "acfm" not in MB_SKIP:
                    k.wait("tensor", s_v1)
                    k.op("tensor", lambda e: e.transpose(
                        pms[0:32, 96:160], acT[:], iden64), s_t1)
                    k.wait("vector", s_t1)
                    k.wait("vector", s_mv)
                    k.wait("vector", s_ds)
                    k.wait("vector", s_rc)
                    k.wait("vector", s_bc)
                    k.chain("vector", [
                        lambda e: e.tensor_copy(acfm[:], pms[0:32, 96:160]),
                        lambda e: e.tensor_scalar_mul(nacfm[:], acfm[:], -1.0),
                        lambda e: e.tensor_scalar(asb[:], self_f[:],
                                                  acfm[:, 63:64], None,
                                                  ALU.mult),
                    ], s_asb)
                    k.op("vector", lambda e: e.sem_inc(s_v1.h, 1))
                    s_v1.n += 1
                if "rowf" not in MB_SKIP:
                    k.wait("sync", s_asb)
                    k.wait("sync", s_rc)
                    k.dma("sync", rowf[:1, :].rearrange("o (h t) -> o h t",
                                                        h=32),
                          acfm[:], s_mv)
                if "dscfm" not in MB_SKIP:
                    k.wait("scalar", s_asb)
                    k.op("scalar", lambda e: e.activation(
                        dscfm[:], acfm[:], AF.Exp, bias=acfm[:, 63:64],
                        scale=-1.0), s_ds)
                    k.wait("vector", s_ds)
                    k.op("vector", lambda e, cs=cs: e.tensor_mul(
                        combfm[:], dscfm[:], dtad_t[0:32, cs]), s_v1)

                k.wait("tensor", s_cc)
                k.wait("tensor", s_gt, s_gt.n)
                k.wait("tensor", s_mv)
                k.op("tensor", lambda e, cs=cs, c64=c64: e.matmul(
                    pms[0:64, 256:320], xbc_t[0:64, 16 * M_TT + c64:
                                              16 * M_TT + c64 + 64],
                    cc0[:, cs], start=True, stop=True), s_t1)
                k.wait("vector", s_t1)
                k.wait("vector", s_mv)
                k.op("vector", lambda e: e.tensor_mul(
                    GtM8[:, 0:64], pms[0:64, 256:320], tri64[:]), s_gt)
                if MB_DBG and MB_DBG[0] == "psgt" and MB_DBG[1] == it \
                        and MB_DBG[2] == c:
                    k.op("vector", lambda e: e.tensor_copy(
                        vsc[:, 0:64], pms[0:64, 256:320]), s_v3)
                k.wait("sync", s_gt)
                for j in range(1, 8 if "rep" not in MB_SKIP else 1):
                    k.dma("sync", GtM8[:, j * 64:(j + 1) * 64],
                          GtM8[:, 0:64], s_mv)
                k.wait("tensor", s_v1)
                k.op("tensor", lambda e, c64=c64: e.transpose(
                    pms[0:64, 160:224],
                    xbc_t[0:64, 16 * M_TT + c64:16 * M_TT + c64 + 64],
                    iden64), s_t1)
                k.wait("vector", s_t1)
                k.wait("vector", s_tg)
                k.op("vector", lambda e: e.tensor_copy(BTbf[:],
                                                       pms[0:64, 160:224]),
                     s_v1)

                for r in range(4 if MB_PHASE >= 2 else 0):
                    k.wait("tensor", s_vbc)
                    k.wait("tensor", s_ea)
                    k.op("tensor", lambda e, r=r, cs=cs: e.matmul(
                        pbc[0:64, :], dtad_t[0:32, cs],
                        self_f[:, r * 512:(r + 1) * 512],
                        start=True, stop=True), s_bc)
                    k.wait("vector", s_bc)
                    k.op("vector", lambda e, r=r: e.tensor_copy(
                        dtb_b[:, r * 512:(r + 1) * 512], pbc[0:64, :]), s_vbc)
                k.wait("tensor", s_v1)
                for r in range(4 if MB_PHASE >= 2 else 0):
                    k.wait("tensor", s_vbc)
                    k.op("tensor", lambda e, r=r: e.matmul(
                        pbc[0:64, :], combfm[:],
                        self_f[:, r * 512:(r + 1) * 512],
                        start=True, stop=True), s_bc)
                    k.wait("vector", s_bc)
                    k.op("vector", lambda e, r=r: e.tensor_copy(
                        cmb_b[:, r * 512:(r + 1) * 512], pbc[0:64, :]), s_vbc)
                k.wait("tensor", s_asb)
                k.wait("scalar", s_vr, s_vr.n)
                for r in range(4 if MB_PHASE >= 2 else 0):
                    k.wait("tensor", s_ea)
                    k.wait("tensor", s_vbc)
                    k.op("tensor", lambda e, r=r: e.matmul(
                        pbc[0:64, :], o3264[:],
                        asb[:, r * 512:(r + 1) * 512],
                        start=True, stop=True), s_bc)
                    k.wait("scalar", s_bc)
                    k.op("scalar", lambda e, r=r: e.activation(
                        eas_b[:, r * 512:(r + 1) * 512], pbc[0:64, :],
                        AF.Exp), s_ea)

                k.wait("vector", s_tg)
                k.wait("tensor", s_vbc)
                for j in range(16 if MB_PHASE >= 3 else 0):
                    k.wait("tensor", s_vf)
                    k.wait("tensor", s_vb)
                    k.op("tensor", lambda e, j=j, c64=c64: e.transpose(
                        pm[j % 2][0:64, 0:128],
                        xbc_t[:, j * M_TT + c64:j * M_TT + c64 + 64],
                        iden[:]), s_tf)
                    k.wait("vector", s_tf)
                    k.chain("vector", [
                        lambda e, j=j: e.tensor_mul(
                            xsT[:, j * 128:(j + 1) * 128], pm[j % 2][0:64, :],
                            dtb_b[:, j * 128:(j + 1) * 128]),
                        lambda e, j=j: e.tensor_mul(
                            xsd[:, j * 128:(j + 1) * 128], pm[j % 2][0:64, :],
                            cmb_b[:, j * 128:(j + 1) * 128]),
                    ], s_vf)

                for g in range(4 if MB_PHASE >= 4 else 0):
                    g512 = slice(g * 512, (g + 1) * 512)
                    k.wait("tensor", s_v3)
                    k.wait("tensor", s_ep)
                    k.op("tensor", lambda e, g=g: e.matmul(
                        prow[0:64, :], k.sbuf["ones_row"][0:1, 0:64],
                        rowf[0:1, g * 512:(g + 1) * 512],
                        start=True, stop=True), s_rc)
                    k.op("tensor", lambda e, g=g: e.matmul(
                        pcol[0:64, :], nacfm[:],
                        self_f[:, g * 512:(g + 1) * 512],
                        start=True, stop=True), s_rc)
                    k.wait("vector", s_rc)
                    k.wait("vector", s_lt, s_lt.n)
                    k.chain("vector", [
                        lambda e: e.tensor_copy(sc8[:], prow[0:64, :]),
                        lambda e: e.tensor_add(sc8[:], sc8[:], pcol[0:64, :]),
                        lambda e: e.tensor_scalar(sc8[:], sc8[:], 0.0, None,
                                                  ALU.min),
                    ], s_v3)
                    k.wait("scalar", s_v3)
                    k.op("scalar", lambda e: e.activation(Lt8[:], sc8[:],
                                                          AF.Exp), s_lt)
                    k.wait("scalar", s_epi, s_epi.n)
                    k.op("scalar", lambda e: e.activation(
                        ep8[:], pcol[0:64, :], AF.Exp, scale=-1.0), s_ep)
                    k.wait("vector", s_lt)
                    k.wait("vector", s_mv)
                    k.op("vector", lambda e: e.tensor_mul(Mt08[:], Lt8[:],
                                                          GtM8[:]), s_v3)
                    k.wait("tensor", s_v3)
                    k.wait("tensor", s_epi, s_epi.n)
                    k.wait("tensor", s_free)
                    k.wait("tensor", s_vf)
                    k.wait("tensor", s_rbf, s_rbf.n)
                    k.wait("tensor", s_vr)
                    k.wait("tensor", s_ea)
                    for h8 in range(8):
                        h = 8 * g + h8
                        k.op("tensor", lambda e, h8=h8, h=h: e.matmul(
                            pY[0:64, h8 * 64:(h8 + 1) * 64],
                            Mt08[:, h8 * 64:(h8 + 1) * 64],
                            xsT[:, h * 64:(h + 1) * 64],
                            start=True, stop=True),
                            sig=s_tg if h8 == 7 else None)
                    for h8 in range(8):
                        h = 8 * g + h8
                        k.op("tensor", lambda e, h8=h8, h=h, cs=cs: e.matmul(
                            pV[0:64, h8 * 64:(h8 + 1) * 64], cc0b[:, cs],
                            Rbf[:, h * 64:(h + 1) * 64],
                            start=True, stop=True),
                            sig=s_tg if h8 == 7 else None)
                    for h8 in range(8):
                        h = 8 * g + h8
                        k.op("tensor", lambda e, h8=h8, h=h: e.matmul(
                            pbc[0:64, h8 * 64:(h8 + 1) * 64], BTbf[:],
                            xsd[:, h * 64:(h + 1) * 64],
                            start=True, stop=True),
                            sig=s_tg if h8 == 7 else None)
                    k.wait("vector", s_tg)
                    k.wait("vector", s_ep)
                    k.wait("vector", s_tb)
                    k.chain("vector", [
                        lambda e: e.tensor_mul(vsc[:], pV[0:64, :], ep8[:]),
                        lambda e, g512=g512: e.tensor_add(
                            y_tm[:, g512], pY[0:64, :], vsc[:]),
                    ], s_epi)
                    k.wait("vector", s_ea)
                    k.chain("vector", [
                        lambda e, g512=g512: e.tensor_mul(
                            Rf[:, g512], Rf[:, g512], eas_b[:, g512]),
                        lambda e, g512=g512: e.tensor_add(
                            Rf[:, g512], Rf[:, g512], pbc[0:64, :]),
                    ], s_vr)
                k.wait("vector", s_tg)
                k.wait("vector", s_vr)
                k.op("vector", lambda e: e.tensor_copy(Rbf[:], Rf[:]), s_rbf)

                if MB_DBG and MB_DBG[1] == it and MB_DBG[2] == c:
                    for sem in (s_epi, s_vr, s_vf, s_v3, s_gt, s_lt, s_asb):
                        k.wait("gpsimd", sem)
                    if MB_DBG[0] == "psgt":
                        k.dma("gpsimd", r_out[:64, :64],
                              k.sbuf["vsc"][:, 0:64], s_xt)
                    else:
                        buf = k.sbuf[MB_DBG[0]]
                        k.dma("gpsimd", r_out[:buf.shape[0], :buf.shape[1]],
                              buf[:], s_xt)
                    for eng in ("tensor", "vector", "scalar", "sync"):
                        k.wait(eng, s_xt)
                k.wait("tensor", s_epi, s_epi.n)
                for j in range(16 if MB_PHASE >= 5 else 0):
                    k.wait("tensor", s_vb)
                    k.wait("tensor", s_vf)
                    k.op("tensor", lambda e, j=j: e.transpose(
                        pm[j % 2][:, 0:64], y_tm[:, j * 128:(j + 1) * 128],
                        iden64), s_tb)
                    k.wait("vector", s_tb)
                    k.op("vector", lambda e, j=j, c64=c64: e.tensor_copy(
                        y_fm[:, j * M_TT + c64:j * M_TT + c64 + 64],
                        pm[j % 2][:, 0:64]), s_vb)

            for j in range(16 if MB_PHASE >= 6 else 0):
                k.wait("vector", s_zs, s_zs.n)
                k.wait("vector", s_gm, s_gm.n)
                k.chain("vector", [
                    lambda e, j=j: e.tensor_scalar(
                        gsc[:], xbc_t[:, j * M_TT:(j + 1) * M_TT],
                        dcol[:, j:j + 1], None, ALU.mult),
                    lambda e, j=j: e.tensor_add(
                        y_fm[:, j * M_TT:(j + 1) * M_TT],
                        y_fm[:, j * M_TT:(j + 1) * M_TT], gsc[:]),
                ], s_gm)
                k.wait("scalar", s_gm)
                k.op("scalar", lambda e, j=j: e.activation(
                    zsl[:], z_t[:, j * M_TT:(j + 1) * M_TT], AF.Silu), s_zs)
                k.wait("vector", s_zs)
                k.op("vector", lambda e, j=j: e.tensor_mul(
                    y_fm[:, j * M_TT:(j + 1) * M_TT],
                    y_fm[:, j * M_TT:(j + 1) * M_TT], zsl[:]), s_gm)
            k.wait("scalar", s_v3)
            for j in range(16 if MB_PHASE >= 6 else 0):
                k.wait("scalar", s_gm, s_gm.n)
                k.wait("scalar", s_ms)
                k.op("scalar", lambda e, j=j: e.activation(
                    sq[:], y_fm[:, j * M_TT:(j + 1) * M_TT], AF.Square), s_sq)
                k.wait("tensor", s_sq)
                k.op("tensor", lambda e, j=j: e.matmul(
                    prow[0:1, 0:M_TT], k.sbuf["ones_col"][:], sq[:],
                    start=(j == 0), stop=(j == 15)), s_ms)
            if MB_PHASE >= 6:
              k.wait("vector", s_ms)
              k.op("vector", lambda e: e.tensor_scalar_mul(
                rrow[:1, :], prow[0:1, 0:M_TT], 1.0 / 2048.0), s_v3)
              k.wait("scalar", s_v3)
              k.chain("scalar", [
                  lambda e: e.activation(rrow[:1, :], rrow[:1, :], AF.Ln,
                                         bias=k.sbuf["c_eps_rms"][:1, :1]),
                  lambda e: e.activation(rrow[:1, :], rrow[:1, :], AF.Exp,
                                         scale=-0.5),
              ], s_zs)
              k.wait("tensor", s_zs, s_zs.n)
              k.wait("tensor", s_ep)
              k.op("tensor", lambda e: e.matmul(
                  pcol[:, 0:M_TT], k.sbuf["ones_row"][:1, :], rrow[:1, :],
                  start=True, stop=True), s_rc)
              k.wait("vector", s_rc)
              for j in range(16):
                  k.wait("vector", s_ap2, s_ap2.n)
                  k.chain("vector", [
                      lambda e, j=j: e.tensor_mul(
                          gsc[:], y_fm[:, j * M_TT:(j + 1) * M_TT],
                          pcol[:, 0:M_TT]),
                      lambda e, j=j: e.tensor_scalar(
                          ynbf[:, j * M_TT:(j + 1) * M_TT], gsc[:],
                          nwcol[:, j:j + 1], None, ALU.mult),
                  ], s_ap2)
              k.op("vector", lambda e: e.sem_inc(s_ep.h, 1))
              s_ep.n += 1
              k.wait("tensor", s_ap2, s_ap2.n)
              banks2 = [pY, pV]
              freed = {}
              for oc in range(8):
                  b = banks2[oc % 2]
                  if oc >= 2:
                      k.wait("tensor", s_free, freed[oc - 2])
                  for ic in range(16):
                      k.op("tensor", lambda e, b=b, ic=ic, oc=oc: e.matmul(
                          b[:, 0:M_TT],
                          wo_sb[:, ic * 1024 + oc * 128:
                                ic * 1024 + oc * 128 + 128],
                          ynbf[:, ic * M_TT:(ic + 1) * M_TT],
                          start=(ic == 0), stop=(ic == 15)),
                          sig=s_mm if ic == 15 else None)
                  k.wait("vector", s_mm, s_mm.n)
                  k.wait("vector", s_st, 16 * it)
                  k.op("vector", lambda e, b=b, oc=oc: e.tensor_add(
                      outs[:, oc * M_TT:(oc + 1) * M_TT], b[:, 0:M_TT],
                      x_t[:, oc * M_TT:(oc + 1) * M_TT]), s_free)
                  freed[oc] = s_free.n
              k.wait("sync", s_free)
              k.dma("sync", x_out.rearrange("(c p) t -> p c t", p=128)
                    [:, :, t0:t0 + M_TT],
                    outs[:].rearrange("p (c t) -> p c t", c=8), s_st)
        if not MB_DBG:
            k.wait("sync", s_vr)
            k.dma("sync", r_out, Rf[:], s_st)
        k.wait("sync", s_st)
        k.play()
    return nc




def build_l1attn_a():
    nc = bass.Bass("TRN2", target_bir_lowering=False,
                   detect_race_conditions=_RD)
    x_in = nc.dram_tensor("x_in", [D, TLOC], F32, kind="ExternalInput").ap()
    wq = nc.dram_tensor("wq", [D, D], F32, kind="ExternalInput").ap()
    wk = nc.dram_tensor("wk", [D, D], F32, kind="ExternalInput").ap()
    wv = nc.dram_tensor("wv", [D, D], F32, kind="ExternalInput").ap()
    wf = nc.dram_tensor("wf", [D, 256], F32, kind="ExternalInput").ap()
    q_out = nc.dram_tensor("q_out", [D, TLOC], F32, kind="ExternalOutput").ap()
    k_out = nc.dram_tensor("k_out", [D, TLOC], F32, kind="ExternalOutput").ap()
    v_out = nc.dram_tensor("v_out", [D, TLOC], F32, kind="ExternalOutput").ap()
    f_out = nc.dram_tensor("f_out", [256, TLOC], F32,
                           kind="ExternalOutput").ap()

    with ExitStack() as stack:
        k = KB(nc, stack)
        emit_consts(k, {"eps_ln": 1e-5})
        w_sb = k.sb("w_sb", [128, 8 * 3328], BF16)
        xt = k.sb("xt", [128, 8 * TT], F32)
        u_bf = k.sb("u_bf", [128, 8 * TT], BF16)
        qs = k.sb("qs", [128, 8 * TT], F32)
        ks = k.sb("ks", [128, 8 * TT], F32)
        vs = k.sb("vs", [128, 8 * TT], F32)
        fs = k.sb("fs", [128, 2 * TT], F32)
        fsg = k.sb("fsg", [128, TT], F32)

        banks = [k.ps(f"pp{i}", [128, 512]) for i in range(4)]

        s_ld = k.sem("ld")
        s_xt = k.sem("x_ld")
        s_mm = k.sem("mm")
        s_free = k.sem("free")
        s_sg = k.sem("sg")
        s_st = k.sem("st")

        k.dma("gpsimd", w_sb[:, 0:8 * 1024].rearrange("p (c x) -> p c x", c=8),
              wq.rearrange("(c p) x -> p c x", p=128), s_ld)
        k.dma("gpsimd", w_sb[:, 8 * 1024:8 * 2048]
              .rearrange("p (c x) -> p c x", c=8),
              wk.rearrange("(c p) x -> p c x", p=128), s_ld)
        k.dma("gpsimd", w_sb[:, 8 * 2048:8 * 3072]
              .rearrange("p (c x) -> p c x", c=8),
              wv.rearrange("(c p) x -> p c x", p=128), s_ld)
        k.dma("gpsimd", w_sb[:, 8 * 3072:8 * 3328]
              .rearrange("p (c x) -> p c x", c=8),
              wf.rearrange("(c p) x -> p c x", p=128), s_ld)
        k.wait("tensor", s_ld)

        def wslot(ic, oc):
            if oc < 8:
                off = ic * 1024 + oc * 128
            elif oc < 16:
                off = 8 * 1024 + ic * 1024 + (oc - 8) * 128
            elif oc < 24:
                off = 8 * 2048 + ic * 1024 + (oc - 16) * 128
            else:
                off = 8 * 3072 + ic * 256 + (oc - 24) * 128
            return w_sb[:, off:off + 128]

        for it in range(NTT):
            t0 = it * TT
            k.wait("gpsimd", k.sem("ln_apply_s"), 8 * it)
            k.dma("gpsimd", xt[:].rearrange("p (c t) -> p c t", c=8),
                  x_in.rearrange("(c p) t -> p c t", p=128)[:, :, t0:t0 + TT],
                  s_xt)
            for eng in ("scalar", "tensor", "vector"):
                k.wait(eng, s_xt)
            emit_ln_tile(k, lambda c: xt[:, c * TT:(c + 1) * TT],
                         lambda c: u_bf[:, c * TT:(c + 1) * TT],
                         tt=TT, eps_name="eps_ln", nchunks=8,
                         pre_apply_waits=[(s_mm, 26 * it)])
            k.wait("tensor", k.sems["ln_apply_s"])

            def epi(oc, ps_ap, mmv, it=it):
                if oc < 16:
                    dst = (qs if oc < 8 else ks)[:, (oc % 8) * TT:
                                                 (oc % 8 + 1) * TT]
                    k.wait("scalar", s_mm, mmv)
                    k.wait("scalar", s_st, 64 * it)
                    k.op("scalar", lambda e, dst=dst: e.activation(
                        dst, ps_ap[:, :TT], AF.Silu), s_sg)
                    return (s_sg, s_sg.n)
                elif oc < 24:
                    k.wait("vector", s_mm, mmv)
                    k.wait("vector", s_st, 64 * it)
                    k.op("vector", lambda e, oc=oc: e.tensor_copy(
                        vs[:, (oc - 16) * TT:(oc - 15) * TT], ps_ap[:, :TT]),
                        s_free)
                    return (s_free, s_free.n)
                else:
                    k.wait("scalar", s_mm, mmv)
                    k.wait("scalar", s_st, 64 * it)
                    k.wait("scalar", s_sg, s_sg.n)
                    k.chain("scalar", [
                        lambda e: e.activation(fsg[:], ps_ap[:, :TT],
                                               AF.Sigmoid),
                        lambda e, oc=oc: e.activation(
                            fs[:, (oc - 24) * TT:(oc - 23) * TT], fsg[:],
                            AF.Ln),
                    ], s_sg)
                    return (s_sg, s_sg.n)

            freed = {}
            for oc in range(26):
                b = banks[oc % 4]
                if oc >= 4:
                    fsem, fval = freed[oc - 4]
                    k.wait("tensor", fsem, fval)
                for ic in range(8):
                    k.op("tensor", lambda e, b=b, ic=ic, oc=oc: e.matmul(
                        b[:, :TT], wslot(ic, oc),
                        u_bf[:, ic * TT:(ic + 1) * TT],
                        start=(ic == 0), stop=(ic == 7)),
                        sig=s_mm if ic == 7 else None)
                freed[oc] = epi(oc, b[:], s_mm.n)
            k.wait("sync", s_free)
            k.wait("sync", s_sg)
            for dst, src, nch in ((q_out, qs, 8), (k_out, ks, 8),
                                  (v_out, vs, 8), (f_out, fs, 2)):
                k.dma("sync", dst.rearrange("(c p) t -> p c t", p=128)
                      [:, :, t0:t0 + TT],
                      src[:].rearrange("p (c t) -> p c t", c=nch), s_st)
        k.wait("sync", s_st)
        k.play()
    return nc


def build_l1attn_b():
    nc = bass.Bass("TRN2", target_bir_lowering=False,
                   detect_race_conditions=_RD)
    x_in = nc.dram_tensor("x_in", [D, TLOC], F32, kind="ExternalInput").ap()
    q_in = nc.dram_tensor("q_in", [D, TLOC], F32, kind="ExternalInput").ap()
    k_in = nc.dram_tensor("k_in", [D, TLOC], F32, kind="ExternalInput").ap()
    v_in = nc.dram_tensor("v_in", [D, TLOC], F32, kind="ExternalInput").ap()
    f_in = nc.dram_tensor("f_in", [256, TLOC], F32, kind="ExternalInput").ap()
    wo_in = nc.dram_tensor("wo", [D, D], F32, kind="ExternalInput").ap()
    gcol_in = nc.dram_tensor("gcol", [128, 8], F32, kind="ExternalInput").ap()
    iden_in = nc.dram_tensor("iden", [128, 128], F32, kind="ExternalInput").ap()
    tri_in = nc.dram_tensor("tri64", [64, 64], F32, kind="ExternalInput").ap()
    hkv_in = nc.dram_tensor("hkv_in", [64, 2048], F32,
                            kind="ExternalInput").ap()
    x_out = nc.dram_tensor("x_out", [D, TLOC], F32, kind="ExternalOutput").ap()
    hkv_out = nc.dram_tensor("hkv_out", [64, 2048], F32,
                             kind="ExternalOutput").ap()

    with ExitStack() as stack:
        k = KB(nc, stack)
        emit_consts(k, {"eps_rms": 1e-5})
        q_bf = k.sb("q_bf", [128, 8 * TLOC], BF16)
        k_bf = k.sb("k_bf", [128, 8 * TLOC], BF16)
        f_sb = k.sb("f_sb", [128, 2 * TLOC], F32)
        vch = k.sb("vch", [128, 2 * 8 * 64], BF16)
        o_fm = k.sb("o_fm", [128, 8 * TLOC], BF16)
        wg = k.sb("wg", [128, 8 * 1024], BF16)
        x_t = k.sb("x_t", [128, 8 * TT], F32)
        obuf = k.sb("obuf", [128, 2 * TT], F32)
        iden = k.sb("iden_s", [128, 128], F32)
        idbf = k.sb("idbf", [128, 128], BF16)
        tri64 = k.sb("tri64_s", [64, 64], F32)
        tri8 = k.sb("tri8", [64, 64], F32)
        gcol = k.sb("gcol_s", [128, 8], F32)
        hkm = k.sb("hkm", [64, 1024], F32)
        hvm = k.sb("hvm", [64, 1024], F32)
        hvbf = k.sb("hvbf", [64, 1024], BF16)
        hkfm = k.sb("hkfm", [128, 512], BF16)
        f_T = k.sb("f_T", [64, 256], F32)
        G_T = k.sb("G_T", [64, 256], F32)
        E8 = k.sb("E8", [64, 256], F32)
        en8 = k.sb("en8", [64, 256], F32)
        efs = k.sb("efs", [64, 256], F32)
        sT8 = k.sb("sT8", [64, 256], F32)
        sTbf = k.sb("sTbf", [64, 256], BF16)
        vT = k.sb("vT", [64, 1024], BF16)
        kT = k.sb("kT", [64, 1024], BF16)
        eglc = k.sb("eglc", [64, 4], F32)
        AMbf = k.sb("AMbf", [64, 64], BF16)
        lsc = k.sb("lsc", [64, 64], F32)
        rowm = k.sb("rowm", [64, 2], F32)
        psc = k.sb("psc", [64, 64], F32)
        pneT = k.sb("pneT", [64, 64], BF16)
        pnem = k.sb("pnem", [64, 64], BF16)
        smh = k.sb("smh", [64, 64], BF16)
        WtM = k.sb("WtM", [64, 64], BF16)
        sqsc = k.sb("sqsc", [64, 256], F32)
        msc = k.sb("msc", [64, 2], F32)
        onT = k.sb("onT", [64, 256], BF16)

        pA = k.ps("pA", [128, 512])
        pL = k.ps("pL", [128, 512])
        pW = k.ps("pW", [128, 512])
        pO = k.ps("pO", [128, 512])
        pUk = k.ps("pUk", [128, 512])
        pUv = k.ps("pUv", [128, 512])
        pT0 = k.ps("pT0", [128, 512])
        pTb = k.ps("pTb", [128, 1024], BF16)

        s_ld = k.sem("ld")
        s_vl = k.sem("v_ld")
        s_x2 = k.sem("x2_ld")
        s_mv = k.sem("mv")
        s_st = k.sem("st")
        s_tt = k.sem("tt")
        s_ta = k.sem("ta")
        s_tl = k.sem("tl")
        s_tc = k.sem("tc")
        s_tw = k.sem("tw")
        s_to = k.sem("to")
        s_tu = k.sem("tu")
        s_tm = k.sem("tm")
        s_v1 = k.sem("v1")
        s_v2 = k.sem("v2")
        s_v3 = k.sem("v3")
        s_hk = k.sem("hk")
        s_hv = k.sem("hv")
        s_eg = k.sem("eg")
        s_a1 = k.sem("a1")
        s_a2 = k.sem("a2")
        s_on = k.sem("on")
        s_mm = k.sem("mm")
        s_free = k.sem("free")

        k.dma("gpsimd", q_bf[:].rearrange("p (c t) -> p c t", c=8),
              q_in.rearrange("(c p) t -> p c t", p=128), s_ld)
        k.dma("gpsimd", k_bf[:].rearrange("p (c t) -> p c t", c=8),
              k_in.rearrange("(c p) t -> p c t", p=128), s_ld)
        k.dma("gpsimd", f_sb[:].rearrange("p (c t) -> p c t", c=2),
              f_in.rearrange("(c p) t -> p c t", p=128), s_ld)
        k.dma("gpsimd", wg[:].rearrange("p (c x) -> p c x", c=8),
              wo_in.rearrange("(c p) x -> p c x", p=128), s_ld)
        for dst, src in ((iden, iden_in), (tri64, tri_in), (gcol, gcol_in)):
            k.dma("gpsimd", dst[:], src, s_ld)
        k.dma("gpsimd", hkm[:].rearrange("p (h x) -> p h x", h=4),
              hkv_in.rearrange("p (h x) -> p h x", h=4)[:, :, 0:256], s_ld)
        k.dma("gpsimd", hvm[:].rearrange("p (h x) -> p h x", h=4),
              hkv_in.rearrange("p (h x) -> p h x", h=4)[:, :, 256:512], s_ld)
        for eng in ("tensor", "scalar", "vector", "sync"):
            k.wait(eng, s_ld)
        k.wait("vector", k.sems["consts"])
        k.chain("vector", [
            lambda e: e.tensor_scalar_mul(tri8[:], tri64[:], 0.125),
            lambda e: e.tensor_copy(idbf[:], iden[:]),
            lambda e: e.tensor_copy(hvbf[:], hvm[:]),
        ], s_v1, inc=3)
        for ic in range(8):
            k.op("vector", lambda e, ic=ic: e.tensor_scalar(
                wg[:, ic * 1024:(ic + 1) * 1024],
                wg[:, ic * 1024:(ic + 1) * 1024], gcol[:, ic:ic + 1],
                None, ALU.mult), s_v1)
        for h in range(4):
            for kc in range(2):
                k.wait("tensor", s_v1, 3)
                k.wait("tensor", s_v1)
                k.op("tensor", lambda e, h=h, kc=kc: e.transpose(
                    pT0[0:128, 384:448],
                    hkm[:, h * 256 + kc * 128:h * 256 + kc * 128 + 128],
                    iden[0:64, 0:64]), s_tt)
                k.wait("vector", s_tt)
                k.op("vector", lambda e, h=h, kc=kc: e.tensor_copy(
                    hkfm[:, h * 128 + kc * 64:h * 128 + kc * 64 + 64],
                    pT0[0:128, 384:448]), s_v1)

        for c in range(32):
            cs = slice(c * 64, (c + 1) * 64)
            pp = (c % 2) * 8 * 64
            k.wait("gpsimd", s_tt)
            k.dma("gpsimd", vch[:, pp:pp + 512]
                  .rearrange("p (ch t) -> p ch t", ch=8),
                  v_in.rearrange("(ch p) t -> p ch t", p=128)[:, :, cs], s_vl)
            k.wait("vector", s_tm)
            k.wait("vector", s_a1, s_a1.n)
            for fc in range(2):
                k.wait("tensor", s_v1)
                k.op("tensor", lambda e, fc=fc, c=c: e.transpose(
                    pT0[0:64, 0:128],
                    f_sb[:, fc * TLOC + c * 64:fc * TLOC + c * 64 + 64],
                    iden[:]), s_tt)
                k.wait("vector", s_tt)
                k.op("vector", lambda e, fc=fc: e.tensor_copy(
                    f_T[:, fc * 128:(fc + 1) * 128], pT0[0:64, 0:128]),
                    s_v1)
            for fc in range(2):
                k.wait("tensor", s_v1)
                k.wait("tensor", s_a1, s_a1.n)
                k.op("tensor", lambda e, fc=fc: e.matmul(
                    pT0[0:64, 128:256], tri8[:],
                    f_T[:, fc * 128:(fc + 1) * 128],
                    start=True, stop=True), s_tm)
                k.wait("vector", s_tm)
                k.op("vector", lambda e, fc=fc: e.tensor_copy(
                    G_T[:, fc * 128:(fc + 1) * 128],
                    pT0[0:64, 128:256]), s_v1)
            k.wait("scalar", s_v1)
            k.wait("scalar", s_eg, s_eg.n)
            k.wait("scalar", s_v3, s_v3.n)
            k.op("scalar", lambda e: e.activation(E8[:], G_T[:], AF.Exp),
                 s_a1)
            k.op("scalar", lambda e: e.activation(en8[:], G_T[:], AF.Exp,
                                                  scale=-1.0), s_a1)
            k.op("scalar", lambda e: e.activation(efs[:], f_T[:], AF.Exp,
                                                  scale=0.125), s_a1)
            k.wait("vector", s_a1)
            k.wait("vector", s_tc, s_tc.n)
            k.wait("vector", s_tu, s_tu.n)
            k.wait("vector", s_tt, s_tt.n)
            k.chain("vector", [
                lambda e: e.tensor_scalar(sT8[:], efs[:], -1.0, 1.0,
                                          ALU.mult, ALU.add),
                lambda e: e.tensor_mul(sT8[:], sT8[:], en8[:]),
                lambda e: e.tensor_copy(sTbf[:], sT8[:]),
            ], s_v3)
            k.wait("sync", s_a1, s_a1.n - 2)
            k.wait("sync", s_hk)
            k.wait("sync", s_hv)
            for h in range(4):
                k.dma("sync", eglc[:, h:h + 1],
                      E8[63:64, h * 64:(h + 1) * 64], s_mv)
            k.op("sync", lambda e: e.sem_inc(s_eg.h, 1))
            s_eg.n += 1
            k.wait("vector", s_vl)
            for h in range(4):
                for kc in range(2):
                    k.wait("tensor", s_v1)
                    k.wait("tensor", s_tu, s_tu.n)
                    k.op("tensor", lambda e, h=h, kc=kc, c=c: e.transpose(
                        pTb[0:64, 0:128],
                        k_bf[:, (2 * h + kc) * TLOC + c * 64:
                             (2 * h + kc) * TLOC + c * 64 + 64],
                        idbf[:]), s_tt)
                    k.wait("vector", s_tt)
                    k.op("vector", lambda e, h=h, kc=kc: e.tensor_copy(
                        kT[:, h * 256 + kc * 128:h * 256 + kc * 128 + 128],
                        pTb[0:64, 0:128]), s_v1)
                    k.wait("tensor", s_v1)
                    k.op("tensor", lambda e, h=h, kc=kc, pp=pp: e.transpose(
                        pTb[0:64, 128:256],
                        vch[:, pp + (2 * h + kc) * 64:
                            pp + (2 * h + kc) * 64 + 64],
                        idbf[:]), s_tt)
                    k.wait("vector", s_tt)
                    k.op("vector", lambda e, h=h, kc=kc: e.tensor_copy(
                        vT[:, h * 256 + kc * 128:h * 256 + kc * 128 + 128],
                        pTb[0:64, 128:256]), s_v1)

            for h in range(4):
                hs = slice(h * 64, (h + 1) * 64)
                k.wait("tensor", s_v2, s_v2.n)
                for kc in range(2):
                    k.op("tensor", lambda e, h=h, kc=kc, cs=cs: e.matmul(
                        pA[0:64, 0:64],
                        k_bf[:, (2 * h + kc) * TLOC + c * 64:
                             (2 * h + kc) * TLOC + c * 64 + 64],
                        q_bf[:, (2 * h + kc) * TLOC + c * 64:
                             (2 * h + kc) * TLOC + c * 64 + 64],
                        start=(kc == 0), stop=(kc == 1)),
                        sig=s_ta if kc == 1 else None)
                k.wait("vector", s_ta)
                k.op("vector", lambda e: e.tensor_mul(AMbf[:], pA[0:64, 0:64],
                                                      tri64[:]), s_v2)
                k.wait("tensor", s_v3, s_v3.n)
                k.wait("tensor", s_v1, s_v1.n)
                for kc in range(2):
                    k.op("tensor", lambda e, h=h, kc=kc, cs=cs: e.matmul(
                        pL[0:64, 0:64],
                        q_bf[:, (2 * h + kc) * TLOC + c * 64:
                             (2 * h + kc) * TLOC + c * 64 + 64],
                        hkfm[:, h * 128 + kc * 64:h * 128 + kc * 64 + 64],
                        start=(kc == 0), stop=(kc == 1)),
                        sig=s_tl if kc == 1 else None)
                k.wait("tensor", s_v2)
                k.op("tensor", lambda e, h=h: e.matmul(
                    pA[0:64, 64:128], AMbf[:], sTbf[:, h * 64:(h + 1) * 64],
                    start=True, stop=True), s_tc)
                k.wait("vector", s_tl)
                k.wait("vector", s_tc)
                k.chain("vector", [
                    lambda e: e.tensor_copy(lsc[:], pL[0:64, 0:64]),
                    lambda e: e.tensor_add(lsc[:], lsc[:], pA[0:64, 64:128]),
                    lambda e, hs=hs: e.tensor_mul(lsc[:], lsc[:], E8[:, hs]),
                    lambda e: e.tensor_reduce(rowm[:, 0:1], lsc[:], AX.X,
                                              ALU.max),
                    lambda e: e.tensor_scalar(rowm[:, 1:2], rowm[:, 0:1],
                                              -0.0625, None, ALU.mult),
                ], s_v3)
                k.wait("scalar", s_v3)
                k.op("scalar", lambda e: e.activation(
                    psc[:], lsc[:], AF.Exp, bias=rowm[:, 1:2], scale=0.0625),
                    s_a2)
                k.wait("vector", s_a2)
                k.chain("vector", [
                    lambda e: e.tensor_reduce(rowm[:, 0:1], psc[:], AX.X,
                                              ALU.add),
                    lambda e: e.reciprocal(rowm[:, 0:1], rowm[:, 0:1]),
                    lambda e: e.tensor_scalar(psc[:], psc[:], rowm[:, 0:1],
                                              None, ALU.mult),
                    lambda e, hs=hs: e.tensor_mul(pneT[:], psc[:], E8[:, hs]),
                ], s_v3)
                k.wait("tensor", s_v3)
                k.wait("tensor", s_to, s_to.n)
                k.wait("tensor", s_tw, s_tw.n)
                k.op("tensor", lambda e: e.transpose(
                    pTb[0:64, 256:320], pneT[:], idbf[0:64, 0:64]), s_tt)
                k.op("tensor", lambda e, hs=hs: e.transpose(
                    pT0[0:64, 256:320], sT8[:, hs], iden[0:64, 0:64]), s_tt)
                k.wait("vector", s_tt)
                k.chain("vector", [
                    lambda e: e.tensor_copy(pnem[:], pTb[0:64, 256:320]),
                    lambda e: e.tensor_copy(smh[:], pT0[0:64, 256:320]),
                ], s_v2, inc=2)
                k.wait("tensor", s_v2)
                k.op("tensor", lambda e: e.matmul(
                    pW[0:64, 0:64], smh[:], pnem[:], start=True, stop=True),
                    s_tw)
                k.wait("vector", s_tw)
                k.op("vector", lambda e: e.tensor_mul(WtM[:], pW[0:64, 0:64],
                                                      tri64[:]), s_v2)
                k.wait("tensor", s_v2)
                k.wait("tensor", s_on, s_on.n)
                k.wait("tensor", s_hv, s_hv.n)
                k.op("tensor", lambda e, h=h: e.matmul(
                    pO[0:64, 0:256], WtM[:], vT[:, h * 256:(h + 1) * 256],
                    start=True, stop=False))
                k.op("tensor", lambda e, h=h: e.matmul(
                    pO[0:64, 0:256], pnem[:], hvbf[:, h * 256:(h + 1) * 256],
                    start=False, stop=True), s_to)
                k.wait("scalar", s_to)
                k.op("scalar", lambda e: e.activation(sqsc[:], pO[0:64, 0:256],
                                                      AF.Square), s_a2)
                k.wait("vector", s_a2)
                k.chain("vector", [
                    lambda e: e.tensor_reduce(msc[:, 0:1], sqsc[:], AX.X,
                                              ALU.add),
                    lambda e: e.tensor_scalar(msc[:, 0:1], msc[:, 0:1],
                                              1.0 / 256.0, 1e-5,
                                              ALU.mult, ALU.add),
                ], s_v3)
                k.wait("scalar", s_v3)
                k.chain("scalar", [
                    lambda e: e.activation(msc[:, 0:1], msc[:, 0:1], AF.Ln),
                    lambda e: e.activation(msc[:, 1:2], msc[:, 0:1], AF.Exp,
                                           scale=-0.5),
                ], s_a2)
                k.wait("vector", s_a2)
                k.wait("vector", s_tt, s_tt.n)
                k.op("vector", lambda e: e.tensor_scalar(
                    onT[:], pO[0:64, 0:256], msc[:, 1:2], None, ALU.mult),
                    s_on)
                for kc in range(2):
                    k.wait("tensor", s_on)
                    k.wait("tensor", s_v1)
                    k.op("tensor", lambda e, kc=kc: e.transpose(
                        pTb[0:128, 320:384],
                        onT[:, kc * 128:(kc + 1) * 128], idbf[0:64, 0:64]),
                        s_tt)
                    k.wait("vector", s_tt)
                    k.op("vector", lambda e, h=h, kc=kc, c=c: e.tensor_copy(
                        o_fm[:, (2 * h + kc) * TLOC + c * 64:
                             (2 * h + kc) * TLOC + c * 64 + 64],
                        pTb[0:128, 320:384]), s_v1)
                k.wait("tensor", s_hk, s_hk.n)
                k.wait("tensor", s_hv, s_hv.n)
                k.wait("vector", s_mv)
                k.op("tensor", lambda e, hs=hs, h=h: e.matmul(
                    pUk[0:64, 0:256], sTbf[:, hs], kT[:, h * 256:(h + 1) * 256],
                    start=True, stop=True), s_tu)
                k.op("tensor", lambda e, hs=hs, h=h: e.matmul(
                    pUv[0:64, 0:256], sTbf[:, hs], vT[:, h * 256:(h + 1) * 256],
                    start=True, stop=True), s_tu)
                k.wait("vector", s_tu)
                k.chain("vector", [
                    lambda e, h=h: e.tensor_add(
                        hkm[:, h * 256:(h + 1) * 256],
                        hkm[:, h * 256:(h + 1) * 256], pUk[0:64, 0:256]),
                    lambda e, h=h: e.tensor_scalar(
                        hkm[:, h * 256:(h + 1) * 256],
                        hkm[:, h * 256:(h + 1) * 256], eglc[:, h:h + 1],
                        None, ALU.mult),
                ], s_hk)
                k.chain("vector", [
                    lambda e, h=h: e.tensor_add(
                        hvm[:, h * 256:(h + 1) * 256],
                        hvm[:, h * 256:(h + 1) * 256], pUv[0:64, 0:256]),
                    lambda e, h=h: e.tensor_scalar(
                        hvm[:, h * 256:(h + 1) * 256],
                        hvm[:, h * 256:(h + 1) * 256], eglc[:, h:h + 1],
                        None, ALU.mult),
                    lambda e, h=h: e.tensor_copy(hvbf[:, h * 256:(h + 1) * 256],
                                                 hvm[:, h * 256:(h + 1) * 256]),
                ], s_hv)
                for kc in range(2):
                    k.wait("tensor", s_hk)
                    k.wait("tensor", s_v1)
                    k.wait("tensor", s_tl, s_tl.n)
                    k.op("tensor", lambda e, h=h, kc=kc: e.transpose(
                        pT0[0:128, 320:384],
                        hkm[:, h * 256 + kc * 128:h * 256 + kc * 128 + 128],
                        iden[0:64, 0:64]), s_tt)
                    k.wait("vector", s_tt)
                    k.op("vector", lambda e, h=h, kc=kc: e.tensor_copy(
                        hkfm[:, h * 128 + kc * 64:h * 128 + kc * 64 + 64],
                        pT0[0:128, 320:384]), s_v1)

        for it in range(NTT):
            t0 = it * TT
            k.wait("gpsimd", s_free)
            k.dma("gpsimd", x_t[:].rearrange("p (ch t) -> p ch t", ch=8),
                  x_in.rearrange("(ch p) t -> p ch t", p=128)
                  [:, :, t0:t0 + TT], s_x2)
            for eng in ("tensor", "vector"):
                k.wait(eng, s_x2)
            k.wait("tensor", k.sems["v1"])
            for oc in range(8):
                pb = (pO if oc % 2 == 0 else pW)
                ob = obuf[:, (oc % 2) * TT:(oc % 2 + 1) * TT]
                k.wait("tensor", s_free, s_free.n - (1 if oc >= 2 else 0)
                       if oc >= 2 else 0)
                for ic in range(8):
                    k.op("tensor", lambda e, pb=pb, ic=ic, oc=oc, t0=t0:
                         e.matmul(pb[:, 0:TT],
                                  wg[:, ic * 1024 + oc * 128:
                                     ic * 1024 + oc * 128 + 128],
                                  o_fm[:, ic * TLOC + t0:ic * TLOC + t0 + TT],
                                  start=(ic == 0), stop=(ic == 7)),
                         sig=s_mm if ic == 7 else None)
                k.wait("vector", s_mm, s_mm.n)
                _v = 16 * (8 * it + oc - 1) if 8 * it + oc >= 1 else 0
                k.wait("vector", s_st, ((_v + 31) // 32) * 32)
                k.op("vector", lambda e, pb=pb, ob=ob, oc=oc: e.tensor_add(
                    ob[0:128, :], pb[:, 0:TT],
                    x_t[:, oc * TT:(oc + 1) * TT]), s_free)
                k.wait("sync", s_free)
                k.dma("sync", x_out.rearrange("(ch p) t -> p ch t", p=128)
                      [:, oc, t0:t0 + TT], ob[0:128, :], s_st)
        k.wait("sync", s_hk)
        k.wait("sync", s_hv)
        k.dma("sync", hkv_out.rearrange("p (h x) -> p h x", h=4)[:, :, 0:256],
              hkm[:].rearrange("p (h x) -> p h x", h=4), s_st)
        k.dma("sync", hkv_out.rearrange("p (h x) -> p h x", h=4)[:, :, 256:512],
              hvm[:].rearrange("p (h x) -> p h x", h=4), s_st)
        k.wait("sync", s_st)
        k.play()
    return nc


def host_l1attn_inputs(z):
    return dict(
        wq=np.ascontiguousarray(z["L1_attn_Wq"]),
        wk=np.ascontiguousarray(z["L1_attn_Wk"]),
        wv=np.ascontiguousarray(z["L1_attn_Wv"]),
        wf=np.ascontiguousarray(z["L1_attn_Wf"]),
        wo=np.ascontiguousarray(z["L1_attn_Wo"]),
        gcol=col128(np.tile(z["L1_attn_g_norm_w"], 4)),
        iden=np.eye(128, dtype=np.float32),
        tri64=np.triu(np.ones((64, 64), np.float32)))



def fm(x_tokens_d):
    return np.ascontiguousarray(x_tokens_d.T)


def col128(v):
    n = v.shape[0]
    return np.ascontiguousarray(v.reshape(n // 128, 128).T)


def host_l0attn_inputs(z):
    cw = z["L0_attn_conv_w"][:, 0, :]
    convw = np.zeros((128, 68), np.float32)
    for c in range(17):
        for j in range(4):
            convw[:, c * 4 + j] = cw[c * 128:(c + 1) * 128, j]
    tri64 = np.triu(np.ones((64, 64), np.float32))
    self_f = np.zeros((32, 2048), np.float32)
    for pg in range(2048):
        self_f[pg // 64, pg] = 1.0
    return dict(
        w_in=np.ascontiguousarray(z["L0_attn_W_in"].T),
        w_out=np.ascontiguousarray(z["L0_attn_W_out"].T),
        convw=convw, convb=col128(z["L0_attn_conv_b"])[:, :17],
        dtb=np.ascontiguousarray(z["L0_attn_dt_bias"].reshape(32, 1)),
        alog=np.ascontiguousarray(z["L0_attn_A_log"].reshape(32, 1)),
        dcol=col128(np.repeat(z["L0_attn_D"], 64)),
        nwcol=col128(z["L0_attn_norm_w"]),
        iden=np.eye(128, dtype=np.float32), tri64=tri64, self_f=self_f)



def build_l2ffn():
    nc = bass.Bass("TRN2", target_bir_lowering=False,
                   detect_race_conditions=_RD)
    x_in = nc.dram_tensor("x_in", [D, TLOC], F32, kind="ExternalInput").ap()
    w1t = nc.dram_tensor("w1t", [D, 4096], F32, kind="ExternalInput").ap()
    w2t = nc.dram_tensor("w2t", [4096, D], F32, kind="ExternalInput").ap()
    b1c = nc.dram_tensor("b1c", [128, 32], F32, kind="ExternalInput").ap()
    b2c = nc.dram_tensor("b2c", [128, 8], F32, kind="ExternalInput").ap()
    x_out = nc.dram_tensor("x_out", [D, TLOC], F32, kind="ExternalOutput").ap()

    FF = 4096
    FC = FF // 128

    with ExitStack() as stack:
        k = KB(nc, stack)
        emit_consts(k, {"eps_ln": 1e-5})

        x_t = k.sb("x_t", [128, DC * TT], F32)
        xn = k.sb("xn", [128, DC * TT], BF16)
        h_sb = k.sb("h_sb", [128, FC * TT], BF16)
        w1_sb = k.sb("w1_sb", [128, DC * FF], BF16)
        w2_sb = k.sb("w2_sb", [128, FC * D], BF16)
        b1_sb = k.sb("b1_sb", [128, 32], F32)
        b2_sb = k.sb("b2_sb", [128, 8], F32)


        ps_mm = [k.ps(f"mm{i}", [128, TT]) for i in range(2)]

        s_xt = k.sem("x_ld")
        s_b = k.sem("b_ld")
        s_w = k.sem("w_done")
        s_mm1 = k.sem("mm1")
        s_hf = k.sem("h_free")
        s_mm2 = k.sem("mm2")
        s_of = k.sem("o_free")
        s_st = k.sem("st")

        k.dma("sync", b1_sb[:], b1c, s_b)
        k.dma("sync", b2_sb[:], b2c, s_b)

        emit_ternarize(k, "w1", w1t, D, FF, w1_sb, s_w)
        emit_ternarize(k, "w2", w2t, FF, D, w2_sb, s_w)

        k.wait("tensor", s_w)
        k.wait("scalar", s_b)
        k.wait("vector", s_b)

        for it in range(NTT):
            t0 = it * TT
            k.wait("gpsimd", s_st, 16 * it)
            k.dma("gpsimd", x_t[:].rearrange("p (c t) -> p c t", c=DC),
                  x_in.rearrange("(c p) t -> p c t", p=128)[:, :, t0:t0 + TT],
                  s_xt)
            for eng in ("scalar", "tensor", "vector"):
                k.wait(eng, s_xt)

            def x_ap(c):
                return x_t[:, c * TT:(c + 1) * TT]

            def xn_ap(c):
                return xn[:, c * TT:(c + 1) * TT]

            s_ap = emit_ln_tile(k, x_ap, xn_ap,
                                pre_apply_waits=[(s_mm1, 32 * it)])
            k.wait("tensor", s_ap)
            k.wait("tensor", s_of)

            def ep_h(oc, ps, mmv, it=it):
                k.wait("scalar", s_mm1, mmv)
                k.wait("scalar", s_mm2, 8 * it)
                k.op("scalar", lambda e, oc=oc, ps=ps: e.activation(
                    h_sb[:, oc * TT:(oc + 1) * TT], ps, AF.Gelu,
                    bias=b1_sb[:, oc:oc + 1]), s_hf)

            emit_proj(k,
                      lambda ic, oc: w1_sb[:, ic * FF + oc * 128: ic * FF + oc * 128 + 128],
                      lambda ic: xn[:, ic * TT:(ic + 1) * TT],
                      ep_h, DC, FC, ps_mm, s_mm1, s_hf)

            k.wait("tensor", s_hf)

            def ep_o(oc, ps, mmv):
                ep_t = k.sb("vscratch", [128, TT], F32)
                k.wait("vector", s_mm2, mmv)
                k.wait("vector", s_of)
                k.chain("vector", [
                    lambda e, oc=oc, ps=ps: e.tensor_scalar(
                        ep_t[:], ps, b2_sb[:, oc:oc + 1], 0.5, ALU.add, ALU.mult),
                    lambda e, oc=oc, ep_t=ep_t: e.tensor_add(
                        x_t[:, oc * TT:(oc + 1) * TT],
                        x_t[:, oc * TT:(oc + 1) * TT], ep_t[:]),
                ], s_of)

            emit_proj(k,
                      lambda ic, oc: w2_sb[:, ic * D + oc * 128: ic * D + oc * 128 + 128],
                      lambda ic: h_sb[:, ic * TT:(ic + 1) * TT],
                      ep_o, FC, DC, ps_mm, s_mm2, s_of)

            k.wait("gpsimd", s_of)
            k.dma("gpsimd",
                  x_out.rearrange("(c p) t -> p c t", p=128)[:, :, t0:t0 + TT],
                  x_t[:].rearrange("p (c t) -> p c t", c=DC), s_st)

        k.wait("gpsimd", s_st)
        k.play()
    return nc


def host_l2ffn_inputs(x_full_fm, W1, b1, W2, b2):
    w1t = np.ascontiguousarray(W1.T)
    w2t = np.ascontiguousarray(W2.T)
    b1c = col128(b1)
    b2c = col128(b2)
    return [dict(x_in=x_full_fm[c], w1t=w1t, w2t=w2t, b1c=b1c, b2c=b2c)
            for c in range(N_CORES)]



def shard_fm(x):
    out = []
    for b in range(4):
        for h in range(2):
            out.append(fm(x[b, h * TLOC:(h + 1) * TLOC, :]))
    return out


def unshard_fm(shards):
    x = np.empty((4, 4096, 1024), np.float32)
    for b in range(4):
        for h in range(2):
            x[b, h * TLOC:(h + 1) * TLOC, :] = shards[b * 2 + h].T
    return x


def _l3_rotary_consts():
    DIM_HEAD, WINDOW, n = 64, 256, 4096
    half = np.arange(0, DIM_HEAD, 2, dtype=np.float64)
    inv_freq = 1.0 / 10000.0 ** (half / DIM_HEAD)
    t = np.arange(n, dtype=np.float64)
    freqs = t[:, None] * inv_freq[None, :]
    freqs = np.concatenate([freqs, freqs], -1)
    base_scale = (half + 0.4 * DIM_HEAD) / (1.4 * DIM_HEAD)
    power = (t - n // 2) / (WINDOW // 2)
    scale = base_scale[None, :] ** power[:, None]
    scale = np.concatenate([scale, scale], -1)
    cos, sin = np.cos(freqs), np.sin(freqs)
    sign = np.concatenate([-np.ones(32), np.ones(32)])
    out = {}
    for hf in range(2):
        qpos = np.arange(TLOC) + hf * TLOC
        kposc = np.clip(np.arange(-L3H, TLOC) + hf * TLOC, 0, n - 1)
        tq = np.concatenate([(cos[qpos] * scale[qpos]).T,
                             (sin[qpos] * scale[qpos] * sign[None, :]).T],
                            axis=1).astype(np.float32)
        tk = np.concatenate([(cos[kposc] / scale[kposc]).T,
                             (sin[kposc] / scale[kposc] * sign[None, :]).T],
                            axis=1).astype(np.float32)
        out[hf] = (np.ascontiguousarray(tq), np.ascontiguousarray(tk))
    return out


def _l3_halo_shard(zfull, b, half):
    lo = half * TLOC - L3H
    out = np.zeros((L3T, D), np.float32)
    src_lo = max(lo, 0)
    out[src_lo - lo:] = zfull[b, src_lo:half * TLOC + TLOC]
    return np.ascontiguousarray(out.T)


def kernel(**z):
    cores = list(range(N_CORES))

    def run(nc, in_maps):
        return run_bass_kernel_spmd(nc, in_maps, core_ids=cores)

    S = shard_fm(np.ascontiguousarray(z["L0_attn_u"], dtype=np.float32))

    gwt = np.ascontiguousarray(z["L0_ffn_gate_w"].T)
    gbr = z["L0_ffn_gate_b"].reshape(1, 8).astype(np.float32)
    ewt = np.ascontiguousarray(z["L0_ffn_expert_w"].transpose(0, 2, 1))
    ebc = np.ascontiguousarray(z["L0_ffn_expert_b"]
                               .reshape(8, 8, 128).transpose(2, 0, 1)
                               .reshape(128, 64))
    ncA = build_l0ffn(gate_only=True)
    ncB = build_l0ffn(gate_only=False)

    def l0ffn(S):
        rA = run(ncA, [dict(x_in=S[c], gwt=gwt, gbr=gbr) for c in cores])
        parts = np.stack([rA.results[c]["cnt_out"][0] for c in cores])
        counts_all = np.ascontiguousarray(
            parts.reshape(1, 64).astype(np.float32))
        rB = run(ncB, [dict(x_in=S[c], gwt=gwt, gbr=gbr, ewt=ewt, ebc=ebc,
                            counts_all=counts_all) for c in cores])
        return [rB.results[c]["x_out"] for c in cores]

    S = l0ffn(S)

    m_com = host_l0attn_inputs(z)
    nc_ma = build_l0attn_a()
    in_a = []
    for c in cores:
        half = c % 2
        xh = np.zeros((D, 2051), np.float32)
        if half == 0:
            xh[:, 3:] = S[c]
        else:
            xh[:, :3] = S[c - 1][:, TLOC - 3:]
            xh[:, 3:] = S[c]
        in_a.append(dict(x_in=np.ascontiguousarray(xh), w_in=m_com["w_in"],
                         convw=m_com["convw"], convb=m_com["convb"],
                         dtb=m_com["dtb"], alog=m_com["alog"]))
    rMA = run(nc_ma, in_a)
    nc_mb = build_l0attn_b()

    def run_mb(r_ins):
        return run(nc_mb, [dict(
            x_in=S[c], xbc=rMA.results[c]["xbc_out"],
            z_in=rMA.results[c]["z_out"], dtad=rMA.results[c]["dtad_out"],
            w_out=m_com["w_out"], dcol=m_com["dcol"], nwcol=m_com["nwcol"],
            iden=m_com["iden"], tri64=m_com["tri64"], self_f=m_com["self_f"],
            r_in=r_ins[c]) for c in cores])

    zeros64 = np.zeros((64, 2048), np.float32)
    rB1 = run_mb([zeros64] * 8)
    rB2 = run_mb([zeros64 if c % 2 == 0 else rB1.results[c - 1]["r_out"]
                  for c in cores])
    S = [rB2.results[c]["x_out"] for c in cores]

    S = l0ffn(S)

    rwt = np.concatenate([z["L1_ffn_rw1"], z["L1_ffn_sw1"][None]], 0)
    rw2t = np.concatenate([z["L1_ffn_rw2"], z["L1_ffn_sw2"][None]], 0)
    b1all = np.concatenate([z["L1_ffn_rb1"], z["L1_ffn_sb1"][None]], 0)
    rb1c = np.ascontiguousarray(
        b1all.reshape(9, 16, 128).transpose(2, 0, 1).reshape(128, 9 * 16))
    b2all = np.concatenate([z["L1_ffn_rb2"], z["L1_ffn_sb2"][None]], 0)
    rb2r = np.ascontiguousarray(b2all.reshape(1, 9 * 1024))
    rbr = z["L1_ffn_router_b"].reshape(1, 8).astype(np.float32)
    rwtr = np.ascontiguousarray(z["L1_ffn_router_w"])
    nc_l1f = build_l1ffn()

    def l1ffn(S):
        r = run(nc_l1f, [dict(x_in=S[c], rwt=rwt, rw2t=rw2t, rb1c=rb1c,
                              rb2r=rb2r, rbr=rbr, rwtr=rwtr) for c in cores])
        return [r.results[c]["x_out"] for c in cores]

    S = l1ffn(S)

    g_com = host_l1attn_inputs(z)
    nc_ga = build_l1attn_a()
    rGA = run(nc_ga, [dict(x_in=S[c], wq=g_com["wq"], wk=g_com["wk"],
                           wv=g_com["wv"], wf=g_com["wf"]) for c in cores])
    nc_gb = build_l1attn_b()

    def run_gb(h_ins):
        return run(nc_gb, [dict(
            x_in=S[c], q_in=rGA.results[c]["q_out"],
            k_in=rGA.results[c]["k_out"], v_in=rGA.results[c]["v_out"],
            f_in=rGA.results[c]["f_out"], wo=g_com["wo"], gcol=g_com["gcol"],
            iden=g_com["iden"], tri64=g_com["tri64"], hkv_in=h_ins[c])
            for c in cores])

    rG1 = run_gb([zeros64] * 8)
    rG2 = run_gb([zeros64 if c % 2 == 0 else rG1.results[c - 1]["hkv_out"]
                  for c in cores])
    S = [rG2.results[c]["x_out"] for c in cores]

    S = l1ffn(S)

    nc_l2f = build_l2ffn()

    def l2ffn(S):
        r = run(nc_l2f, host_l2ffn_inputs(S, z["L2_ffn_W1"], z["L2_ffn_b1"],
                                          z["L2_ffn_W2"], z["L2_ffn_b2"]))
        return [r.results[c]["x_out"] for c in cores]

    S = l2ffn(S)

    a_com = host_l2attn_inputs(z)
    nc_p1 = build_l2attn_p1()
    r1 = run(nc_p1, [dict(x_in=S[c], wk=a_com["wk"], projdn=a_com["projdn"])
                     for c in cores])
    kmax_all = np.ascontiguousarray(np.array(
        [[r1.results[c]["kmax_out"][0, 0] for c in cores]], np.float32))
    nc_p2 = build_l2attn_p2()
    r2 = run(nc_p2, [dict(x_in=S[c], wk=a_com["wk"], wv=a_com["wv"],
                          projdn=a_com["projdn"], kmax_all=kmax_all)
                     for c in cores])
    nc_p3a = build_l2attn_p3a()
    in3 = []
    for c in cores:
        p = c ^ 1
        kvp2 = np.concatenate([r2.results[c]["kvp_out"],
                               r2.results[p]["kvp_out"]], axis=1)
        sv2 = np.concatenate([r2.results[c]["sv_out"],
                              r2.results[p]["sv_out"]], axis=1)
        in3.append(dict(x_in=S[c], wq=a_com["wq"], projdn=a_com["projdn"],
                        wo_r=a_com["wo_r"], kvp2=np.ascontiguousarray(kvp2),
                        sv2=np.ascontiguousarray(sv2),
                        ln1gc=a_com["ln1gc"], ln1bc=a_com["ln1bc"]))
    r3 = run(nc_p3a, in3)
    nc_p3b = build_l2attn_p3b()
    r4 = run(nc_p3b, [dict(x_in=S[c], o1_in=r3.results[c]["out1"],
                           w1t=a_com["w1t"], b1c=a_com["b1c"],
                           w2t=a_com["w2t"], b2c=a_com["b2c"],
                           ln2gc=a_com["ln2gc"], ln2bc=a_com["ln2bc"])
             for c in cores])
    S = [r4.results[c]["x_out"] for c in cores]

    S = l2ffn(S)

    rot = _l3_rotary_consts()
    sposc = []
    qposr = []
    for c in cores:
        half = c % 2
        kp = (np.arange(-L3H, TLOC) + half * TLOC).astype(np.float32)
        kp[kp < 0] = 1e9
        sposc.append(np.ascontiguousarray(
            kp.reshape(L3ST, 128).T.astype(np.float32)))
        qposr.append(np.ascontiguousarray(
            (np.arange(TLOC) + half * TLOC).astype(np.float32).reshape(1, -1)))
    perm = np.concatenate([np.arange(32, 64), np.arange(0, 32)]).astype(int)
    Wqkv = z["L3_ffn_Wqkv"]; qsc = z["L3_ffn_q_scale"]
    ksc = z["L3_ffn_k_scale"]; Wo3 = z["L3_ffn_Wo"]
    lng = z["L3_ffn_ln_g"]; lnb = z["L3_ffn_ln_b"]
    W13 = z["L3_ffn_W1"]; b13 = z["L3_ffn_b1"]
    W23 = z["L3_ffn_W2"]; b23 = z["L3_ffn_b2"]
    nc_pre = build_l3pre()
    nc_a3 = build_l3ffn_attn()
    nc_b3 = build_l3ffn_ffn()
    nc_post = build_l3post()

    def l3ffn(S):
        xs_in = S
        r = run(nc_pre, [dict(x_in=S[c]) for c in cores])
        zfull = unshard_fm([r.results[c]["z_out"] for c in cores])
        for i in range(6):
            wqkv_i = np.ascontiguousarray(Wqkv[i])
            wo_i = np.ascontiguousarray(Wo3[i])
            qks2 = np.ascontiguousarray(np.stack(
                [qsc[i], qsc[i][perm], ksc[i], ksc[i][perm]],
                axis=1).astype(np.float32))
            in_a = []
            for c in cores:
                b, half = c // 2, c % 2
                tq, tk = rot[half]
                in_a.append(dict(zh_in=_l3_halo_shard(zfull, b, half),
                                 wqkv=wqkv_i, wo=wo_i, tq=tq, tk=tk,
                                 qks2=qks2, qposr=qposr[c], sposc=sposc[c]))
            ra = run(nc_a3, in_a)
            zfull = unshard_fm([ra.results[c]["za_out"] for c in cores])
            w1p = np.zeros((1024, DIMI2), np.float32)
            w1p[:, :2730] = W13[i][:, :2730]
            w1p[:, 2816:2816 + 2730] = W13[i][:, 2730:]
            b1a = np.zeros(2816, np.float32); b1a[:2730] = b13[i][:2730]
            b1g = np.zeros(2816, np.float32); b1g[:2730] = b13[i][2730:]
            w2p = np.zeros((DIMI2 // 2, 1024), np.float32)
            w2p[:2730] = W23[i]
            in_b = []
            for c in cores:
                b, half = c // 2, c % 2
                in_b.append(dict(zh_in=_l3_halo_shard(zfull, b, half),
                                 lngc=col128(lng[i]), lnbc=col128(lnb[i]),
                                 w1p=w1p, b1ac=col128(b1a), b1gc=col128(b1g),
                                 w2p=w2p, b2c=col128(b23[i])))
            rb = run(nc_b3, in_b)
            zfull = unshard_fm([rb.results[c]["z_out"] for c in cores])
        zsh = shard_fm(zfull)
        r = run(nc_post, [dict(x_in=xs_in[c], z_in=zsh[c]) for c in cores])
        return [r.results[c]["x_out"] for c in cores]

    S = l3ffn(S)

    at_com = dict(
        wdq_t=np.ascontiguousarray(z["L3_attn_W_dq"].T),
        wdq=np.ascontiguousarray(z["L3_attn_W_dq"]),
        wuq=np.ascontiguousarray(z["L3_attn_W_uq"]),
        wuq_tt=np.ascontiguousarray(z["L3_attn_W_uq"].T),
        wuk=np.ascontiguousarray(z["L3_attn_W_uk"]),
        wuv=np.ascontiguousarray(z["L3_attn_W_uv"]),
        wo_tt=np.ascontiguousarray(z["L3_attn_W_o"].T),
        wdkv_t=np.ascontiguousarray(z["L3_attn_W_dkv"].T))
    scolc = np.ascontiguousarray(
        (np.arange(4096).reshape(32, 128).T).astype(np.float32))
    nc_mla = build_l3attn(phase=99)
    in_m = []
    for c in cores:
        b, half = c // 2, c % 2
        in_m.append(dict(x_in=S[c], xk1_in=S[b * 2], xk2_in=S[b * 2 + 1],
                         qposr=qposr[c], scolc=scolc, **at_com))
    rM = run(nc_mla, in_m)
    S = [rM.results[c]["x_out"] for c in cores]

    S = l3ffn(S)

    return unshard_fm(S)


# version 04 (final)
# speedup vs baseline: 1.0317x; 1.0317x over previous
import numpy as np
import concourse.bass as bass
import concourse.mybir as mybir
from concourse.bass_utils import run_bass_kernel_spmd
from contextlib import ExitStack

F32 = mybir.dt.float32
BF16 = mybir.dt.bfloat16
I32 = mybir.dt.int32
AF = mybir.ActivationFunctionType
ALU = mybir.AluOpType
AX = mybir.AxisListType

TLOC = 2048
D = 1024
DC = D // 128
TT = 512
NTT = TLOC // TT
N_CORES = 8
_RD = False



class Sem:
    __slots__ = ("h", "n", "name")

    def __init__(self, h, name):
        self.h = h
        self.n = 0
        self.name = name


class KB:
    ENGINES = ("tensor", "vector", "scalar", "gpsimd", "sync")

    def __init__(self, nc, stack: ExitStack):
        self.nc = nc
        self.stack = stack
        self.q = {e: [] for e in self.ENGINES}
        self.sems = {}
        self.sbuf = {}
        self.psum = {}

    def sem(self, name) -> Sem:
        if name not in self.sems:
            h = self.stack.enter_context(self.nc.semaphore(name))
            self.sems[name] = Sem(h, name)
        return self.sems[name]

    def sb(self, name, shape=None, dtype=None):
        if name not in self.sbuf:
            self.sbuf[name] = self.stack.enter_context(
                self.nc.sbuf_tensor(name, shape, dtype))
        return self.sbuf[name]

    def ps(self, name, shape=None, dtype=F32):
        if name not in self.psum:
            self.psum[name] = self.stack.enter_context(
                self.nc.psum_tensor(name, shape, dtype))
        return self.psum[name]

    def on(self, eng, fn):
        self.q[eng].append(fn)

    def wait(self, eng, sem: Sem, val=None):
        v = sem.n if val is None else val
        if v > 0:
            self.on(eng, lambda e, h=sem.h, v=v: e.wait_ge(h, v))

    def op(self, eng, fn, sig: Sem = None, inc=1):
        if sig is not None:
            sig.n += inc
            self.on(eng, lambda e, fn=fn, h=sig.h, inc=inc: fn(e).then_inc(h, inc))
        else:
            self.on(eng, fn)

    def chain(self, eng, fns, sig: Sem = None, inc=1):
        s = self.sem(eng + "_self")
        for i, fn in enumerate(fns):
            last = i == len(fns) - 1
            self.op(eng, fn, sig=(sig if last and sig is not None else s),
                    inc=(inc if last and sig is not None else 1))
            if not last:
                self.wait(eng, s)

    def dma(self, eng, out, in_, sig: Sem):
        self.op(eng, lambda e, out=out, in_=in_: e.dma_start(out=out, in_=in_),
                sig, 16)

    def play(self):
        block = self.stack.enter_context(self.nc.Block())
        for eng in self.ENGINES:
            thunks = self.q[eng]
            if not thunks:
                continue
            getattr(block, eng)(lambda e, thunks=thunks: [f(e) for f in thunks])


def emit_consts(k: KB, consts=()):
    oc = k.sb("ones_col", [128, 1], F32)
    orow = k.sb("ones_row", [1, 128], F32)
    s = k.sem("consts")
    orbf = k.sb("ones_row_bf", [1, 128], BF16)
    ocbf = k.sb("ones_col_bf", [128, 1], BF16)
    k.op("gpsimd", lambda e: e.memset(oc[:], 1.0))
    k.op("gpsimd", lambda e: e.memset(orbf[:], 1.0))
    k.op("gpsimd", lambda e: e.memset(ocbf[:], 1.0))
    for name, val in dict(consts).items():
        t = k.sb("c_" + name, [1, 1], F32)
        k.op("gpsimd", lambda e, t=t, val=val: e.memset(t[:], val))
    k.op("gpsimd", lambda e: e.memset(orow[:], 1.0), s)
    for eng in ("tensor", "scalar", "vector"):
        k.wait(eng, s)
    return s



def emit_ln_tile(k: KB, x_ap, xn_ap, tt=TT, eps_name="eps_ln", nchunks=DC,
                 pre_apply_waits=(), affine=None):
    sq = k.sb("ln_sq", [128, TT], F32)
    apt = k.sb("vscratch", [128, TT], F32)
    rows = k.sb("ln_rows", [1, 4 * TT], F32)
    ps_s1 = k.ps("ln_ps_s1", [1, TT])
    ps_s2 = k.ps("ln_ps_s2", [1, TT])
    ps_b0 = k.ps("ln_ps_b0", [128, TT])
    ps_b1 = k.ps("ln_ps_b1", [128, TT])
    oc = k.sbuf["ones_col"]
    orow = k.sbuf["ones_row"]

    s_sq = k.sem("ln_sq_s")
    s_sum = k.sem("ln_sum_s")
    s_row = k.sem("ln_row_s")
    s_rstd = k.sem("ln_rstd_s")
    s_row2 = k.sem("ln_row2_s")
    s_bc = k.sem("ln_bc_s")
    s_ap = k.sem("ln_apply_s")

    prev_ap = s_ap.n
    prev_row = s_row.n

    inv_d = 1.0 / (128 * nchunks)

    k.wait("tensor", s_row, prev_row)
    for c in range(nchunks):
        k.wait("scalar", s_sum)
        k.op("scalar", lambda e, c=c: e.activation(sq[:, :tt], x_ap(c), AF.Square),
             s_sq)
        k.wait("tensor", s_sq)
        k.op("tensor", lambda e, c=c: e.matmul(
            ps_s1[:1, :tt], oc[:], x_ap(c), start=(c == 0), stop=(c == nchunks - 1)))
        k.op("tensor", lambda e, c=c: e.matmul(
            ps_s2[:1, :tt], oc[:], sq[:, :tt], start=(c == 0), stop=(c == nchunks - 1)),
            s_sum)
    k.wait("vector", s_sum)
    k.chain("vector", [
        lambda e: e.tensor_scalar_mul(rows[:1, 0 * TT:0 * TT + tt], ps_s1[:1, :tt], inv_d),
        lambda e: e.tensor_scalar_mul(rows[:1, 1 * TT:1 * TT + tt], ps_s2[:1, :tt], inv_d),
        lambda e: e.tensor_mul(rows[:1, 3 * TT:3 * TT + tt], rows[:1, 0 * TT:0 * TT + tt], rows[:1, 0 * TT:0 * TT + tt]),
        lambda e: e.tensor_sub(rows[:1, 1 * TT:1 * TT + tt], rows[:1, 1 * TT:1 * TT + tt], rows[:1, 3 * TT:3 * TT + tt]),
    ], s_row)
    k.wait("scalar", s_row)
    epsap = k.sbuf["c_" + eps_name]
    k.chain("scalar", [
        lambda e: e.activation(rows[:1, 1 * TT:1 * TT + tt],
                               rows[:1, 1 * TT:1 * TT + tt], AF.Ln,
                               bias=epsap[:1, :1]),
        lambda e: e.activation(rows[:1, 1 * TT:1 * TT + tt],
                               rows[:1, 1 * TT:1 * TT + tt], AF.Exp,
                               scale=-0.5),
    ], s_rstd)
    k.wait("vector", s_rstd)
    k.chain("vector", [
        lambda e: e.tensor_mul(rows[:1, 2 * TT:2 * TT + tt], rows[:1, 0 * TT:0 * TT + tt], rows[:1, 1 * TT:1 * TT + tt]),
        lambda e: e.tensor_scalar_mul(rows[:1, 2 * TT:2 * TT + tt], rows[:1, 2 * TT:2 * TT + tt], -1.0),
    ], s_row2)
    k.wait("tensor", s_row2)
    k.wait("tensor", s_ap, prev_ap)
    k.op("tensor", lambda e: e.matmul(ps_b0[:, :tt], orow[:], rows[:1, 1 * TT:1 * TT + tt],
                                      start=True, stop=True))
    k.op("tensor", lambda e: e.matmul(ps_b1[:, :tt], orow[:], rows[:1, 2 * TT:2 * TT + tt],
                                      start=True, stop=True), s_bc)
    vself = k.sem("vector_self")
    k.wait("vector", s_bc)
    for (sem, val) in pre_apply_waits:
        k.wait("vector", sem, val)
    for c in range(nchunks):
        if c > 0:
            k.wait("vector", s_ap)
        if affine is None:
            k.op("vector", lambda e, c=c: e.tensor_mul(apt[:, :tt], x_ap(c),
                                                       ps_b0[:, :tt]), vself)
            k.wait("vector", vself)
            k.op("vector", lambda e, c=c: e.tensor_add(xn_ap(c), apt[:, :tt],
                                                       ps_b1[:, :tt]), s_ap)
        else:
            g_ap, b_ap = affine
            k.chain("vector", [
                lambda e, c=c: e.tensor_mul(apt[:, :tt], x_ap(c), ps_b0[:, :tt]),
                lambda e, c=c: e.tensor_add(apt[:, :tt], apt[:, :tt],
                                            ps_b1[:, :tt]),
                lambda e, c=c: e.tensor_scalar(xn_ap(c), apt[:, :tt], g_ap(c),
                                               b_ap(c), ALU.mult, ALU.add),
            ], s_ap)
    return s_ap



def emit_proj(k: KB, lhsT, rhs, epi, n_ic, n_oc, banks, s_mm: Sem, s_free: Sem):
    freed = {}
    base_free = s_free.n
    for oc in range(n_oc):
        b = banks[oc % len(banks)]
        if oc >= len(banks):
            k.wait("tensor", s_free, freed[oc - len(banks)])
        for ic in range(n_ic):
            k.op("tensor", lambda e, b=b, ic=ic, oc=oc: e.matmul(
                b[:], lhsT(ic, oc), rhs(ic), start=(ic == 0), stop=(ic == n_ic - 1)),
                sig=s_mm if ic == n_ic - 1 else None)
        epi(oc, b[:], s_mm.n)
        assert s_free.n > base_free, "epi must signal s_free"
        freed[oc] = s_free.n
        base_free = s_free.n



def emit_ternarize(k: KB, tag, src, rows, cols, dst_sb, s_done: Sem):
    n_rt = rows // 128
    CB = 512
    n_cb = (cols + CB - 1) // CB
    wload = k.sb("tz_load", [128, CB], F32)
    wabs = k.sb("tz_abs_t", [128, CB], F32)
    sgn = k.sb("tz_sgn", [128, CB], F32)
    stage = k.sb("tz_stage", [128, 512], F32)
    colsum = k.sb("tz_colsum", [128, 1], F32)
    tot = k.sb("tz_tot", [1, 1], F32)
    thr = k.sb("tz_thr", [128, 1], F32)
    ps_tot = k.ps("tz_ps_tot", [1, 1])
    ps_thr = k.ps("tz_ps_thr", [128, 1])
    oc = k.sbuf["ones_col"]
    orow = k.sbuf["ones_row"]

    s_ld = k.sem(f"tz_ld_{tag}")
    s_abs = k.sem(f"tz_abs_{tag}")
    s_red = k.sem(f"tz_red_{tag}")
    s_thr = k.sem(f"tz_thr_{tag}")
    s_trn = k.sem(f"tz_trn_{tag}")

    tiles = [(r, cb) for r in range(n_rt) for cb in range(n_cb)]
    assert len(tiles) <= 512
    assert cols % CB == 0

    for name, sem in list(k.sems.items()):
        if (name.startswith("tz_abs_") or name.startswith("tz_trn_")) \
                and not name.endswith(f"_{tag}"):
            k.wait("sync", sem)

    for i, (r, cb) in enumerate(tiles):
        k.wait("sync", s_abs)
        k.dma("sync", wload[:], src[r * 128:(r + 1) * 128, cb * CB:(cb + 1) * CB],
              s_ld)
        k.wait("vector", s_ld)
        k.op("vector", lambda e, i=i: e.tensor_reduce(
            stage[:, i:i + 1], wload[:], AX.X, ALU.add,
            apply_absolute_value=True), s_abs)
    k.op("vector", lambda e, n=len(tiles): e.tensor_reduce(
        colsum[:], stage[:, :n], AX.X, ALU.add), s_red)
    k.wait("tensor", s_red)
    k.op("tensor", lambda e: e.matmul(ps_tot[:1, :1], oc[:], colsum[:],
                                      start=True, stop=True), s_red)
    k.wait("vector", s_red)
    k.op("vector", lambda e: e.tensor_scalar(
        tot[:1, :1], ps_tot[:1, :1], 0.5 / (rows * cols), 5e-9, ALU.mult, ALU.max),
        s_red)
    k.wait("tensor", s_red)
    k.op("tensor", lambda e: e.matmul(ps_thr[:, :1], orow[:], tot[:1, :1],
                                      start=True, stop=True), s_thr)
    k.wait("vector", s_thr)
    k.op("vector", lambda e: e.tensor_copy(thr[:], ps_thr[:, :1]), s_thr)
    k.wait("sync", s_abs)
    for i, (r, cb) in enumerate(tiles):
        k.wait("sync", s_trn)
        k.dma("sync", wload[:], src[r * 128:(r + 1) * 128, cb * CB:(cb + 1) * CB],
              s_ld)
        k.wait("scalar", s_ld)
        k.wait("scalar", s_trn)
        k.op("scalar", lambda e: e.activation(sgn[:], wload[:], AF.Sign))
        k.op("scalar", lambda e: e.activation(wabs[:], wload[:], AF.Abs), s_abs)
        k.wait("vector", s_abs)
        k.op("vector", lambda e: e.tensor_scalar(
            wabs[:], wabs[:], thr[:, :1], None, ALU.is_ge))
        dst = dst_sb[:, r * cols + cb * CB: r * cols + (cb + 1) * CB]
        k.op("vector", lambda e, dst=dst: e.tensor_mul(dst, wabs[:], sgn[:]),
             s_trn)
    s_done.n += 1
    k.on("vector", lambda e, h=s_done.h: e.sem_inc(h, 1))



def build_l0ffn(gate_only):
    nc = bass.Bass("TRN2", target_bir_lowering=False,
                   detect_race_conditions=_RD)
    x_in = nc.dram_tensor("x_in", [D, TLOC], F32, kind="ExternalInput").ap()
    gwt = nc.dram_tensor("gwt", [D, 8], F32, kind="ExternalInput").ap()
    gbr = nc.dram_tensor("gbr", [1, 8], F32, kind="ExternalInput").ap()
    cnt_out = nc.dram_tensor("cnt_out", [1, 8], F32, kind="ExternalOutput").ap()
    if not gate_only:
        ewt = nc.dram_tensor("ewt", [8, D, D], F32, kind="ExternalInput").ap()
        ebc = nc.dram_tensor("ebc", [128, 64], F32, kind="ExternalInput").ap()
        counts_all = nc.dram_tensor("counts_all", [1, 64], F32,
                                    kind="ExternalInput").ap()
        x_out = nc.dram_tensor("x_out", [D, TLOC], F32, kind="ExternalOutput").ap()

    with ExitStack() as stack:
        k = KB(nc, stack)
        emit_consts(k, {"eps_ln": 1e-5})

        x_t = k.sb("x_t", [128, DC * TT], F32)
        xn = k.sb("xn", [128, DC * TT], BF16)
        gw_sb = k.sb("gw_sb", [128, DC * 8], BF16)
        gb_sb = k.sb("gb_sb", [1, 8], BF16)
        m1 = k.sb("m1", [128, 1], F32)
        m2 = k.sb("m2", [128, 1], F32)
        eqb = k.sb("eqb", [128, 8], F32)
        mkd = k.sb("mkd", [128, 8], F32)
        indb = k.sb("indb", [128, 8], F32)
        cnt_sb = k.sb("cnt_sb", [1, 8], F32)
        ps_g = k.ps("ps_g", [128, 8])
        ps_cnt = k.ps("ps_cnt", [1, 8])

        s_xt = k.sem("x_ld")
        s_w = k.sem("w_ld")
        s_g = k.sem("g_mm")
        s_ind = k.sem("ind")
        s_cm = k.sem("cnt_mm")
        s_cnt = k.sem("cnt_done")
        s_st = k.sem("st")

        if not gate_only:
            ew_sb = k.sb("ew_sb", [128, 8 * DC * D], BF16)
            eb_sb = k.sb("eb_sb", [128, 64], F32)
            cb_sb = k.sb("cb_sb", [128, 8], F32)
            acc = k.sb("acc", [128, DC * TT], F32)
            ept = k.sb("ept", [128, TT], F32)
            ps_mm = [k.ps(f"mm{i}", [128, TT]) for i in range(2)]
            s_mm = k.sem("mm")
            s_ac = k.sem("acc_s")
            s_cb = k.sem("cb_s")
            s_ws = k.sem("w_ld_sync")
            k.dma("sync", eb_sb[:], ebc, s_ws)
            ca_sb = k.sb("ca_sb", [1, 64], F32)
            k.dma("sync", ca_sb[:], counts_all, s_ws)
            for e in range(8):
                k.dma("gpsimd", ew_sb[:].rearrange(
                    "p (e c h) -> p e c h", e=8, c=DC)[:, e:e + 1],
                    ewt.rearrange("e (c p) h -> p e c h", p=128)[:, e:e + 1],
                    s_w)
            k.wait("vector", s_w)
            k.wait("vector", k.sems["w_ld_sync"])
            k.op("vector", lambda e: e.tensor_reduce(
                cnt_sb[:1, :8], bass.AP(ca_sb, 0, [[64, 1], [1, 8], [8, 8]]),
                AX.X, ALU.add), s_cb)
            k.wait("tensor", s_cb)
            k.op("tensor", lambda e: e.matmul(ps_g[:, :8],
                                              k.sbuf["ones_row"][:1, :],
                                              cnt_sb[:1, :8],
                                              start=True, stop=True), s_cb)
            k.wait("vector", s_cb)
            k.op("vector", lambda e: e.tensor_copy(cb_sb[:], ps_g[:, :8]), s_cb)
            k.wait("tensor", s_cb)

        k.dma("gpsimd", gw_sb[:].rearrange("p (c e) -> p c e", c=DC),
              gwt.rearrange("(c p) e -> p c e", p=128), s_w)
        k.dma("gpsimd", gb_sb[:], gbr, s_w)

        k.wait("tensor", s_w)
        if not gate_only:
            k.wait("tensor", k.sems["w_ld_sync"])
            k.wait("scalar", k.sems["w_ld_sync"])

        for it in range(NTT):
            t0 = it * TT
            if gate_only:
                k.wait("gpsimd", k.sem("go_rel"), it)
            else:
                k.wait("gpsimd", s_st, 16 * it)
            k.dma("gpsimd", x_t[:].rearrange("p (c t) -> p c t", c=DC),
                  x_in.rearrange("(c p) t -> p c t", p=128)[:, :, t0:t0 + TT],
                  s_xt)
            for eng in ("scalar", "tensor", "vector"):
                k.wait(eng, s_xt)

            def x_ap(c):
                return x_t[:, c * TT:(c + 1) * TT]

            def xn_ap(c):
                return xn[:, c * TT:(c + 1) * TT]

            paw = ([(s_g, 4 * it)] if gate_only else [(k.sems["mm"], 64 * it)])
            s_ap = emit_ln_tile(k, x_ap, xn_ap, pre_apply_waits=paw)
            k.wait("tensor", s_ap)

            for st in range(4):
                ts = t0 + st * 128
                sub = it * 4 + st
                k.wait("tensor", s_ind)
                for ic in range(DC):
                    k.op("tensor", lambda e, ic=ic, st=st: e.matmul(
                        ps_g[:, :8], xn[:, ic * TT + st * 128: ic * TT + st * 128 + 128],
                        gw_sb[:, ic * 8:(ic + 1) * 8], start=(ic == 0), stop=False))
                k.op("tensor", lambda e: e.matmul(
                    ps_g[:, :8], k.sbuf["ones_row_bf"][:1, :128], gb_sb[:1, :8],
                    start=False, stop=True), s_g)
                k.wait("vector", s_g)
                k.chain("vector", [
                    lambda e: e.reduce_max(m1[:], ps_g[:, :8], AX.X),
                    lambda e: e.tensor_scalar(
                        eqb[:], ps_g[:, :8], m1[:, :1], None, ALU.is_ge),
                    lambda e: e.scalar_tensor_tensor(
                        mkd[:], eqb[:], -1e9, ps_g[:, :8], ALU.mult, ALU.add),
                    lambda e: e.reduce_max(m2[:], mkd[:], AX.X),
                    lambda e: e.tensor_scalar(
                        indb[:], ps_g[:, :8], m2[:, :1], None, ALU.is_ge),
                ], s_ind)
                k.wait("tensor", s_ind)
                k.op("tensor", lambda e, sub=sub: e.matmul(
                    ps_cnt[:1, :8], k.sbuf["ones_col"][:], indb[:],
                    start=(sub == 0), stop=(sub == 4 * NTT - 1)), s_cm)

            if not gate_only:
                s_mm = k.sems["mm"]
                s_ac = k.sems["acc_s"]
                for e_i in range(8):
                    for oc in range(DC):
                        b = ps_mm[(e_i * DC + oc) % 2][:]
                        if e_i * DC + oc >= 2:
                            k.wait("tensor", s_ac, s_ac.n - 1)
                        for ic in range(DC):
                            k.op("tensor", lambda e, b=b, e_i=e_i, ic=ic, oc=oc:
                                 e.matmul(b, ew_sb[:, (e_i * DC + ic) * D + oc * 128:
                                                   (e_i * DC + ic) * D + oc * 128 + 128],
                                          xn[:, ic * TT:(ic + 1) * TT],
                                          start=(ic == 0), stop=(ic == DC - 1)),
                                 sig=s_mm if ic == DC - 1 else None)
                        k.wait("vector", s_mm)
                        k.wait("vector", s_ac)
                        if e_i == 0:
                            k.chain("vector", [
                                lambda e, b=b, e_i=e_i, oc=oc: e.tensor_scalar(
                                    ept[:], b,
                                    eb_sb[:, e_i * 8 + oc:e_i * 8 + oc + 1], 0.0,
                                    ALU.add, ALU.max),
                                lambda e, oc=oc: e.tensor_scalar(
                                    acc[:, oc * TT:(oc + 1) * TT], ept[:],
                                    cb_sb[:, 0:1], None, ALU.mult),
                            ], s_ac)
                        else:
                            k.chain("vector", [
                                lambda e, b=b, e_i=e_i, oc=oc: e.tensor_scalar(
                                    ept[:], b,
                                    eb_sb[:, e_i * 8 + oc:e_i * 8 + oc + 1], 0.0,
                                    ALU.add, ALU.max),
                                lambda e, e_i=e_i, oc=oc: e.scalar_tensor_tensor(
                                    acc[:, oc * TT:(oc + 1) * TT], ept[:],
                                    cb_sb[:, e_i:e_i + 1],
                                    acc[:, oc * TT:(oc + 1) * TT],
                                    ALU.mult, ALU.add),
                            ], s_ac)
                s_of = k.sem("o_free")
                for oc in range(DC):
                    xa = x_t[:, oc * TT:(oc + 1) * TT]
                    aa = acc[:, oc * TT:(oc + 1) * TT]
                    k.op("vector", lambda e, xa=xa, aa=aa: e.scalar_tensor_tensor(
                        xa, aa, 0.5 / 32768.0, xa, ALU.mult, ALU.add), s_of)
                k.wait("gpsimd", s_of)
                k.dma("gpsimd",
                      x_out.rearrange("(c p) t -> p c t", p=128)[:, :, t0:t0 + TT],
                      x_t[:].rearrange("p (c t) -> p c t", c=DC), s_st)
            else:
                s_go = k.sem("go_rel")
                k.wait("gpsimd", s_ap)
                s_go.n += 1
                k.on("gpsimd", lambda e, h=s_go.h: e.sem_inc(h, 1))

        k.wait("vector", k.sems["cnt_mm"])
        k.op("vector", lambda e: e.tensor_copy(cnt_sb[:1, :8], ps_cnt[:1, :8]),
             s_cnt)
        k.wait("gpsimd", s_cnt)
        k.dma("gpsimd", cnt_out, cnt_sb[:1, :8], s_st)
        k.wait("gpsimd", s_st)
        k.play()
    return nc



def build_l1ffn():
    nc = bass.Bass("TRN2", target_bir_lowering=False,
                   detect_race_conditions=_RD)
    x_in = nc.dram_tensor("x_in", [D, TLOC], F32, kind="ExternalInput").ap()
    rwt = nc.dram_tensor("rwt", [9, D, 2048], F32, kind="ExternalInput").ap()
    rw2t = nc.dram_tensor("rw2t", [9, 2048, D], F32, kind="ExternalInput").ap()
    rb1c = nc.dram_tensor("rb1c", [128, 9 * 16], F32, kind="ExternalInput").ap()
    rb2r = nc.dram_tensor("rb2r", [1, 9 * D], F32, kind="ExternalInput").ap()
    rbr = nc.dram_tensor("rbr", [1, 8], F32, kind="ExternalInput").ap()
    rwtr = nc.dram_tensor("rwtr", [D, 8], F32, kind="ExternalInput").ap()
    x_out = nc.dram_tensor("x_out", [D, TLOC], F32, kind="ExternalOutput").ap()

    H = 2048
    HC = H // 128

    with ExitStack() as stack:
        k = KB(nc, stack)
        emit_consts(k, {"eps_ln": 1e-5})

        x_t = k.sb("x_t", [128, DC * TT], F32)
        xn = k.sb("xnf", [128, DC * TLOC], BF16)
        h1 = k.sb("h1", [128, HC * TT], BF16)
        acc = k.sb("acc", [128, 16 * D], BF16)
        rw1_sb = k.sb("rw1_sb", [128, DC * H], BF16)
        rw2_sb = k.sb("rw2_sb", [128, HC * D], BF16)
        ident = k.sb("ident", [128, 128], BF16)
        rb1_sb = k.sb("rb1_sb", [128, 9 * 16], F32)
        rb2_sb = k.sb("rb2_sb", [1, 9 * D], BF16)
        rbr_sb = k.sb("rbr_sb", [1, 8], BF16)
        rw_sb = k.sb("rw_sb", [128, DC * 8], BF16)
        wgt = k.sb("wgt", [128, 16 * 8], F32)
        m1 = k.sb("m1", [128, 1], F32)
        nm = k.sb("nm", [128, 1], F32)
        exs = k.sb("exs", [128, 8], F32)
        ssum = k.sb("ssum", [128, 1], F32)
        rs = k.sb("rs", [128, 1], F32)
        pp = k.sb("pp", [128, 8], F32)
        eqb = k.sb("eqb", [128, 8], F32)
        mkd = k.sb("mkd", [128, 8], F32)
        indb = k.sb("indb", [128, 8], F32)
        wsc = k.sb("wsc", [128, TT], BF16)

        ps_mm = [k.ps(f"mm{i}", [128, TT]) for i in range(2)]
        ps_g = k.ps("ps_g", [128, 8])
        ps_T = k.ps("ps_T", [128, 128], BF16)

        s_xt = k.sem("x_ld")
        s_b = k.sem("b_ld")
        s_we = k.sem("w_e")
        s_rt = k.sem("rt")
        s_sm1 = k.sem("sm1")
        s_sm2 = k.sem("sm2")
        s_wg = k.sem("wg")
        s_mm1 = k.sem("mm1")
        s_hf = k.sem("h_free")
        s_mm2 = k.sem("mm2")
        s_acd = k.sem("acd")
        s_tp = k.sem("tp")
        s_res = k.sem("res")
        s_st = k.sem("st")

        s_bi = k.sem("b_eng")
        s_bg = k.sem("b_gp")
        k.chain("gpsimd", [
            lambda e: e.memset(ident[:], 0.0),
            lambda e: e.affine_select(
                ident[:], ident[:], [[-1, 128]], ALU.not_equal, 1.0, base=0,
                channel_multiplier=1),
        ], s_bi)

        k.dma("sync", rb1_sb[:], rb1c, s_b)
        k.dma("gpsimd", rb2_sb[:], rb2r, s_bg)
        k.dma("gpsimd", rbr_sb[:], rbr, s_bg)
        k.dma("gpsimd", rw_sb[:].rearrange("p (c e) -> p c e", c=DC),
              rwtr.rearrange("(c p) e -> p c e", p=128), s_bg)
        k.op("vector", lambda e: e.memset(acc[:], 0.0), s_bi)

        for eng in ("tensor", "scalar", "vector"):
            k.wait(eng, s_b)
            k.wait(eng, s_bg)
            k.wait(eng, s_bi)

        s_lnap = k.sem("ln_apply_s")
        for it in range(NTT):
            t0 = it * TT
            k.wait("gpsimd", s_lnap, DC * it)
            k.dma("gpsimd", x_t[:].rearrange("p (c t) -> p c t", c=DC),
                  x_in.rearrange("(c p) t -> p c t", p=128)[:, :, t0:t0 + TT],
                  s_xt)
            for eng in ("scalar", "tensor", "vector"):
                k.wait(eng, s_xt)

            def x_ap(c):
                return x_t[:, c * TT:(c + 1) * TT]

            def xn_ap(c, t0=t0):
                return xn[:, c * TLOC + t0: c * TLOC + t0 + TT]

            s_ap = emit_ln_tile(k, x_ap, xn_ap)
            k.wait("tensor", s_ap)

            for st in range(4):
                sub = it * 4 + st
                k.wait("tensor", s_wg)
                for ic in range(DC):
                    k.op("tensor", lambda e, ic=ic, t0=t0, st=st: e.matmul(
                        ps_g[:, :8],
                        xn[:, ic * TLOC + t0 + st * 128: ic * TLOC + t0 + st * 128 + 128],
                        rw_sb[:, ic * 8:(ic + 1) * 8], start=(ic == 0), stop=False))
                k.op("tensor", lambda e: e.matmul(
                    ps_g[:, :8], k.sbuf["ones_row_bf"][:1, :128], rbr_sb[:1, :8],
                    start=False, stop=True), s_rt)
                k.wait("vector", s_rt)
                k.chain("vector", [
                    lambda e: e.reduce_max(m1[:], ps_g[:, :8], AX.X),
                    lambda e: e.tensor_scalar_mul(nm[:], m1[:], -1.0),
                ], s_sm1)
                k.wait("scalar", s_sm1)
                k.op("scalar", lambda e: e.activation(
                    exs[:], ps_g[:, :8], AF.Exp, bias=nm[:, :1]), s_sm2)
                k.wait("vector", s_sm2)
                k.chain("vector", [
                    lambda e: e.reduce_sum(ssum[:], exs[:], AX.X),
                    lambda e: e.reciprocal(rs[:], ssum[:]),
                    lambda e: e.tensor_scalar(pp[:], exs[:], rs[:, :1], None,
                                              ALU.mult),
                    lambda e: e.reduce_max(m1[:], pp[:], AX.X),
                    lambda e: e.tensor_scalar(eqb[:], pp[:], m1[:, :1], None,
                                              ALU.is_ge),
                    lambda e: e.scalar_tensor_tensor(mkd[:], eqb[:], -1e9, pp[:],
                                                     ALU.mult, ALU.add),
                    lambda e: e.reduce_max(m1[:], mkd[:], AX.X),
                    lambda e: e.tensor_scalar(indb[:], pp[:], m1[:, :1], None,
                                              ALU.is_ge),
                    lambda e, sub=sub: e.tensor_mul(wgt[:, sub * 8:(sub + 1) * 8],
                                                    pp[:], indb[:]),
                ], s_wg)

        for ei in range(9):
            k.wait("gpsimd", s_mm1, ei * NTT * HC)
            k.wait("gpsimd", s_mm2, ei * NTT * 8)
            k.dma("gpsimd", rw1_sb[:].rearrange("p (c h) -> p c h", c=DC),
                  rwt.rearrange("e (c p) h -> p e c h", p=128)[:, ei], s_we)
            k.dma("gpsimd", rw2_sb[:].rearrange("p (c h) -> p c h", c=HC),
                  rw2t.rearrange("e (c p) h -> p e c h", p=128)[:, ei], s_we)
            k.wait("tensor", s_we)
            for it in range(NTT):
                t0 = it * TT
                k.wait("tensor", s_acd)
                for oc in range(HC):
                    b = ps_mm[oc % 2][:]
                    k.wait("tensor", s_hf, s_hf.n - 1)
                    for ic in range(DC):
                        k.op("tensor", lambda e, b=b, ic=ic, oc=oc, t0=t0:
                             e.matmul(b, rw1_sb[:, ic * H + oc * 128:
                                              ic * H + oc * 128 + 128],
                                      xn[:, ic * TLOC + t0: ic * TLOC + t0 + TT],
                                      start=(ic == 0), stop=(ic == DC - 1)),
                             sig=s_mm1 if ic == DC - 1 else None)
                    k.wait("scalar", s_mm1)
                    if oc == 0:
                        k.wait("scalar", s_mm2)
                    k.op("scalar", lambda e, b=b, oc=oc, ei=ei: e.activation(
                        h1[:, oc * TT:(oc + 1) * TT], b, AF.Gelu,
                        bias=rb1_sb[:, ei * 16 + oc:ei * 16 + oc + 1]), s_hf)
                for st in range(4):
                    sub = it * 4 + st
                    for dh in range(2):
                        b = ps_mm[dh][:]
                        k.wait("tensor", s_acd, s_acd.n - 1)
                        k.wait("tensor", s_hf)
                        for fc in range(HC):
                            k.op("tensor", lambda e, b=b, fc=fc, st=st, dh=dh:
                                 e.matmul(b, h1[:, fc * TT + st * 128:
                                                fc * TT + st * 128 + 128],
                                          rw2_sb[:, fc * D + dh * TT:
                                                 fc * D + dh * TT + TT],
                                          start=(fc == 0), stop=False))
                        k.op("tensor", lambda e, b=b, ei=ei, dh=dh: e.matmul(
                            b, k.sbuf["ones_row_bf"][:1, :128],
                            rb2_sb[:1, ei * D + dh * TT: ei * D + dh * TT + TT],
                            start=False, stop=True), s_mm2)
                        k.wait("vector", s_mm2)
                        k.wait("vector", s_acd)
                        aslc = acc[:, sub * D + dh * TT: sub * D + dh * TT + TT]
                        if ei < 8:
                            fn1 = (lambda e, b=b, sub=sub, ei=ei: e.tensor_scalar(
                                wsc[:], b, wgt[:, sub * 8 + ei:sub * 8 + ei + 1],
                                None, ALU.mult))
                        else:
                            fn1 = (lambda e, b=b: e.tensor_copy(wsc[:], b))
                        k.chain("vector", [
                            fn1,
                            lambda e, aslc=aslc: e.tensor_add(aslc, aslc, wsc[:]),
                        ], s_acd)

        k.wait("gpsimd", s_lnap)
        for it in range(NTT):
            t0 = it * TT
            k.wait("gpsimd", s_acd)
            k.wait("gpsimd", s_st, 16 * it)
            k.dma("gpsimd", x_t[:].rearrange("p (c t) -> p c t", c=DC),
                  x_in.rearrange("(c p) t -> p c t", p=128)[:, :, t0:t0 + TT],
                  s_xt)
            k.wait("tensor", s_xt)
            k.wait("vector", s_xt)
            for st in range(4):
                sub = it * 4 + st
                for c in range(DC):
                    k.wait("tensor", s_acd)
                    k.wait("tensor", s_res)
                    k.op("tensor", lambda e, sub=sub, c=c: e.transpose(
                        ps_T[:], acc[:, sub * D + c * 128: sub * D + c * 128 + 128],
                        ident[:]), s_tp)
                    k.wait("vector", s_tp)
                    xa = x_t[:, c * TT + st * 128: c * TT + st * 128 + 128]
                    k.op("vector", lambda e, xa=xa: e.scalar_tensor_tensor(
                        xa, ps_T[:], 0.5, xa, ALU.mult, ALU.add), s_res)
            k.wait("gpsimd", s_res)
            k.dma("gpsimd",
                  x_out.rearrange("(c p) t -> p c t", p=128)[:, :, t0:t0 + TT],
                  x_t[:].rearrange("p (c t) -> p c t", c=DC), s_st)
        k.wait("gpsimd", s_st)
        k.play()
    return nc



def build_l3attn(phase=99):
    nc = bass.Bass("TRN2", target_bir_lowering=False,
                   detect_race_conditions=_RD)
    x_in = nc.dram_tensor("x_in", [D, TLOC], F32, kind="ExternalInput").ap()
    xk1 = nc.dram_tensor("xk1_in", [D, TLOC], F32, kind="ExternalInput").ap()
    xk2 = nc.dram_tensor("xk2_in", [D, TLOC], F32, kind="ExternalInput").ap()
    wdq_t = nc.dram_tensor("wdq_t", [D, 128], F32, kind="ExternalInput").ap()
    wdq = nc.dram_tensor("wdq", [128, D], F32, kind="ExternalInput").ap()
    wuq = nc.dram_tensor("wuq", [D, 128], F32, kind="ExternalInput").ap()
    wuq_tt = nc.dram_tensor("wuq_tt", [128, D], F32, kind="ExternalInput").ap()
    wuk = nc.dram_tensor("wuk", [D, 128], F32, kind="ExternalInput").ap()
    wuv = nc.dram_tensor("wuv", [D, 128], F32, kind="ExternalInput").ap()
    wo_tt = nc.dram_tensor("wo_tt", [D, D], F32, kind="ExternalInput").ap()
    wdkv_t = nc.dram_tensor("wdkv_t", [D, 128], F32, kind="ExternalInput").ap()
    qposr = nc.dram_tensor("qposr", [1, TLOC], F32, kind="ExternalInput").ap()
    scolc = nc.dram_tensor("scolc", [128, 32], F32, kind="ExternalInput").ap()
    x_out = nc.dram_tensor("x_out", [D, TLOC], F32, kind="ExternalOutput").ap()

    S = 4096
    SB = S // 128
    NH = 16

    with ExitStack() as stack:
        k = KB(nc, stack)
        emit_consts(k, {"eps_ln": 1e-5})

        x_t = k.sb("x_t", [128, DC * TT], F32)
        xn = k.sb("xn", [128, DC * TT], BF16)
        q_sb = k.sb("q_sb", [128, DC * TLOC], BF16)
        q1_sb = k.sb("q1_sb", [128, TT], BF16)
        ckv = k.sb("ckv", [128, S], BF16)
        ckvT = k.sb("ckvT", [128, SB * 128], BF16)
        keepq = k.sb("keepq", [128, SB * TT], BF16)
        pb = k.sb("pb", [128, SB * TT], BF16)
        y_sb = k.sb("y_sb", [128, DC * TT], BF16)
        ql_sb = k.sb("ql_sb", [128, TT], BF16)
        ctx_sb = k.sb("ctx_sb", [128, TT], BF16)
        rden = k.sb("rden", [1, TT], F32)
        rden_b = k.sb("rden_b", [128, TT], F32)
        qrow_b = k.sb("qrow_b", [128, TT], F32)
        onec_bf = k.sb("onec_bf", [128, 1], BF16)
        m32c = k.sb("m32c", [128, 1], F32)
        wsm = k.sb("wsm", [128, 5 * DC * 128], BF16)
        wdq_sb = k.sb("wdq_sb", [128, D], BF16)
        wuqtt_sb = k.sb("wuqtt_sb", [128, D], BF16)
        Bm = k.sb("Bm", [128, 128], BF16)
        keff = k.sb("keff", [128, DC * 128], BF16)
        veff = k.sb("veff", [128, D], BF16)
        scol_sb = k.sb("scol_sb", [128, 32], F32)

        ps_mm = [k.ps(f"mm{i}", [128, TT]) for i in range(2)]
        ps_den = k.ps("ps_den", [1, TT])
        ps_tp = k.ps("ps_tp", [128, 128], BF16)

        s_ld = k.sem("ld")
        s_pre = k.sem("pre")
        s_evac = k.sem("evac")
        s_ck = k.sem("ck")
        s_tp = k.sem("tp")
        s_qd = k.sem("qd")
        s_kp = k.sem("kp")
        s_sc = k.sem("sc")
        s_ex = k.sem("ex")
        s_pk = k.sem("pk")
        s_dn = k.sem("dn")
        s_rd = k.sem("rd")
        s_cx = k.sem("cx")
        s_cn = k.sem("cn")
        s_y = k.sem("y")
        s_ya = k.sem("ya")
        s_res = k.sem("res")
        s_st = k.sem("st")
        s_xt = k.sem("x_ld")

        s_cst = k.sem("l3_consts")
        k.op("gpsimd", lambda e: e.memset(onec_bf[:], 1.0))
        k.op("gpsimd", lambda e: e.memset(m32c[:], -32.0), s_cst)
        k.wait("tensor", s_cst)
        k.wait("scalar", s_cst)

        for name, dst, srcap, nch in [
            ("wdqt", "A", wdq_t, 1), ("wuq", "B", wuq, 1), ("wuk", "C", wuk, 1),
            ("wuv", "D2", wuv, 1), ("wdkvt", "E", wdkv_t, 1),
        ]:
            pass
        k.dma("gpsimd", wsm[:].rearrange("p (n c x) -> p n c x", n=5, c=DC),
              None, s_ld) if False else None
        for n, srcap in enumerate([wdq_t, wuq, wuk, wuv, wdkv_t]):
            k.dma("gpsimd",
                  wsm[:].rearrange("p (n c x) -> p n c x", n=5, c=DC)[:, n],
                  srcap.rearrange("(c p) x -> p c x", p=128), s_ld)

        def wslot(n, ic):
            return wsm[:, (n * DC + ic) * 128:(n * DC + ic) * 128 + 128]

        k.dma("gpsimd", wdq_sb[:], wdq, s_ld)
        k.dma("gpsimd", wuqtt_sb[:], wuq_tt, s_ld)
        k.dma("gpsimd", pb[:, :DC * D].rearrange("p (c x) -> p c x", c=DC),
              wo_tt.rearrange("(c p) x -> p c x", p=128), s_ld)
        s_ld2 = k.sem("ld2")
        k.dma("sync", scol_sb[:], scolc, s_ld2)
        qrow_sb = k.sb("qrow_sb", [1, TLOC], F32)
        k.dma("sync", qrow_sb[:], qposr, s_ld2)

        k.wait("tensor", s_ld)
        k.wait("vector", s_ld)
        k.wait("tensor", s_ld2)
        k.wait("vector", s_ld2)

        k.op("tensor", lambda e: [e.matmul(ps_mm[0][:, :128], wslot(1, ic),
                                           wslot(2, ic), start=(ic == 0),
                                           stop=(ic == DC - 1))
                                  for ic in range(DC)][-1], s_pre)
        k.wait("vector", s_pre)
        k.op("vector", lambda e: e.tensor_copy(Bm[:], ps_mm[0][:, :128]), s_evac)
        k.wait("tensor", s_evac)
        for c in (range(DC) if phase != 10 else []):
            k.wait("tensor", s_evac)
            k.op("tensor", lambda e, c=c: e.matmul(
                ps_mm[1][:, :128], wdq_sb[:, c * 128:(c + 1) * 128], Bm[:],
                start=True, stop=True), s_pre)
            k.wait("vector", s_pre)
            k.op("vector", lambda e, c=c: e.tensor_copy(
                keff[:, c * 128:(c + 1) * 128], ps_mm[1][:, :128]), s_evac)
        for j in (range(2) if phase not in (10, 11) else []):
            k.wait("tensor", s_evac)
            k.op("tensor", lambda e, j=j: [e.matmul(
                ps_mm[0][:], wslot(3, ic), pb[:, ic * D + j * TT: ic * D + j * TT + TT],
                start=(ic == 0), stop=(ic == DC - 1)) for ic in range(DC)][-1],
                s_pre)
            k.wait("vector", s_pre)
            k.op("vector", lambda e, j=j: e.tensor_copy(
                veff[:, j * TT:(j + 1) * TT], ps_mm[0][:]), s_evac)

        for half, src in (enumerate([xk1, xk2]) if phase >= 2 else []):
            for it in range(NTT):
                t0 = it * TT
                k.wait("gpsimd", k.sem("ln_apply_s"), DC * (half * NTT + it))
                k.dma("gpsimd", x_t[:].rearrange("p (c t) -> p c t", c=DC),
                      src.rearrange("(c p) t -> p c t", p=128)[:, :, t0:t0 + TT],
                      s_xt)
                for eng in ("scalar", "tensor", "vector"):
                    k.wait(eng, s_xt)
                s_ap = emit_ln_tile(k, lambda c: x_t[:, c * TT:(c + 1) * TT],
                                    lambda c: xn[:, c * TT:(c + 1) * TT])
                k.wait("tensor", s_ap)
                k.wait("tensor", s_ck)
                k.wait("tensor", s_evac)
                k.op("tensor", lambda e: [e.matmul(
                    ps_mm[0][:], wslot(4, ic), xn[:, ic * TT:(ic + 1) * TT],
                    start=(ic == 0), stop=(ic == DC - 1)) for ic in range(DC)][-1],
                    s_sc)
                k.wait("vector", s_sc)
                off = half * TLOC + t0
                k.op("vector", lambda e, off=off: e.tensor_copy(
                    ckv[:, off:off + TT], ps_mm[0][:]), s_ck)
        DO_T = phase >= 2
        ident = k.sb("identb", [128, 128], BF16)
        k.chain("gpsimd", [
            lambda e: e.memset(ident[:], 0.0),
            lambda e: e.affine_select(
                ident[:], ident[:], [[-1, 128]], ALU.not_equal, 1.0, base=0,
                channel_multiplier=1),
        ], s_pre)
        k.wait("tensor", s_pre)
        k.wait("tensor", s_ck)
        for sb in (range(SB) if DO_T else []):
            k.wait("tensor", s_tp)
            k.op("tensor", lambda e, sb=sb: e.transpose(
                ps_tp[:], ckv[:, sb * 128:(sb + 1) * 128], ident[:]), s_tp)
            k.wait("vector", s_tp)
            k.op("vector", lambda e, sb=sb: e.tensor_copy(
                ckvT[:, sb * 128:(sb + 1) * 128], ps_tp[:]), s_tp)

        for it in (range(NTT) if phase >= 3 else []):
            t0 = it * TT
            k.wait("gpsimd", k.sem("ln_apply_s"), DC * (2 * NTT + it))
            k.dma("gpsimd", x_t[:].rearrange("p (c t) -> p c t", c=DC),
                  x_in.rearrange("(c p) t -> p c t", p=128)[:, :, t0:t0 + TT],
                  s_xt)
            for eng in ("scalar", "tensor", "vector"):
                k.wait(eng, s_xt)
            s_ap = emit_ln_tile(k, lambda c: x_t[:, c * TT:(c + 1) * TT],
                                lambda c: xn[:, c * TT:(c + 1) * TT])
            k.wait("tensor", s_ap)
            k.wait("tensor", s_qd)
            k.op("tensor", lambda e: [e.matmul(
                ps_mm[0][:], wslot(0, ic), xn[:, ic * TT:(ic + 1) * TT],
                start=(ic == 0), stop=(ic == DC - 1)) for ic in range(DC)][-1],
                s_sc)
            k.wait("vector", s_sc)
            k.op("vector", lambda e: e.tensor_copy(q1_sb[:], ps_mm[0][:]), s_qd)
            k.wait("tensor", s_qd)
            for oc in range(DC):
                k.wait("tensor", s_qd)
                k.op("tensor", lambda e, oc=oc: e.matmul(
                    ps_mm[1][:], wuqtt_sb[:, oc * 128:(oc + 1) * 128], q1_sb[:],
                    start=True, stop=True), s_sc)
                k.wait("vector", s_sc)
                k.op("vector", lambda e, oc=oc, t0=t0: e.tensor_copy(
                    q_sb[:, oc * TLOC + t0: oc * TLOC + t0 + TT], ps_mm[1][:]),
                    s_qd)

        NHL = 0 if phase < 4 else (1 if phase == 4 else NH)
        NQTL = NTT if phase >= 5 else 1
        for qt in range(NTT if phase >= 4 else 0):
            t0 = qt * TT
            k.wait("tensor", s_kp)
            k.wait("tensor", s_ya)
            k.op("tensor", lambda e, t0=t0: e.matmul(
                ps_mm[0][:], k.sbuf["ones_row"][:1, :],
                qrow_sb[:1, t0:t0 + TT], start=True, stop=True), s_pre)
            k.wait("vector", s_pre)
            k.op("vector", lambda e: e.tensor_copy(qrow_b[:], ps_mm[0][:]),
                 s_evac)
            k.wait("vector", s_evac)
            for sb in range(SB):
                k.op("vector", lambda e, sb=sb: e.tensor_scalar(
                    keepq[:, sb * TT:(sb + 1) * TT],
                    qrow_b[:], scol_sb[:, sb:sb + 1], None,
                    ALU.is_ge), s_kp)
            for h in range(NHL):
                hc, hr = h // 2, (h % 2) * 64
                k.wait("tensor", s_ya)
                k.wait("tensor", s_evac)
                k.op("tensor", lambda e, hc=hc, hr=hr, t0=t0: e.matmul(
                    ps_mm[0][:], keff[hr:hr + 64, hc * 128:hc * 128 + 128],
                    q_sb[hr:hr + 64, hc * TLOC + t0: hc * TLOC + t0 + TT],
                    start=True, stop=True), s_sc)
                k.wait("vector", s_sc)
                k.op("vector", lambda e: e.tensor_scalar(
                    ql_sb[:], ps_mm[0][:], 0.125, None, ALU.mult), s_qd)
                k.wait("tensor", s_qd)
                k.wait("scalar", s_cx)
                for sb in range(SB):
                    bk = ps_mm[sb % 2][:]
                    k.wait("tensor", s_ex, s_ex.n - 1)
                    k.op("tensor", lambda e, bk=bk, sb=sb: e.matmul(
                        bk, ckv[:, sb * 128:(sb + 1) * 128], ql_sb[:],
                        start=True, stop=True), s_sc)
                    k.wait("scalar", s_sc)
                    k.op("scalar", lambda e, bk=bk, sb=sb: e.activation(
                        pb[:, sb * TT:(sb + 1) * TT], bk, AF.Exp,
                        bias=m32c[:, :1]), s_ex)
                k.wait("vector", s_ex)
                k.wait("vector", s_kp)
                for sb in range(SB):
                    k.op("vector", lambda e, sb=sb: e.tensor_mul(
                        pb[:, sb * TT:(sb + 1) * TT], pb[:, sb * TT:(sb + 1) * TT],
                        keepq[:, sb * TT:(sb + 1) * TT]), s_pk)
                k.wait("tensor", s_pk)
                k.wait("tensor", s_rd)
                k.op("tensor", lambda e: [e.matmul(
                    ps_den[:1, :], onec_bf[:], pb[:, sb * TT:(sb + 1) * TT],
                    start=(sb == 0), stop=(sb == SB - 1)) for sb in range(SB)][-1],
                    s_dn)
                k.wait("vector", s_dn)
                k.chain("vector", [
                    lambda e: e.reciprocal(rden[:1, :], ps_den[:1, :]),
                ], s_rd)
                k.wait("tensor", s_rd)
                k.op("tensor", lambda e: e.matmul(
                    ps_mm[0][:], k.sbuf["ones_row"][:1, :], rden[:1, :],
                    start=True, stop=True), s_dn)
                k.wait("vector", s_dn)
                k.op("vector", lambda e: e.tensor_copy(rden_b[:], ps_mm[0][:]),
                     s_rd)
                k.wait("tensor", s_rd)
                k.op("tensor", lambda e: [e.matmul(
                    ps_mm[1][:], ckvT[:, sb * 128:(sb + 1) * 128],
                    pb[:, sb * TT:(sb + 1) * TT],
                    start=(sb == 0), stop=(sb == SB - 1)) for sb in range(SB)][-1],
                    s_cx)
                k.wait("vector", s_cx)
                k.wait("vector", s_rd)
                k.op("vector", lambda e: e.tensor_mul(
                    ctx_sb[:], ps_mm[1][:], rden_b[:]), s_cn)
                k.wait("tensor", s_cn)
                k.op("tensor", lambda e, h=h: e.matmul(
                    ps_mm[0][:64, :], veff[:, h * 64:(h + 1) * 64], ctx_sb[:],
                    start=True, stop=True), s_y)
                k.wait("vector", s_y)
                k.op("vector", lambda e, hc=hc, hr=hr: e.tensor_copy(
                    y_sb[hr:hr + 64, hc * TT:hc * TT + TT], ps_mm[0][:64, :]),
                    s_ya)
            if NHL == 0:
                k.op("vector", lambda e: e.memset(y_sb[:], 0.0), s_ya)
            k.wait("gpsimd", s_ya)
            k.wait("gpsimd", k.sem("ln_apply_s"), DC * 3 * NTT)
            k.wait("gpsimd", s_st, 16 * qt)
            k.dma("gpsimd", x_t[:].rearrange("p (c t) -> p c t", c=DC),
                  x_in.rearrange("(c p) t -> p c t", p=128)[:, :, t0:t0 + TT],
                  s_xt)
            k.wait("vector", s_xt)
            for c in range(DC):
                xa = x_t[:, c * TT:(c + 1) * TT]
                k.op("vector", lambda e, xa=xa, c=c: e.tensor_add(
                    xa, xa, y_sb[:, c * TT:(c + 1) * TT]), s_res)
            k.wait("gpsimd", s_res)
            k.dma("gpsimd",
                  x_out.rearrange("(c p) t -> p c t", p=128)[:, :, t0:t0 + TT],
                  x_t[:].rearrange("p (c t) -> p c t", c=DC), s_st)
        k.wait("gpsimd", s_st)
        k.play()
    return nc



L2_EPS_K = 1e-06
L2_RATIO = 0.125
L2_DN2H = 0.0625


def _l2_common(nc):
    x_in = nc.dram_tensor("x_in", [D, TLOC], F32, kind="ExternalInput").ap()
    return x_in


def _emit_k_dd(k, wk_sb, projdn_sb, xn, kf_cb, t0):
    pass


def build_l2attn_p1():
    nc = bass.Bass("TRN2", target_bir_lowering=False,
                   detect_race_conditions=_RD)
    x_in = _l2_common(nc)
    wk = nc.dram_tensor("wk", [D, D], F32, kind="ExternalInput").ap()
    projdn = nc.dram_tensor("projdn", [128, 64], F32, kind="ExternalInput").ap()
    kmax_out = nc.dram_tensor("kmax_out", [1, 1], F32, kind="ExternalOutput").ap()

    with ExitStack() as stack:
        k = KB(nc, stack)
        emit_consts(k, {"eps_ln": 1e-5})
        x_t = k.sb("x_t", [128, DC * TT], F32)
        xn = k.sb("xn", [128, DC * TT], BF16)
        wk_sb = k.sb("wk_sb", [128, DC * D], BF16)
        pj_sb = k.sb("pj_sb", [128, 64], BF16)
        kfm = k.sb("kfm", [128, TT], BF16)
        mcol = k.sb("mcol", [64, 1], F32)
        mrow = k.sb("mrow", [1, 64], F32)
        mloc = k.sb("mloc", [64, 1], F32)
        mloc_bf = k.sb("mloc_bf", [64, 1], BF16)
        ident = k.sb("identb", [128, 128], BF16)
        ps_mm = [k.ps(f"mm{i}", [128, TT]) for i in range(2)]
        ps_dd = k.ps("ps_dd", [64, TT])
        ps_tp = k.ps("ps_tp", [128, 128], BF16)

        s_ld = k.sem("ld")
        s_xt = k.sem("x_ld")
        s_mm = k.sem("mm")
        s_ev = k.sem("evac")
        s_dd = k.sem("dd")
        s_mx = k.sem("mx")
        s_st = k.sem("st")
        vself = k.sem("vector_self")

        k.chain("gpsimd", [
            lambda e: e.memset(ident[:], 0.0),
            lambda e: e.affine_select(ident[:], ident[:], [[-1, 128]],
                                      ALU.not_equal, 1.0, base=0,
                                      channel_multiplier=1),
        ], k.sem("idn"))
        k.dma("gpsimd", wk_sb[:].rearrange("p (c x) -> p c x", c=DC),
              wk.rearrange("(c p) x -> p c x", p=128), s_ld)
        k.dma("gpsimd", pj_sb[:], projdn, s_ld)
        k.wait("vector", s_ld)
        k.op("vector", lambda e: e.tensor_scalar_mul(
            pj_sb[:], pj_sb[:], 2.0 ** -1.5), k.sem("pjs"))
        k.wait("tensor", k.sems["pjs"])
        k.wait("tensor", s_ld)
        k.wait("tensor", k.sems["idn"])
        k.op("vector", lambda e: e.memset(mloc[:], -1e30), k.sem("ml0"))
        k.wait("vector", k.sems["ml0"])

        for it in range(NTT):
            t0 = it * TT
            k.wait("gpsimd", k.sem("ln_apply_s"), DC * it)
            k.dma("gpsimd", x_t[:].rearrange("p (c t) -> p c t", c=DC),
                  x_in.rearrange("(c p) t -> p c t", p=128)[:, :, t0:t0 + TT],
                  s_xt)
            for eng in ("scalar", "tensor", "vector"):
                k.wait(eng, s_xt)
            s_ap = emit_ln_tile(k, lambda c: x_t[:, c * TT:(c + 1) * TT],
                                lambda c: xn[:, c * TT:(c + 1) * TT])
            k.wait("tensor", s_ap)
            for hc in range(DC):
                k.wait("tensor", s_dd)
                k.op("tensor", lambda e, hc=hc: [e.matmul(
                    ps_mm[0][:], wk_sb[:, ic * D + hc * 128: ic * D + hc * 128 + 128],
                    xn[:, ic * TT:(ic + 1) * TT], start=(ic == 0),
                    stop=(ic == DC - 1)) for ic in range(DC)][-1], s_mm)
                k.wait("vector", s_mm)
                k.op("vector", lambda e: e.tensor_copy(kfm[:], ps_mm[0][:]), s_ev)
                k.wait("tensor", s_ev)
                for hh in range(2):
                    k.wait("tensor", s_dd)
                    k.op("tensor", lambda e, hh=hh: e.matmul(
                        ps_dd[:, :TT], pj_sb[hh * 64:(hh + 1) * 64, :],
                        kfm[hh * 64:(hh + 1) * 64, :],
                        start=True, stop=True), s_mm)
                    k.wait("vector", s_mm)
                    k.chain("vector", [
                        lambda e: e.reduce_max(mcol[:], ps_dd[:, :TT], AX.X),
                        lambda e: e.tensor_max(mloc[:], mloc[:], mcol[:]),
                    ], s_dd)
        k.wait("vector", s_dd)
        k.op("vector", lambda e: e.tensor_copy(mloc_bf[:], mloc[:]), s_mx)
        k.wait("tensor", s_mx)
        k.op("tensor", lambda e: e.transpose(ps_tp[:1, :64], mloc_bf[:, :1],
                                             ident[:64, :64]), s_mx)
        k.wait("vector", s_mx)
        k.chain("vector", [
            lambda e: e.tensor_copy(mrow[:1, :64], ps_tp[:1, :64]),
            lambda e: e.reduce_max(mcol[:1, :1], mrow[:1, :64], AX.X),
        ], s_mx)
        k.wait("gpsimd", s_mx)
        k.dma("gpsimd", kmax_out, mcol[:1, :1], s_st)
        k.wait("gpsimd", s_st)
        k.play()
    return nc


def build_l2attn_p2():
    nc = bass.Bass("TRN2", target_bir_lowering=False,
                   detect_race_conditions=_RD)
    x_in = _l2_common(nc)
    wk = nc.dram_tensor("wk", [D, D], F32, kind="ExternalInput").ap()
    wv = nc.dram_tensor("wv", [D, D], F32, kind="ExternalInput").ap()
    projdn = nc.dram_tensor("projdn", [128, 64], F32, kind="ExternalInput").ap()
    kmax_all = nc.dram_tensor("kmax_all", [1, 8], F32, kind="ExternalInput").ap()
    kvp_out = nc.dram_tensor("kvp_out", [64, 16 * 65], F32,
                             kind="ExternalOutput").ap()
    sv_out = nc.dram_tensor("sv_out", [1, D], F32, kind="ExternalOutput").ap()

    with ExitStack() as stack:
        k = KB(nc, stack)
        emit_consts(k, {"eps_ln": 1e-5})
        x_t = k.sb("x_t", [128, DC * TT], F32)
        xn = k.sb("xn", [128, DC * TT], BF16)
        wk_sb = k.sb("wk_sb", [128, DC * D], BF16)
        wv_sb = k.sb("wv_sb", [128, DC * D], BF16)
        pj_sb = k.sb("pj_sb", [128, 64], BF16)
        ident = k.sb("identb", [128, 128], BF16)
        kfm = k.sb("kfm", [128, TT], BF16)
        k2f = k.sb("k2f", [128, TT], F32)
        ek = k.sb("ek", [64, TT], BF16)
        wrow = k.sb("wrow", [1, TT], BF16)
        wcolb = k.sb("wcolb", [128, 4], F32)
        v_tm = k.sb("v_tm", [128, 4 * 128], BF16)
        kfT = k.sb("kfT", [128, 64], BF16)
        mxc = k.sb("mxc", [1, 2], F32)
        mx8 = k.sb("mx8", [1, 8], F32)
        kvacc = k.sb("kvacc", [64, 16 * 65], F32)
        sxc = k.sb("sxc", [128, DC], F32)
        sxb = k.sb("sxb", [128, DC], BF16)
        sv_sb = k.sb("sv_sb", [1, D], F32)
        ps_mm = [k.ps(f"mm{i}", [128, TT]) for i in range(2)]
        ps_dd = k.ps("ps_dd", [64, TT])
        ps_tp = k.ps("ps_tp", [128, 128], BF16)
        ps_kv = ps_dd
        ps_r = k.ps("ln_ps_s1", [1, TT])

        s_ld = k.sem("ld")
        s_ld2 = k.sem("ld2")
        s_xt = k.sem("x_ld")
        s_mm = k.sem("mm")
        s_ev = k.sem("evac")
        s_w = k.sem("wr")
        s_tp = k.sem("tp")
        s_kv = k.sem("kv")
        s_sx = k.sem("sx")
        s_st = k.sem("st")

        k.chain("gpsimd", [
            lambda e: e.memset(ident[:], 0.0),
            lambda e: e.affine_select(ident[:], ident[:], [[-1, 128]],
                                      ALU.not_equal, 1.0, base=0,
                                      channel_multiplier=1),
        ], k.sem("idn"))
        k.dma("gpsimd", wk_sb[:].rearrange("p (c x) -> p c x", c=DC),
              wk.rearrange("(c p) x -> p c x", p=128), s_ld)
        k.dma("gpsimd", wv_sb[:].rearrange("p (c x) -> p c x", c=DC),
              wv.rearrange("(c p) x -> p c x", p=128), s_ld)
        k.dma("gpsimd", pj_sb[:], projdn, s_ld)
        k.dma("sync", mx8[:], kmax_all, s_ld2)
        k.wait("vector", s_ld)
        k.op("vector", lambda e: e.tensor_scalar_mul(
            pj_sb[:], pj_sb[:], 2.0 ** -1.5), k.sem("pjs"))
        k.wait("tensor", k.sems["pjs"])
        k.wait("vector", s_ld2)
        k.chain("vector", [
            lambda e: e.reduce_max(mxc[:1, :1], mx8[:1, :8], AX.X),
            lambda e: e.tensor_scalar_mul(mxc[:1, 1:2], mxc[:1, :1], -1.0),
            lambda e: e.memset(kvacc[:], 0.0),
            lambda e: e.memset(sxc[:], 0.0),
        ], k.sem("init"))
        k.wait("scalar", k.sems["init"])
        k.wait("vector", k.sems["init"])
        k.wait("tensor", s_ld)
        k.wait("tensor", k.sems["idn"])

        for it in range(NTT):
            t0 = it * TT
            k.wait("gpsimd", k.sem("ln_apply_s"), DC * it)
            k.dma("gpsimd", x_t[:].rearrange("p (c t) -> p c t", c=DC),
                  x_in.rearrange("(c p) t -> p c t", p=128)[:, :, t0:t0 + TT],
                  s_xt)
            for eng in ("scalar", "tensor", "vector"):
                k.wait(eng, s_xt)
            k.wait("tensor", s_w)
            s_ap = emit_ln_tile(k, lambda c: x_t[:, c * TT:(c + 1) * TT],
                                lambda c: xn[:, c * TT:(c + 1) * TT])
            k.wait("tensor", s_ap)
            sxt = k.sb("sxt", [128, 1], F32)
            for c in range(DC):
                k.wait("vector", s_sx)
                k.chain("vector", [
                    lambda e, c=c: e.tensor_reduce(sxt[:, :1],
                                                   xn[:, c * TT:(c + 1) * TT],
                                                   AX.X, ALU.add),
                    lambda e, c=c: e.tensor_add(sxc[:, c:c + 1], sxc[:, c:c + 1],
                                                sxt[:, :1]),
                ], s_sx)
            for hc in range(DC):
                k.wait("tensor", s_ev)
                k.op("tensor", lambda e, hc=hc: [e.matmul(
                    ps_mm[0][:], wk_sb[:, ic * D + hc * 128: ic * D + hc * 128 + 128],
                    xn[:, ic * TT:(ic + 1) * TT], start=(ic == 0),
                    stop=(ic == DC - 1)) for ic in range(DC)][-1], s_mm)
                k.wait("vector", s_mm)
                k.wait("vector", s_tp)
                k.op("vector", lambda e: e.tensor_copy(kfm[:], ps_mm[0][:]), s_ev)
                k.wait("scalar", s_ev)
                k.op("scalar", lambda e: e.activation(k2f[:], kfm[:], AF.Square),
                     s_ev)
                for st in range(4):
                    k.wait("tensor", s_tp)
                    k.op("tensor", lambda e, st=st, hc=hc: [e.matmul(
                        ps_mm[1][:, :128],
                        xn[:, ic * TT + st * 128: ic * TT + st * 128 + 128],
                        wv_sb[:, ic * D + hc * 128: ic * D + hc * 128 + 128],
                        start=(ic == 0), stop=(ic == DC - 1))
                        for ic in range(DC)][-1], s_mm)
                    k.wait("vector", s_mm)
                    k.op("vector", lambda e, st=st: e.tensor_copy(
                        v_tm[:, st * 128:(st + 1) * 128], ps_mm[1][:, :128]), s_tp)
                for hh in range(2):
                    h = hc * 2 + hh
                    k.wait("tensor", s_ev)
                    k.wait("tensor", s_kv)
                    k.op("tensor", lambda e, hh=hh: e.matmul(
                        ps_dd[:, :TT], pj_sb[hh * 64:(hh + 1) * 64, :],
                        kfm[hh * 64:(hh + 1) * 64, :],
                        start=True, stop=True), s_mm)
                    k.wait("tensor", k.sem("ln_row_s"))
                    k.wait("tensor", s_w)
                    k.op("tensor", lambda e, hh=hh: e.matmul(
                        ps_r[:1, :], k.sbuf["ones_col"][hh * 64:hh * 64 + 64, :],
                        k2f[hh * 64:(hh + 1) * 64, :], start=True, stop=True),
                        s_mm)
                    k.wait("scalar", s_mm)
                    k.wait("scalar", s_tp)
                    k.op("scalar", lambda e: e.activation(ek[:], ps_dd[:, :TT],
                                                          AF.Exp), s_ev)
                    k.op("scalar", lambda e: e.activation(
                        wrow[:1, :], ps_r[:1, :], AF.Exp, scale=-L2_DN2H,
                        bias=mxc[:1, 1:2]), s_w)
                    for st in range(4):
                        k.wait("tensor", s_w)
                        k.wait("tensor", s_tp)
                        k.op("tensor", lambda e, st=st: e.transpose(
                            ps_tp[:128, :1], wrow[:1, st * 128:(st + 1) * 128],
                            ident[:1, :1]), s_tp)
                        k.wait("vector", s_tp)
                        k.op("vector", lambda e, st=st: e.tensor_copy(
                            wcolb[:, st:st + 1], ps_tp[:128, :1]), s_tp)
                    for st in range(4):
                        k.wait("tensor", s_tp)
                        k.wait("tensor", s_kv)
                        k.op("tensor", lambda e, st=st: e.transpose(
                            ps_tp[:, :64], ek[:, st * 128:(st + 1) * 128],
                            ident[:64, :64]), s_tp)
                        k.wait("vector", s_tp)
                        k.op("vector", lambda e, st=st: e.tensor_scalar(
                            kfT[:], ps_tp[:, :64], wcolb[:, st:st + 1], None,
                            ALU.mult), s_tp)
                        k.wait("tensor", s_tp)
                        k.op("tensor", lambda e, hh=hh, st=st: e.matmul(
                            ps_kv[:, :64], kfT[:, :64],
                            v_tm[:, st * 128 + hh * 64: st * 128 + hh * 64 + 64],
                            start=True, stop=False))
                        k.op("tensor", lambda e: e.matmul(
                            ps_kv[:, 64:65], kfT[:, :64],
                            k.sbuf["ones_col_bf"][:, :1],
                            start=False, stop=True), s_mm)
                        k.wait("vector", s_mm)
                        k.op("vector", lambda e, h=h: e.tensor_add(
                            kvacc[:, h * 65:(h + 1) * 65],
                            kvacc[:, h * 65:(h + 1) * 65], ps_kv[:, :65]), s_kv)
        k.wait("vector", s_sx)
        k.op("vector", lambda e: e.tensor_copy(sxb[:], sxc[:]), s_sx)
        k.wait("tensor", s_sx)
        k.wait("tensor", s_kv)
        k.wait("tensor", k.sem("ln_row_s"))
        k.wait("tensor", s_w)
        for j in range(2):
            k.wait("tensor", s_ev)
            k.op("tensor", lambda e, j=j: [e.matmul(
                ps_r[:1, :], sxb[:, ic:ic + 1],
                wv_sb[:, ic * D + j * TT: ic * D + j * TT + TT],
                start=(ic == 0), stop=(ic == DC - 1)) for ic in range(DC)][-1],
                s_mm)
            k.wait("vector", s_mm)
            k.op("vector", lambda e, j=j: e.tensor_copy(
                sv_sb[:1, j * TT:(j + 1) * TT], ps_r[:1, :]), s_ev)
        k.wait("gpsimd", s_kv)
        k.wait("gpsimd", s_ev)
        k.dma("gpsimd", kvp_out, kvacc[:], s_st)
        k.dma("gpsimd", sv_out, sv_sb[:], s_st)
        k.wait("gpsimd", s_st)
        k.play()
    return nc


L2P3A_PHASE = 99


def build_l2attn_p3a():
    nc = bass.Bass("TRN2", target_bir_lowering=False,
                   detect_race_conditions=_RD)
    x_in = _l2_common(nc)
    wq = nc.dram_tensor("wq", [D, D], F32, kind="ExternalInput").ap()
    projdn = nc.dram_tensor("projdn", [128, 64], F32, kind="ExternalInput").ap()
    wo_r = nc.dram_tensor("wo_r", [D, D], F32, kind="ExternalInput").ap()
    kvp2 = nc.dram_tensor("kvp2", [64, 2 * 16 * 65], F32, kind="ExternalInput").ap()
    sv2 = nc.dram_tensor("sv2", [1, 2 * D], F32, kind="ExternalInput").ap()
    ln1gc = nc.dram_tensor("ln1gc", [128, 8], F32, kind="ExternalInput").ap()
    ln1bc = nc.dram_tensor("ln1bc", [128, 8], F32, kind="ExternalInput").ap()
    out1 = nc.dram_tensor("out1", [D, TLOC], F32, kind="ExternalOutput").ap()

    with ExitStack() as stack:
        k = KB(nc, stack)
        emit_consts(k, {"eps_ln": 1e-5, "eps_ln1": 1e-6})
        x_t = k.sb("x_t", [128, DC * TT], F32)
        xn = k.sb("xn", [128, DC * TT], BF16)
        wq_sb = k.sb("wq_sb", [128, DC * D], BF16)
        wo_sb = k.sb("wo_sb", [128, DC * D], BF16)
        pj_sb = k.sb("pj_sb", [128, 64], BF16)
        ident = k.sb("identb", [128, 128], BF16)
        qfm = k.sb("qfm", [128, TT], BF16)
        q2f = k.sb("q2f", [128, TT], F32)
        eq = k.sb("eq", [64, TT], BF16)
        kv_sb = k.sb("kv_sb", [64, 16 * 65], F32)
        kvb = k.sb("kvb", [64, 16 * 65], BF16)
        n2b = k.sb("n2b", [128, 16 * 64], BF16)
        d2c = k.sb("d2c", [128, 1], F32)
        c625 = k.sb("c625", [128, 1], F32)
        fcol = k.sb("fcol", [128, 1], F32)
        mxq = k.sb("mxq", [128, 1], F32)
        dgq = k.sb("dgq", [128, 1], F32)
        rden = k.sb("rden", [128, 1], F32)
        frc = k.sb("frc", [128, 1], F32)
        erc = k.sb("erc", [128, 1], F32)
        at_sb = k.sb("at_sb", [128, 64], BF16)
        atT = k.sb("atT", [128, 8 * 4 * 128], BF16)
        o1p = k.sb("o1p", [128, DC * TT], F32)
        n2r = k.sb("n2r", [1, 16 * 64], F32)
        g1 = k.sb("g1", [128, DC], F32)
        b1 = k.sb("b1", [128, DC], F32)
        ps_mm = [k.ps(f"mm{i}", [128, TT]) for i in range(2)]
        ps_dd = k.ps("ps_dd", [64, TT])
        ps_tp = k.ps("ps_tp", [128, 128], BF16)

        s_ld = k.sem("ld")
        s_ld2 = k.sem("ld2")
        s_xt = k.sem("x_ld")
        s_mm = k.sem("mm")
        s_ev = k.sem("evac")
        s_tp = k.sem("tp")
        s_at = k.sem("at")
        s_wo = k.sem("wo")
        s_st = k.sem("st")
        vself = k.sem("vector_self")

        k.chain("gpsimd", [
            lambda e: e.memset(ident[:], 0.0),
            lambda e: e.affine_select(ident[:], ident[:], [[-1, 128]],
                                      ALU.not_equal, 1.0, base=0,
                                      channel_multiplier=1),
            lambda e: e.memset(c625[:], L2_DN2H),
        ], k.sem("idn"))
        k.dma("gpsimd", wq_sb[:].rearrange("p (c x) -> p c x", c=DC),
              wq.rearrange("(c p) x -> p c x", p=128), s_ld)
        k.dma("gpsimd", wo_sb[:].rearrange("p (c x) -> p c x", c=DC),
              wo_r.rearrange("(c p) x -> p c x", p=128), s_ld)
        k.dma("gpsimd", pj_sb[:], projdn, s_ld)
        k.wait("vector", s_ld)
        k.op("vector", lambda e: e.tensor_scalar_mul(
            pj_sb[:], pj_sb[:], 2.0 ** -1.5), k.sem("pjs"))
        k.wait("tensor", k.sems["pjs"])
        kvp_sb = k.sb("kvp_sb", [64, 2 * 16 * 65], F32)
        sv_sb = k.sb("sv_sb", [1, 2 * D], F32)
        k.dma("sync", kvp_sb[:], kvp2, s_ld2)
        k.dma("sync", sv_sb[:], sv2, s_ld2)
        k.dma("sync", g1[:], ln1gc, s_ld2)
        k.dma("sync", b1[:], ln1bc, s_ld2)
        k.wait("vector", s_ld2)
        k.wait("tensor", s_ld)
        k.wait("tensor", s_ld2)
        k.wait("tensor", k.sems["idn"])

        s_ini = k.sem("ini")
        k.op("vector", lambda e: e.tensor_add(kv_sb[:], kvp_sb[:, :16 * 65],
                                              kvp_sb[:, 16 * 65:]), s_ini)
        k.wait("tensor", s_ini)
        for h in range(16):
            k.wait("tensor", s_ini)
            k.op("tensor", lambda e, h=h: e.matmul(
                ps_dd[:, :64], k.sbuf["ones_row"][:1, :64],
                sv_sb[:1, h * 64:(h + 1) * 64], start=True, stop=False))
            k.op("tensor", lambda e, h=h: e.matmul(
                ps_dd[:, :64], k.sbuf["ones_row"][:1, :64],
                sv_sb[:1, D + h * 64: D + h * 64 + 64], start=False, stop=True),
                s_mm)
            k.wait("vector", s_mm)
            k.chain("vector", [
                lambda e, h=h: e.tensor_scalar_mul(
                    kv_sb[:, h * 65:h * 65 + 64], kv_sb[:, h * 65:h * 65 + 64],
                    L2_RATIO),
                lambda e, h=h: e.scalar_tensor_tensor(
                    kv_sb[:, h * 65:h * 65 + 64], ps_dd[:, :64],
                    L2_RATIO * L2_EPS_K, kv_sb[:, h * 65:h * 65 + 64],
                    ALU.mult, ALU.add),
                lambda e, h=h: e.tensor_scalar(
                    kv_sb[:, h * 65 + 64:h * 65 + 65],
                    kv_sb[:, h * 65 + 64:h * 65 + 65],
                    L2_RATIO, L2_RATIO * L2_EPS_K * 4096.0, ALU.mult, ALU.add),
            ], s_ini)
        k.wait("vector", s_ini)
        k.op("vector", lambda e: e.tensor_copy(kvb[:], kv_sb[:]), s_ini)
        dsc = k.sb("dsc", [1, 17], F32)
        k.wait("tensor", s_ini)
        for h in range(16):
            k.wait("tensor", s_tp)
            k.op("tensor", lambda e, h=h: e.matmul(
                ps_mm[0][:1, :64], k.sbuf["ones_col"][:64, :],
                kv_sb[:, h * 65:h * 65 + 64], start=True, stop=True), s_mm)
            k.wait("vector", s_mm)
            k.op("vector", lambda e, h=h: e.tensor_copy(
                n2r[:1, h * 64:(h + 1) * 64], ps_mm[0][:1, :64]), s_tp)
        ksg = k.sb("ksg", [64, 16], F32)
        k.op("vector", lambda e: e.tensor_copy(
            ksg[:], bass.AP(kv_sb, 64, [[65 * 16, 64], [65, 16]])), s_ini)
        k.wait("tensor", s_ini)
        k.wait("tensor", s_tp)
        k.op("tensor", lambda e: e.matmul(
            ps_mm[0][:1, :16], k.sbuf["ones_col"][:64, :], ksg[:],
            start=True, stop=True), s_mm)
        k.wait("vector", s_mm)
        k.chain("vector", [
            lambda e: e.tensor_copy(dsc[:1, :16], ps_mm[0][:1, :16]),
            lambda e: e.tensor_reduce(dsc[:1, 16:17], dsc[:1, :16], AX.X, ALU.add),
        ], s_tp)
        k.wait("tensor", s_tp)
        for j in range(2):
            k.op("tensor", lambda e, j=j: e.matmul(
                ps_mm[0][:], k.sbuf["ones_row"][:1, :],
                n2r[:1, j * TT:(j + 1) * TT], start=True, stop=True), s_mm)
            k.wait("vector", s_mm)
            k.op("vector", lambda e, j=j: e.tensor_copy(
                n2b[:, j * TT:(j + 1) * TT], ps_mm[0][:]), s_tp)
            k.wait("tensor", s_tp)
        k.op("tensor", lambda e: e.matmul(
            ps_mm[0][:, :1], k.sbuf["ones_row"][:1, :], dsc[:1, 16:17],
            start=True, stop=True), s_mm)
        k.wait("vector", s_mm)
        k.op("vector", lambda e: e.tensor_copy(d2c[:], ps_mm[0][:, :1]), s_tp)
        k.wait("tensor", s_tp)

        for it in range(NTT if L2P3A_PHASE >= 2 else 0):
            t0 = it * TT
            k.wait("gpsimd", k.sem("ln_apply_s"), DC * 2 * it)
            k.dma("gpsimd", x_t[:].rearrange("p (c t) -> p c t", c=DC),
                  x_in.rearrange("(c p) t -> p c t", p=128)[:, :, t0:t0 + TT],
                  s_xt)
            for eng in ("scalar", "tensor", "vector"):
                k.wait(eng, s_xt)
            k.wait("vector", s_st, 16 * it)
            s_ap = emit_ln_tile(k, lambda c: x_t[:, c * TT:(c + 1) * TT],
                                lambda c: xn[:, c * TT:(c + 1) * TT])
            k.wait("tensor", s_ap)
            for hc in range(DC if L2P3A_PHASE >= 3 else 0):
                k.wait("tensor", s_ev)
                k.op("tensor", lambda e, hc=hc: [e.matmul(
                    ps_mm[0][:], wq_sb[:, ic * D + hc * 128: ic * D + hc * 128 + 128],
                    xn[:, ic * TT:(ic + 1) * TT], start=(ic == 0),
                    stop=(ic == DC - 1)) for ic in range(DC)][-1], s_mm)
                k.wait("vector", s_mm)
                k.wait("vector", s_at)
                k.wait("vector", s_ev)
                k.op("vector", lambda e: e.tensor_copy(qfm[:], ps_mm[0][:]), s_ev)
                k.wait("scalar", s_ev)
                k.op("scalar", lambda e: e.activation(q2f[:], qfm[:], AF.Square),
                     s_ev)
                for hh in range(2):
                    h = hc * 2 + hh
                    k.wait("tensor", s_ev)
                    k.op("tensor", lambda e, hh=hh: e.matmul(
                        ps_dd[:, :TT], pj_sb[hh * 64:(hh + 1) * 64, :],
                        qfm[hh * 64:(hh + 1) * 64, :],
                        start=True, stop=True), s_mm)
                    k.wait("scalar", s_mm)
                    k.wait("scalar", s_at)
                    k.op("scalar", lambda e: e.activation(eq[:], ps_dd[:, :TT],
                                                          AF.Exp), s_ev)
                    for st in range(4):
                        k.wait("tensor", s_ev)
                        k.wait("tensor", s_at)
                        k.op("tensor", lambda e, hh=hh, st=st: e.matmul(
                            ps_mm[1][:, :64],
                            qfm[hh * 64:(hh + 1) * 64, st * 128:(st + 1) * 128],
                            pj_sb[hh * 64:(hh + 1) * 64, :],
                            start=True, stop=True), s_mm)
                        k.wait("vector", s_mm)
                        k.op("vector", lambda e: e.reduce_max(
                            mxq[:], ps_mm[1][:, :64], AX.X), s_at)
                        k.wait("tensor", s_at)
                        k.op("tensor", lambda e, hh=hh, st=st: e.matmul(
                            ps_mm[1][:, 64:65],
                            q2f[hh * 64:(hh + 1) * 64, st * 128:(st + 1) * 128],
                            c625[hh * 64:(hh + 1) * 64, :],
                            start=True, stop=True), s_mm)
                        k.wait("vector", s_mm)
                        k.chain("vector", [
                            lambda e: e.tensor_add(dgq[:], ps_mm[1][:, 64:65],
                                                   mxq[:]),
                        ], s_at)
                        k.wait("scalar", s_at)
                        k.op("scalar", lambda e: e.activation(
                            fcol[:], dgq[:], AF.Exp, scale=-1.0), s_at)
                        k.wait("tensor", s_at)
                        k.op("tensor", lambda e, hh=hh, st=st, h=h: e.matmul(
                            ps_mm[1][:, :65], eq[:, st * 128:(st + 1) * 128],
                            kvb[:, h * 65:(h + 1) * 65], start=True, stop=True),
                            s_mm)
                        k.wait("vector", s_mm)
                        k.chain("vector", [
                            lambda e: e.tensor_scalar(rden[:], ps_mm[1][:, 64:65],
                                                      fcol[:, :1], None, ALU.mult),
                            lambda e: e.scalar_tensor_tensor(
                                rden[:], d2c[:], L2_EPS_K, rden[:],
                                ALU.mult, ALU.add),
                            lambda e: e.reciprocal(rden[:], rden[:]),
                            lambda e: e.tensor_mul(frc[:], fcol[:], rden[:]),
                            lambda e: e.tensor_scalar_mul(erc[:], rden[:],
                                                          L2_EPS_K),
                            lambda e: e.tensor_scalar(at_sb[:], ps_mm[1][:, :64],
                                                      frc[:, :1], None, ALU.mult),
                            lambda e, h=h: e.scalar_tensor_tensor(
                                at_sb[:], n2b[:, h * 64:(h + 1) * 64], erc[:, :1],
                                at_sb[:], ALU.mult, ALU.add),
                        ], s_at)
                        k.wait("tensor", s_at)
                        k.wait("tensor", s_tp)
                        k.op("tensor", lambda e: e.transpose(
                            ps_tp[:64, :128], at_sb[:, :64], ident[:]), s_tp)
                        k.wait("vector", s_tp)
                        slot = ((h // 2) * 4 + st) * 128
                        hb = (h % 2) * 64
                        k.op("vector", lambda e, hb=hb, slot=slot: e.tensor_copy(
                            atT[hb:hb + 64, slot:slot + 128],
                            ps_tp[:64, :128]), s_tp)
            k.wait("tensor", s_tp)
            for oc in range(DC if L2P3A_PHASE >= 4 else 0):
                for st in range(4):
                    k.wait("tensor", s_wo)
                    k.op("tensor", lambda e, oc=oc, st=st: [e.matmul(
                        ps_mm[0][:, :128],
                        wo_sb[:, hc2 * D + oc * 128: hc2 * D + oc * 128 + 128],
                        atT[:, (hc2 * 4 + st) * 128:(hc2 * 4 + st) * 128 + 128],
                        start=(hc2 == 0), stop=(hc2 == DC - 1))
                        for hc2 in range(DC)][-1], s_mm)
                    k.wait("vector", s_mm)
                    k.op("vector", lambda e, oc=oc, st=st: e.tensor_add(
                        o1p[:, oc * TT + st * 128: oc * TT + st * 128 + 128],
                        x_t[:, oc * TT + st * 128: oc * TT + st * 128 + 128],
                        ps_mm[0][:, :128]), s_wo)
            for eng in ("scalar", "tensor", "vector"):
                k.wait(eng, s_wo)
            s_ap1 = emit_ln_tile(k, lambda c: o1p[:, c * TT:(c + 1) * TT],
                                 lambda c: o1p[:, c * TT:(c + 1) * TT],
                                 eps_name="eps_ln1",
                                 affine=(lambda c: g1[:, c:c + 1],
                                         lambda c: b1[:, c:c + 1]))
            k.wait("gpsimd", s_ap1)
            k.dma("gpsimd",
                  out1.rearrange("(c p) t -> p c t", p=128)[:, :, t0:t0 + TT],
                  o1p[:].rearrange("p (c t) -> p c t", c=DC), s_st)
        k.wait("gpsimd", s_st)
        k.play()
    return nc


def build_l2attn_p3b():
    nc = bass.Bass("TRN2", target_bir_lowering=False,
                   detect_race_conditions=_RD)
    x_in = _l2_common(nc)
    o1_in = nc.dram_tensor("o1_in", [D, TLOC], F32, kind="ExternalInput").ap()
    w1t = nc.dram_tensor("w1t", [D, 4096], F32, kind="ExternalInput").ap()
    b1c = nc.dram_tensor("b1c", [128, 32], F32, kind="ExternalInput").ap()
    w2t = nc.dram_tensor("w2t", [4096, D], F32, kind="ExternalInput").ap()
    b2c = nc.dram_tensor("b2c", [128, 8], F32, kind="ExternalInput").ap()
    ln2gc = nc.dram_tensor("ln2gc", [128, 8], F32, kind="ExternalInput").ap()
    ln2bc = nc.dram_tensor("ln2bc", [128, 8], F32, kind="ExternalInput").ap()
    x_out = nc.dram_tensor("x_out", [D, TLOC], F32, kind="ExternalOutput").ap()

    FF = 4096
    FC = FF // 128

    with ExitStack() as stack:
        k = KB(nc, stack)
        emit_consts(k, {"eps_ln2": 1e-6})
        x_t = k.sb("x_t", [128, DC * TT], F32)
        o1f = k.sb("o1f", [128, DC * TT], F32)
        o1b = k.sb("o1b", [128, DC * TT], BF16)
        h_sb = k.sb("h_sb", [128, FC * (TT // 2)], BF16)
        w1_sb = k.sb("w1_sb", [128, DC * FF], BF16)
        w2_sb = k.sb("w2_sb", [128, FC * D], BF16)
        b1_sb = k.sb("b1_sb", [128, 32], F32)
        b2_sb = k.sb("b2_sb", [128, 8], F32)
        g2 = k.sb("g2", [128, DC], F32)
        bb2 = k.sb("bb2", [128, DC], F32)
        zt = k.sb("vscratch", [128, TT], F32)
        am = k.sb("am", [128, TT], BF16)
        eb = k.sb("eb", [128, TT], BF16)
        rm = am
        ps_mm = [k.ps(f"mm{i}", [128, TT]) for i in range(2)]

        s_w = k.sem("w_ld")
        s_b = k.sem("b_ld")
        s_xt = k.sem("x_ld")
        s_o1 = k.sem("o1_ld")
        s_o1s = k.sem("o1_lds")
        s_mm1 = k.sem("mm1")
        s_e1 = k.sem("e1")
        s_e2 = k.sem("e2")
        s_hf = k.sem("h_free")
        s_mm2 = k.sem("mm2")
        s_of = k.sem("o_free")
        s_res = k.sem("res")
        s_st = k.sem("st")

        k.dma("gpsimd", w1_sb[:].rearrange("p (c x) -> p c x", c=DC),
              w1t.rearrange("(c p) x -> p c x", p=128), s_w)
        k.dma("gpsimd", w2_sb[:].rearrange("p (c x) -> p c x", c=FC),
              w2t.rearrange("(c p) x -> p c x", p=128), s_w)
        k.dma("sync", b1_sb[:], b1c, s_b)
        k.dma("sync", b2_sb[:], b2c, s_b)
        k.dma("sync", g2[:], ln2gc, s_b)
        k.dma("sync", bb2[:], ln2bc, s_b)
        k.wait("tensor", s_w)
        for eng in ("scalar", "vector"):
            k.wait(eng, s_b)

        for it in range(NTT):
            t0 = it * TT
            k.wait("gpsimd", s_st, 16 * it)
            k.dma("gpsimd", x_t[:].rearrange("p (c t) -> p c t", c=DC),
                  x_in.rearrange("(c p) t -> p c t", p=128)[:, :, t0:t0 + TT],
                  s_xt)
            k.wait("gpsimd", s_res, DC * it)
            k.dma("gpsimd", o1b[:].rearrange("p (c t) -> p c t", c=DC),
                  o1_in.rearrange("(c p) t -> p c t", p=128)[:, :, t0:t0 + TT],
                  s_o1)
            k.wait("sync", s_res, DC * it)
            k.dma("sync", o1f[:].rearrange("p (c t) -> p c t", c=DC),
                  o1_in.rearrange("(c p) t -> p c t", p=128)[:, :, t0:t0 + TT],
                  s_o1s)
            k.wait("tensor", s_o1)
            k.wait("tensor", s_o1s)
            for eng in ("scalar", "vector"):
                k.wait(eng, s_o1)
                k.wait(eng, s_o1s)
                k.wait(eng, s_xt)

            HT = TT // 2
            for tb in range(2):
                tof = tb * HT
                k.wait("tensor", s_of)
                for oc in range(FC):
                    b = ps_mm[oc % 2][:, :HT]
                    k.wait("tensor", s_hf, k.sems["h_free"].n - 1)
                    for ic in range(DC):
                        k.op("tensor", lambda e, b=b, ic=ic, oc=oc, tof=tof:
                             e.matmul(b, w1_sb[:, ic * FF + oc * 128:
                                               ic * FF + oc * 128 + 128],
                                      o1b[:, ic * TT + tof: ic * TT + tof + HT],
                                      start=(ic == 0), stop=(ic == DC - 1)),
                             sig=s_mm1 if ic == DC - 1 else None)
                    k.wait("vector", s_mm1)
                    k.wait("vector", s_hf)
                    if oc == 0:
                        k.wait("vector", s_mm2, 16 * it + 8 * tb)
                    k.chain("vector", [
                        lambda e, b=b, oc=oc: e.tensor_scalar(
                            zt[:, :HT], b, b1_sb[:, oc:oc + 1], None, ALU.add),
                        lambda e: e.tensor_scalar(am[:, :HT], zt[:, :HT], 0.0,
                                                  None, ALU.min),
                    ], s_e1)
                    k.wait("scalar", s_e1)
                    k.op("scalar", lambda e: e.activation(eb[:, :HT], am[:, :HT],
                                                          AF.Exp), s_e2)
                    k.wait("vector", s_e2)
                    k.chain("vector", [
                        lambda e: e.tensor_scalar(am[:, :HT], zt[:, :HT], 0.0,
                                                  None, ALU.max),
                        lambda e, oc=oc: e.scalar_tensor_tensor(
                            h_sb[:, oc * HT:(oc + 1) * HT], am[:, :HT], -1.0,
                            eb[:, :HT], ALU.add, ALU.add),
                    ], s_hf)
                k.wait("tensor", s_hf)
                for oc in range(DC):
                    b = ps_mm[oc % 2][:, :HT]
                    k.wait("tensor", s_of, k.sems["o_free"].n - 1)
                    for ic in range(FC):
                        k.op("tensor", lambda e, b=b, ic=ic, oc=oc: e.matmul(
                            b, w2_sb[:, ic * D + oc * 128: ic * D + oc * 128 + 128],
                            h_sb[:, ic * HT:(ic + 1) * HT],
                            start=(ic == 0), stop=(ic == FC - 1)),
                            sig=s_mm2 if ic == FC - 1 else None)
                    k.wait("vector", s_mm2)
                    k.wait("vector", s_of)
                    k.chain("vector", [
                        lambda e, b=b, oc=oc: e.tensor_scalar(
                            zt[:, :HT], b, b2_sb[:, oc:oc + 1], None, ALU.add),
                        lambda e, oc=oc, tof=tof: e.tensor_add(
                            o1f[:, oc * TT + tof: oc * TT + tof + HT],
                            o1f[:, oc * TT + tof: oc * TT + tof + HT],
                            zt[:, :HT]),
                    ], s_of)
            for eng in ("scalar", "tensor", "vector"):
                k.wait(eng, s_of)
            s_ap2 = emit_ln_tile(k, lambda c: o1f[:, c * TT:(c + 1) * TT],
                                 lambda c: o1f[:, c * TT:(c + 1) * TT],
                                 eps_name="eps_ln2",
                                 affine=(lambda c: g2[:, c:c + 1],
                                         lambda c: bb2[:, c:c + 1]))
            k.wait("vector", s_ap2)
            for c in range(DC):
                k.op("vector", lambda e, c=c: e.tensor_add(
                    x_t[:, c * TT:(c + 1) * TT], x_t[:, c * TT:(c + 1) * TT],
                    o1f[:, c * TT:(c + 1) * TT]), s_res)
            k.wait("gpsimd", s_res)
            k.dma("gpsimd",
                  x_out.rearrange("(c p) t -> p c t", p=128)[:, :, t0:t0 + TT],
                  x_t[:].rearrange("p (c t) -> p c t", c=DC), s_st)
        k.wait("gpsimd", s_st)
        k.play()
    return nc


def host_l2attn_inputs(z):
    Wq = z["L2_attn_Wq"].reshape(D, D)
    Wk = z["L2_attn_Wk"].reshape(D, D)
    Wv = z["L2_attn_Wv"].reshape(D, D)
    Wo = z["L2_attn_Wo"].reshape(D, D)
    proj = z["L2_attn_proj"]
    pj1 = np.ascontiguousarray(proj.T)
    projdn = np.concatenate([pj1, pj1], axis=0)
    return dict(
        wq=np.ascontiguousarray(Wq), wk=np.ascontiguousarray(Wk),
        wv=np.ascontiguousarray(Wv), wo_r=np.ascontiguousarray(Wo),
        projdn=projdn.astype(np.float32),
        w1t=np.ascontiguousarray(z["L2_attn_W1"]),
        b1c=col128(z["L2_attn_b1"]),
        w2t=np.ascontiguousarray(z["L2_attn_W2"]),
        b2c=col128(z["L2_attn_b2"]),
        ln1gc=col128(z["L2_attn_ln1_g"]), ln1bc=col128(z["L2_attn_ln1_b"]),
        ln2gc=col128(z["L2_attn_ln2_g"]), ln2bc=col128(z["L2_attn_ln2_b"]),
    )



L3H = 384
L3T = TLOC + L3H
L3ST = L3T // 128
L3QT = TLOC // 256
INNER = 1024
DIMI2 = 5632
HCI = DIMI2 // 128


def build_l3pre():
    nc = bass.Bass("TRN2", target_bir_lowering=False,
                   detect_race_conditions=_RD)
    x_in = nc.dram_tensor("x_in", [D, TLOC], F32, kind="ExternalInput").ap()
    z_out = nc.dram_tensor("z_out", [D, TLOC], F32, kind="ExternalOutput").ap()
    with ExitStack() as stack:
        k = KB(nc, stack)
        emit_consts(k, {"eps_ln": 1e-5})
        x_t = k.sb("x_t", [128, DC * TT], F32)
        zn = k.sb("zn", [128, DC * TT], F32)
        s_xt = k.sem("x_ld")
        s_st = k.sem("st")
        for it in range(NTT):
            t0 = it * TT
            k.wait("gpsimd", s_st, 16 * it)
            k.dma("gpsimd", x_t[:].rearrange("p (c t) -> p c t", c=DC),
                  x_in.rearrange("(c p) t -> p c t", p=128)[:, :, t0:t0 + TT],
                  s_xt)
            for eng in ("scalar", "tensor", "vector"):
                k.wait(eng, s_xt)
            s_ap = emit_ln_tile(k, lambda c: x_t[:, c * TT:(c + 1) * TT],
                                lambda c: zn[:, c * TT:(c + 1) * TT])
            k.wait("gpsimd", s_ap)
            k.dma("gpsimd",
                  z_out.rearrange("(c p) t -> p c t", p=128)[:, :, t0:t0 + TT],
                  zn[:].rearrange("p (c t) -> p c t", c=DC), s_st)
        k.wait("gpsimd", s_st)
        k.play()
    return nc


def build_l3post():
    nc = bass.Bass("TRN2", target_bir_lowering=False,
                   detect_race_conditions=_RD)
    x_in = nc.dram_tensor("x_in", [D, TLOC], F32, kind="ExternalInput").ap()
    z_in = nc.dram_tensor("z_in", [D, TLOC], F32, kind="ExternalInput").ap()
    x_out = nc.dram_tensor("x_out", [D, TLOC], F32, kind="ExternalOutput").ap()
    with ExitStack() as stack:
        k = KB(nc, stack)
        x_t = k.sb("x_t", [128, DC * TT], F32)
        z_t = k.sb("z_t", [128, DC * TT], F32)
        s_xt = k.sem("x_ld")
        s_zt = k.sem("z_ld")
        s_ad = k.sem("add")
        s_st = k.sem("st")
        for it in range(NTT):
            t0 = it * TT
            k.wait("gpsimd", s_st, 16 * it)
            k.dma("gpsimd", x_t[:].rearrange("p (c t) -> p c t", c=DC),
                  x_in.rearrange("(c p) t -> p c t", p=128)[:, :, t0:t0 + TT],
                  s_xt)
            k.wait("sync", s_ad, DC * it)
            k.dma("sync", z_t[:].rearrange("p (c t) -> p c t", c=DC),
                  z_in.rearrange("(c p) t -> p c t", p=128)[:, :, t0:t0 + TT],
                  s_zt)
            k.wait("vector", s_xt)
            k.wait("vector", s_zt)
            for c in range(DC):
                k.op("vector", lambda e, c=c: e.scalar_tensor_tensor(
                    x_t[:, c * TT:(c + 1) * TT], z_t[:, c * TT:(c + 1) * TT],
                    0.5, x_t[:, c * TT:(c + 1) * TT], ALU.mult, ALU.add), s_ad)
            k.wait("gpsimd", s_ad)
            k.dma("gpsimd",
                  x_out.rearrange("(c p) t -> p c t", p=128)[:, :, t0:t0 + TT],
                  x_t[:].rearrange("p (c t) -> p c t", c=DC), s_st)
        k.wait("gpsimd", s_st)
        k.play()
    return nc


def build_l3ffn_attn():
    nc = bass.Bass("TRN2", target_bir_lowering=False,
                   detect_race_conditions=_RD)
    zh_in = nc.dram_tensor("zh_in", [D, L3T], F32, kind="ExternalInput").ap()
    wqkv = nc.dram_tensor("wqkv", [D, 3 * INNER], F32, kind="ExternalInput").ap()
    wo = nc.dram_tensor("wo", [INNER, D], F32, kind="ExternalInput").ap()
    tq = nc.dram_tensor("tq", [64, 2 * TLOC], F32, kind="ExternalInput").ap()
    tk = nc.dram_tensor("tk", [64, 2 * L3T], F32, kind="ExternalInput").ap()
    qks2 = nc.dram_tensor("qks2", [64, 4], F32, kind="ExternalInput").ap()
    qposr = nc.dram_tensor("qposr", [1, TLOC], F32, kind="ExternalInput").ap()
    sposc = nc.dram_tensor("sposc", [128, L3ST], F32, kind="ExternalInput").ap()
    za_out = nc.dram_tensor("za_out", [D, TLOC], F32, kind="ExternalOutput").ap()

    QTILES = [(0, 512), (512, 512), (1024, 512), (1536, 512)]
    KTILES = QTILES + [(2048, 384)]
    I3 = 3 * INNER

    with ExitStack() as stack:
        k = KB(nc, stack)
        emit_consts(k, {})
        zb = k.sb("zb", [128, DC * L3T], BF16)
        qpre = k.sb("qpre", [64, 512], BF16)
        kpre = k.sb("kpre", [64, 512], BF16)
        qh = k.sb("qh", [64, TLOC], BF16)
        kh = k.sb("kh", [64, L3T], BF16)
        rotb = k.sb("rotb", [64, 512], BF16)
        rt1 = k.sb("rt1", [64, 512], BF16)
        rdb = k.sb("rdb", [64, 256], F32)
        vp = k.sb("vp", [128, L3ST * 128], BF16)
        pb = k.sb("pb", [128, 4 * 256], BF16)
        pbf = pb.bitcast(F32)
        q2f = pbf
        kt1 = pbf
        kt2 = pbf
        keepall = k.sb("keepall", [128, L3QT * 4 * 256], BF16)
        x_t = keepall.bitcast(F32)
        ctxs = k.sb("ctxs", [128, 4 * TLOC], BF16)
        aacc = k.sb("aacc", [128, DC * TLOC], BF16)
        wqkv_sb = k.sb("wqkv_sb", [128, DC * I3], BF16)
        wo_sb = k.sb("wo_sb", [128, DC * D], BF16)
        tq_sb = k.sb("tq_sb", [64, 2 * TLOC], BF16)
        tk_sb = k.sb("tk_sb", [64, 2 * L3T], BF16)
        qpr = ctxs.bitcast(F32)
        spc = k.sb("spc", [128, L3ST], F32)
        spc2 = k.sb("spc2", [128, L3ST], F32)
        m8c = k.sb("m8c", [128, 1], F32)
        o64bf = k.sb("o64bf", [128, 64], BF16)
        o64f = k.sb("o64f", [64, 64], F32)
        nr2 = k.sb("nr2", [64, 512], F32)

        ps_mm = [k.ps(f"mm{i}", [128, TT]) for i in range(2)]
        ps_bc = k.ps("ps_bc", [64, TT])
        ps_v2 = k.ps("ps_v2", [128, TT])
        ps_q2 = k.ps("ps_q2", [128, TT])

        s_ld = k.sem("ld")
        s_ld2 = k.sem("ld2")
        s_xt = k.sem("x_ld")
        s_mm = k.sem("mm")
        s_ev = k.sem("evac")
        s_rw = k.sem("rw")
        s_rot = k.sem("rot")
        s_kp = k.sem("kp")
        s_ex = k.sem("ex")
        s_pk = k.sem("pk")
        s_cx = k.sem("cx")
        s_wo = k.sem("wo_s")
        s_res = k.sem("res")
        s_st = k.sem("st")
        s_c2 = k.sem("cst2")

        k.op("gpsimd", lambda e: e.memset(m8c[:], -8.0))
        k.op("gpsimd", lambda e: e.memset(o64f[:], 1.0))
        k.op("gpsimd", lambda e: e.memset(o64bf[:], 1.0), s_c2)
        k.dma("gpsimd", zb[:].rearrange("p (c t) -> p c t", c=DC),
              zh_in.rearrange("(c p) t -> p c t", p=128), s_ld)
        k.dma("gpsimd", wqkv_sb[:].rearrange("p (c x) -> p c x", c=DC),
              wqkv.rearrange("(c p) x -> p c x", p=128), s_ld)
        k.dma("gpsimd", wo_sb[:].rearrange("p (c x) -> p c x", c=DC),
              wo.rearrange("(c p) x -> p c x", p=128), s_ld)
        k.dma("gpsimd", tq_sb[:], tq, s_ld)
        k.dma("gpsimd", tk_sb[:], tk, s_ld)
        qk2 = k.sb("qk2", [64, 4], F32)
        k.dma("sync", qpr[:1, :TLOC], qposr, s_ld2)
        k.dma("sync", spc[:], sposc, s_ld2)
        k.dma("sync", qk2[:], qks2, s_ld2)
        for eng in ("tensor", "scalar", "vector"):
            k.wait(eng, s_ld)
            k.wait(eng, s_ld2)
            k.wait(eng, s_c2)
        for i, (tb, w) in enumerate(((tq_sb, TLOC), (tk_sb, L3T))):
            for half in range(2):
                k.op("vector", lambda e, tb=tb, w=w, half=half, i=i:
                     e.tensor_scalar(tb[:, half * w:(half + 1) * w],
                                     tb[:, half * w:(half + 1) * w],
                                     qk2[:, 2 * i + half:2 * i + half + 1],
                                     None, ALU.mult), k.sem("tbl"))
        k.wait("tensor", k.sems["tbl"])
        k.wait("scalar", k.sems["tbl"])
        k.op("vector", lambda e: e.tensor_scalar_add(spc2[:], spc[:], 256.0),
             s_kp)
        k.wait("vector", s_kp)

        def zs(c, col, w):
            off = c * L3T + col - (1 if c >= 4 else 0)
            return zb[:, off:off + w]

        for qt in range(L3QT):
            q0 = qt * 256
            k.wait("tensor", s_kp)
            k.op("tensor", lambda e, q0=q0: e.matmul(
                ps_mm[0][:, :256], k.sbuf["ones_row"][:1, :],
                qpr[:1, q0:q0 + 256], start=True, stop=True), s_mm)
            k.wait("vector", s_mm)
            for j in range(4):
                ss = qt * 2 + 1 + j
                kslc = keepall[:, (qt * 4 + j) * 256:(qt * 4 + j) * 256 + 256]
                k.wait("vector", s_kp)
                k.chain("vector", [
                    lambda e, ss=ss: e.tensor_scalar(
                        kt1[:, :256], ps_mm[0][:, :256], spc[:, ss:ss + 1],
                        None, ALU.is_ge),
                    lambda e, ss=ss: e.tensor_scalar(
                        kt2[:, 256:512], ps_mm[0][:, :256], spc2[:, ss:ss + 1],
                        None, ALU.is_le),
                    lambda e, kslc=kslc: e.tensor_mul(
                        kslc, kt1[:, :256], kt2[:, 256:512]),
                ], s_kp)

        for hp in range(DC):
            half, c2 = hp // 4, hp % 4
            for st in range(L3ST):
                vb = ps_mm[1] if (st % 2 == 0 or not (L3A_PIPE & 1)) else ps_v2
                k.wait("tensor", s_cx)
                if st < 2 or not (L3A_PIPE & 1):
                    k.wait("tensor", s_ev)
                    k.wait("tensor", s_wo)
                else:
                    k.wait("tensor", s_ev, s_ev.n - 1)
                k.op("tensor", lambda e, st=st, hp=hp, vb=vb: [e.matmul(
                    vb[:, :128], zs(ic, st * 128, 128),
                    wqkv_sb[:, ic * I3 + 2 * INNER + hp * 128:
                            ic * I3 + 2 * INNER + hp * 128 + 128],
                    start=(ic == 0), stop=(ic == DC - 1))
                    for ic in range(DC)][-1], s_mm)
                k.wait("vector", s_mm)
                k.op("vector", lambda e, st=st, vb=vb: e.tensor_copy(
                    vp[:, st * 128:(st + 1) * 128], vb[:, :128]), s_ev)
            for hh in range(2):
                h = hp * 2 + hh
                hcol = h * 64
                k.wait("vector", s_cx)
                k.wait("scalar", s_cx)
                k.wait("scalar", s_kp)
                ti = 0
                for which in range(2):
                    tiles = QTILES if which == 0 else KTILES
                    base = L3H if which == 0 else 0
                    proj_off = hcol if which == 0 else INNER + hcol
                    tbl = tq_sb if which == 0 else tk_sb
                    tspan = TLOC if which == 0 else L3T
                    dst = qh if which == 0 else kh
                    for (t0, w) in tiles:
                        if L3A_PIPE & 4:
                            pre = qpre if ti % 2 == 0 else kpre
                            slot = ps_mm[0] if ti % 2 == 0 else ps_q2
                        else:
                            pre = qpre if which == 0 else kpre
                            slot = ps_mm[0]
                        ti += 1
                        k.wait("tensor", s_rot,
                               max(0, s_rot.n - 1) if L3A_PIPE & 4
                               else s_rot.n)
                        k.wait("tensor", s_cx)
                        k.op("tensor", lambda e, t0=t0, w=w, base=base,
                             proj_off=proj_off, slot=slot: [e.matmul(
                                 slot[:64, :w],
                                 wqkv_sb[:, ic * I3 + proj_off:
                                         ic * I3 + proj_off + 64],
                                 zs(ic, base + t0, w),
                                 start=(ic == 0), stop=(ic == DC - 1))
                                 for ic in range(DC)][-1], s_mm)
                        k.wait("vector", s_mm)
                        k.op("vector", lambda e, pre=pre, w=w, slot=slot:
                             e.tensor_copy(pre[:, :w], slot[:64, :w]), s_ev)
                        k.wait("scalar", s_ev)
                        k.wait("scalar", s_mm, s_mm.n)
                        k.op("scalar", lambda e, pre=pre, w=w: e.activation(
                            q2f[:64, :w], pre[:, :w], AF.Square), s_ev)
                        k.wait("tensor", s_ev)
                        k.wait("tensor", s_rw)
                        k.op("tensor", lambda e, w=w: e.matmul(
                            ps_bc[:, :w], o64f[:, :],
                            q2f[:64, :w], start=True, stop=True), s_mm)
                        k.wait("vector", s_mm)
                        k.op("vector", lambda e, w=w: e.tensor_scalar(
                            nr2[:, :w], ps_bc[:, :w], 1e-24, None, ALU.max),
                            s_rw)
                        k.wait("scalar", s_rw)
                        k.wait("scalar", s_rot, s_rot.n)
                        k.chain("scalar", [
                            lambda e, w=w: e.activation(nr2[:, :w],
                                                        nr2[:, :w], AF.Ln),
                            lambda e, w=w: e.activation(nr2[:, :w],
                                                        nr2[:, :w], AF.Exp,
                                                        scale=-0.5),
                        ], s_rw)
                        k.wait("vector", s_rw)
                        k.chain("vector", [
                            lambda e, pre=pre, w=w: e.tensor_copy(
                                rotb[0:32, :w], pre[32:64, :w]),
                            lambda e, pre=pre, w=w: e.tensor_copy(
                                rotb[32:64, :w], pre[0:32, :w]),
                            lambda e, pre=pre, tbl=tbl, t0=t0, w=w:
                                e.tensor_mul(rt1[:, :w], pre[:, :w],
                                             tbl[:, t0:t0 + w]),
                            lambda e, tbl=tbl, t0=t0, w=w, tspan=tspan:
                                e.tensor_mul(rotb[:, :w], rotb[:, :w],
                                             tbl[:, tspan + t0:tspan + t0 + w]),
                            lambda e, w=w: e.tensor_add(rt1[:, :w], rt1[:, :w],
                                                        rotb[:, :w]),
                            lambda e, dst=dst, t0=t0, w=w: e.tensor_mul(
                                dst[:, t0:t0 + w], rt1[:, :w], nr2[:, :w]),
                        ], s_rot)
                k.wait("tensor", s_ev)
                for qt in range(L3QT):
                    q0 = qt * 256
                    for j in range(4):
                        ss = qt * 2 + 1 + j
                        sb_ = (ps_mm[1] if (j % 2 == 0 or not (L3A_PIPE & 2))
                               else ps_v2)
                        k.wait("tensor", s_ex,
                               max(0, s_ex.n - 1) if L3A_PIPE & 2
                               else s_ex.n)
                        k.wait("tensor", s_rot)
                        k.op("tensor", lambda e, ss=ss, q0=q0, sb_=sb_:
                             e.matmul(sb_[:, :256],
                                      kh[:, ss * 128:(ss + 1) * 128],
                                      qh[:, q0:q0 + 256],
                                      start=True, stop=True), s_mm)
                        k.wait("scalar", s_mm)
                        k.wait("scalar", s_cx)
                        k.op("scalar", lambda e, j=j, sb_=sb_: e.activation(
                            pb[:, j * 256:(j + 1) * 256], sb_[:, :256],
                            AF.Exp, bias=m8c[:, :1], scale=8.0), s_ex)
                    k.wait("vector", s_ex)
                    for j in range(4):
                        kslc = keepall[:, (qt * 4 + j) * 256:(qt * 4 + j) * 256 + 256]
                        k.op("vector", lambda e, j=j, kslc=kslc: e.tensor_mul(
                            pb[:, j * 256:(j + 1) * 256],
                            pb[:, j * 256:(j + 1) * 256], kslc), s_pk)
                    k.wait("tensor", s_pk)
                    k.wait("tensor", s_rw)
                    k.op("tensor", lambda e: [e.matmul(
                        ps_bc[:, :256], o64bf[:, :],
                        pb[:, j * 256:(j + 1) * 256],
                        start=(j == 0), stop=(j == 3)) for j in range(4)][-1],
                        s_mm)
                    k.wait("vector", s_mm)
                    k.op("vector", lambda e: e.reciprocal(rdb[:, :256],
                                                          ps_bc[:, :256]), s_rw)
                    k.wait("tensor", s_rw)
                    k.wait("tensor", s_cx)
                    k.op("tensor", lambda e, qt=qt, hh=hh: [e.matmul(
                        ps_mm[0][hh * 64:hh * 64 + 64, :256],
                        vp[:, (qt * 2 + 1 + j) * 128 + hh * 64:
                           (qt * 2 + 1 + j) * 128 + hh * 64 + 64],
                        pb[:, j * 256:(j + 1) * 256],
                        start=(j == 0), stop=(j == 3)) for j in range(4)][-1],
                        s_mm)
                    k.wait("vector", s_mm)
                    slot = (c2 * L3QT + qt) * 256
                    k.op("vector", lambda e, hh=hh, slot=slot: e.tensor_mul(
                        ctxs[hh * 64:hh * 64 + 64, slot:slot + 256],
                        ps_mm[0][hh * 64:hh * 64 + 64, :256], rdb[:, :256]),
                        s_cx)
            if c2 == 3:
                k.wait("tensor", s_cx)
                for oc in range(DC):
                    for tl in range(NTT):
                        k.wait("tensor", s_wo)
                        k.op("tensor", lambda e, oc=oc, tl=tl, half=half:
                             [e.matmul(
                                 ps_mm[1][:],
                                 wo_sb[:, (half * 4 + cc) * D + oc * 128:
                                       (half * 4 + cc) * D + oc * 128 + 128],
                                 ctxs[:, (cc * L3QT + 2 * tl) * 256:
                                      (cc * L3QT + 2 * tl) * 256 + 512],
                                 start=(cc == 0), stop=(cc == 3))
                                 for cc in range(4)][-1], s_mm)
                        k.wait("vector", s_mm)
                        aslc = aacc[:, oc * TLOC + tl * TT: oc * TLOC + tl * TT + TT]
                        if half == 0:
                            k.op("vector", lambda e, aslc=aslc: e.tensor_copy(
                                aslc, ps_mm[1][:]), s_wo)
                        else:
                            k.op("vector", lambda e, aslc=aslc: e.tensor_add(
                                aslc, aslc, ps_mm[1][:]), s_wo)

        for it in range(NTT):
            t0 = it * TT
            k.wait("gpsimd", s_wo)
            k.wait("gpsimd", s_st, 16 * it)
            k.dma("gpsimd", x_t[:].rearrange("p (c t) -> p c t", c=DC),
                  zh_in.rearrange("(c p) t -> p c t", p=128)[:, :, L3H + t0:
                                                            L3H + t0 + TT],
                  s_xt)
            k.wait("vector", s_xt)
            for c in range(DC):
                k.op("vector", lambda e, c=c, t0=t0: e.tensor_add(
                    x_t[:, c * TT:(c + 1) * TT], x_t[:, c * TT:(c + 1) * TT],
                    aacc[:, c * TLOC + t0: c * TLOC + t0 + TT]), s_res)
            k.wait("gpsimd", s_res)
            k.dma("gpsimd",
                  za_out.rearrange("(c p) t -> p c t", p=128)[:, :, t0:t0 + TT],
                  x_t[:].rearrange("p (c t) -> p c t", c=DC), s_st)
        k.wait("gpsimd", s_st)
        k.play()
    return nc


def build_l3ffn_ffn():
    nc = bass.Bass("TRN2", target_bir_lowering=False,
                   detect_race_conditions=_RD)
    zh_in = nc.dram_tensor("zh_in", [D, L3T], F32, kind="ExternalInput").ap()
    lngc = nc.dram_tensor("lngc", [128, 8], F32, kind="ExternalInput").ap()
    lnbc = nc.dram_tensor("lnbc", [128, 8], F32, kind="ExternalInput").ap()
    w1p = nc.dram_tensor("w1p", [D, DIMI2], F32, kind="ExternalInput").ap()
    b1ac = nc.dram_tensor("b1ac", [128, 22], F32, kind="ExternalInput").ap()
    b1gc = nc.dram_tensor("b1gc", [128, 22], F32, kind="ExternalInput").ap()
    w2p = nc.dram_tensor("w2p", [DIMI2 // 2, D], F32, kind="ExternalInput").ap()
    b2c = nc.dram_tensor("b2c", [128, 8], F32, kind="ExternalInput").ap()
    z_out = nc.dram_tensor("z_out", [D, TLOC], F32, kind="ExternalOutput").ap()

    HC2 = 22

    with ExitStack() as stack:
        k = KB(nc, stack)
        emit_consts(k, {"eps_l3": 1e-5})
        x_t = k.sb("x_t", [128, DC * TT], F32)
        xn = k.sb("xnf", [128, DC * TLOC], BF16)
        h_sb = k.sb("h_sb", [128, HC2 * 256], BF16)
        w1_sb = k.sb("w1_sb", [128, DC * DIMI2], BF16)
        w2_sb = k.sb("w2_sb", [128, HC2 * D], BF16)
        g1 = k.sb("g1c", [128, DC], F32)
        b1 = k.sb("b1c_", [128, DC], F32)
        b1a = k.sb("b1a", [128, HC2], F32)
        b1g = k.sb("b1g", [128, HC2], F32)
        b2 = k.sb("b2_", [128, DC], F32)
        gsc = k.sb("gsc", [128, 512], F32)
        av = k.sb("av", [128, 256], F32)

        ps_mm = [k.ps(f"mm{i}", [128, TT]) for i in range(4)]

        zo = x_t[:, :DC * 256]
        xr = x_t[:, DC * 256:DC * 512]

        s_ld = k.sem("ld")
        s_ld2 = k.sem("ld2")
        s_xt = k.sem("x_ld")
        s_mm1 = k.sem("mm1")
        s_gel = k.sem("gel")
        s_hf = k.sem("h_free")
        s_mm2 = k.sem("mm2")
        s_of = k.sem("o_free")
        s_st = k.sem("st")

        k.dma("gpsimd", w1_sb[:].rearrange("p (c x) -> p c x", c=DC),
              w1p.rearrange("(c p) x -> p c x", p=128), s_ld)
        k.dma("gpsimd", w2_sb[:].rearrange("p (c x) -> p c x", c=HC2),
              w2p.rearrange("(c p) x -> p c x", p=128), s_ld)
        for name, dst in [("lngc", g1), ("lnbc", b1), ("b1ac", b1a),
                          ("b1gc", b1g), ("b2c", b2)]:
            k.dma("sync", dst[:], {"lngc": lngc, "lnbc": lnbc, "b1ac": b1ac,
                                   "b1gc": b1gc, "b2c": b2c}[name], s_ld2)
        k.wait("tensor", s_ld)
        for eng in ("scalar", "vector"):
            k.wait(eng, s_ld2)

        for it in range(NTT):
            t0 = it * TT
            k.wait("gpsimd", k.sem("ln_apply_s"), DC * it)
            for cg, sh in ((0, 0), (4, 1)):
                k.dma("gpsimd",
                      x_t[:].rearrange("p (c t) -> p c t", c=DC)[:, cg:cg + 4],
                      zh_in.rearrange("(c p) t -> p c t", p=128)[
                          :, cg:cg + 4, L3H + t0 - sh: L3H + t0 - sh + TT],
                      s_xt)
            for eng in ("scalar", "tensor", "vector"):
                k.wait(eng, s_xt)
            s_ap = emit_ln_tile(k, lambda c: x_t[:, c * TT:(c + 1) * TT],
                                lambda c, t0=t0: xn[:, c * TLOC + t0:
                                                    c * TLOC + t0 + TT],
                                eps_name="eps_l3",
                                affine=(lambda c: g1[:, c:c + 1],
                                        lambda c: b1[:, c:c + 1]))
        for tt2 in range(TLOC // 256):
            t0 = tt2 * 256
            k.wait("tensor", k.sems["ln_apply_s"])
            k.wait("tensor", s_of)
            for oc in range(HC2):
                p = oc % 2
                ba = ps_mm[2 * p][:, :256]
                bg = ps_mm[2 * p + 1][:, :256]
                k.wait("tensor", s_hf, max(0, s_hf.n - 1))
                k.wait("tensor", s_gel, max(0, s_gel.n - 1))
                for ic in range(DC):
                    k.op("tensor", lambda e, ba=ba, ic=ic, oc=oc, t0=t0:
                         e.matmul(ba, w1_sb[:, ic * DIMI2 + oc * 128:
                                            ic * DIMI2 + oc * 128 + 128],
                                  xn[:, ic * TLOC + t0: ic * TLOC + t0 + 256],
                                  start=(ic == 0), stop=(ic == DC - 1)))
                for ic in range(DC):
                    k.op("tensor", lambda e, bg=bg, ic=ic, oc=oc, t0=t0:
                         e.matmul(bg, w1_sb[:, ic * DIMI2 + (HC2 + oc) * 128:
                                            ic * DIMI2 + (HC2 + oc) * 128 + 128],
                                  xn[:, ic * TLOC + t0: ic * TLOC + t0 + 256],
                                  start=(ic == 0), stop=(ic == DC - 1)),
                         sig=s_mm1 if ic == DC - 1 else None)
                k.wait("scalar", s_mm1)
                k.wait("scalar", s_hf, max(0, s_hf.n - 1))
                k.op("scalar", lambda e, bg=bg, oc=oc, p=p: e.activation(
                    gsc[:, p * 256:p * 256 + 256], bg, AF.Gelu,
                    bias=b1g[:, oc:oc + 1]), s_gel)
                k.wait("vector", s_gel)
                k.wait("vector", s_hf, s_hf.n)
                k.chain("vector", [
                    lambda e, ba=ba, oc=oc: e.tensor_scalar(
                        av[:], ba, b1a[:, oc:oc + 1], None, ALU.add),
                    lambda e, oc=oc, p=p: e.tensor_mul(
                        h_sb[:, oc * 256:(oc + 1) * 256], av[:],
                        gsc[:, p * 256:p * 256 + 256]),
                ], s_hf)
            k.wait("tensor", s_hf)
            for oc in range(DC):
                b = ps_mm[oc % 2][:, :256]
                k.wait("tensor", s_of, k.sems["o_free"].n - 1)
                for ic in range(HC2):
                    k.op("tensor", lambda e, b=b, ic=ic, oc=oc: e.matmul(
                        b, w2_sb[:, ic * D + oc * 128: ic * D + oc * 128 + 128],
                        h_sb[:, ic * 256:(ic + 1) * 256],
                        start=(ic == 0), stop=(ic == HC2 - 1)),
                        sig=s_mm2 if ic == HC2 - 1 else None)
                k.wait("vector", s_mm2)
                k.op("vector", lambda e, b=b, oc=oc: e.tensor_scalar(
                    zo[:, oc * 256:(oc + 1) * 256], b, b2[:, oc:oc + 1],
                    None, ALU.add), s_of)
            k.wait("gpsimd", s_of)
            k.wait("gpsimd", s_st, 16 * tt2)
            k.dma("gpsimd", xr.rearrange("p (c t) -> p c t", c=DC),
                  zh_in.rearrange("(c p) t -> p c t", p=128)[:, :, L3H + t0:
                                                            L3H + t0 + 256],
                  s_xt)
            k.wait("vector", s_xt)
            for c in range(DC):
                k.op("vector", lambda e, c=c: e.tensor_add(
                    xr[:, c * 256:(c + 1) * 256], xr[:, c * 256:(c + 1) * 256],
                    zo[:, c * 256:(c + 1) * 256]), s_of)
            k.wait("gpsimd", s_of)
            k.dma("gpsimd",
                  z_out.rearrange("(c p) t -> p c t", p=128)[:, :, t0:t0 + 256],
                  xr.rearrange("p (c t) -> p c t", c=DC), s_st)
        k.wait("gpsimd", s_st)
        k.play()
    return nc



M_TT = 256
M_NT = TLOC // M_TT
MB_DBG = None
MB_PHASE = 9
MB_SKIP = ()
L3A_PIPE = 7
M_CD = 2176
M_CC = 17


def build_l0attn_a():
    nc = bass.Bass("TRN2", target_bir_lowering=False,
                   detect_race_conditions=_RD)
    x_in = nc.dram_tensor("x_in", [D, 2051], F32, kind="ExternalInput").ap()
    w_in = nc.dram_tensor("w_in", [D, 4256], F32, kind="ExternalInput").ap()
    convw = nc.dram_tensor("convw", [128, 68], F32, kind="ExternalInput").ap()
    convb = nc.dram_tensor("convb", [128, 17], F32, kind="ExternalInput").ap()
    dtb = nc.dram_tensor("dtb", [32, 1], F32, kind="ExternalInput").ap()
    alog = nc.dram_tensor("alog", [32, 1], F32, kind="ExternalInput").ap()
    xbc_out = nc.dram_tensor("xbc_out", [M_CD, TLOC], F32,
                             kind="ExternalOutput").ap()
    z_out = nc.dram_tensor("z_out", [2048, TLOC], F32,
                           kind="ExternalOutput").ap()
    dtad_out = nc.dram_tensor("dtad_out", [64, TLOC], F32,
                              kind="ExternalOutput").ap()

    TW = 259

    with ExitStack() as stack:
        k = KB(nc, stack)
        emit_consts(k, {"eps_ln": 1e-5})
        w_sb = k.sb("w_sb", [128, 8 * 4256], BF16)
        xt = k.sb("xt", [128, 8 * 260], F32)
        u_bf = k.sb("u_bf", [128, 8 * 260], BF16)
        xbcr = k.sb("xbcr", [128, 17 * 260], F32)
        xbcs = k.sb("xbcs", [128, 17 * M_TT], F32)
        zst = k.sb("zst", [128, 16 * M_TT], F32)
        dtr = k.sb("dtr", [32, M_TT], F32)
        dte = k.sb("dte", [32, M_TT], F32)
        dtad = k.sb("dtad", [64, M_TT], F32)
        cw = k.sb("cw", [128, 68], F32)
        cb = k.sb("cb", [128, 17], F32)
        dtbc_ = k.sb("dtbc", [32, 1], F32)
        alg = k.sb("alg", [32, 1], F32)
        aneg = k.sb("aneg", [32, 1], F32)
        cacc = k.sb("cacc", [128, M_TT], F32)
        cm0 = k.sb("cm0", [128, M_TT], F32)

        banks = [k.ps(f"pp{i}", [128, 512]) for i in range(4)]

        s_ld = k.sem("ld")
        s_ld2 = k.sem("ld2")
        s_xt = k.sem("x_ld")
        s_mm = k.sem("mm")
        s_free = k.sem("free")
        s_cv = k.sem("cv")
        s_sl = k.sem("sl")
        s_dt = k.sem("dt")
        s_st = k.sem("st")
        vself = k.sem("vector_self")

        k.dma("gpsimd", w_sb[:].rearrange("p (c x) -> p c x", c=8),
              w_in.rearrange("(c p) x -> p c x", p=128), s_ld)
        for dst, src in ((cw, convw), (cb, convb), (dtbc_, dtb), (alg, alog)):
            k.dma("sync", dst[:], src, s_ld2)
        for eng in ("scalar", "vector"):
            k.wait(eng, s_ld2)
        k.wait("scalar", k.sems["consts"])
        k.op("scalar", lambda e: e.activation(aneg[:], alg[:], AF.Exp), s_dt)
        k.wait("vector", s_dt)
        k.op("vector", lambda e: e.tensor_scalar_mul(aneg[:], aneg[:], -1.0),
             s_dt)
        k.wait("tensor", s_ld)

        for it in range(M_NT):
            t0 = it * M_TT
            k.wait("gpsimd", k.sem("ln_apply_s"), 8 * it)
            k.wait("vector", s_st, 48 * it)
            k.dma("gpsimd", xt[:].rearrange("p (c t) -> p c t", c=8)[:, :, :TW],
                  x_in.rearrange("(c p) t -> p c t", p=128)[:, :, t0:t0 + TW],
                  s_xt)
            for eng in ("scalar", "tensor", "vector"):
                k.wait(eng, s_xt)
            emit_ln_tile(k, lambda c: xt[:, c * 260:c * 260 + TW],
                         lambda c: u_bf[:, c * 260:c * 260 + TW],
                         tt=TW, eps_name="eps_ln", nchunks=8,
                         pre_apply_waits=[(s_mm, 34 * it)])
            k.wait("tensor", k.sems["ln_apply_s"])

            def epi(oc, ps_ap, mmv, it=it):
                k.wait("vector", s_mm, mmv)
                if oc < 16:
                    k.op("vector", lambda e, oc=oc: e.tensor_copy(
                        zst[:, oc * M_TT:(oc + 1) * M_TT], ps_ap[:, 3:TW]),
                        s_free)
                elif oc < 33:
                    c = oc - 16
                    k.op("vector", lambda e, c=c: e.tensor_copy(
                        xbcr[:, c * 260:c * 260 + TW], ps_ap[:, :TW]), s_free)
                else:
                    k.wait("vector", s_dt)
                    k.op("vector", lambda e: e.tensor_copy(
                        dtr[:], ps_ap[0:32, 3:TW]), s_free)

            freed = {}
            base_free = s_free.n
            for oc in range(34):
                b = banks[oc % 4]
                wid = 128 if oc < 33 else 32
                if oc >= 4:
                    k.wait("tensor", s_free, freed[oc - 4])
                for ic in range(8):
                    k.op("tensor", lambda e, b=b, ic=ic, oc=oc, wid=wid:
                         e.matmul(b[:wid, :TW],
                                  w_sb[:, ic * 4256 + oc * 128:
                                       ic * 4256 + oc * 128 + wid],
                                  u_bf[:, ic * 260:ic * 260 + TW],
                                  start=(ic == 0), stop=(ic == 7)),
                         sig=s_mm if ic == 7 else None)
                epi(oc, b[:], s_mm.n)
                freed[oc] = s_free.n

            k.wait("vector", s_free)
            k.wait("scalar", s_free)
            for c in range(M_CC):
                r = lambda j, c=c: xbcr[:, c * 260 + j:c * 260 + j + M_TT]
                k.wait("vector", s_sl)
                k.chain("vector", [
                    lambda e, c=c, r=r: e.tensor_scalar(
                        cacc[:], r(0), cw[:, c * 4:c * 4 + 1], None, ALU.mult),
                    lambda e, c=c, r=r: e.tensor_scalar(
                        cm0[:], r(1), cw[:, c * 4 + 1:c * 4 + 2], None,
                        ALU.mult),
                    lambda e: e.tensor_add(cacc[:], cacc[:], cm0[:]),
                    lambda e, c=c, r=r: e.tensor_scalar(
                        cm0[:], r(2), cw[:, c * 4 + 2:c * 4 + 3], None,
                        ALU.mult),
                    lambda e: e.tensor_add(cacc[:], cacc[:], cm0[:]),
                    lambda e, c=c, r=r: e.tensor_scalar(
                        cm0[:], r(3), cw[:, c * 4 + 3:c * 4 + 4], None,
                        ALU.mult),
                    lambda e: e.tensor_add(cacc[:], cacc[:], cm0[:]),
                ], s_cv)
                k.wait("scalar", s_cv)
                k.wait("scalar", s_st, 48 * it)
                k.op("scalar", lambda e, c=c: e.activation(
                    xbcs[:, c * M_TT:(c + 1) * M_TT], cacc[:], AF.Silu,
                    bias=cb[:, c:c + 1]), s_sl)
            k.wait("scalar", s_dt)
            k.chain("scalar", [
                lambda e: e.activation(dte[:], dtr[:], AF.Exp,
                                       bias=dtbc_[:, :1]),
                lambda e: e.activation(dtad[0:32, :], dte[:], AF.Ln,
                                       bias=k.sbuf["ones_col"][0:32, :1]),
            ], s_dt)
            k.wait("vector", s_dt)
            k.op("vector", lambda e: e.tensor_scalar(
                dtad[32:64, :], dtad[0:32, :], aneg[:, :1], None, ALU.mult),
                s_dt)
            k.wait("sync", s_sl, 17 * (it + 1))
            k.wait("sync", s_dt, 2 + 2 * (it + 1))
            k.wait("sync", s_free)
            k.dma("sync", xbc_out.rearrange("(c p) t -> p c t", p=128)
                  [:, :, t0:t0 + M_TT],
                  xbcs[:].rearrange("p (c t) -> p c t", c=M_CC), s_st)
            k.dma("sync", z_out.rearrange("(c p) t -> p c t", p=128)
                  [:, :, t0:t0 + M_TT],
                  zst[:].rearrange("p (c t) -> p c t", c=16), s_st)
            k.dma("sync", dtad_out[:, t0:t0 + M_TT], dtad[:], s_st)
        k.wait("sync", s_st)
        k.play()
    return nc


def build_l0attn_b():
    nc = bass.Bass("TRN2", target_bir_lowering=False,
                   detect_race_conditions=_RD)
    x_in = nc.dram_tensor("x_in", [D, TLOC], F32, kind="ExternalInput").ap()
    xbc = nc.dram_tensor("xbc", [M_CD, TLOC], F32, kind="ExternalInput").ap()
    z_in = nc.dram_tensor("z_in", [2048, TLOC], F32, kind="ExternalInput").ap()
    dtad_in = nc.dram_tensor("dtad", [64, TLOC], F32,
                             kind="ExternalInput").ap()
    w_out = nc.dram_tensor("w_out", [2048, D], F32, kind="ExternalInput").ap()
    dcol_in = nc.dram_tensor("dcol", [128, 16], F32, kind="ExternalInput").ap()
    nw_in = nc.dram_tensor("nwcol", [128, 16], F32, kind="ExternalInput").ap()
    iden_in = nc.dram_tensor("iden", [128, 128], F32, kind="ExternalInput").ap()
    tri_in = nc.dram_tensor("tri64", [64, 64], F32, kind="ExternalInput").ap()
    self_in = nc.dram_tensor("self_f", [32, 2048], F32,
                             kind="ExternalInput").ap()
    rin = nc.dram_tensor("r_in", [64, 2048], F32, kind="ExternalInput").ap()
    x_out = nc.dram_tensor("x_out", [D, TLOC], F32, kind="ExternalOutput").ap()
    r_out = nc.dram_tensor("r_out", [64, 2048], F32,
                           kind="ExternalOutput").ap()

    with ExitStack() as stack:
        k = KB(nc, stack)
        emit_consts(k, {"eps_rms": 1e-5})
        wo_sb = k.sb("wo_sb", [128, 16 * 1024], BF16)
        xbc_t = k.sb("xbc_t", [128, 17 * M_TT], F32)
        z_t = k.sb("z_t", [128, 16 * M_TT], F32)
        x_t = k.sb("x_t", [128, 8 * M_TT], F32)
        dtad_t = k.sb("dtad_t", [64, M_TT], F32)
        cc0 = k.sb("cc0", [64, M_TT], F32)
        cc0b = k.sb("cc0b", [64, M_TT], BF16)
        dtT = k.sb("dtT", [64, 64], F32)
        acT = k.sb("acT", [64, 32], F32)
        acfm = k.sb("acfm", [32, 64], F32)
        nacfm = k.sb("nacfm", [32, 64], F32)
        dscfm = k.sb("dscfm", [32, 64], F32)
        combfm = k.sb("combfm", [32, 64], F32)
        rowf = k.sb("rowf", [1, 2048], F32)
        asb = k.sb("asb", [32, 2048], F32)
        dtb_b = k.sb("dtb_b", [64, 2048], F32)
        cmb_b = k.sb("cmb_b", [64, 2048], F32)
        eas_b = k.sb("eas_b", [64, 2048], F32)
        GtM8 = k.sb("GtM8", [64, 512], BF16)
        BTbf = k.sb("BTbf", [64, 64], BF16)
        Lt8 = k.sb("Lt8", [64, 512], BF16)
        sc8 = k.sb("sc8", [64, 512], F32)
        Mt08 = k.sb("Mt08", [64, 512], BF16)
        ep8 = k.sb("ep8", [64, 512], F32)
        vsc = k.sb("vsc", [64, 512], F32)
        xsT = k.sb("xsT", [64, 2048], BF16)
        xsd = k.sb("xsd", [64, 2048], BF16)
        y_tm = k.sb("y_tm", [64, 2048], F32)
        y_fm = k.sb("y_fm", [128, 16 * M_TT], F32)
        Rf = k.sb("Rf", [64, 2048], F32)
        Rbf = k.sb("Rbf", [64, 2048], BF16)
        iden = k.sb("iden_s", [128, 128], F32)
        tri64 = k.sb("tri64_s", [64, 64], F32)
        self_f = k.sb("self_s", [32, 2048], F32)
        dcol = k.sb("dcol_s", [128, 16], F32)
        nwcol = k.sb("nw_s", [128, 16], F32)
        o3264 = k.sb("o3264", [32, 64], F32)
        gsc = k.sb("gsc", [128, M_TT], F32)
        zsl = k.sb("zsl", [128, M_TT], F32)
        sq = k.sb("sq", [128, M_TT], F32)
        rrow = k.sb("rrow", [1, M_TT], F32)
        ynbf = k.sb("ynbf", [128, 16 * M_TT], BF16)
        outs = k.sb("outs", [128, 8 * M_TT], F32)

        pm = [k.ps(f"pm{i}", [128, 128]) for i in range(2)]
        pms = k.ps("pms", [128, 512])
        pbc = k.ps("pbc", [128, 512])
        prow = k.ps("prow", [128, 512])
        pcol = k.ps("pcol", [128, 512])
        pY = k.ps("pY", [128, 512])
        pV = k.ps("pV", [128, 512])

        s_ld = k.sem("ld")
        s_xt = k.sem("x_ld")
        s_mv = k.sem("mv")
        s_st = k.sem("st")
        s_t1 = k.sem("t1")
        s_tf = k.sem("tf")
        s_tb = k.sem("tb")
        s_tg = k.sem("tg")
        s_bc = k.sem("bc")
        s_rc = k.sem("rc")
        s_ms = k.sem("ms")
        s_v1 = k.sem("v1")
        s_vbc = k.sem("vbc")
        s_vf = k.sem("vf")
        s_vb = k.sem("vb")
        s_v3 = k.sem("v3")
        s_epi = k.sem("epi")
        s_vr = k.sem("vr")
        s_rbf = k.sem("rbf")
        s_gt = k.sem("gt")
        s_cc = k.sem("cc")
        s_ds = k.sem("ds")
        s_lt = k.sem("lt")
        s_ep = k.sem("ep")
        s_ea = k.sem("ea")
        s_asb = k.sem("asb_s")
        s_zs = k.sem("zs")
        s_sq = k.sem("sq_s")
        s_gm = k.sem("gm")
        s_ap2 = k.sem("ap2")
        s_mm = k.sem("mm")
        s_free = k.sem("free")

        k.dma("gpsimd", wo_sb[:].rearrange("p (c x) -> p c x", c=16),
              w_out.rearrange("(c p) x -> p c x", p=128), s_ld)
        for dst, src in ((iden, iden_in), (tri64, tri_in), (self_f, self_in),
                         (dcol, dcol_in), (nwcol, nw_in), (Rf, rin)):
            k.dma("gpsimd", dst[:], src, s_ld)
        s_o = k.sem("o_s")
        k.op("gpsimd", lambda e: e.memset(o3264[:], 1.0), s_o)
        k.wait("tensor", s_o)
        k.wait("vector", s_ld)
        k.op("vector", lambda e: e.tensor_copy(Rbf[:], Rf[:]), s_rbf)
        for eng in ("tensor", "scalar", "sync"):
            k.wait(eng, s_ld)
        iden64 = iden[0:64, 0:64]

        for it in range(M_NT):
            t0 = it * M_TT
            for eng_s in (s_gm, s_vb, s_zs, s_free, s_bc, s_t1, s_cc, s_v1):
                k.wait("gpsimd", eng_s)
            k.dma("gpsimd", xbc_t[:].rearrange("p (c t) -> p c t", c=17),
                  xbc.rearrange("(c p) t -> p c t", p=128)[:, :, t0:t0 + M_TT],
                  s_xt)
            k.dma("gpsimd", z_t[:].rearrange("p (c t) -> p c t", c=16),
                  z_in.rearrange("(c p) t -> p c t", p=128)[:, :, t0:t0 + M_TT],
                  s_xt)
            k.dma("gpsimd", x_t[:].rearrange("p (c t) -> p c t", c=8),
                  x_in.rearrange("(c p) t -> p c t", p=128)[:, :, t0:t0 + M_TT],
                  s_xt)
            k.dma("gpsimd", dtad_t[:], dtad_in[:, t0:t0 + M_TT], s_xt)
            for eng in ("tensor", "scalar", "vector", "sync"):
                k.wait(eng, s_xt)
            k.chain("vector", [
                lambda e: e.tensor_copy(cc0[:], xbc_t[64:128,
                                                      16 * M_TT:17 * M_TT]),
                lambda e: e.tensor_copy(cc0b[:], cc0[:]),
            ], s_cc, inc=2)

            for c in range(4):
                c64 = c * 64
                cs = slice(c64, c64 + 64)

                k.wait("tensor", s_v1)
                k.wait("tensor", s_gt, s_gt.n)
                if "dtadT" not in MB_SKIP:
                    k.op("tensor", lambda e, cs=cs: e.transpose(
                        pms[0:64, 0:64], dtad_t[:, cs], iden64), s_t1)
                    k.wait("vector", s_t1)
                    k.op("vector", lambda e: e.tensor_copy(dtT[:],
                                                           pms[0:64, 0:64]),
                         s_v1)
                if "acum" not in MB_SKIP:
                    k.wait("tensor", s_v1)
                    k.op("tensor", lambda e: e.matmul(
                        pms[0:64, 64:96], tri64[:], dtT[:, 32:64],
                        start=True, stop=True), s_t1)
                    k.wait("vector", s_t1)
                    k.op("vector", lambda e: e.tensor_copy(
                        acT[:], pms[0:64, 64:96]), s_v1)
                if "acfm" not in MB_SKIP:
                    k.wait("tensor", s_v1)
                    k.op("tensor", lambda e: e.transpose(
                        pms[0:32, 96:160], acT[:], iden64), s_t1)
                    k.wait("vector", s_t1)
                    k.wait("vector", s_mv)
                    k.wait("vector", s_ds)
                    k.wait("vector", s_rc)
                    k.wait("vector", s_bc)
                    k.chain("vector", [
                        lambda e: e.tensor_copy(acfm[:], pms[0:32, 96:160]),
                        lambda e: e.tensor_scalar_mul(nacfm[:], acfm[:], -1.0),
                        lambda e: e.tensor_scalar(asb[:], self_f[:],
                                                  acfm[:, 63:64], None,
                                                  ALU.mult),
                    ], s_asb)
                    k.op("vector", lambda e: e.sem_inc(s_v1.h, 1))
                    s_v1.n += 1
                if "rowf" not in MB_SKIP:
                    k.wait("sync", s_asb)
                    k.wait("sync", s_rc)
                    k.dma("sync", rowf[:1, :].rearrange("o (h t) -> o h t",
                                                        h=32),
                          acfm[:], s_mv)
                if "dscfm" not in MB_SKIP:
                    k.wait("scalar", s_asb)
                    k.op("scalar", lambda e: e.activation(
                        dscfm[:], acfm[:], AF.Exp, bias=acfm[:, 63:64],
                        scale=-1.0), s_ds)
                    k.wait("vector", s_ds)
                    k.op("vector", lambda e, cs=cs: e.tensor_mul(
                        combfm[:], dscfm[:], dtad_t[0:32, cs]), s_v1)

                k.wait("tensor", s_cc)
                k.wait("tensor", s_gt, s_gt.n)
                k.wait("tensor", s_mv)
                k.op("tensor", lambda e, cs=cs, c64=c64: e.matmul(
                    pms[0:64, 256:320], xbc_t[0:64, 16 * M_TT + c64:
                                              16 * M_TT + c64 + 64],
                    cc0[:, cs], start=True, stop=True), s_t1)
                k.wait("vector", s_t1)
                k.wait("vector", s_mv)
                k.op("vector", lambda e: e.tensor_mul(
                    GtM8[:, 0:64], pms[0:64, 256:320], tri64[:]), s_gt)
                if MB_DBG and MB_DBG[0] == "psgt" and MB_DBG[1] == it \
                        and MB_DBG[2] == c:
                    k.op("vector", lambda e: e.tensor_copy(
                        vsc[:, 0:64], pms[0:64, 256:320]), s_v3)
                k.wait("sync", s_gt)
                for j in range(1, 8 if "rep" not in MB_SKIP else 1):
                    k.dma("sync", GtM8[:, j * 64:(j + 1) * 64],
                          GtM8[:, 0:64], s_mv)
                k.wait("tensor", s_v1)
                k.op("tensor", lambda e, c64=c64: e.transpose(
                    pms[0:64, 160:224],
                    xbc_t[0:64, 16 * M_TT + c64:16 * M_TT + c64 + 64],
                    iden64), s_t1)
                k.wait("vector", s_t1)
                k.wait("vector", s_tg)
                k.op("vector", lambda e: e.tensor_copy(BTbf[:],
                                                       pms[0:64, 160:224]),
                     s_v1)

                for r in range(4 if MB_PHASE >= 2 else 0):
                    k.wait("tensor", s_vbc)
                    k.wait("tensor", s_ea)
                    k.op("tensor", lambda e, r=r, cs=cs: e.matmul(
                        pbc[0:64, :], dtad_t[0:32, cs],
                        self_f[:, r * 512:(r + 1) * 512],
                        start=True, stop=True), s_bc)
                    k.wait("vector", s_bc)
                    k.op("vector", lambda e, r=r: e.tensor_copy(
                        dtb_b[:, r * 512:(r + 1) * 512], pbc[0:64, :]), s_vbc)
                k.wait("tensor", s_v1)
                for r in range(4 if MB_PHASE >= 2 else 0):
                    k.wait("tensor", s_vbc)
                    k.op("tensor", lambda e, r=r: e.matmul(
                        pbc[0:64, :], combfm[:],
                        self_f[:, r * 512:(r + 1) * 512],
                        start=True, stop=True), s_bc)
                    k.wait("vector", s_bc)
                    k.op("vector", lambda e, r=r: e.tensor_copy(
                        cmb_b[:, r * 512:(r + 1) * 512], pbc[0:64, :]), s_vbc)
                k.wait("tensor", s_asb)
                k.wait("scalar", s_vr, s_vr.n)
                for r in range(4 if MB_PHASE >= 2 else 0):
                    k.wait("tensor", s_ea)
                    k.wait("tensor", s_vbc)
                    k.op("tensor", lambda e, r=r: e.matmul(
                        pbc[0:64, :], o3264[:],
                        asb[:, r * 512:(r + 1) * 512],
                        start=True, stop=True), s_bc)
                    k.wait("scalar", s_bc)
                    k.op("scalar", lambda e, r=r: e.activation(
                        eas_b[:, r * 512:(r + 1) * 512], pbc[0:64, :],
                        AF.Exp), s_ea)

                k.wait("vector", s_tg)
                k.wait("tensor", s_vbc)
                for j in range(16 if MB_PHASE >= 3 else 0):
                    k.wait("tensor", s_vf)
                    k.wait("tensor", s_vb)
                    k.op("tensor", lambda e, j=j, c64=c64: e.transpose(
                        pm[j % 2][0:64, 0:128],
                        xbc_t[:, j * M_TT + c64:j * M_TT + c64 + 64],
                        iden[:]), s_tf)
                    k.wait("vector", s_tf)
                    k.chain("vector", [
                        lambda e, j=j: e.tensor_mul(
                            xsT[:, j * 128:(j + 1) * 128], pm[j % 2][0:64, :],
                            dtb_b[:, j * 128:(j + 1) * 128]),
                        lambda e, j=j: e.tensor_mul(
                            xsd[:, j * 128:(j + 1) * 128], pm[j % 2][0:64, :],
                            cmb_b[:, j * 128:(j + 1) * 128]),
                    ], s_vf)

                for g in range(4 if MB_PHASE >= 4 else 0):
                    g512 = slice(g * 512, (g + 1) * 512)
                    k.wait("tensor", s_v3)
                    k.wait("tensor", s_ep)
                    k.op("tensor", lambda e, g=g: e.matmul(
                        prow[0:64, :], k.sbuf["ones_row"][0:1, 0:64],
                        rowf[0:1, g * 512:(g + 1) * 512],
                        start=True, stop=True), s_rc)
                    k.op("tensor", lambda e, g=g: e.matmul(
                        pcol[0:64, :], nacfm[:],
                        self_f[:, g * 512:(g + 1) * 512],
                        start=True, stop=True), s_rc)
                    k.wait("vector", s_rc)
                    k.wait("vector", s_lt, s_lt.n)
                    k.chain("vector", [
                        lambda e: e.tensor_copy(sc8[:], prow[0:64, :]),
                        lambda e: e.tensor_add(sc8[:], sc8[:], pcol[0:64, :]),
                        lambda e: e.tensor_scalar(sc8[:], sc8[:], 0.0, None,
                                                  ALU.min),
                    ], s_v3)
                    k.wait("scalar", s_v3)
                    k.op("scalar", lambda e: e.activation(Lt8[:], sc8[:],
                                                          AF.Exp), s_lt)
                    k.wait("scalar", s_epi, s_epi.n)
                    k.op("scalar", lambda e: e.activation(
                        ep8[:], pcol[0:64, :], AF.Exp, scale=-1.0), s_ep)
                    k.wait("vector", s_lt)
                    k.wait("vector", s_mv)
                    k.op("vector", lambda e: e.tensor_mul(Mt08[:], Lt8[:],
                                                          GtM8[:]), s_v3)
                    k.wait("tensor", s_v3)
                    k.wait("tensor", s_epi, s_epi.n)
                    k.wait("tensor", s_free)
                    k.wait("tensor", s_vf)
                    k.wait("tensor", s_rbf, s_rbf.n)
                    k.wait("tensor", s_vr)
                    k.wait("tensor", s_ea)
                    for h8 in range(8):
                        h = 8 * g + h8
                        k.op("tensor", lambda e, h8=h8, h=h: e.matmul(
                            pY[0:64, h8 * 64:(h8 + 1) * 64],
                            Mt08[:, h8 * 64:(h8 + 1) * 64],
                            xsT[:, h * 64:(h + 1) * 64],
                            start=True, stop=True),
                            sig=s_tg if h8 == 7 else None)
                    for h8 in range(8):
                        h = 8 * g + h8
                        k.op("tensor", lambda e, h8=h8, h=h, cs=cs: e.matmul(
                            pV[0:64, h8 * 64:(h8 + 1) * 64], cc0b[:, cs],
                            Rbf[:, h * 64:(h + 1) * 64],
                            start=True, stop=True),
                            sig=s_tg if h8 == 7 else None)
                    for h8 in range(8):
                        h = 8 * g + h8
                        k.op("tensor", lambda e, h8=h8, h=h: e.matmul(
                            pbc[0:64, h8 * 64:(h8 + 1) * 64], BTbf[:],
                            xsd[:, h * 64:(h + 1) * 64],
                            start=True, stop=True),
                            sig=s_tg if h8 == 7 else None)
                    k.wait("vector", s_tg)
                    k.wait("vector", s_ep)
                    k.wait("vector", s_tb)
                    k.chain("vector", [
                        lambda e: e.tensor_mul(vsc[:], pV[0:64, :], ep8[:]),
                        lambda e, g512=g512: e.tensor_add(
                            y_tm[:, g512], pY[0:64, :], vsc[:]),
                    ], s_epi)
                    k.wait("vector", s_ea)
                    k.chain("vector", [
                        lambda e, g512=g512: e.tensor_mul(
                            Rf[:, g512], Rf[:, g512], eas_b[:, g512]),
                        lambda e, g512=g512: e.tensor_add(
                            Rf[:, g512], Rf[:, g512], pbc[0:64, :]),
                    ], s_vr)
                k.wait("vector", s_tg)
                k.wait("vector", s_vr)
                k.op("vector", lambda e: e.tensor_copy(Rbf[:], Rf[:]), s_rbf)

                if MB_DBG and MB_DBG[1] == it and MB_DBG[2] == c:
                    for sem in (s_epi, s_vr, s_vf, s_v3, s_gt, s_lt, s_asb):
                        k.wait("gpsimd", sem)
                    if MB_DBG[0] == "psgt":
                        k.dma("gpsimd", r_out[:64, :64],
                              k.sbuf["vsc"][:, 0:64], s_xt)
                    else:
                        buf = k.sbuf[MB_DBG[0]]
                        k.dma("gpsimd", r_out[:buf.shape[0], :buf.shape[1]],
                              buf[:], s_xt)
                    for eng in ("tensor", "vector", "scalar", "sync"):
                        k.wait(eng, s_xt)
                k.wait("tensor", s_epi, s_epi.n)
                for j in range(16 if MB_PHASE >= 5 else 0):
                    k.wait("tensor", s_vb)
                    k.wait("tensor", s_vf)
                    k.op("tensor", lambda e, j=j: e.transpose(
                        pm[j % 2][:, 0:64], y_tm[:, j * 128:(j + 1) * 128],
                        iden64), s_tb)
                    k.wait("vector", s_tb)
                    k.op("vector", lambda e, j=j, c64=c64: e.tensor_copy(
                        y_fm[:, j * M_TT + c64:j * M_TT + c64 + 64],
                        pm[j % 2][:, 0:64]), s_vb)

            for j in range(16 if MB_PHASE >= 6 else 0):
                k.wait("vector", s_zs, s_zs.n)
                k.wait("vector", s_gm, s_gm.n)
                k.chain("vector", [
                    lambda e, j=j: e.tensor_scalar(
                        gsc[:], xbc_t[:, j * M_TT:(j + 1) * M_TT],
                        dcol[:, j:j + 1], None, ALU.mult),
                    lambda e, j=j: e.tensor_add(
                        y_fm[:, j * M_TT:(j + 1) * M_TT],
                        y_fm[:, j * M_TT:(j + 1) * M_TT], gsc[:]),
                ], s_gm)
                k.wait("scalar", s_gm)
                k.op("scalar", lambda e, j=j: e.activation(
                    zsl[:], z_t[:, j * M_TT:(j + 1) * M_TT], AF.Silu), s_zs)
                k.wait("vector", s_zs)
                k.op("vector", lambda e, j=j: e.tensor_mul(
                    y_fm[:, j * M_TT:(j + 1) * M_TT],
                    y_fm[:, j * M_TT:(j + 1) * M_TT], zsl[:]), s_gm)
            k.wait("scalar", s_v3)
            for j in range(16 if MB_PHASE >= 6 else 0):
                k.wait("scalar", s_gm, s_gm.n)
                k.wait("scalar", s_ms)
                k.op("scalar", lambda e, j=j: e.activation(
                    sq[:], y_fm[:, j * M_TT:(j + 1) * M_TT], AF.Square), s_sq)
                k.wait("tensor", s_sq)
                k.op("tensor", lambda e, j=j: e.matmul(
                    prow[0:1, 0:M_TT], k.sbuf["ones_col"][:], sq[:],
                    start=(j == 0), stop=(j == 15)), s_ms)
            if MB_PHASE >= 6:
              k.wait("vector", s_ms)
              k.op("vector", lambda e: e.tensor_scalar_mul(
                rrow[:1, :], prow[0:1, 0:M_TT], 1.0 / 2048.0), s_v3)
              k.wait("scalar", s_v3)
              k.chain("scalar", [
                  lambda e: e.activation(rrow[:1, :], rrow[:1, :], AF.Ln,
                                         bias=k.sbuf["c_eps_rms"][:1, :1]),
                  lambda e: e.activation(rrow[:1, :], rrow[:1, :], AF.Exp,
                                         scale=-0.5),
              ], s_zs)
              k.wait("tensor", s_zs, s_zs.n)
              k.wait("tensor", s_ep)
              k.op("tensor", lambda e: e.matmul(
                  pcol[:, 0:M_TT], k.sbuf["ones_row"][:1, :], rrow[:1, :],
                  start=True, stop=True), s_rc)
              k.wait("vector", s_rc)
              for j in range(16):
                  k.wait("vector", s_ap2, s_ap2.n)
                  k.chain("vector", [
                      lambda e, j=j: e.tensor_mul(
                          gsc[:], y_fm[:, j * M_TT:(j + 1) * M_TT],
                          pcol[:, 0:M_TT]),
                      lambda e, j=j: e.tensor_scalar(
                          ynbf[:, j * M_TT:(j + 1) * M_TT], gsc[:],
                          nwcol[:, j:j + 1], None, ALU.mult),
                  ], s_ap2)
              k.op("vector", lambda e: e.sem_inc(s_ep.h, 1))
              s_ep.n += 1
              k.wait("tensor", s_ap2, s_ap2.n)
              banks2 = [pY, pV]
              freed = {}
              for oc in range(8):
                  b = banks2[oc % 2]
                  if oc >= 2:
                      k.wait("tensor", s_free, freed[oc - 2])
                  for ic in range(16):
                      k.op("tensor", lambda e, b=b, ic=ic, oc=oc: e.matmul(
                          b[:, 0:M_TT],
                          wo_sb[:, ic * 1024 + oc * 128:
                                ic * 1024 + oc * 128 + 128],
                          ynbf[:, ic * M_TT:(ic + 1) * M_TT],
                          start=(ic == 0), stop=(ic == 15)),
                          sig=s_mm if ic == 15 else None)
                  k.wait("vector", s_mm, s_mm.n)
                  k.wait("vector", s_st, 16 * it)
                  k.op("vector", lambda e, b=b, oc=oc: e.tensor_add(
                      outs[:, oc * M_TT:(oc + 1) * M_TT], b[:, 0:M_TT],
                      x_t[:, oc * M_TT:(oc + 1) * M_TT]), s_free)
                  freed[oc] = s_free.n
              k.wait("sync", s_free)
              k.dma("sync", x_out.rearrange("(c p) t -> p c t", p=128)
                    [:, :, t0:t0 + M_TT],
                    outs[:].rearrange("p (c t) -> p c t", c=8), s_st)
        if not MB_DBG:
            k.wait("sync", s_vr)
            k.dma("sync", r_out, Rf[:], s_st)
        k.wait("sync", s_st)
        k.play()
    return nc




def build_l1attn_a():
    nc = bass.Bass("TRN2", target_bir_lowering=False,
                   detect_race_conditions=_RD)
    x_in = nc.dram_tensor("x_in", [D, TLOC], F32, kind="ExternalInput").ap()
    wq = nc.dram_tensor("wq", [D, D], F32, kind="ExternalInput").ap()
    wk = nc.dram_tensor("wk", [D, D], F32, kind="ExternalInput").ap()
    wv = nc.dram_tensor("wv", [D, D], F32, kind="ExternalInput").ap()
    wf = nc.dram_tensor("wf", [D, 256], F32, kind="ExternalInput").ap()
    q_out = nc.dram_tensor("q_out", [D, TLOC], F32, kind="ExternalOutput").ap()
    k_out = nc.dram_tensor("k_out", [D, TLOC], F32, kind="ExternalOutput").ap()
    v_out = nc.dram_tensor("v_out", [D, TLOC], F32, kind="ExternalOutput").ap()
    f_out = nc.dram_tensor("f_out", [256, TLOC], F32,
                           kind="ExternalOutput").ap()

    with ExitStack() as stack:
        k = KB(nc, stack)
        emit_consts(k, {"eps_ln": 1e-5})
        w_sb = k.sb("w_sb", [128, 8 * 3328], BF16)
        xt = k.sb("xt", [128, 8 * TT], F32)
        u_bf = k.sb("u_bf", [128, 8 * TT], BF16)
        qs = k.sb("qs", [128, 8 * TT], F32)
        ks = k.sb("ks", [128, 8 * TT], F32)
        vs = k.sb("vs", [128, 8 * TT], F32)
        fs = k.sb("fs", [128, 2 * TT], F32)
        fsg = k.sb("fsg", [128, TT], F32)

        banks = [k.ps(f"pp{i}", [128, 512]) for i in range(4)]

        s_ld = k.sem("ld")
        s_xt = k.sem("x_ld")
        s_mm = k.sem("mm")
        s_free = k.sem("free")
        s_sg = k.sem("sg")
        s_st = k.sem("st")

        k.dma("gpsimd", w_sb[:, 0:8 * 1024].rearrange("p (c x) -> p c x", c=8),
              wq.rearrange("(c p) x -> p c x", p=128), s_ld)
        k.dma("gpsimd", w_sb[:, 8 * 1024:8 * 2048]
              .rearrange("p (c x) -> p c x", c=8),
              wk.rearrange("(c p) x -> p c x", p=128), s_ld)
        k.dma("gpsimd", w_sb[:, 8 * 2048:8 * 3072]
              .rearrange("p (c x) -> p c x", c=8),
              wv.rearrange("(c p) x -> p c x", p=128), s_ld)
        k.dma("gpsimd", w_sb[:, 8 * 3072:8 * 3328]
              .rearrange("p (c x) -> p c x", c=8),
              wf.rearrange("(c p) x -> p c x", p=128), s_ld)
        k.wait("tensor", s_ld)

        def wslot(ic, oc):
            if oc < 8:
                off = ic * 1024 + oc * 128
            elif oc < 16:
                off = 8 * 1024 + ic * 1024 + (oc - 8) * 128
            elif oc < 24:
                off = 8 * 2048 + ic * 1024 + (oc - 16) * 128
            else:
                off = 8 * 3072 + ic * 256 + (oc - 24) * 128
            return w_sb[:, off:off + 128]

        for it in range(NTT):
            t0 = it * TT
            k.wait("gpsimd", k.sem("ln_apply_s"), 8 * it)
            k.dma("gpsimd", xt[:].rearrange("p (c t) -> p c t", c=8),
                  x_in.rearrange("(c p) t -> p c t", p=128)[:, :, t0:t0 + TT],
                  s_xt)
            for eng in ("scalar", "tensor", "vector"):
                k.wait(eng, s_xt)
            emit_ln_tile(k, lambda c: xt[:, c * TT:(c + 1) * TT],
                         lambda c: u_bf[:, c * TT:(c + 1) * TT],
                         tt=TT, eps_name="eps_ln", nchunks=8,
                         pre_apply_waits=[(s_mm, 26 * it)])
            k.wait("tensor", k.sems["ln_apply_s"])

            def epi(oc, ps_ap, mmv, it=it):
                if oc < 16:
                    dst = (qs if oc < 8 else ks)[:, (oc % 8) * TT:
                                                 (oc % 8 + 1) * TT]
                    k.wait("scalar", s_mm, mmv)
                    k.wait("scalar", s_st, 64 * it)
                    k.op("scalar", lambda e, dst=dst: e.activation(
                        dst, ps_ap[:, :TT], AF.Silu), s_sg)
                    return (s_sg, s_sg.n)
                elif oc < 24:
                    k.wait("vector", s_mm, mmv)
                    k.wait("vector", s_st, 64 * it)
                    k.op("vector", lambda e, oc=oc: e.tensor_copy(
                        vs[:, (oc - 16) * TT:(oc - 15) * TT], ps_ap[:, :TT]),
                        s_free)
                    return (s_free, s_free.n)
                else:
                    k.wait("scalar", s_mm, mmv)
                    k.wait("scalar", s_st, 64 * it)
                    k.wait("scalar", s_sg, s_sg.n)
                    k.chain("scalar", [
                        lambda e: e.activation(fsg[:], ps_ap[:, :TT],
                                               AF.Sigmoid),
                        lambda e, oc=oc: e.activation(
                            fs[:, (oc - 24) * TT:(oc - 23) * TT], fsg[:],
                            AF.Ln),
                    ], s_sg)
                    return (s_sg, s_sg.n)

            freed = {}
            for oc in range(26):
                b = banks[oc % 4]
                if oc >= 4:
                    fsem, fval = freed[oc - 4]
                    k.wait("tensor", fsem, fval)
                for ic in range(8):
                    k.op("tensor", lambda e, b=b, ic=ic, oc=oc: e.matmul(
                        b[:, :TT], wslot(ic, oc),
                        u_bf[:, ic * TT:(ic + 1) * TT],
                        start=(ic == 0), stop=(ic == 7)),
                        sig=s_mm if ic == 7 else None)
                freed[oc] = epi(oc, b[:], s_mm.n)
            k.wait("sync", s_free)
            k.wait("sync", s_sg)
            for dst, src, nch in ((q_out, qs, 8), (k_out, ks, 8),
                                  (v_out, vs, 8), (f_out, fs, 2)):
                k.dma("sync", dst.rearrange("(c p) t -> p c t", p=128)
                      [:, :, t0:t0 + TT],
                      src[:].rearrange("p (c t) -> p c t", c=nch), s_st)
        k.wait("sync", s_st)
        k.play()
    return nc


def build_l1attn_b():
    nc = bass.Bass("TRN2", target_bir_lowering=False,
                   detect_race_conditions=_RD)
    x_in = nc.dram_tensor("x_in", [D, TLOC], F32, kind="ExternalInput").ap()
    q_in = nc.dram_tensor("q_in", [D, TLOC], F32, kind="ExternalInput").ap()
    k_in = nc.dram_tensor("k_in", [D, TLOC], F32, kind="ExternalInput").ap()
    v_in = nc.dram_tensor("v_in", [D, TLOC], F32, kind="ExternalInput").ap()
    f_in = nc.dram_tensor("f_in", [256, TLOC], F32, kind="ExternalInput").ap()
    wo_in = nc.dram_tensor("wo", [D, D], F32, kind="ExternalInput").ap()
    gcol_in = nc.dram_tensor("gcol", [128, 8], F32, kind="ExternalInput").ap()
    iden_in = nc.dram_tensor("iden", [128, 128], F32, kind="ExternalInput").ap()
    tri_in = nc.dram_tensor("tri64", [64, 64], F32, kind="ExternalInput").ap()
    hkv_in = nc.dram_tensor("hkv_in", [64, 2048], F32,
                            kind="ExternalInput").ap()
    x_out = nc.dram_tensor("x_out", [D, TLOC], F32, kind="ExternalOutput").ap()
    hkv_out = nc.dram_tensor("hkv_out", [64, 2048], F32,
                             kind="ExternalOutput").ap()

    with ExitStack() as stack:
        k = KB(nc, stack)
        emit_consts(k, {"eps_rms": 1e-5})
        q_bf = k.sb("q_bf", [128, 8 * TLOC], BF16)
        k_bf = k.sb("k_bf", [128, 8 * TLOC], BF16)
        f_sb = k.sb("f_sb", [128, 2 * TLOC], F32)
        vch = k.sb("vch", [128, 2 * 8 * 64], BF16)
        o_fm = k.sb("o_fm", [128, 8 * TLOC], BF16)
        wg = k.sb("wg", [128, 8 * 1024], BF16)
        x_t = k.sb("x_t", [128, 8 * TT], F32)
        obuf = k.sb("obuf", [128, 2 * TT], F32)
        iden = k.sb("iden_s", [128, 128], F32)
        idbf = k.sb("idbf", [128, 128], BF16)
        tri64 = k.sb("tri64_s", [64, 64], F32)
        tri8 = k.sb("tri8", [64, 64], F32)
        gcol = k.sb("gcol_s", [128, 8], F32)
        hkm = k.sb("hkm", [64, 1024], F32)
        hvm = k.sb("hvm", [64, 1024], F32)
        hvbf = k.sb("hvbf", [64, 1024], BF16)
        hkfm = k.sb("hkfm", [128, 512], BF16)
        f_T = k.sb("f_T", [64, 256], F32)
        G_T = k.sb("G_T", [64, 256], F32)
        E8 = k.sb("E8", [64, 256], F32)
        en8 = k.sb("en8", [64, 256], F32)
        efs = k.sb("efs", [64, 256], F32)
        sT8 = k.sb("sT8", [64, 256], F32)
        sTbf = k.sb("sTbf", [64, 256], BF16)
        vT = k.sb("vT", [64, 1024], BF16)
        kT = k.sb("kT", [64, 1024], BF16)
        eglc = k.sb("eglc", [64, 4], F32)
        AMbf = k.sb("AMbf", [64, 64], BF16)
        lsc = k.sb("lsc", [64, 64], F32)
        rowm = k.sb("rowm", [64, 2], F32)
        psc = k.sb("psc", [64, 64], F32)
        pneT = k.sb("pneT", [64, 64], BF16)
        pnem = k.sb("pnem", [64, 64], BF16)
        smh = k.sb("smh", [64, 64], BF16)
        WtM = k.sb("WtM", [64, 64], BF16)
        sqsc = k.sb("sqsc", [64, 256], F32)
        msc = k.sb("msc", [64, 2], F32)
        onT = k.sb("onT", [64, 256], BF16)

        pA = k.ps("pA", [128, 512])
        pL = k.ps("pL", [128, 512])
        pW = k.ps("pW", [128, 512])
        pO = k.ps("pO", [128, 512])
        pUk = k.ps("pUk", [128, 512])
        pUv = k.ps("pUv", [128, 512])
        pT0 = k.ps("pT0", [128, 512])
        pTb = k.ps("pTb", [128, 1024], BF16)

        s_ld = k.sem("ld")
        s_vl = k.sem("v_ld")
        s_x2 = k.sem("x2_ld")
        s_mv = k.sem("mv")
        s_st = k.sem("st")
        s_tt = k.sem("tt")
        s_ta = k.sem("ta")
        s_tl = k.sem("tl")
        s_tc = k.sem("tc")
        s_tw = k.sem("tw")
        s_to = k.sem("to")
        s_tu = k.sem("tu")
        s_tm = k.sem("tm")
        s_v1 = k.sem("v1")
        s_v2 = k.sem("v2")
        s_v3 = k.sem("v3")
        s_hk = k.sem("hk")
        s_hv = k.sem("hv")
        s_eg = k.sem("eg")
        s_a1 = k.sem("a1")
        s_a2 = k.sem("a2")
        s_on = k.sem("on")
        s_mm = k.sem("mm")
        s_free = k.sem("free")

        k.dma("gpsimd", q_bf[:].rearrange("p (c t) -> p c t", c=8),
              q_in.rearrange("(c p) t -> p c t", p=128), s_ld)
        k.dma("gpsimd", k_bf[:].rearrange("p (c t) -> p c t", c=8),
              k_in.rearrange("(c p) t -> p c t", p=128), s_ld)
        k.dma("gpsimd", f_sb[:].rearrange("p (c t) -> p c t", c=2),
              f_in.rearrange("(c p) t -> p c t", p=128), s_ld)
        k.dma("gpsimd", wg[:].rearrange("p (c x) -> p c x", c=8),
              wo_in.rearrange("(c p) x -> p c x", p=128), s_ld)
        for dst, src in ((iden, iden_in), (tri64, tri_in), (gcol, gcol_in)):
            k.dma("gpsimd", dst[:], src, s_ld)
        k.dma("gpsimd", hkm[:].rearrange("p (h x) -> p h x", h=4),
              hkv_in.rearrange("p (h x) -> p h x", h=4)[:, :, 0:256], s_ld)
        k.dma("gpsimd", hvm[:].rearrange("p (h x) -> p h x", h=4),
              hkv_in.rearrange("p (h x) -> p h x", h=4)[:, :, 256:512], s_ld)
        for eng in ("tensor", "scalar", "vector", "sync"):
            k.wait(eng, s_ld)
        k.wait("vector", k.sems["consts"])
        k.chain("vector", [
            lambda e: e.tensor_scalar_mul(tri8[:], tri64[:], 0.125),
            lambda e: e.tensor_copy(idbf[:], iden[:]),
            lambda e: e.tensor_copy(hvbf[:], hvm[:]),
        ], s_v1, inc=3)
        for ic in range(8):
            k.op("vector", lambda e, ic=ic: e.tensor_scalar(
                wg[:, ic * 1024:(ic + 1) * 1024],
                wg[:, ic * 1024:(ic + 1) * 1024], gcol[:, ic:ic + 1],
                None, ALU.mult), s_v1)
        for h in range(4):
            for kc in range(2):
                k.wait("tensor", s_v1, 3)
                k.wait("tensor", s_v1)
                k.op("tensor", lambda e, h=h, kc=kc: e.transpose(
                    pT0[0:128, 384:448],
                    hkm[:, h * 256 + kc * 128:h * 256 + kc * 128 + 128],
                    iden[0:64, 0:64]), s_tt)
                k.wait("vector", s_tt)
                k.op("vector", lambda e, h=h, kc=kc: e.tensor_copy(
                    hkfm[:, h * 128 + kc * 64:h * 128 + kc * 64 + 64],
                    pT0[0:128, 384:448]), s_v1)

        for c in range(32):
            cs = slice(c * 64, (c + 1) * 64)
            pp = (c % 2) * 8 * 64
            k.wait("gpsimd", s_tt)
            k.dma("gpsimd", vch[:, pp:pp + 512]
                  .rearrange("p (ch t) -> p ch t", ch=8),
                  v_in.rearrange("(ch p) t -> p ch t", p=128)[:, :, cs], s_vl)
            k.wait("vector", s_tm)
            k.wait("vector", s_a1, s_a1.n)
            for fc in range(2):
                k.wait("tensor", s_v1)
                k.op("tensor", lambda e, fc=fc, c=c: e.transpose(
                    pT0[0:64, 0:128],
                    f_sb[:, fc * TLOC + c * 64:fc * TLOC + c * 64 + 64],
                    iden[:]), s_tt)
                k.wait("vector", s_tt)
                k.op("vector", lambda e, fc=fc: e.tensor_copy(
                    f_T[:, fc * 128:(fc + 1) * 128], pT0[0:64, 0:128]),
                    s_v1)
            for fc in range(2):
                k.wait("tensor", s_v1)
                k.wait("tensor", s_a1, s_a1.n)
                k.op("tensor", lambda e, fc=fc: e.matmul(
                    pT0[0:64, 128:256], tri8[:],
                    f_T[:, fc * 128:(fc + 1) * 128],
                    start=True, stop=True), s_tm)
                k.wait("vector", s_tm)
                k.op("vector", lambda e, fc=fc: e.tensor_copy(
                    G_T[:, fc * 128:(fc + 1) * 128],
                    pT0[0:64, 128:256]), s_v1)
            k.wait("scalar", s_v1)
            k.wait("scalar", s_eg, s_eg.n)
            k.wait("scalar", s_v3, s_v3.n)
            k.op("scalar", lambda e: e.activation(E8[:], G_T[:], AF.Exp),
                 s_a1)
            k.op("scalar", lambda e: e.activation(en8[:], G_T[:], AF.Exp,
                                                  scale=-1.0), s_a1)
            k.op("scalar", lambda e: e.activation(efs[:], f_T[:], AF.Exp,
                                                  scale=0.125), s_a1)
            k.wait("vector", s_a1)
            k.wait("vector", s_tc, s_tc.n)
            k.wait("vector", s_tu, s_tu.n)
            k.wait("vector", s_tt, s_tt.n)
            k.chain("vector", [
                lambda e: e.tensor_scalar(sT8[:], efs[:], -1.0, 1.0,
                                          ALU.mult, ALU.add),
                lambda e: e.tensor_mul(sT8[:], sT8[:], en8[:]),
                lambda e: e.tensor_copy(sTbf[:], sT8[:]),
            ], s_v3)
            k.wait("sync", s_a1, s_a1.n - 2)
            k.wait("sync", s_hk)
            k.wait("sync", s_hv)
            for h in range(4):
                k.dma("sync", eglc[:, h:h + 1],
                      E8[63:64, h * 64:(h + 1) * 64], s_mv)
            k.op("sync", lambda e: e.sem_inc(s_eg.h, 1))
            s_eg.n += 1
            k.wait("vector", s_vl)
            for h in range(4):
                for kc in range(2):
                    k.wait("tensor", s_v1)
                    k.wait("tensor", s_tu, s_tu.n)
                    k.op("tensor", lambda e, h=h, kc=kc, c=c: e.transpose(
                        pTb[0:64, 0:128],
                        k_bf[:, (2 * h + kc) * TLOC + c * 64:
                             (2 * h + kc) * TLOC + c * 64 + 64],
                        idbf[:]), s_tt)
                    k.wait("vector", s_tt)
                    k.op("vector", lambda e, h=h, kc=kc: e.tensor_copy(
                        kT[:, h * 256 + kc * 128:h * 256 + kc * 128 + 128],
                        pTb[0:64, 0:128]), s_v1)
                    k.wait("tensor", s_v1)
                    k.op("tensor", lambda e, h=h, kc=kc, pp=pp: e.transpose(
                        pTb[0:64, 128:256],
                        vch[:, pp + (2 * h + kc) * 64:
                            pp + (2 * h + kc) * 64 + 64],
                        idbf[:]), s_tt)
                    k.wait("vector", s_tt)
                    k.op("vector", lambda e, h=h, kc=kc: e.tensor_copy(
                        vT[:, h * 256 + kc * 128:h * 256 + kc * 128 + 128],
                        pTb[0:64, 128:256]), s_v1)

            for h in range(4):
                hs = slice(h * 64, (h + 1) * 64)
                k.wait("tensor", s_v2, s_v2.n)
                for kc in range(2):
                    k.op("tensor", lambda e, h=h, kc=kc, cs=cs: e.matmul(
                        pA[0:64, 0:64],
                        k_bf[:, (2 * h + kc) * TLOC + c * 64:
                             (2 * h + kc) * TLOC + c * 64 + 64],
                        q_bf[:, (2 * h + kc) * TLOC + c * 64:
                             (2 * h + kc) * TLOC + c * 64 + 64],
                        start=(kc == 0), stop=(kc == 1)),
                        sig=s_ta if kc == 1 else None)
                k.wait("vector", s_ta)
                k.op("vector", lambda e: e.tensor_mul(AMbf[:], pA[0:64, 0:64],
                                                      tri64[:]), s_v2)
                k.wait("tensor", s_v3, s_v3.n)
                k.wait("tensor", s_v1, s_v1.n)
                for kc in range(2):
                    k.op("tensor", lambda e, h=h, kc=kc, cs=cs: e.matmul(
                        pL[0:64, 0:64],
                        q_bf[:, (2 * h + kc) * TLOC + c * 64:
                             (2 * h + kc) * TLOC + c * 64 + 64],
                        hkfm[:, h * 128 + kc * 64:h * 128 + kc * 64 + 64],
                        start=(kc == 0), stop=(kc == 1)),
                        sig=s_tl if kc == 1 else None)
                k.wait("tensor", s_v2)
                k.op("tensor", lambda e, h=h: e.matmul(
                    pA[0:64, 64:128], AMbf[:], sTbf[:, h * 64:(h + 1) * 64],
                    start=True, stop=True), s_tc)
                k.wait("vector", s_tl)
                k.wait("vector", s_tc)
                k.chain("vector", [
                    lambda e: e.tensor_copy(lsc[:], pL[0:64, 0:64]),
                    lambda e: e.tensor_add(lsc[:], lsc[:], pA[0:64, 64:128]),
                    lambda e, hs=hs: e.tensor_mul(lsc[:], lsc[:], E8[:, hs]),
                    lambda e: e.tensor_reduce(rowm[:, 0:1], lsc[:], AX.X,
                                              ALU.max),
                    lambda e: e.tensor_scalar(rowm[:, 1:2], rowm[:, 0:1],
                                              -0.0625, None, ALU.mult),
                ], s_v3)
                k.wait("scalar", s_v3)
                k.op("scalar", lambda e: e.activation(
                    psc[:], lsc[:], AF.Exp, bias=rowm[:, 1:2], scale=0.0625),
                    s_a2)
                k.wait("vector", s_a2)
                k.chain("vector", [
                    lambda e: e.tensor_reduce(rowm[:, 0:1], psc[:], AX.X,
                                              ALU.add),
                    lambda e: e.reciprocal(rowm[:, 0:1], rowm[:, 0:1]),
                    lambda e: e.tensor_scalar(psc[:], psc[:], rowm[:, 0:1],
                                              None, ALU.mult),
                    lambda e, hs=hs: e.tensor_mul(pneT[:], psc[:], E8[:, hs]),
                ], s_v3)
                k.wait("tensor", s_v3)
                k.wait("tensor", s_to, s_to.n)
                k.wait("tensor", s_tw, s_tw.n)
                k.op("tensor", lambda e: e.transpose(
                    pTb[0:64, 256:320], pneT[:], idbf[0:64, 0:64]), s_tt)
                k.op("tensor", lambda e, hs=hs: e.transpose(
                    pT0[0:64, 256:320], sT8[:, hs], iden[0:64, 0:64]), s_tt)
                k.wait("vector", s_tt)
                k.chain("vector", [
                    lambda e: e.tensor_copy(pnem[:], pTb[0:64, 256:320]),
                    lambda e: e.tensor_copy(smh[:], pT0[0:64, 256:320]),
                ], s_v2, inc=2)
                k.wait("tensor", s_v2)
                k.op("tensor", lambda e: e.matmul(
                    pW[0:64, 0:64], smh[:], pnem[:], start=True, stop=True),
                    s_tw)
                k.wait("vector", s_tw)
                k.op("vector", lambda e: e.tensor_mul(WtM[:], pW[0:64, 0:64],
                                                      tri64[:]), s_v2)
                k.wait("tensor", s_v2)
                k.wait("tensor", s_on, s_on.n)
                k.wait("tensor", s_hv, s_hv.n)
                k.op("tensor", lambda e, h=h: e.matmul(
                    pO[0:64, 0:256], WtM[:], vT[:, h * 256:(h + 1) * 256],
                    start=True, stop=False))
                k.op("tensor", lambda e, h=h: e.matmul(
                    pO[0:64, 0:256], pnem[:], hvbf[:, h * 256:(h + 1) * 256],
                    start=False, stop=True), s_to)
                k.wait("scalar", s_to)
                k.op("scalar", lambda e: e.activation(sqsc[:], pO[0:64, 0:256],
                                                      AF.Square), s_a2)
                k.wait("vector", s_a2)
                k.chain("vector", [
                    lambda e: e.tensor_reduce(msc[:, 0:1], sqsc[:], AX.X,
                                              ALU.add),
                    lambda e: e.tensor_scalar(msc[:, 0:1], msc[:, 0:1],
                                              1.0 / 256.0, 1e-5,
                                              ALU.mult, ALU.add),
                ], s_v3)
                k.wait("scalar", s_v3)
                k.chain("scalar", [
                    lambda e: e.activation(msc[:, 0:1], msc[:, 0:1], AF.Ln),
                    lambda e: e.activation(msc[:, 1:2], msc[:, 0:1], AF.Exp,
                                           scale=-0.5),
                ], s_a2)
                k.wait("vector", s_a2)
                k.wait("vector", s_tt, s_tt.n)
                k.op("vector", lambda e: e.tensor_scalar(
                    onT[:], pO[0:64, 0:256], msc[:, 1:2], None, ALU.mult),
                    s_on)
                for kc in range(2):
                    k.wait("tensor", s_on)
                    k.wait("tensor", s_v1)
                    k.op("tensor", lambda e, kc=kc: e.transpose(
                        pTb[0:128, 320:384],
                        onT[:, kc * 128:(kc + 1) * 128], idbf[0:64, 0:64]),
                        s_tt)
                    k.wait("vector", s_tt)
                    k.op("vector", lambda e, h=h, kc=kc, c=c: e.tensor_copy(
                        o_fm[:, (2 * h + kc) * TLOC + c * 64:
                             (2 * h + kc) * TLOC + c * 64 + 64],
                        pTb[0:128, 320:384]), s_v1)
                k.wait("tensor", s_hk, s_hk.n)
                k.wait("tensor", s_hv, s_hv.n)
                k.wait("vector", s_mv)
                k.op("tensor", lambda e, hs=hs, h=h: e.matmul(
                    pUk[0:64, 0:256], sTbf[:, hs], kT[:, h * 256:(h + 1) * 256],
                    start=True, stop=True), s_tu)
                k.op("tensor", lambda e, hs=hs, h=h: e.matmul(
                    pUv[0:64, 0:256], sTbf[:, hs], vT[:, h * 256:(h + 1) * 256],
                    start=True, stop=True), s_tu)
                k.wait("vector", s_tu)
                k.chain("vector", [
                    lambda e, h=h: e.tensor_add(
                        hkm[:, h * 256:(h + 1) * 256],
                        hkm[:, h * 256:(h + 1) * 256], pUk[0:64, 0:256]),
                    lambda e, h=h: e.tensor_scalar(
                        hkm[:, h * 256:(h + 1) * 256],
                        hkm[:, h * 256:(h + 1) * 256], eglc[:, h:h + 1],
                        None, ALU.mult),
                ], s_hk)
                k.chain("vector", [
                    lambda e, h=h: e.tensor_add(
                        hvm[:, h * 256:(h + 1) * 256],
                        hvm[:, h * 256:(h + 1) * 256], pUv[0:64, 0:256]),
                    lambda e, h=h: e.tensor_scalar(
                        hvm[:, h * 256:(h + 1) * 256],
                        hvm[:, h * 256:(h + 1) * 256], eglc[:, h:h + 1],
                        None, ALU.mult),
                    lambda e, h=h: e.tensor_copy(hvbf[:, h * 256:(h + 1) * 256],
                                                 hvm[:, h * 256:(h + 1) * 256]),
                ], s_hv)
                for kc in range(2):
                    k.wait("tensor", s_hk)
                    k.wait("tensor", s_v1)
                    k.wait("tensor", s_tl, s_tl.n)
                    k.op("tensor", lambda e, h=h, kc=kc: e.transpose(
                        pT0[0:128, 320:384],
                        hkm[:, h * 256 + kc * 128:h * 256 + kc * 128 + 128],
                        iden[0:64, 0:64]), s_tt)
                    k.wait("vector", s_tt)
                    k.op("vector", lambda e, h=h, kc=kc: e.tensor_copy(
                        hkfm[:, h * 128 + kc * 64:h * 128 + kc * 64 + 64],
                        pT0[0:128, 320:384]), s_v1)

        for it in range(NTT):
            t0 = it * TT
            k.wait("gpsimd", s_free)
            k.dma("gpsimd", x_t[:].rearrange("p (ch t) -> p ch t", ch=8),
                  x_in.rearrange("(ch p) t -> p ch t", p=128)
                  [:, :, t0:t0 + TT], s_x2)
            for eng in ("tensor", "vector"):
                k.wait(eng, s_x2)
            k.wait("tensor", k.sems["v1"])
            for oc in range(8):
                pb = (pO if oc % 2 == 0 else pW)
                ob = obuf[:, (oc % 2) * TT:(oc % 2 + 1) * TT]
                k.wait("tensor", s_free, s_free.n - (1 if oc >= 2 else 0)
                       if oc >= 2 else 0)
                for ic in range(8):
                    k.op("tensor", lambda e, pb=pb, ic=ic, oc=oc, t0=t0:
                         e.matmul(pb[:, 0:TT],
                                  wg[:, ic * 1024 + oc * 128:
                                     ic * 1024 + oc * 128 + 128],
                                  o_fm[:, ic * TLOC + t0:ic * TLOC + t0 + TT],
                                  start=(ic == 0), stop=(ic == 7)),
                         sig=s_mm if ic == 7 else None)
                k.wait("vector", s_mm, s_mm.n)
                _v = 16 * (8 * it + oc - 1) if 8 * it + oc >= 1 else 0
                k.wait("vector", s_st, ((_v + 31) // 32) * 32)
                k.op("vector", lambda e, pb=pb, ob=ob, oc=oc: e.tensor_add(
                    ob[0:128, :], pb[:, 0:TT],
                    x_t[:, oc * TT:(oc + 1) * TT]), s_free)
                k.wait("sync", s_free)
                k.dma("sync", x_out.rearrange("(ch p) t -> p ch t", p=128)
                      [:, oc, t0:t0 + TT], ob[0:128, :], s_st)
        k.wait("sync", s_hk)
        k.wait("sync", s_hv)
        k.dma("sync", hkv_out.rearrange("p (h x) -> p h x", h=4)[:, :, 0:256],
              hkm[:].rearrange("p (h x) -> p h x", h=4), s_st)
        k.dma("sync", hkv_out.rearrange("p (h x) -> p h x", h=4)[:, :, 256:512],
              hvm[:].rearrange("p (h x) -> p h x", h=4), s_st)
        k.wait("sync", s_st)
        k.play()
    return nc


def host_l1attn_inputs(z):
    return dict(
        wq=np.ascontiguousarray(z["L1_attn_Wq"]),
        wk=np.ascontiguousarray(z["L1_attn_Wk"]),
        wv=np.ascontiguousarray(z["L1_attn_Wv"]),
        wf=np.ascontiguousarray(z["L1_attn_Wf"]),
        wo=np.ascontiguousarray(z["L1_attn_Wo"]),
        gcol=col128(np.tile(z["L1_attn_g_norm_w"], 4)),
        iden=np.eye(128, dtype=np.float32),
        tri64=np.triu(np.ones((64, 64), np.float32)))



def fm(x_tokens_d):
    return np.ascontiguousarray(x_tokens_d.T)


def col128(v):
    n = v.shape[0]
    return np.ascontiguousarray(v.reshape(n // 128, 128).T)


def host_l0attn_inputs(z):
    cw = z["L0_attn_conv_w"][:, 0, :]
    convw = np.zeros((128, 68), np.float32)
    for c in range(17):
        for j in range(4):
            convw[:, c * 4 + j] = cw[c * 128:(c + 1) * 128, j]
    tri64 = np.triu(np.ones((64, 64), np.float32))
    self_f = np.zeros((32, 2048), np.float32)
    for pg in range(2048):
        self_f[pg // 64, pg] = 1.0
    return dict(
        w_in=np.ascontiguousarray(z["L0_attn_W_in"].T),
        w_out=np.ascontiguousarray(z["L0_attn_W_out"].T),
        convw=convw, convb=col128(z["L0_attn_conv_b"])[:, :17],
        dtb=np.ascontiguousarray(z["L0_attn_dt_bias"].reshape(32, 1)),
        alog=np.ascontiguousarray(z["L0_attn_A_log"].reshape(32, 1)),
        dcol=col128(np.repeat(z["L0_attn_D"], 64)),
        nwcol=col128(z["L0_attn_norm_w"]),
        iden=np.eye(128, dtype=np.float32), tri64=tri64, self_f=self_f)



def build_l2ffn():
    nc = bass.Bass("TRN2", target_bir_lowering=False,
                   detect_race_conditions=_RD)
    x_in = nc.dram_tensor("x_in", [D, TLOC], F32, kind="ExternalInput").ap()
    w1t = nc.dram_tensor("w1t", [D, 4096], F32, kind="ExternalInput").ap()
    w2t = nc.dram_tensor("w2t", [4096, D], F32, kind="ExternalInput").ap()
    b1c = nc.dram_tensor("b1c", [128, 32], F32, kind="ExternalInput").ap()
    b2c = nc.dram_tensor("b2c", [128, 8], F32, kind="ExternalInput").ap()
    x_out = nc.dram_tensor("x_out", [D, TLOC], F32, kind="ExternalOutput").ap()

    FF = 4096
    FC = FF // 128

    with ExitStack() as stack:
        k = KB(nc, stack)
        emit_consts(k, {"eps_ln": 1e-5})

        x_t = k.sb("x_t", [128, DC * TT], F32)
        xn = k.sb("xn", [128, DC * TT], BF16)
        h_sb = k.sb("h_sb", [128, FC * TT], BF16)
        w1_sb = k.sb("w1_sb", [128, DC * FF], BF16)
        w2_sb = k.sb("w2_sb", [128, FC * D], BF16)
        b1_sb = k.sb("b1_sb", [128, 32], F32)
        b2_sb = k.sb("b2_sb", [128, 8], F32)


        ps_mm = [k.ps(f"mm{i}", [128, TT]) for i in range(2)]

        s_xt = k.sem("x_ld")
        s_b = k.sem("b_ld")
        s_w = k.sem("w_done")
        s_mm1 = k.sem("mm1")
        s_hf = k.sem("h_free")
        s_mm2 = k.sem("mm2")
        s_of = k.sem("o_free")
        s_st = k.sem("st")

        k.dma("sync", b1_sb[:], b1c, s_b)
        k.dma("sync", b2_sb[:], b2c, s_b)

        emit_ternarize(k, "w1", w1t, D, FF, w1_sb, s_w)
        emit_ternarize(k, "w2", w2t, FF, D, w2_sb, s_w)

        k.wait("tensor", s_w)
        k.wait("scalar", s_b)
        k.wait("vector", s_b)

        for it in range(NTT):
            t0 = it * TT
            k.wait("gpsimd", s_st, 16 * it)
            k.dma("gpsimd", x_t[:].rearrange("p (c t) -> p c t", c=DC),
                  x_in.rearrange("(c p) t -> p c t", p=128)[:, :, t0:t0 + TT],
                  s_xt)
            for eng in ("scalar", "tensor", "vector"):
                k.wait(eng, s_xt)

            def x_ap(c):
                return x_t[:, c * TT:(c + 1) * TT]

            def xn_ap(c):
                return xn[:, c * TT:(c + 1) * TT]

            s_ap = emit_ln_tile(k, x_ap, xn_ap,
                                pre_apply_waits=[(s_mm1, 32 * it)])
            k.wait("tensor", s_ap)
            k.wait("tensor", s_of)

            def ep_h(oc, ps, mmv, it=it):
                k.wait("scalar", s_mm1, mmv)
                k.wait("scalar", s_mm2, 8 * it)
                k.op("scalar", lambda e, oc=oc, ps=ps: e.activation(
                    h_sb[:, oc * TT:(oc + 1) * TT], ps, AF.Gelu,
                    bias=b1_sb[:, oc:oc + 1]), s_hf)

            emit_proj(k,
                      lambda ic, oc: w1_sb[:, ic * FF + oc * 128: ic * FF + oc * 128 + 128],
                      lambda ic: xn[:, ic * TT:(ic + 1) * TT],
                      ep_h, DC, FC, ps_mm, s_mm1, s_hf)

            k.wait("tensor", s_hf)

            def ep_o(oc, ps, mmv):
                ep_t = k.sb("vscratch", [128, TT], F32)
                k.wait("vector", s_mm2, mmv)
                k.wait("vector", s_of)
                k.chain("vector", [
                    lambda e, oc=oc, ps=ps: e.tensor_scalar(
                        ep_t[:], ps, b2_sb[:, oc:oc + 1], 0.5, ALU.add, ALU.mult),
                    lambda e, oc=oc, ep_t=ep_t: e.tensor_add(
                        x_t[:, oc * TT:(oc + 1) * TT],
                        x_t[:, oc * TT:(oc + 1) * TT], ep_t[:]),
                ], s_of)

            emit_proj(k,
                      lambda ic, oc: w2_sb[:, ic * D + oc * 128: ic * D + oc * 128 + 128],
                      lambda ic: h_sb[:, ic * TT:(ic + 1) * TT],
                      ep_o, FC, DC, ps_mm, s_mm2, s_of)

            k.wait("gpsimd", s_of)
            k.dma("gpsimd",
                  x_out.rearrange("(c p) t -> p c t", p=128)[:, :, t0:t0 + TT],
                  x_t[:].rearrange("p (c t) -> p c t", c=DC), s_st)

        k.wait("gpsimd", s_st)
        k.play()
    return nc


def host_l2ffn_inputs(x_full_fm, W1, b1, W2, b2):
    w1t = np.ascontiguousarray(W1.T)
    w2t = np.ascontiguousarray(W2.T)
    b1c = col128(b1)
    b2c = col128(b2)
    return [dict(x_in=x_full_fm[c], w1t=w1t, w2t=w2t, b1c=b1c, b2c=b2c)
            for c in range(N_CORES)]



def shard_fm(x):
    out = []
    for b in range(4):
        for h in range(2):
            out.append(fm(x[b, h * TLOC:(h + 1) * TLOC, :]))
    return out


def unshard_fm(shards):
    x = np.empty((4, 4096, 1024), np.float32)
    for b in range(4):
        for h in range(2):
            x[b, h * TLOC:(h + 1) * TLOC, :] = shards[b * 2 + h].T
    return x


def _l3_rotary_consts():
    DIM_HEAD, WINDOW, n = 64, 256, 4096
    half = np.arange(0, DIM_HEAD, 2, dtype=np.float64)
    inv_freq = 1.0 / 10000.0 ** (half / DIM_HEAD)
    t = np.arange(n, dtype=np.float64)
    freqs = t[:, None] * inv_freq[None, :]
    freqs = np.concatenate([freqs, freqs], -1)
    base_scale = (half + 0.4 * DIM_HEAD) / (1.4 * DIM_HEAD)
    power = (t - n // 2) / (WINDOW // 2)
    scale = base_scale[None, :] ** power[:, None]
    scale = np.concatenate([scale, scale], -1)
    cos, sin = np.cos(freqs), np.sin(freqs)
    sign = np.concatenate([-np.ones(32), np.ones(32)])
    out = {}
    for hf in range(2):
        qpos = np.arange(TLOC) + hf * TLOC
        kposc = np.clip(np.arange(-L3H, TLOC) + hf * TLOC, 0, n - 1)
        tq = np.concatenate([(cos[qpos] * scale[qpos]).T,
                             (sin[qpos] * scale[qpos] * sign[None, :]).T],
                            axis=1).astype(np.float32)
        tk = np.concatenate([(cos[kposc] / scale[kposc]).T,
                             (sin[kposc] / scale[kposc] * sign[None, :]).T],
                            axis=1).astype(np.float32)
        out[hf] = (np.ascontiguousarray(tq), np.ascontiguousarray(tk))
    return out


def _l3_halo_shard(zfull, b, half):
    lo = half * TLOC - L3H
    out = np.zeros((L3T, D), np.float32)
    src_lo = max(lo, 0)
    out[src_lo - lo:] = zfull[b, src_lo:half * TLOC + TLOC]
    return np.ascontiguousarray(out.T)


def kernel(**z):
    cores = list(range(N_CORES))

    def run(nc, in_maps):
        return run_bass_kernel_spmd(nc, in_maps, core_ids=cores)

    S = shard_fm(np.ascontiguousarray(z["L0_attn_u"], dtype=np.float32))

    gwt = np.ascontiguousarray(z["L0_ffn_gate_w"].T)
    gbr = z["L0_ffn_gate_b"].reshape(1, 8).astype(np.float32)
    ewt = np.ascontiguousarray(z["L0_ffn_expert_w"].transpose(0, 2, 1))
    ebc = np.ascontiguousarray(z["L0_ffn_expert_b"]
                               .reshape(8, 8, 128).transpose(2, 0, 1)
                               .reshape(128, 64))
    ncA = build_l0ffn(gate_only=True)
    ncB = build_l0ffn(gate_only=False)

    def l0ffn(S):
        rA = run(ncA, [dict(x_in=S[c], gwt=gwt, gbr=gbr) for c in cores])
        parts = np.stack([rA.results[c]["cnt_out"][0] for c in cores])
        counts_all = np.ascontiguousarray(
            parts.reshape(1, 64).astype(np.float32))
        rB = run(ncB, [dict(x_in=S[c], gwt=gwt, gbr=gbr, ewt=ewt, ebc=ebc,
                            counts_all=counts_all) for c in cores])
        return [rB.results[c]["x_out"] for c in cores]

    S = l0ffn(S)

    m_com = host_l0attn_inputs(z)
    nc_ma = build_l0attn_a()
    in_a = []
    for c in cores:
        half = c % 2
        xh = np.zeros((D, 2051), np.float32)
        if half == 0:
            xh[:, 3:] = S[c]
        else:
            xh[:, :3] = S[c - 1][:, TLOC - 3:]
            xh[:, 3:] = S[c]
        in_a.append(dict(x_in=np.ascontiguousarray(xh), w_in=m_com["w_in"],
                         convw=m_com["convw"], convb=m_com["convb"],
                         dtb=m_com["dtb"], alog=m_com["alog"]))
    rMA = run(nc_ma, in_a)
    nc_mb = build_l0attn_b()

    def run_mb(r_ins):
        return run(nc_mb, [dict(
            x_in=S[c], xbc=rMA.results[c]["xbc_out"],
            z_in=rMA.results[c]["z_out"], dtad=rMA.results[c]["dtad_out"],
            w_out=m_com["w_out"], dcol=m_com["dcol"], nwcol=m_com["nwcol"],
            iden=m_com["iden"], tri64=m_com["tri64"], self_f=m_com["self_f"],
            r_in=r_ins[c]) for c in cores])

    zeros64 = np.zeros((64, 2048), np.float32)
    rB1 = run_mb([zeros64] * 8)
    rB2 = run_mb([zeros64 if c % 2 == 0 else rB1.results[c - 1]["r_out"]
                  for c in cores])
    S = [rB2.results[c]["x_out"] for c in cores]

    S = l0ffn(S)

    rwt = np.concatenate([z["L1_ffn_rw1"], z["L1_ffn_sw1"][None]], 0)
    rw2t = np.concatenate([z["L1_ffn_rw2"], z["L1_ffn_sw2"][None]], 0)
    b1all = np.concatenate([z["L1_ffn_rb1"], z["L1_ffn_sb1"][None]], 0)
    rb1c = np.ascontiguousarray(
        b1all.reshape(9, 16, 128).transpose(2, 0, 1).reshape(128, 9 * 16))
    b2all = np.concatenate([z["L1_ffn_rb2"], z["L1_ffn_sb2"][None]], 0)
    rb2r = np.ascontiguousarray(b2all.reshape(1, 9 * 1024))
    rbr = z["L1_ffn_router_b"].reshape(1, 8).astype(np.float32)
    rwtr = np.ascontiguousarray(z["L1_ffn_router_w"])
    nc_l1f = build_l1ffn()

    def l1ffn(S):
        r = run(nc_l1f, [dict(x_in=S[c], rwt=rwt, rw2t=rw2t, rb1c=rb1c,
                              rb2r=rb2r, rbr=rbr, rwtr=rwtr) for c in cores])
        return [r.results[c]["x_out"] for c in cores]

    S = l1ffn(S)

    g_com = host_l1attn_inputs(z)
    nc_ga = build_l1attn_a()
    rGA = run(nc_ga, [dict(x_in=S[c], wq=g_com["wq"], wk=g_com["wk"],
                           wv=g_com["wv"], wf=g_com["wf"]) for c in cores])
    nc_gb = build_l1attn_b()

    def run_gb(h_ins):
        return run(nc_gb, [dict(
            x_in=S[c], q_in=rGA.results[c]["q_out"],
            k_in=rGA.results[c]["k_out"], v_in=rGA.results[c]["v_out"],
            f_in=rGA.results[c]["f_out"], wo=g_com["wo"], gcol=g_com["gcol"],
            iden=g_com["iden"], tri64=g_com["tri64"], hkv_in=h_ins[c])
            for c in cores])

    rG1 = run_gb([zeros64] * 8)
    rG2 = run_gb([zeros64 if c % 2 == 0 else rG1.results[c - 1]["hkv_out"]
                  for c in cores])
    S = [rG2.results[c]["x_out"] for c in cores]

    S = l1ffn(S)

    nc_l2f = build_l2ffn()

    def l2ffn(S):
        r = run(nc_l2f, host_l2ffn_inputs(S, z["L2_ffn_W1"], z["L2_ffn_b1"],
                                          z["L2_ffn_W2"], z["L2_ffn_b2"]))
        return [r.results[c]["x_out"] for c in cores]

    S = l2ffn(S)

    a_com = host_l2attn_inputs(z)
    nc_p1 = build_l2attn_p1()
    r1 = run(nc_p1, [dict(x_in=S[c], wk=a_com["wk"], projdn=a_com["projdn"])
                     for c in cores])
    kmax_all = np.ascontiguousarray(np.array(
        [[r1.results[c]["kmax_out"][0, 0] for c in cores]], np.float32))
    nc_p2 = build_l2attn_p2()
    r2 = run(nc_p2, [dict(x_in=S[c], wk=a_com["wk"], wv=a_com["wv"],
                          projdn=a_com["projdn"], kmax_all=kmax_all)
                     for c in cores])
    nc_p3a = build_l2attn_p3a()
    in3 = []
    for c in cores:
        p = c ^ 1
        kvp2 = np.concatenate([r2.results[c]["kvp_out"],
                               r2.results[p]["kvp_out"]], axis=1)
        sv2 = np.concatenate([r2.results[c]["sv_out"],
                              r2.results[p]["sv_out"]], axis=1)
        in3.append(dict(x_in=S[c], wq=a_com["wq"], projdn=a_com["projdn"],
                        wo_r=a_com["wo_r"], kvp2=np.ascontiguousarray(kvp2),
                        sv2=np.ascontiguousarray(sv2),
                        ln1gc=a_com["ln1gc"], ln1bc=a_com["ln1bc"]))
    r3 = run(nc_p3a, in3)
    nc_p3b = build_l2attn_p3b()
    r4 = run(nc_p3b, [dict(x_in=S[c], o1_in=r3.results[c]["out1"],
                           w1t=a_com["w1t"], b1c=a_com["b1c"],
                           w2t=a_com["w2t"], b2c=a_com["b2c"],
                           ln2gc=a_com["ln2gc"], ln2bc=a_com["ln2bc"])
             for c in cores])
    S = [r4.results[c]["x_out"] for c in cores]

    S = l2ffn(S)

    rot = _l3_rotary_consts()
    sposc = []
    qposr = []
    for c in cores:
        half = c % 2
        kp = (np.arange(-L3H, TLOC) + half * TLOC).astype(np.float32)
        kp[kp < 0] = 1e9
        sposc.append(np.ascontiguousarray(
            kp.reshape(L3ST, 128).T.astype(np.float32)))
        qposr.append(np.ascontiguousarray(
            (np.arange(TLOC) + half * TLOC).astype(np.float32).reshape(1, -1)))
    perm = np.concatenate([np.arange(32, 64), np.arange(0, 32)]).astype(int)
    Wqkv = z["L3_ffn_Wqkv"]; qsc = z["L3_ffn_q_scale"]
    ksc = z["L3_ffn_k_scale"]; Wo3 = z["L3_ffn_Wo"]
    lng = z["L3_ffn_ln_g"]; lnb = z["L3_ffn_ln_b"]
    W13 = z["L3_ffn_W1"]; b13 = z["L3_ffn_b1"]
    W23 = z["L3_ffn_W2"]; b23 = z["L3_ffn_b2"]
    nc_pre = build_l3pre()
    nc_a3 = build_l3ffn_attn()
    nc_b3 = build_l3ffn_ffn()
    nc_post = build_l3post()

    def l3ffn(S):
        xs_in = S
        r = run(nc_pre, [dict(x_in=S[c]) for c in cores])
        zfull = unshard_fm([r.results[c]["z_out"] for c in cores])
        for i in range(6):
            wqkv_i = np.ascontiguousarray(Wqkv[i])
            wo_i = np.ascontiguousarray(Wo3[i])
            qks2 = np.ascontiguousarray(np.stack(
                [qsc[i], qsc[i][perm], ksc[i], ksc[i][perm]],
                axis=1).astype(np.float32))
            in_a = []
            for c in cores:
                b, half = c // 2, c % 2
                tq, tk = rot[half]
                in_a.append(dict(zh_in=_l3_halo_shard(zfull, b, half),
                                 wqkv=wqkv_i, wo=wo_i, tq=tq, tk=tk,
                                 qks2=qks2, qposr=qposr[c], sposc=sposc[c]))
            ra = run(nc_a3, in_a)
            zfull = unshard_fm([ra.results[c]["za_out"] for c in cores])
            w1p = np.zeros((1024, DIMI2), np.float32)
            w1p[:, :2730] = W13[i][:, :2730]
            w1p[:, 2816:2816 + 2730] = W13[i][:, 2730:]
            b1a = np.zeros(2816, np.float32); b1a[:2730] = b13[i][:2730]
            b1g = np.zeros(2816, np.float32); b1g[:2730] = b13[i][2730:]
            w2p = np.zeros((DIMI2 // 2, 1024), np.float32)
            w2p[:2730] = W23[i]
            in_b = []
            for c in cores:
                b, half = c // 2, c % 2
                in_b.append(dict(zh_in=_l3_halo_shard(zfull, b, half),
                                 lngc=col128(lng[i]), lnbc=col128(lnb[i]),
                                 w1p=w1p, b1ac=col128(b1a), b1gc=col128(b1g),
                                 w2p=w2p, b2c=col128(b23[i])))
            rb = run(nc_b3, in_b)
            zfull = unshard_fm([rb.results[c]["z_out"] for c in cores])
        zsh = shard_fm(zfull)
        r = run(nc_post, [dict(x_in=xs_in[c], z_in=zsh[c]) for c in cores])
        return [r.results[c]["x_out"] for c in cores]

    S = l3ffn(S)

    at_com = dict(
        wdq_t=np.ascontiguousarray(z["L3_attn_W_dq"].T),
        wdq=np.ascontiguousarray(z["L3_attn_W_dq"]),
        wuq=np.ascontiguousarray(z["L3_attn_W_uq"]),
        wuq_tt=np.ascontiguousarray(z["L3_attn_W_uq"].T),
        wuk=np.ascontiguousarray(z["L3_attn_W_uk"]),
        wuv=np.ascontiguousarray(z["L3_attn_W_uv"]),
        wo_tt=np.ascontiguousarray(z["L3_attn_W_o"].T),
        wdkv_t=np.ascontiguousarray(z["L3_attn_W_dkv"].T))
    scolc = np.ascontiguousarray(
        (np.arange(4096).reshape(32, 128).T).astype(np.float32))
    nc_mla = build_l3attn(phase=99)
    in_m = []
    for c in cores:
        b, half = c // 2, c % 2
        in_m.append(dict(x_in=S[c], xk1_in=S[b * 2], xk2_in=S[b * 2 + 1],
                         qposr=qposr[c], scolc=scolc, **at_com))
    rM = run(nc_mla, in_m)
    S = [rM.results[c]["x_out"] for c in cores]

    S = l3ffn(S)

    return unshard_fm(S)
